# version 06
# baseline (speedup 1.0000x reference)
import sys
sys.path.insert(0, "/opt/trn_rl_repo")
import contextlib
import math

import numpy as np
import ml_dtypes
import concourse.bass as bass
import concourse.mybir as mybir
from concourse.bass_utils import run_bass_kernel_spmd

F32 = mybir.dt.float32
BF16 = mybir.dt.bfloat16
I32 = mybir.dt.int32
AF = mybir.ActivationFunctionType
ALU = mybir.AluOpType
AX = mybir.AxisListType.X
bfdt = ml_dtypes.bfloat16

T = 8192
D = 2048
DLAT = 1024
NH = 16
DH = 128
HID = 5632
NSLOT = 8
SCALE = DH ** -0.5
TWO_PI = 2.0 * math.pi
C1 = 6.28125
C2 = TWO_PI - C1
INV_2PI = 1.0 / TWO_PI
LN1E4 = math.log(10000.0)

QUADS = [[7, 6, 1, 0], [5, 4, 3, 2]]
QCOL = {s: (qi, ci) for qi, qs in enumerate(QUADS) for ci, s in enumerate(qs)}

DMA_CH = ("SH", "SS", "SG")


class Buf:
    __slots__ = ("w", "rs", "name", "acc")

    def __init__(self, name=""):
        self.w = {}
        self.rs = []
        self.name = name
        self.acc = False


class FakeInstr:
    __slots__ = ("rec",)

    def __init__(self, rec):
        self.rec = rec

    def then_inc(self, sem, n):
        self.rec[3] = (sem, n)
        return self


class EngProxy:

    def __init__(self, q):
        self._q = q

    def __getattr__(self, name):
        def call(*args, **kwargs):
            rec = [name, args, kwargs, None]
            self._q.append(rec)
            return FakeInstr(rec)
        return call


class FakeBlock:
    def __init__(self):
        self.q = {"T": [], "V": [], "A": [], "G": [], "SY": []}
        self.prox = {k_: EngProxy(v) for k_, v in self.q.items()}

    def _dec(self, ch):
        def deco(fn):
            fn(self.prox[ch])
            return None
        return deco

    @property
    def tensor(self):
        return self._dec("T")

    @property
    def vector(self):
        return self._dec("V")

    @property
    def scalar(self):
        return self._dec("A")

    @property
    def gpsimd(self):
        return self._dec("G")

    @property
    def sync(self):
        return self._dec("SY")

    def flush(self, blk):
        for ch, eng_attr in [("T", "tensor"), ("V", "vector"), ("A", "scalar"),
                             ("G", "gpsimd"), ("SY", "sync")]:
            recs = self.q[ch]
            if not recs:
                continue

            def body(eng, recs=recs):
                for name, args, kwargs, inc in recs:
                    ins = getattr(eng, name)(*args, **kwargs)
                    if inc is not None:
                        ins.then_inc(*inc)
            getattr(blk, eng_attr)(body)


class B:
    def __init__(self, nc, es):
        self.nc = nc
        self.es = es
        self.sems = {}
        self.cnt = {}
        self.seen = {}

    def sem(self, name):
        if name not in self.sems:
            self.sems[name] = self.es.enter_context(self.nc.semaphore(name))
            self.cnt[name] = 0
        return self.sems[name]

    @staticmethod
    def ekey(ch):
        return {"SH": "SY", "SS": "A", "SG": "G"}.get(ch, ch)

    def pre(self, eng, ch, reads=(), writes=()):
        ek = self.ekey(ch)
        seen = self.seen.setdefault(ek, {})
        need = {}
        for b in reads:
            for c, v in b.w.items():
                need[c] = max(need.get(c, 0), v)
        for b in writes:
            if not b.acc:
                for c, v in b.w.items():
                    need[c] = max(need.get(c, 0), v)
            for c, v in b.rs:
                need[c] = max(need.get(c, 0), v)
        for c in sorted(need):
            v = need[c]
            if c == "T" and ch == "T":
                continue
            if seen.get(c, 0) >= v:
                continue
            eng.wait_ge(self.sem(c), v)
            seen[c] = v

    def post(self, instr, ch, reads=(), writes=()):
        if ch in DMA_CH:
            key = None
            for b in writes:
                if not b.acc:
                    key = b
                    break
            if key is None:
                for b in reads:
                    if not b.acc:
                        key = b
                        break
            sname = f"D_{ch}_{key.name if key is not None else 'misc'}"
            n = 16
        else:
            sname = ch
            n = 1
        s = self.sem(sname)
        self.cnt[sname] += n
        instr.then_inc(s, n)
        m = (sname, self.cnt[sname])
        for b in writes:
            if b.acc:
                b.w[m[0]] = m[1]
            else:
                b.w = {m[0]: m[1]}
                b.rs = []
        for b in reads:
            b.rs.append(m)
        return m

    def barrier(self, blk):
        chans = [c for c in self.cnt if self.cnt[c] > 0]

        def wait_all(eng, ek):
            seen = self.seen.setdefault(ek, {})
            for c in chans:
                v = self.cnt[c]
                if seen.get(c, 0) >= v:
                    continue
                eng.wait_ge(self.sem(c), v)
                seen[c] = v

        @blk.tensor
        def _(t):
            wait_all(t, "T")

        @blk.vector
        def _(v):
            wait_all(v, "V")

        @blk.scalar
        def _(s):
            wait_all(s, "A")

        @blk.gpsimd
        def _(g):
            wait_all(g, "G")

        @blk.sync
        def _(sy):
            wait_all(sy, "SY")

    def sb(self, name, shape, dtype):
        return self.es.enter_context(self.nc.sbuf_tensor(name, shape, dtype))

    def ps(self, name, shape, dtype):
        return self.es.enter_context(self.nc.psum_tensor(name, shape, dtype))


def build(dev=False, upto=99):
    nc = bass.Bass("TRN2", target_bir_lowering=False, debug=False)
    sk = "ExternalOutput" if dev else "Internal"

    io = {}
    io["xf"] = nc.dram_tensor("xf", [T, D], F32, kind="ExternalInput").ap()
    io["xo"] = nc.dram_tensor("xo", [NSLOT, 128, D], F32, kind="ExternalInput").ap()
    io["qpos_c"] = nc.dram_tensor("qpos_c", [128, NSLOT], F32, kind="ExternalInput").ap()
    io["qrel_c"] = nc.dram_tensor("qrel_c", [128, 1], F32, kind="ExternalInput").ap()
    io["wq"] = nc.dram_tensor("wq", [D, D], F32, kind="ExternalInput").ap()
    io["wdn"] = nc.dram_tensor("wdn", [D, DLAT], F32, kind="ExternalInput").ap()
    io["wup"] = nc.dram_tensor("wup", [DLAT, 2 * D], F32, kind="ExternalInput").ap()
    io["wo_"] = nc.dram_tensor("wo_", [D, D], F32, kind="ExternalInput").ap()
    io["w1"] = nc.dram_tensor("w1", [HID, D], F32, kind="ExternalInput").ap()
    io["w2"] = nc.dram_tensor("w2", [HID, D], F32, kind="ExternalInput").ap()
    io["w3"] = nc.dram_tensor("w3", [D, HID], F32, kind="ExternalInput").ap()
    io["ident"] = nc.dram_tensor("ident", [128, 128], BF16, kind="ExternalInput").ap()
    io["out"] = nc.dram_tensor("out", [NSLOT, 128, D], F32, kind="ExternalOutput").ap()
    io["ckv_d"] = nc.dram_tensor("ckv_d", [8, 128, T], BF16, kind=sk).ap()
    io["w1qn"] = nc.dram_tensor("w1qn", [HID, D], BF16, kind="Internal").ap()
    io["w2qn"] = nc.dram_tensor("w2qn", [HID, D], BF16, kind="Internal").ap()
    io["w3qn"] = nc.dram_tensor("w3qn", [D, HID], BF16, kind="Internal").ap()
    io["hq_d"] = nc.dram_tensor("hq_d", [NSLOT, 128, HID], F32, kind=sk).ap()
    io["hqt_d"] = nc.dram_tensor("hqt_d", [44, 128, NSLOT * 128], BF16, kind=sk).ap()
    io["x1_d"] = nc.dram_tensor("x1_d", [NSLOT, 128, D], F32, kind=sk).ap()
    for nm in ["dbg_q", "dbg_kf", "dbg_vt", "dbg_at"]:
        io[nm] = nc.dram_tensor(nm, [128, {"dbg_q": NSLOT * D, "dbg_kf": T, "dbg_vt": T,
                                           "dbg_at": NH * 1024}[nm]], BF16,
                                kind="ExternalOutput").ap() if dev else None
    io["dbg_x1"] = None

    with contextlib.ExitStack() as es:
        k = B(nc, es)
        e = dict(io)
        e["bufs"] = {}

        def bufs(*names):
            return [e["bufs"].setdefault(n, Buf(n)) for n in names]

        e["B"] = bufs
        for nm in ["ckv_d", "hq_d", "hqt_d", "x1_d", "w1qn", "w2qn", "w3qn", "out_d"]:
            bufs(nm)[0].acc = True

        e["ident_s"] = k.sb("ident_s", [128, 128], BF16)
        e["qposc"] = k.sb("qposc", [128, NSLOT], F32)
        e["qrelc"] = k.sb("qrelc", [128, 1], F32)
        e["IVF"] = k.sb("IVF", [128, 64], F32)
        e["onescol"] = k.sb("onescol", [128, 1], BF16)
        e["onescol_f"] = k.sb("onescol_f", [128, 1], F32)
        e["onesrow_f"] = k.sb("onesrow_f", [1, 128], F32)
        e["onesrow_b"] = k.sb("onesrow_b", [1, 128], BF16)
        e["identf"] = k.sb("identf", [128, 128], F32)
        e["thr"] = [k.sb(f"thrw{i}", [128, 1], F32) for i in range(3)]
        e["nthr"] = [k.sb(f"nthrw{i}", [128, 1], F32) for i in range(3)]
        e["mwb"] = [k.sb(f"mwb{i}", [128, 1], F32) for i in range(3)]

        with nc.Block() as real_blk:
            blk = FakeBlock()
            with contextlib.ExitStack() as es_t:
                sbt = lambda n, s, d: es_t.enter_context(nc.sbuf_tensor(n, s, d))
                e["CC"] = sbt("CC", [128, 64 * 128], BF16)
                e["SS"] = sbt("SS_t", [128, 64 * 128], BF16)
                e["bmq"] = sbt("bmq", [128, 1024], BF16)
                e["bmk"] = sbt("bmk", [128, 1024], BF16)
                e["qrot"] = sbt("qrot", [128, NSLOT * D], BF16)
                p0(k, blk, e)
                k.barrier(blk)
                if upto >= 1:
                    p0w(k, blk, e)
                    k.barrier(blk)
                if upto >= 2:
                    p1(k, blk, e)
                    k.barrier(blk)
                e["attn"] = sbt("attn", [128, NH * 1024], BF16)
                if upto >= 3:
                    p2(k, blk, e)
                    k.barrier(blk)
                if upto >= 4:
                    p3(k, blk, e)
                    k.barrier(blk)
            if upto >= 5:
                p4(k, blk, e)

            @blk.scalar
            def _(s):
                s.wait_ge(k.sem("SS"), k.cnt["SS"])

            @blk.sync
            def _(sy):
                sy.wait_ge(k.sem("SH"), k.cnt["SH"])

            @blk.gpsimd
            def _(g):
                g.wait_ge(k.sem("SG"), k.cnt["SG"])

            blk.flush(real_blk)
    return nc


def sincos(k, blk, e, ang, shape, nbuf, t1, t2, outbuf, shift, extra_reads=()):
    (b_ang, b_n, b_t1, b_t2, b_out) = e["B"](*[f"sc_{id(x)}" for x in (ang, nbuf, t1, t2, outbuf)])

    @blk.vector
    def _(v):
        k.pre(v, "V", reads=[b_ang, *extra_reads], writes=[b_t1])
        if shift != 0.0:
            k.post(v.tensor_scalar(t1[:], ang, shift, None, ALU.add), "V", [b_ang], [b_t1])
        else:
            k.post(v.tensor_copy(t1[:], ang), "V", [b_ang], [b_t1])

    @blk.scalar
    def _(s):
        k.pre(s, "A", reads=[b_t1], writes=[b_n])
        k.post(s.activation(nbuf[:], t1[:], AF.Identity, scale=INV_2PI), "A", [b_t1], [b_n])

    @blk.vector
    def _(v):
        k.pre(v, "V", reads=[b_n, b_t1], writes=[b_t2])
        k.post(v.scalar_tensor_tensor(t2[:], nbuf[:], -C1, t1[:], ALU.mult, ALU.add),
               "V", [b_n, b_t1], [b_t2])
        k.pre(v, "V", reads=[b_n, b_t2], writes=[b_t1])
        k.post(v.scalar_tensor_tensor(t1[:], nbuf[:], -C2, t2[:], ALU.mult, ALU.add),
               "V", [b_n, b_t2], [b_t1])
        k.pre(v, "V", reads=[b_t1], writes=[b_t2])
        k.post(v.tensor_scalar(t2[:], t1[:], math.pi, TWO_PI, ALU.is_gt, ALU.mult),
               "V", [b_t1], [b_t2])
        k.pre(v, "V", reads=[b_t1, b_t2], writes=[b_t1])
        k.post(v.tensor_tensor(t1[:], t1[:], t2[:], ALU.subtract), "V", [b_t1, b_t2], [b_t1])
        k.pre(v, "V", reads=[b_t1], writes=[b_t2])
        k.post(v.tensor_scalar(t2[:], t1[:], -math.pi, TWO_PI, ALU.is_lt, ALU.mult),
               "V", [b_t1], [b_t2])
        k.pre(v, "V", reads=[b_t1, b_t2], writes=[b_t1])
        k.post(v.tensor_tensor(t1[:], t1[:], t2[:], ALU.add), "V", [b_t1, b_t2], [b_t1])

    @blk.scalar
    def _(s):
        k.pre(s, "A", reads=[b_t1], writes=[b_out])
        k.post(s.activation(outbuf[:], t1[:], AF.Sin), "A", [b_t1], [b_out])
    return b_out


def p0(k, blk, e):
    nc = k.nc
    B = e["B"]
    es0 = contextlib.ExitStack()
    sb = lambda n, s, d: es0.enter_context(nc.sbuf_tensor(n, s, d))
    POS = sb("p0_pos", [128, 64], F32)
    J = sb("p0_j", [128, 64], F32)
    ANG = sb("p0_ang", [128, 4096], F32)
    NN = sb("p0_n", [128, 4096], I32)
    S1 = sb("p0_s1", [128, 4096], F32)
    S2 = sb("p0_s2", [128, 4096], F32)
    SINX = sb("p0_sin", [128, 4096], F32)
    COSX = sb("p0_cos", [128, 4096], F32)
    KJ = sb("p0_kj", [128, 1024], F32)
    e["_es0"] = es0
    (b_id, b_qp, b_qr, b_pos, b_j, b_kj, b_ivf, b_ang, b_cc, b_ss, b_bmq, b_ones) = B(
        "ident", "qposc", "qrelc", "pos", "j", "kj", "ivf", "ang", "CC", "SS", "bmq", "ones")

    @blk.sync
    def _(sy):
        k.post(sy.dma_start(out=e["ident_s"][:], in_=e["ident"][:, :]), "SH", [], [b_id])
        k.post(sy.dma_start(out=e["qposc"][:], in_=e["qpos_c"][:, :]), "SH", [], [b_qp])
        k.post(sy.dma_start(out=e["qrelc"][:], in_=e["qrel_c"][:, :]), "SH", [], [b_qr])

    @blk.gpsimd
    def _(g):
        k.post(g.iota(POS[:], pattern=[[128, 64]], base=0, channel_multiplier=1,
                      allow_small_or_imprecise_dtypes=True), "G", [], [b_pos])
        k.post(g.iota(J[:], pattern=[[1, 64]], base=0, channel_multiplier=0,
                      allow_small_or_imprecise_dtypes=True), "G", [], [b_j])
        k.post(g.iota(KJ[:], pattern=[[1, 1024]], base=0, channel_multiplier=0,
                      allow_small_or_imprecise_dtypes=True), "G", [], [b_kj])

    @blk.scalar
    def _(s):
        k.pre(s, "A", reads=[b_j], writes=[b_ivf])
        k.post(s.activation(e["IVF"][:], J[:], AF.Exp, scale=-LN1E4 / 64.0), "A", [b_j], [b_ivf])

    @blk.vector
    def _(v):
        k.post(v.memset(e["onescol"][:], 1.0), "V", [], [b_ones])
        k.post(v.memset(e["onescol_f"][:], 1.0), "V", [], [e["B"]("onescol_f")[0]])
        k.post(v.memset(e["onesrow_f"][:], 1.0), "V", [], [e["B"]("onesrow_f")[0]])
        k.post(v.memset(e["onesrow_b"][:], 1.0), "V", [], [e["B"]("onesrow_b")[0]])
        k.pre(v, "V", reads=[b_id], writes=[e["B"]("identf")[0]])
        k.post(v.tensor_copy(e["identf"][:], e["ident_s"][:]), "V", [b_id],
               [e["B"]("identf")[0]])
        k.pre(v, "V", reads=[b_pos, b_ivf], writes=[b_ang])
        k.post(v.tensor_tensor(
            ANG[:],
            POS[:].rearrange("p (kt one) -> p kt one", one=1).broadcast_to([128, 64, 64]),
            e["IVF"][:].rearrange("p (one j) -> p one j", one=1).broadcast_to([128, 64, 64]),
            ALU.mult), "V", [b_pos, b_ivf], [b_ang])

    b_sin = sincos(k, blk, e, ANG[:], None, NN, S1, S2, SINX, 0.0, extra_reads=[b_ang])
    b_cos = sincos(k, blk, e, ANG[:], None, NN, S1, S2, COSX, math.pi / 2.0, extra_reads=[b_ang])

    @blk.vector
    def _(v):
        cview = COSX[:].rearrange("p (kt j) -> p kt j", j=64)
        sview = SINX[:].rearrange("p (kt j) -> p kt j", j=64)
        cc = e["CC"][:].rearrange("p (kt j) -> p kt j", j=128)
        ss = e["SS"][:].rearrange("p (kt j) -> p kt j", j=128)
        k.pre(v, "V", reads=[b_sin, b_cos], writes=[b_cc, b_ss])
        k.post(v.tensor_copy(cc[:, :, 0:128:2], cview), "V", [b_cos], [b_cc])
        k.post(v.tensor_copy(cc[:, :, 1:128:2], cview), "V", [b_cos], [b_cc])
        k.post(v.tensor_copy(ss[:, :, 1:128:2], sview), "V", [b_sin], [b_ss])
        k.post(v.tensor_scalar(ss[:, :, 0:128:2], sview, -1.0, None, ALU.mult),
               "V", [b_sin], [b_ss])
        k.pre(v, "V", reads=[b_kj, b_qr], writes=[b_bmq])
        k.post(v.tensor_scalar(e["bmq"][:], KJ[:], e["qrelc"][:, 0:1], None, ALU.is_le),
               "V", [b_kj, b_qr], [b_bmq])
    ICOL = sb("p0_icol", [128, 1], F32)
    TC0 = sb("p0_tc0", [128, 1], F32)
    TCJ = sb("p0_tcj", [128, 8], F32)
    b_icol, b_tc0, b_tcj, b_bmk = B("p0_icol", "p0_tc0", "p0_tcj", "bmk")

    @blk.gpsimd
    def _(g):
        k.post(g.iota(ICOL[:], pattern=[[1, 1]], base=0, channel_multiplier=1,
                      allow_small_or_imprecise_dtypes=True), "G", [], [b_icol])

    @blk.vector
    def _(v):
        k.pre(v, "V", reads=[b_icol, b_qr], writes=[b_tc0])
        k.post(v.scalar_tensor_tensor(TC0[:], ICOL[:], 2.0, e["qrelc"][:], ALU.mult,
                                      ALU.subtract), "V", [b_icol, b_qr], [b_tc0])
        for jm in range(8):
            k.pre(v, "V", reads=[b_tc0], writes=[b_tcj])
            k.post(v.tensor_scalar(TCJ[:, jm:jm + 1], TC0[:], float(jm * 128), None,
                                   ALU.add), "V", [b_tc0], [b_tcj])
        for jm in range(8):
            k.pre(v, "V", reads=[b_tcj, b_kj], writes=[b_bmk])
            k.post(v.tensor_scalar(e["bmk"][:, jm * 128:(jm + 1) * 128], KJ[:, 0:128],
                                   TCJ[:, jm:jm + 1], None, ALU.is_ge),
                   "V", [b_tcj, b_kj], [b_bmk])
    e.pop("_es0").close()


def p0w(k, blk, e):
    nc = k.nc
    B = e["B"]
    CH = 4096
    nchunk = HID * D // (128 * CH)
    wl = [(e["w1"], None), (e["w2"], None), (e["w3"], None)]

    esw = contextlib.ExitStack()
    e["_esw"] = esw
    sbw = lambda n, s, d: esw.enter_context(nc.sbuf_tensor(n, s, d))
    wbuf = [sbw(f"p0w_buf{i}", [128, CH], F32) for i in range(2)]
    abscr = [sbw(f"p0w_abs{i}", [128, CH], F32) for i in range(2)]
    acc = sbw("p0w_acc", [128, 3 * nchunk + 3], F32)
    accr = sbw("p0w_accr", [128, nchunk], F32)
    csb = sbw("p0w_csb", [128, 4], F32)
    tot = sbw("p0w_tot", [1, 16], F32)
    ps_tot = esw.enter_context(nc.psum_tensor("p0w_pstot", [1, 128], F32))

    b_wb = B("p0w_wb0", "p0w_wb1")
    b_scr = B("p0w_scr0", "p0w_scr1")
    (b_acc, b_csb, b_tot, b_pst, b_ones) = B("p0w_acc", "p0w_csb", "p0w_tot", "p0w_pst", "ones")

    for wi, (wsrc, wdst) in enumerate(wl):
        wflat = wsrc.rearrange("a b -> (a b)").rearrange("(r p c) -> r p c", p=128, c=CH)
        for ci in range(nchunk):
            pa = ci % 2

            @blk.sync
            def _(sy, pa=pa, wflat=wflat, ci=ci):
                k.pre(sy, "SH", writes=[b_wb[pa]])
                k.post(sy.dma_start(out=wbuf[pa][:], in_=wflat[ci, :, :]), "SH", [], [b_wb[pa]])

            @blk.scalar
            def _(s, pa=pa, wi=wi, ci=ci):
                k.pre(s, "A", reads=[b_wb[pa]], writes=[b_scr[pa], b_acc])
                k.post(s.activation(abscr[pa][:], wbuf[pa][:], AF.Abs,
                                    accum_out=acc[:, wi * nchunk + ci: wi * nchunk + ci + 1]),
                       "A", [b_wb[pa]], [b_scr[pa], b_acc])


    CL2 = 64.0
    CL3 = CL2 * nchunk
    b_onesf = B("onescol_f")[0]

    @blk.vector
    def _(v):
        k.pre(v, "V", reads=[b_acc], writes=[b_acc, b_csb])
        for wi in range(3):
            k.post(v.tensor_scalar(accr[:],
                                   acc[:, wi * nchunk:(wi + 1) * nchunk], -CL2, None,
                                   ALU.add), "V", [b_acc], [b_acc])
            v.drain()
            k.post(v.reduce_sum(acc[:, 3 * nchunk + wi: 3 * nchunk + wi + 1],
                                accr[:], axis=AX),
                   "V", [b_acc], [b_acc])
            v.drain()
        k.pre(v, "V", reads=[b_acc], writes=[b_csb])
        k.post(v.tensor_copy(csb[:, 0:3], acc[:, 3 * nchunk:3 * nchunk + 3]),
               "V", [b_acc], [b_csb])

    @blk.tensor
    def _(t):
        k.pre(t, "T", reads=[b_csb, b_onesf], writes=[b_pst])
        for wi in range(3):
            k.post(t.matmul(ps_tot[0:1, wi * 8:wi * 8 + 1], csb[:, wi:wi + 1],
                            e["onescol_f"][:], start=True, stop=True, skip_group_check=True),
                   "T", [b_csb, b_onesf], [b_pst])

    @blk.vector
    def _(v):
        k.pre(v, "V", reads=[b_pst], writes=[b_tot])
        for wi in range(3):
            k.post(v.tensor_scalar(tot[0:1, wi:wi + 1], ps_tot[0:1, wi * 8:wi * 8 + 1],
                                   float(128.0 * CL3), None, ALU.add),
                   "V", [b_pst], [b_tot])

    b_mwb = B("mwb0", "mwb1", "mwb2")
    b_thr = B("thr0", "thr1", "thr2")
    b_nth = B("nthr0", "nthr1", "nthr2")
    b_orf = B("onesrow_f")[0]
    b_psbc = B("p0w_psbc")[0]
    ps_bc = esw.enter_context(nc.psum_tensor("p0w_psbc", [128, 4], F32))

    @blk.tensor
    def _(t):
        k.pre(t, "T", reads=[b_tot, b_orf], writes=[b_psbc])
        for wi in range(3):
            k.post(t.matmul(ps_bc[:, wi:wi + 1], e["onesrow_f"][0:1, :],
                            tot[0:1, wi:wi + 1], start=(wi == 0), stop=(wi == 2),
                            skip_group_check=True), "T", [b_tot, b_orf], [b_psbc])

    @blk.scalar
    def _(s):
        k.pre(s, "A", reads=[b_psbc], writes=list(b_mwb))
        for wi in range(3):
            k.post(s.copy(e["mwb"][wi][:], ps_bc[:, wi:wi + 1]), "A", [b_psbc], [b_mwb[wi]])

    CH2 = 512
    nch2 = HID * D // (128 * CH2)
    e["p0w2_sched"] = []
    for wi, (wsrc, qdst) in enumerate([(e["w1"], e["w1qn"]), (e["w2"], e["w2qn"]),
                                       (e["w3"], e["w3qn"])]):
        wf = wsrc.rearrange("a b -> (a b)").rearrange("(r p c) -> r p c", p=128, c=CH2)
        qf = qdst.rearrange("a b -> (a b)").rearrange("(r p c) -> r p c", p=128, c=CH2)
        for ci in range(nch2):
            e["p0w2_sched"].append((wf, qf, ci, wi))

    @blk.vector
    def _(v):
        numel = float(HID * D)
        for wi in range(3):
            k.pre(v, "V", reads=[b_mwb[wi]], writes=[b_mwb[wi]])
            k.post(v.tensor_scalar(e["mwb"][wi][:], e["mwb"][wi][:], 1.0 / numel, 1e-5,
                                   ALU.mult, ALU.max), "V", [b_mwb[wi]], [b_mwb[wi]])
            k.pre(v, "V", reads=[b_mwb[wi]], writes=[b_thr[wi], b_nth[wi]])
            k.post(v.tensor_scalar(e["thr"][wi][:], e["mwb"][wi][:], 0.5, None, ALU.mult),
                   "V", [b_mwb[wi]], [b_thr[wi]])
            k.post(v.tensor_scalar(e["nthr"][wi][:], e["mwb"][wi][:], -0.5, None, ALU.mult),
                   "V", [b_mwb[wi]], [b_nth[wi]])
    e.pop("_esw").close()


def p0w2_emit(k, blk, e, frac_lo, frac_hi, bufs2):
    B = e["B"]
    sched = e["p0w2_sched"]
    n = len(sched)
    lo = int(round(frac_lo * n))
    hi = int(round(frac_hi * n))
    wc2, pos2, q2 = bufs2
    b_wc2 = B("p2_wc2_0", "p2_wc2_1")
    b_pos2 = B("p2_pos2")[0]
    b_q2 = B("p2_q2_0", "p2_q2_1")
    b_thr = B("thr0", "thr1", "thr2")
    b_nth = B("nthr0", "nthr1", "nthr2")
    b_qn = B("w1qn", "w2qn", "w3qn")
    for idx in range(lo, hi):
        wf, qf, ci, wi = sched[idx]
        pa = idx % 2

        @blk.sync
        def _(sy, wf=wf, ci=ci, pa=pa):
            k.pre(sy, "SH", writes=[b_wc2[pa]])
            k.post(sy.dma_start(out=wc2[pa][:], in_=wf[ci, :, :]), "SH", [], [b_wc2[pa]])

        @blk.vector
        def _(v, pa=pa, wi=wi):
            k.pre(v, "V", reads=[b_wc2[pa], b_thr[wi]], writes=[b_pos2])
            k.post(v.tensor_scalar(pos2[:], wc2[pa][:], e["thr"][wi][:, 0:1], None,
                                   ALU.is_ge), "V", [b_wc2[pa], b_thr[wi]], [b_pos2])
            k.pre(v, "V", reads=[b_wc2[pa], b_pos2, b_nth[wi]], writes=[b_q2[pa]])
            k.post(v.scalar_tensor_tensor(q2[pa][:], wc2[pa][:], e["nthr"][wi][:, 0:1],
                                          pos2[:], ALU.is_le, ALU.subtract),
                   "V", [b_wc2[pa], b_pos2, b_nth[wi]], [b_q2[pa]])

        @blk.scalar
        def _(s, qf=qf, ci=ci, pa=pa, wi=wi):
            k.pre(s, "SS", reads=[b_q2[pa]], writes=[b_qn[wi]])
            k.post(s.dma_start(out=qf[ci, :, :], in_=q2[pa][:]), "SS", [b_q2[pa]], [b_qn[wi]])


def p1(k, blk, e):
    nc = k.nc
    B = e["B"]
    b_qrot = B("qrot")[0]
    b_id = B("ident")[0]
    b_ckvd = B("ckv_d")[0]

    with contextlib.ExitStack() as es1:
        sb = lambda n, s, d: es1.enter_context(nc.sbuf_tensor(n, s, d))
        psa = lambda n, s, d: es1.enter_context(nc.psum_tensor(n, s, d))
        es_a = contextlib.ExitStack()
        sba = lambda n, s, d: es_a.enter_context(nc.sbuf_tensor(n, s, d))
        wdn_sb = sba("p1_wdn", [128, 16 * 1024], BF16)
        xb = [sba(f"p1_xb{i}", [128, D], BF16) for i in range(2)]
        xT = sba("p1_xT", [128, 16 * 1024], BF16)
        ckv_ev = [sba(f"p1_ckve{i}", [128, 1024], BF16) for i in range(2)]
        es_ps1 = contextlib.ExitStack()
        ptr = [es_ps1.enter_context(nc.psum_tensor(f"p1_ptr{i}", [128, 128], BF16))
               for i in range(4)]
        ps_ckv = [es_ps1.enter_context(nc.psum_tensor(f"p1_psckv{i}", [128, 1024], F32))
                  for i in range(2)]

        b_wdn = B("p1_wdn")[0]
        b_xb = B("p1_xb0", "p1_xb1")
        b_xT = B("p1_xT")[0]
        b_ckve = B("p1_ckve0", "p1_ckve1")
        b_ptr = B(*[f"p1_ptr{i}" for i in range(4)])
        b_psckv = B("p1_psckv0", "p1_psckv1")

        @blk.gpsimd
        def _(g):
            k.pre(g, "SG", writes=[b_wdn])
            k.post(g.dma_start(out=wdn_sb[:].rearrange("p (kd l) -> p kd l", kd=16),
                               in_=e["wdn"].rearrange("(kd p) l -> p kd l", p=128)),
                   "SG", [], [b_wdn])

        for bp in range(8):
            for sub in range(8):
                tt_i = bp * 8 + sub
                pa = sub % 2

                @blk.gpsimd
                def _(g, pa=pa, tt_i=tt_i):
                    k.pre(g, "SG", writes=[b_xb[pa]])
                    k.post(g.dma_start(out=xb[pa][:],
                                       in_=e["xf"][tt_i * 128:(tt_i + 1) * 128, :]),
                           "SG", [], [b_xb[pa]])

                for qb in range(4):
                    @blk.tensor
                    def _(t, pa=pa, qb=qb):
                        k.pre(t, "T", reads=[b_xb[pa], b_id], writes=b_ptr)
                        for i in range(4):
                            kd = qb * 4 + i
                            k.post(t.transpose(ptr[i][:], xb[pa][:, kd * 128:(kd + 1) * 128],
                                               e["ident_s"][:]),
                                   "T", [b_xb[pa], b_id], [b_ptr[i]])

                    @blk.vector
                    def _(v, sub=sub, qb=qb):
                        k.pre(v, "V", reads=b_ptr, writes=[b_xT])
                        ins = None
                        for i in range(4):
                            kd = qb * 4 + i
                            ins = v.tensor_copy(
                                xT[:, kd * 1024 + sub * 128: kd * 1024 + (sub + 1) * 128],
                                ptr[i][:])
                        k.post(ins, "V", b_ptr, [b_xT])

            for l in range(8):
                pa = l % 2

                @blk.tensor
                def _(t, l=l, pa=pa):
                    k.pre(t, "T", reads=[b_xT, b_wdn], writes=[b_psckv[pa]])
                    for kd in range(16):
                        ws = wdn_sb[:, kd * 1024 + l * 128: kd * 1024 + (l + 1) * 128]
                        t.ldweights(ws)
                        ins = None
                        for b2 in range(2):
                            ins = t.matmul(
                                ps_ckv[pa][:, b2 * 512:(b2 + 1) * 512],
                                ws, xT[:, kd * 1024 + b2 * 512: kd * 1024 + (b2 + 1) * 512],
                                start=(kd == 0), stop=(kd == 15), skip_group_check=True)
                    k.post(ins, "T", [b_xT, b_wdn], [b_psckv[pa]])

                @blk.vector
                def _(v, l=l, pa=pa):
                    k.pre(v, "V", reads=[b_psckv[pa]], writes=[b_ckve[pa]])
                    k.post(v.tensor_copy(ckv_ev[pa][:], ps_ckv[pa][:]),
                           "V", [b_psckv[pa]], [b_ckve[pa]])

                @blk.scalar
                def _(s, l=l, pa=pa, bp=bp):
                    k.pre(s, "SS", reads=[b_ckve[pa]], writes=[b_ckvd])
                    k.post(s.dma_start(out=e["ckv_d"][l, :, bp * 1024:(bp + 1) * 1024],
                                       in_=ckv_ev[pa][:]),
                           "SS", [b_ckve[pa]], [b_ckvd])

        es_a.close()
        k.barrier(blk)
        xoT = sb("p1_xoT", [128, 16 * NSLOT * 128], BF16)
        xob = [sb(f"p1_xob{i}", [128, D], F32) for i in range(2)]
        xobb = sb("p1_xobb", [128, D], BF16)
        b_xoT = B("p1_xoT")[0]
        b_xob = B("p1_xob0", "p1_xob1")
        b_xobb = B("p1_xobb")[0]

        for s_ in range(NSLOT):
            pa = s_ % 2

            @blk.sync
            def _(sy, pa=pa, s_=s_):
                k.pre(sy, "SH", writes=[b_xob[pa]])
                k.post(sy.dma_start(out=xob[pa][:], in_=e["xo"][s_, :, :]),
                       "SH", [], [b_xob[pa]])

            @blk.vector
            def _(v, pa=pa):
                k.pre(v, "V", reads=[b_xob[pa]], writes=[b_xobb])
                k.post(v.tensor_copy(xobb[:], xob[pa][:]), "V", [b_xob[pa]], [b_xobb])

            for qb in range(4):
                @blk.tensor
                def _(t, qb=qb):
                    k.pre(t, "T", reads=[b_xobb, b_id], writes=b_ptr)
                    for i in range(4):
                        kd = qb * 4 + i
                        k.post(t.transpose(ptr[i][:], xobb[:, kd * 128:(kd + 1) * 128],
                                           e["ident_s"][:]), "T", [b_xobb, b_id], [b_ptr[i]])

                @blk.vector
                def _(v, s_=s_, qb=qb):
                    k.pre(v, "V", reads=b_ptr, writes=[b_xoT])
                    ins = None
                    for i in range(4):
                        kd = qb * 4 + i
                        ins = v.tensor_copy(
                            xoT[:, kd * 1024 + s_ * 128: kd * 1024 + s_ * 128 + 128], ptr[i][:])
                    k.post(ins, "V", b_ptr, [b_xoT])

        ccq = [sb(f"p1_ccq{i}", [128, 128], BF16) for i in range(NSLOT)]
        ssq = [sb(f"p1_ssq{i}", [128, 128], BF16) for i in range(NSLOT)]
        angq = sb("p1_angq", [128, 64], F32)
        nq = sb("p1_nq", [128, 64], I32)
        tq1 = sb("p1_tq1", [128, 64], F32)
        tq2 = sb("p1_tq2", [128, 64], F32)
        sinq = sb("p1_sinq", [128, 64], F32)
        cosq = sb("p1_cosq", [128, 64], F32)
        b_ccq = B(*[f"p1_ccq{i}" for i in range(NSLOT)])
        b_ssq = B(*[f"p1_ssq{i}" for i in range(NSLOT)])
        b_angq, b_ivf, b_qp = B("p1_angq", "ivf", "qposc")

        for s_ in range(NSLOT):
            @blk.vector
            def _(v, s_=s_):
                k.pre(v, "V", reads=[b_ivf, b_qp], writes=[b_angq])
                k.post(v.tensor_scalar(angq[:], e["IVF"][:], e["qposc"][:, s_:s_ + 1],
                                       None, ALU.mult), "V", [b_ivf, b_qp], [b_angq])
            b_sq = sincos(k, blk, e, angq[:], None, nq, tq1, tq2, sinq, 0.0, [b_angq])
            b_cq = sincos(k, blk, e, angq[:], None, nq, tq1, tq2, cosq, math.pi / 2.0, [b_angq])

            @blk.vector
            def _(v, s_=s_):
                k.pre(v, "V", reads=[b_sq, b_cq], writes=[b_ccq[s_], b_ssq[s_]])
                k.post(v.tensor_copy(ccq[s_][:, 0:128:2], cosq[:]), "V", [b_cq], [b_ccq[s_]])
                k.post(v.tensor_copy(ccq[s_][:, 1:128:2], cosq[:]), "V", [b_cq], [b_ccq[s_]])
                k.post(v.tensor_copy(ssq[s_][:, 1:128:2], sinq[:]), "V", [b_sq], [b_ssq[s_]])
                k.post(v.tensor_scalar(ssq[s_][:, 0:128:2], sinq[:], -1.0, None, ALU.mult),
                       "V", [b_sq], [b_ssq[s_]])

        es_ps1.close()
        k.barrier(blk)
        wqb = [sb(f"p1_wqb{i}", [128, 1024], BF16) for i in range(2)]
        ps_q = [psa(f"p1_psq{i}", [128, 1024], F32) for i in range(4)]
        rt1 = sb("p1_rt1", [128, 1024], BF16)
        rt2 = sb("p1_rt2", [128, 1024], BF16)
        b_wqb = B("p1_wqb0", "p1_wqb1")
        b_psq = B(*[f"p1_psq{i}" for i in range(4)])
        b_rt1, b_rt2 = B("p1_rt1", "p1_rt2")
        wq_r = e["wq"].rearrange("(kd p) n -> kd p n", p=128)

        for sq in range(2):
            slots = [4 * sq + i for i in range(4)]
            for hf in range(2):
                for kd in range(16):
                    pa = kd % 2

                    @blk.gpsimd
                    def _(g, kd=kd, hf=hf, pa=pa):
                        k.pre(g, "SG", writes=[b_wqb[pa]])
                        k.post(g.dma_start(out=wqb[pa][:],
                                           in_=wq_r[kd, :, hf * 1024:(hf + 1) * 1024]),
                               "SG", [], [b_wqb[pa]])

                    @blk.tensor
                    def _(t, kd=kd, pa=pa, slots=slots):
                        k.pre(t, "T", reads=[b_wqb[pa], b_xoT], writes=b_psq)
                        ins = None
                        for si, s_ in enumerate(slots):
                            xs = xoT[:, kd * 1024 + s_ * 128: kd * 1024 + s_ * 128 + 128]
                            t.ldweights(xs)
                            for b2 in range(2):
                                ins = t.matmul(ps_q[si][:, b2 * 512:(b2 + 1) * 512], xs,
                                               wqb[pa][:, b2 * 512:(b2 + 1) * 512],
                                               start=(kd == 0), stop=(kd == 15),
                                               skip_group_check=True)
                        k.post(ins, "T", [b_wqb[pa], b_xoT], list(b_psq))

                for si, s_ in enumerate(slots):
                    @blk.vector
                    def _(v, si=si, s_=s_, hf=hf):
                        src = ps_q[si][:]
                        swp = src.rearrange("p (h pr two) -> p h pr two", pr=64, two=2)[
                            :, :, :, ::-1]
                        ccb = ccq[s_][:].rearrange("p (one j) -> p one j", one=1).broadcast_to(
                            [128, 8, 128])
                        ssb = ssq[s_][:].rearrange("p (one j) -> p one j", one=1).broadcast_to(
                            [128, 8, 128])
                        k.pre(v, "V", reads=[b_psq[si], b_ccq[s_], b_ssq[s_]],
                              writes=[b_rt1, b_rt2])
                        k.post(v.tensor_tensor(rt1[:], src, ccb, ALU.mult),
                               "V", [b_psq[si], b_ccq[s_]], [b_rt1])
                        k.post(v.tensor_tensor(rt2[:], swp, ssb, ALU.mult),
                               "V", [b_psq[si], b_ssq[s_]], [b_rt2])
                        k.pre(v, "V", reads=[b_rt1, b_rt2], writes=[b_qrot])
                        k.post(v.tensor_tensor(
                            e["qrot"][:, s_ * D + hf * 1024: s_ * D + (hf + 1) * 1024],
                            rt1[:], rt2[:], ALU.add), "V", [b_rt1, b_rt2], [b_qrot])

    if e["dbg_q"] is not None:
        @blk.scalar
        def _(s):
            k.pre(s, "SS", reads=[b_qrot])
            k.post(s.dma_start(out=e["dbg_q"][:, :], in_=e["qrot"][:]), "SS", [b_qrot], [])


def p2(k, blk, e):
    nc = k.nc
    B = e["B"]
    b_attn = B("attn")[0]
    b_id = B("ident")[0]
    b_qrot = B("qrot")[0]
    b_ckvd = B("ckv_d")[0]
    b_cc, b_ss, b_bmq = B("CC", "SS", "bmq")

    with contextlib.ExitStack() as es2:
        sb = lambda n, s, d: es2.enter_context(nc.sbuf_tensor(n, s, d))
        psa = lambda n, s, d: es2.enter_context(nc.psum_tensor(n, s, d))
        Kf = sb("p2_Kf", [128, T], BF16)
        Vt = sb("p2_Vt", [128, T], BF16)
        ckvq = [sb(f"p2_ckvq{i}", [128, 8 * 1024], BF16) for i in range(2)]
        wkv = sb("p2_wkv", [128, 8 * 256], BF16)
        kraw = [sb(f"p2_kraw{i}", [128, 1024], BF16) for i in range(2)]
        ktok = [sb(f"p2_ktok{i}", [128, 512], BF16) for i in range(2)]
        rr1 = [sb(f"p2_rr1_{i}", [128, 512], F32) for i in range(2)]
        rr2 = [sb(f"p2_rr2_{i}", [128, 512], F32) for i in range(2)]
        qfq = sb("p2_qfq", [128, 512], BF16)
        PT = [sb(f"p2_PT{i}", [128, 512], BF16) for i in range(2)]
        ptmpb = sb("p2_ptmpb", [128, 128], BF16)
        zq = sb("p2_zq", [1, 512], F32)
        zr = sb("p2_zr", [1, 512], F32)
        zbc = sb("p2_zbc", [128, 512], F32)
        wc2 = [sb(f"p2_wc2_{i}", [128, 512], F32) for i in range(2)]
        pos2 = sb("p2_pos2", [128, 512], BF16)
        q2 = [sb(f"p2_q2_{i}", [128, 512], BF16) for i in range(2)]

        psKV = psa("p2_psKV", [128, 1024], F32)
        psc = [psa(f"p2_psc{i}", [128, 512], F32) for i in range(2)]
        psAV = psa("p2_psAV", [128, 512], F32)
        psZr = psa("p2_psZr", [1, 512], F32)
        ptr = [psa(f"p2_ptr{i}", [128, 128], BF16) for i in range(2)]

        b_Kf, b_Vt, b_wkv = B("p2_Kf", "p2_Vt", "p2_wkv")
        b_ckvq = B("p2_ckvq0", "p2_ckvq1")
        b_kraw = B("p2_kraw0", "p2_kraw1")
        b_ktok = B("p2_ktok0", "p2_ktok1")
        b_rr1 = B("p2_rr1_0", "p2_rr1_1")
        b_rr2 = B("p2_rr2_0", "p2_rr2_1")
        b_qfq, b_PTb0, b_PTb1, b_ptmpb = B("p2_qfq", "p2_PT0", "p2_PT1", "p2_ptmpb")
        b_PT = [b_PTb0, b_PTb1]
        b_zq, b_zr, b_zbc = B("p2_zq", "p2_zr", "p2_zbc")
        b_idf = B("identf")[0]
        b_kvps = B("p2_psKV_kv")[0]
        b_psc = B("p2_psc0", "p2_psc1")
        b_psAV, b_psZr = B("p2_psAV", "p2_psZr")
        b_ptr = B(*[f"p2_ptr{i}" for i in range(2)])
        b_bmk = B("bmk")[0]
        b_ones = B("ones")[0]

        wup_r = e["wup"].rearrange("(l p) n -> l p n", p=128)

        for h in range(NH):
            p0w2_emit(k, blk, e, h / NH, (h + 1) / NH, (wc2, pos2, q2))

            @blk.gpsimd
            def _(g, h=h):
                k.pre(g, "SG", writes=[b_wkv])
                k.post(g.dma_start(out=wkv[:].rearrange("p (l c) -> p l c", l=8),
                                   in_=wup_r[:, :, h * 256:(h + 1) * 256]), "SG", [], [b_wkv])

            for er in range(8):
                cpa = er % 2

                @blk.sync
                def _(sy, er=er, cpa=cpa):
                    k.pre(sy, "SH", reads=[b_ckvd], writes=[b_ckvq[cpa]])
                    k.post(sy.dma_start(out=ckvq[cpa][:].rearrange("p (l t) -> p l t", l=8),
                                        in_=e["ckv_d"][:, :, er * 1024:(er + 1) * 1024]),
                           "SH", [b_ckvd], [b_ckvq[cpa]])

                for part in range(2):
                    base = part * 128
                    rpa = part

                    @blk.tensor
                    def _(t, base=base, cpa=cpa, rpa=rpa):
                        k.pre(t, "T", reads=[b_ckvq[cpa], b_wkv], writes=[b_kvps])
                        ins = None
                        for l in range(8):
                            w = wkv[:, l * 256 + base: l * 256 + base + 128]
                            t.ldweights(w)
                            for b in range(2):
                                ins = t.matmul(
                                    psKV[:, b * 512:(b + 1) * 512], w,
                                    ckvq[cpa][:, l * 1024 + b * 512: l * 1024 + (b + 1) * 512],
                                    start=(l == 0), stop=(l == 7), skip_group_check=True)
                        k.post(ins, "T", [b_ckvq[cpa], b_wkv], [b_kvps])

                    @blk.scalar
                    def _(s, rpa=rpa):
                        k.pre(s, "A", reads=[b_kvps], writes=[b_kraw[rpa]])
                        k.post(s.activation(kraw[rpa][:], psKV[:], AF.Identity),
                               "A", [b_kvps], [b_kraw[rpa]])

                    if part == 1:
                        for g2 in range(4):
                            @blk.tensor
                            def _(t, g2=g2, rpa=rpa):
                                k.pre(t, "T", reads=[b_kraw[rpa], b_id], writes=list(b_ptr))
                                for i in range(2):
                                    kt = g2 * 2 + i
                                    k.post(t.transpose(ptr[i][:],
                                                       kraw[rpa][:, kt * 128:(kt + 1) * 128],
                                                       e["ident_s"][:]),
                                           "T", [b_kraw[rpa], b_id], [b_ptr[i]])

                            @blk.scalar
                            def _(s, er=er, g2=g2):
                                k.pre(s, "A", reads=list(b_ptr), writes=[b_Vt])
                                ins = None
                                for i in range(2):
                                    kt_g = er * 8 + g2 * 2 + i
                                    ins = s.activation(Vt[:, kt_g * 128:(kt_g + 1) * 128],
                                                       ptr[i][:], AF.Identity)
                                k.post(ins, "A", list(b_ptr), [b_Vt])
                    else:
                        for b4 in range(2):
                            gpa = b4

                            for h2 in range(2):
                                @blk.tensor
                                def _(t, b4=b4, h2=h2, rpa=rpa):
                                    k.pre(t, "T", reads=[b_kraw[rpa], b_id], writes=list(b_ptr))
                                    for i in range(2):
                                        kt = b4 * 4 + h2 * 2 + i
                                        k.post(t.transpose(ptr[i][:],
                                                           kraw[rpa][:, kt * 128:(kt + 1) * 128],
                                                           e["ident_s"][:]),
                                               "T", [b_kraw[rpa], b_id], [b_ptr[i]])

                                @blk.vector
                                def _(v, er=er, b4=b4, h2=h2, gpa=gpa):
                                    bkt = er * 8 + b4 * 4 + h2 * 2
                                    k.pre(v, "V", reads=list(b_ptr) + [b_cc, b_ss],
                                          writes=[b_rr1[gpa], b_rr2[gpa]])
                                    i1 = i2 = None
                                    for i in range(2):
                                        col = (h2 * 2 + i) * 128
                                        srcp = ptr[i][:]
                                        swp = srcp.rearrange("p (pr two) -> p pr two", two=2)[
                                            :, :, ::-1]
                                        i1 = v.tensor_tensor(
                                            rr1[gpa][:, col:col + 128], srcp,
                                            e["CC"][:, (bkt + i) * 128:(bkt + i + 1) * 128],
                                            ALU.mult)
                                        i2 = v.tensor_tensor(
                                            rr2[gpa][:, col:col + 128], swp,
                                            e["SS"][:, (bkt + i) * 128:(bkt + i + 1) * 128],
                                            ALU.mult)
                                    k.post(i1, "V", list(b_ptr) + [b_cc], [b_rr1[gpa]])
                                    k.post(i2, "V", list(b_ptr) + [b_ss], [b_rr2[gpa]])

                            @blk.gpsimd
                            def _(g, gpa=gpa):
                                k.pre(g, "G", reads=[b_rr1[gpa], b_rr2[gpa]],
                                      writes=[b_ktok[gpa]])
                                k.post(g.tensor_tensor(ktok[gpa][:], rr1[gpa][:], rr2[gpa][:],
                                                       ALU.add),
                                       "G", [b_rr1[gpa], b_rr2[gpa]], [b_ktok[gpa]])

                            for h2 in range(2):
                                @blk.tensor
                                def _(t, h2=h2, gpa=gpa):
                                    k.pre(t, "T", reads=[b_ktok[gpa], b_id], writes=list(b_ptr))
                                    for i in range(2):
                                        kt = h2 * 2 + i
                                        k.post(t.transpose(ptr[i][:],
                                                           ktok[gpa][:, kt * 128:(kt + 1) * 128],
                                                           e["ident_s"][:]),
                                               "T", [b_ktok[gpa], b_id], [b_ptr[i]])

                                @blk.scalar
                                def _(s, er=er, b4=b4, h2=h2):
                                    k.pre(s, "A", reads=list(b_ptr), writes=[b_Kf])
                                    ins = None
                                    for i in range(2):
                                        kt_g = er * 8 + b4 * 4 + h2 * 2 + i
                                        ins = s.activation(Kf[:, kt_g * 128:(kt_g + 1) * 128],
                                                           ptr[i][:], AF.Identity)
                                    k.post(ins, "A", list(b_ptr), [b_Kf])
            if e["dbg_kf"] is not None and h == 0:
                @blk.scalar
                def _(s):
                    k.pre(s, "SS", reads=[b_Kf, b_Vt])
                    k.post(s.dma_start(out=e["dbg_kf"][:, :], in_=Kf[:]), "SS", [b_Kf], [])
                    k.post(s.dma_start(out=e["dbg_vt"][:, :], in_=Vt[:]), "SS", [b_Vt], [])

            for qi, slots in enumerate(QUADS):
                for ci, s_ in enumerate(slots):
                    @blk.tensor
                    def _(t, s_=s_, ci=ci, h=h):
                        k.pre(t, "T", reads=[b_qrot, b_id], writes=[b_ptr[ci % 2]])
                        k.post(t.transpose(ptr[ci % 2][:],
                                           e["qrot"][:, s_ * D + h * 128: s_ * D + (h + 1) * 128],
                                           e["ident_s"][:]), "T", [b_qrot, b_id],
                               [b_ptr[ci % 2]])

                    @blk.scalar
                    def _(s, ci=ci):
                        k.pre(s, "A", reads=[b_ptr[ci % 2]], writes=[b_qfq])
                        k.post(s.activation(qfq[:, ci * 128:(ci + 1) * 128], ptr[ci % 2][:],
                                            AF.Identity), "A", [b_ptr[ci % 2]], [b_qfq])

                maxc = (slots[0] + 1) * 8
                for j in range(maxc):
                    nval = sum(1 for s_ in slots if (s_ + 1) * 8 > j)
                    ncol = nval * 128
                    pa = j % 2
                    sb_ = j // 8
                    boundary = sb_ in slots and slots.index(sb_) == nval - 1

                    @blk.tensor
                    def _(t, j=j, pa=pa, ncol=ncol):
                        k.pre(t, "T", reads=[b_Kf, b_qfq], writes=[b_psc[pa]])
                        t.ldweights(Kf[:, j * 128:(j + 1) * 128])
                        k.post(t.matmul(psc[pa][:, 0:ncol], Kf[:, j * 128:(j + 1) * 128],
                                        qfq[:, 0:ncol], start=True, stop=True,
                                        skip_group_check=True),
                               "T", [b_Kf, b_qfq], [b_psc[pa]])

                    if not boundary:
                        @blk.scalar
                        def _(s, j=j, pa=pa, ncol=ncol):
                            k.pre(s, "A", reads=[b_psc[pa]], writes=[b_PT[pa]])
                            k.post(s.activation(PT[pa][:, 0:ncol], psc[pa][:, 0:ncol],
                                                AF.Exp, scale=SCALE),
                                   "A", [b_psc[pa]], [b_PT[pa]])
                    else:
                        @blk.scalar
                        def _(s, j=j, pa=pa, ncol=ncol):
                            k.pre(s, "A", reads=[b_psc[pa]], writes=[b_PT[pa], b_ptmpb])
                            ins = None
                            if ncol > 128:
                                ins = s.activation(PT[pa][:, 0:ncol - 128],
                                                   psc[pa][:, 0:ncol - 128], AF.Exp,
                                                   scale=SCALE)
                            ins2 = s.activation(ptmpb[:], psc[pa][:, ncol - 128:ncol],
                                                AF.Exp, scale=SCALE)
                            if ins is not None:
                                k.post(ins, "A", [b_psc[pa]], [b_PT[pa]])
                            k.post(ins2, "A", [b_psc[pa]], [b_ptmpb])

                        @blk.vector
                        def _(v, j=j, pa=pa, ncol=ncol):
                            k.pre(v, "V", reads=[b_ptmpb, b_bmk], writes=[b_PT[pa]])
                            jm = j % 8
                            k.post(v.tensor_tensor(PT[pa][:, ncol - 128:ncol], ptmpb[:],
                                                   e["bmk"][:, jm * 128:(jm + 1) * 128],
                                                   ALU.mult), "V", [b_ptmpb, b_bmk],
                                   [b_PT[pa]])

                    @blk.tensor
                    def _(t, j=j, pa=pa, ncol=ncol, maxc=maxc):
                        k.pre(t, "T", reads=[b_PT[pa], b_Vt, b_ones], writes=[b_psAV, b_psZr])
                        t.ldweights(Vt[:, j * 128:(j + 1) * 128])
                        k.post(t.matmul(psAV[:, 0:ncol], Vt[:, j * 128:(j + 1) * 128],
                                        PT[pa][:, 0:ncol], start=(j == 0),
                                        stop=(j == maxc - 1), skip_group_check=True),
                               "T", [b_PT[pa], b_Vt], [b_psAV])
                        t.ldweights(e["onescol"][:])
                        k.post(t.matmul(psZr[0:1, 0:ncol], e["onescol"][:],
                                        PT[pa][:, 0:ncol], start=(j == 0),
                                        stop=(j == maxc - 1), skip_group_check=True),
                               "T", [b_PT[pa], b_ones], [b_psZr])

                @blk.scalar
                def _(s):
                    k.pre(s, "A", reads=[b_psZr], writes=[b_zq])
                    k.post(s.activation(zq[0:1, :], psZr[0:1, :], AF.Identity),
                           "A", [b_psZr], [b_zq])

                @blk.vector
                def _(v):
                    k.pre(v, "V", reads=[b_zq], writes=[b_zr])
                    k.post(v.reciprocal(zr[0:1, :], zq[0:1, :]), "V", [b_zq], [b_zr])

                @blk.tensor
                def _(t):
                    k.pre(t, "T", reads=[b_zr, B("onesrow_f")[0]], writes=[b_psc[0]])
                    k.post(t.matmul(psc[0][:, 0:512], e["onesrow_f"][0:1, :], zr[0:1, :],
                                    start=True, stop=True, skip_group_check=True),
                           "T", [b_zr, B("onesrow_f")[0]], [b_psc[0]])

                @blk.scalar
                def _(s):
                    k.pre(s, "A", reads=[b_psc[0]], writes=[b_zbc])
                    k.post(s.activation(zbc[:], psc[0][:, 0:512], AF.Identity),
                           "A", [b_psc[0]], [b_zbc])

                @blk.vector
                def _(v, h=h, qi=qi):
                    k.pre(v, "V", reads=[b_psAV, b_zbc], writes=[b_attn])
                    k.post(v.tensor_tensor(
                        e["attn"][:, h * 1024 + qi * 512: h * 1024 + (qi + 1) * 512],
                        psAV[:], zbc[:], ALU.mult), "V", [b_psAV, b_zbc], [b_attn])

    if e["dbg_at"] is not None:
        @blk.scalar
        def _(s):
            k.pre(s, "SS", reads=[b_attn])
            k.post(s.dma_start(out=e["dbg_at"][:, :], in_=e["attn"][:]), "SS", [b_attn], [])


def p3(k, blk, e):
    nc = k.nc
    B = e["B"]
    b_x1d = B("x1_d")[0]
    b_attn = B("attn")[0]

    with contextlib.ExitStack() as es3:
        sb = lambda n, s, d: es3.enter_context(nc.sbuf_tensor(n, s, d))
        psa = lambda n, s, d: es3.enter_context(nc.psum_tensor(n, s, d))
        wob = [sb(f"p3_wob{i}", [128, D], BF16) for i in range(2)]
        xot = [sb(f"p3_xot{i}", [128, D], F32) for i in range(2)]
        vsl = sb("p3_v", [128, D], F32)
        sqs = sb("p3_sq", [128, D], F32)
        x1t = sb("p3_x1t", [128, D], F32)
        cols = sb("p3_cols", [128, 8], F32)
        ps_wo = [psa(f"p3_pswo{i}", [128, 2048], F32) for i in range(2)]
        b_wob = B("p3_wob0", "p3_wob1")
        b_xot = B("p3_xot0", "p3_xot1")
        b_v, b_sq, b_cols = B("p3_v", "p3_sq", "p3_cols")
        b_pswo = B("p3_pswo0", "p3_pswo1")
        wo_r = e["wo_"].rearrange("(f p) n -> f p n", p=128)

        for sp in range(4):
            slots = [2 * sp, 2 * sp + 1]
            for f in range(16):
                pa = f % 2

                @blk.gpsimd
                def _(g, f=f, pa=pa):
                    k.pre(g, "SG", writes=[b_wob[pa]])
                    k.post(g.dma_start(out=wob[pa][:], in_=wo_r[f, :, :]), "SG", [], [b_wob[pa]])

                @blk.tensor
                def _(t, f=f, pa=pa, slots=slots):
                    k.pre(t, "T", reads=[b_wob[pa], b_attn], writes=list(b_pswo))
                    ins = None
                    for si, s_ in enumerate(slots):
                        qi, ci = QCOL[s_]
                        col = f * 1024 + qi * 512 + ci * 128
                        t.ldweights(e["attn"][:, col:col + 128])
                        for cb in range(4):
                            ins = t.matmul(ps_wo[si][:, cb * 512:(cb + 1) * 512],
                                           e["attn"][:, col:col + 128],
                                           wob[pa][:, cb * 512:(cb + 1) * 512],
                                           start=(f == 0), stop=(f == 15),
                                           skip_group_check=True)
                    k.post(ins, "T", [b_wob[pa], b_attn], list(b_pswo))

            for si, s_ in enumerate(slots):
                @blk.sync
                def _(sy, si=si, s_=s_):
                    k.pre(sy, "SH", writes=[b_xot[si]])
                    k.post(sy.dma_start(out=xot[si][:], in_=e["xo"][s_, :, :]),
                           "SH", [], [b_xot[si]])

                @blk.vector
                def _(v, si=si):
                    k.pre(v, "V", reads=[b_pswo[si], b_xot[si]], writes=[b_v, b_cols])
                    k.post(v.scalar_tensor_tensor(vsl[:], ps_wo[si][:], 0.0, xot[si][:],
                                                  ALU.add, ALU.add, accum_out=cols[:, 0:1]),
                           "V", [b_pswo[si], b_xot[si]], [b_v, b_cols])

                b_x1t = B("p3_x1t")[0]
                ln_epilogue(k, blk, e, vsl, b_v, sqs, b_sq, cols, b_cols,
                            x1t[:], b_x1t, D)

                @blk.scalar
                def _(s, s_=s_):
                    k.pre(s, "SS", reads=[b_x1t], writes=[b_x1d])
                    k.post(s.dma_start(out=e["x1_d"][s_, :, :], in_=x1t[:]),
                           "SS", [b_x1t], [b_x1d])


def ln_epilogue(k, blk, e, vsl, b_v, sqs, b_sq, cols, b_cols, out_ap, b_out, dim):
    @blk.scalar
    def _(s):
        k.pre(s, "A", reads=[b_v], writes=[b_sq, b_cols])
        k.post(s.activation(sqs[:], vsl[:], AF.Square, accum_out=cols[:, 1:2]),
               "A", [b_v], [b_sq, b_cols])

    @blk.vector
    def _(v):
        k.pre(v, "V", reads=[b_cols], writes=[b_cols])
        k.post(v.tensor_scalar(cols[:, 2:3], cols[:, 0:1], 1.0 / dim, None, ALU.mult),
               "V", [b_cols], [b_cols])
        k.pre(v, "V", reads=[b_cols], writes=[b_cols])
        k.post(v.tensor_tensor(cols[:, 3:4], cols[:, 2:3], cols[:, 2:3], ALU.mult),
               "V", [b_cols], [b_cols])
        k.pre(v, "V", reads=[b_cols], writes=[b_cols])
        k.post(v.scalar_tensor_tensor(cols[:, 4:5], cols[:, 1:2], 1.0 / dim, cols[:, 3:4],
                                      ALU.mult, ALU.subtract), "V", [b_cols], [b_cols])
        k.pre(v, "V", reads=[b_cols], writes=[b_cols])
        k.post(v.tensor_scalar(cols[:, 5:6], cols[:, 4:5], 1e-5, None, ALU.add),
               "V", [b_cols], [b_cols])
        k.pre(v, "V", reads=[b_cols], writes=[b_cols])
        k.post(v.reciprocal(cols[:, 6:7], cols[:, 5:6]), "V", [b_cols], [b_cols])

    @blk.scalar
    def _(s):
        k.pre(s, "A", reads=[b_cols], writes=[b_cols])
        k.post(s.activation(cols[:, 7:8], cols[:, 6:7], AF.Sqrt), "A", [b_cols], [b_cols])

    @blk.vector
    def _(v):
        k.pre(v, "V", reads=[b_cols], writes=[b_cols])
        k.post(v.scalar_tensor_tensor(cols[:, 3:4], cols[:, 2:3], -1.0, cols[:, 7:8],
                                      ALU.mult, ALU.mult), "V", [b_cols], [b_cols])

    @blk.scalar
    def _(s):
        k.pre(s, "A", reads=[b_v, b_cols], writes=[b_out])
        k.post(s.activation(out_ap, vsl[:], AF.Identity, bias=cols[:, 3:4],
                            scale=cols[:, 7:8]), "A", [b_v, b_cols], [b_out])


def p4(k, blk, e):
    nc = k.nc
    B = e["B"]
    b_x1d = B("x1_d")[0]
    b_id = B("ident")[0]
    b_mwb = B("mwb0", "mwb1", "mwb2")
    b_thr = B("thr0", "thr1", "thr2")
    b_nth = B("nthr0", "nthr1", "nthr2")
    b_w1qn, b_w2qn, b_w3qn = B("w1qn", "w2qn", "w3qn")
    b_hqd, b_hqtd, b_out = B("hq_d", "hqt_d", "out_d")

    with contextlib.ExitStack() as es4:
        sb = lambda n, s, d: es4.enter_context(nc.sbuf_tensor(n, s, d))
        xqT = sb("p4_xqT", [128, 16 * NSLOT * 128], BF16)
        msc1 = sb("p4_msc1", [128, NSLOT], F32)
        msc2 = sb("p4_msc2", [128, NSLOT], F32)
        msc3 = sb("p4_msc3", [128, NSLOT], F32)
        ccols = sb("p4_cols", [128, 8], F32)
        posb = sb("p4_pos", [128, 512], BF16)
        b_xqT, b_ms1, b_ms2, b_ms3, b_cc4 = B(
            "p4_xqT", "p4_msc1", "p4_msc2", "p4_msc3", "p4_cols")
        b_xqt, b_n4, b_r4, b_t4, b_sq4 = B("p4_xqtok", "p4_n", "p4_r", "p4_t", "p4_sq")
        b_pos = B("p4_pos")[0]

        with contextlib.ExitStack() as esp:
            sba = lambda n, s, d: esp.enter_context(nc.sbuf_tensor(n, s, d))
            xq_tok = sba("p4_xqtok", [128, D], BF16)
            ni32 = sba("p4_n", [128, D], I32)
            rbuf = sba("p4_r", [128, D], F32)
            tbuf = sba("p4_t", [128, D], F32)
            sqs = sba("p4_sq", [128, D], F32)
            x1row = sba("p4_x1row", [128, D], F32)
            b_x1r = B("p4_x1row")[0]
            ptr = [esp.enter_context(nc.psum_tensor(f"p4_ptr{i}", [128, 128], BF16))
                   for i in range(8)]
            b_ptr = B(*[f"p4_ptr{i}" for i in range(8)])
            for s_ in range(NSLOT):
                @blk.sync
                def _(sy, s_=s_):
                    k.pre(sy, "SH", reads=[b_x1d], writes=[b_x1r])
                    k.post(sy.dma_start(out=x1row[:], in_=e["x1_d"][s_, :, :]),
                           "SH", [b_x1d], [b_x1r])
                x1s = x1row[:]

                @blk.vector
                def _(v, x1s=x1s):
                    k.pre(v, "V", reads=[b_x1r], writes=[b_cc4])
                    k.post(v.reduce_max(ccols[:, 0:1], x1s, axis=AX,
                                        apply_absolute_value=True), "V", [b_x1r], [b_cc4])

                @blk.scalar
                def _(s, x1s=x1s):
                    k.pre(s, "A", reads=[b_x1r], writes=[b_sq4, b_cc4])
                    k.post(s.activation(sqs[:], x1s, AF.Square, accum_out=ccols[:, 1:2]),
                           "A", [b_x1r], [b_sq4, b_cc4])

                quant_scale_chain(k, blk, e, ccols, b_cc4, msc1, b_ms1, msc2, b_ms2, s_,
                                  D, b_mwb[0], b_mwb[1], e["mwb"][0], e["mwb"][1])

                @blk.scalar
                def _(s, x1s=x1s):
                    k.pre(s, "A", reads=[b_x1r, b_cc4], writes=[b_n4])
                    k.post(s.activation(ni32[:], x1s, AF.Identity, scale=ccols[:, 7:8]),
                           "A", [b_x1r, b_cc4], [b_n4])

                @blk.vector
                def _(v, x1s=x1s):
                    k.pre(v, "V", reads=[b_x1r, b_cc4, b_n4], writes=[b_r4])
                    k.post(v.scalar_tensor_tensor(rbuf[:], x1s, ccols[:, 7:8], ni32[:],
                                                  ALU.mult, ALU.subtract),
                           "V", [b_x1r, b_cc4, b_n4], [b_r4])
                    k.pre(v, "V", reads=[b_r4, b_n4], writes=[b_t4])
                    k.post(v.scalar_tensor_tensor(tbuf[:], rbuf[:], 0.5, ni32[:],
                                                  ALU.is_ge, ALU.add), "V", [b_r4, b_n4], [b_t4])
                    k.pre(v, "V", reads=[b_r4, b_t4], writes=[b_xqt])
                    k.post(v.scalar_tensor_tensor(xq_tok[:], rbuf[:], -0.5, tbuf[:],
                                                  ALU.is_le, ALU.subtract),
                           "V", [b_r4, b_t4], [b_xqt])

                for half in range(2):
                    @blk.tensor
                    def _(t, half=half):
                        k.pre(t, "T", reads=[b_xqt, b_id], writes=b_ptr)
                        for i in range(8):
                            kd = half * 8 + i
                            k.post(t.transpose(ptr[i][:], xq_tok[:, kd * 128:(kd + 1) * 128],
                                               e["ident_s"][:]), "T", [b_xqt, b_id], [b_ptr[i]])

                    @blk.vector
                    def _(v, s_=s_, half=half):
                        k.pre(v, "V", reads=b_ptr, writes=[b_xqT])
                        ins = None
                        for i in range(8):
                            kd = half * 8 + i
                            ins = v.tensor_copy(
                                xqT[:, kd * 1024 + s_ * 128: kd * 1024 + s_ * 128 + 128],
                                ptr[i][:])
                        k.post(ins, "V", b_ptr, [b_xqT])

        k.barrier(blk)
        es_b = contextlib.ExitStack()
        sbb = lambda n, s, d: es_b.enter_context(nc.sbuf_tensor(n, s, d))
        w1qb = sbb("p4_w1qb", [128, 16 * 512], BF16)
        w2qb = sbb("p4_w2qb", [128, 16 * 512], BF16)
        gtmp = sbb("p4_gtmp", [128, 512], F32)
        utmp = sbb("p4_utmp", [128, 512], F32)
        hqc = sbb("p4_hqc", [128, 512], F32)
        b_w1q, b_w2q, b_g, b_u, b_hqc = B("p4_w1qb", "p4_w2qb", "p4_g", "p4_u", "p4_hqc")

        with contextlib.ExitStack() as esp:
            psGU = [esp.enter_context(nc.psum_tensor(f"p4_gu{i}", [128, 512], F32))
                    for i in range(8)]
            b_gu = B(*[f"p4_gu{i}" for i in range(8)])
            for hb in range(11):
                for wsel, (wqn_, qdst, b_qdst) in enumerate(
                        [(e["w1qn"], w1qb, b_w1q), (e["w2qn"], w2qb, b_w2q)]):
                    b_wnx = b_w1qn if wsel == 0 else b_w2qn
                    for kd in range(16):
                        @blk.sync
                        def _(sy, wqn_=wqn_, hb=hb, kd=kd, qdst=qdst, b_qdst=b_qdst,
                              b_wnx=b_wnx):
                            k.pre(sy, "SH", reads=[b_wnx], writes=[b_qdst])
                            k.post(sy.dma_start_transpose(
                                qdst[:, kd * 512:(kd + 1) * 512],
                                wqn_[hb * 512:(hb + 1) * 512, kd * 128:(kd + 1) * 128]),
                                "SH", [b_wnx], [b_qdst])

                for half in range(2):
                    slots = [half * 4 + i for i in range(4)]

                    @blk.tensor
                    def _(t, slots=slots):
                        k.pre(t, "T", reads=[b_w1q, b_w2q, b_xqT], writes=list(b_gu))
                        ins = None
                        for kd in range(16):
                            for si, s_ in enumerate(slots):
                                xs = xqT[:, kd * 1024 + s_ * 128: kd * 1024 + s_ * 128 + 128]
                                t.ldweights(xs)
                                t.matmul(psGU[si][:], xs, w1qb[:, kd * 512:(kd + 1) * 512],
                                         start=(kd == 0), stop=(kd == 15), skip_group_check=True)
                                ins = t.matmul(psGU[4 + si][:], xs,
                                               w2qb[:, kd * 512:(kd + 1) * 512],
                                               start=(kd == 0), stop=(kd == 15),
                                               skip_group_check=True)
                        k.post(ins, "T", [b_w1q, b_w2q, b_xqT], list(b_gu))

                    for si, s_ in enumerate(slots):
                        @blk.scalar
                        def _(s, si=si, s_=s_):
                            k.pre(s, "A", reads=[b_gu[si], b_ms1], writes=[b_g])
                            k.post(s.activation(gtmp[:], psGU[si][:], AF.Silu,
                                                scale=msc1[:, s_:s_ + 1]),
                                   "A", [b_gu[si], b_ms1], [b_g])
                            k.pre(s, "A", reads=[b_gu[4 + si], b_ms2], writes=[b_u])
                            k.post(s.activation(utmp[:], psGU[4 + si][:], AF.Identity,
                                                scale=msc2[:, s_:s_ + 1]),
                                   "A", [b_gu[4 + si], b_ms2], [b_u])

                        @blk.vector
                        def _(v):
                            k.pre(v, "V", reads=[b_g, b_u], writes=[b_hqc])
                            k.post(v.tensor_tensor(hqc[:], gtmp[:], utmp[:], ALU.mult),
                                   "V", [b_g, b_u], [b_hqc])

                        @blk.scalar
                        def _(s, s_=s_, hb=hb):
                            k.pre(s, "SS", reads=[b_hqc], writes=[b_hqd])
                            k.post(s.dma_start(out=e["hq_d"][s_, :, hb * 512:(hb + 1) * 512],
                                               in_=hqc[:]), "SS", [b_hqc], [b_hqd])

        es_b.close()
        k.barrier(blk)
        es_c = contextlib.ExitStack()
        sbc = lambda n, s, d: es_c.enter_context(nc.sbuf_tensor(n, s, d))
        HC4 = HID // 4
        hrow = sbc("p4_hrow", [128, HID], F32)
        hsq = sbc("p4_hsq", [128, HC4], F32)
        hn = sbc("p4_hn", [128, HC4], I32)
        hr = sbc("p4_hr", [128, HC4], F32)
        ht = sbc("p4_ht", [128, HC4], F32)
        hqq = sbc("p4_hqq", [128, HID], BF16)
        hstage = sbc("p4_hstage", [128, 44 * 128], BF16)
        hsums = sbc("p4_hsums", [128, 4], F32)
        b_hrow, b_hsq, b_hn, b_hr, b_ht, b_hqq, b_hst, b_hsu = B(
            "p4_hrow", "p4_hsq", "p4_hn", "p4_hr", "p4_ht", "p4_hqq", "p4_hstage",
            "p4_hsums")

        with contextlib.ExitStack() as esp:
            ptr = [esp.enter_context(nc.psum_tensor(f"p4c_ptr{i}", [128, 128], BF16))
                   for i in range(8)]
            b_ptr = B(*[f"p4c_ptr{i}" for i in range(8)])
            for s_ in range(NSLOT):
                @blk.sync
                def _(sy, s_=s_):
                    k.pre(sy, "SH", reads=[b_hqd], writes=[b_hrow])
                    k.post(sy.dma_start(out=hrow[:], in_=e["hq_d"][s_, :, :]),
                           "SH", [b_hqd], [b_hrow])

                @blk.vector
                def _(v):
                    k.pre(v, "V", reads=[b_hrow], writes=[b_cc4])
                    k.post(v.reduce_max(ccols[:, 0:1], hrow[:], axis=AX,
                                        apply_absolute_value=True), "V", [b_hrow], [b_cc4])

                for c4 in range(4):
                    @blk.scalar
                    def _(s, c4=c4):
                        k.pre(s, "A", reads=[b_hrow], writes=[b_hsq, b_hsu])
                        k.post(s.activation(hsq[:], hrow[:, c4 * HC4:(c4 + 1) * HC4],
                                            AF.Square, accum_out=hsums[:, c4:c4 + 1]),
                               "A", [b_hrow], [b_hsq, b_hsu])

                @blk.vector
                def _(v):
                    k.pre(v, "V", reads=[b_hsu], writes=[b_cc4])
                    k.post(v.reduce_sum(ccols[:, 1:2], hsums[:], axis=AX),
                           "V", [b_hsu], [b_cc4])

                quant_scale_chain(k, blk, e, ccols, b_cc4, msc3, b_ms3, None, None, s_,
                                  HID, b_mwb[2], None, e["mwb"][2], None)

                for c4 in range(4):
                    sl = slice(c4 * HC4, (c4 + 1) * HC4)

                    @blk.scalar
                    def _(s, sl=sl):
                        k.pre(s, "A", reads=[b_hrow, b_cc4], writes=[b_hn])
                        k.post(s.activation(hn[:], hrow[:, sl], AF.Identity,
                                            scale=ccols[:, 7:8]), "A", [b_hrow, b_cc4], [b_hn])

                    @blk.vector
                    def _(v, sl=sl):
                        k.pre(v, "V", reads=[b_hrow, b_cc4, b_hn], writes=[b_hr])
                        k.post(v.scalar_tensor_tensor(hr[:], hrow[:, sl], ccols[:, 7:8], hn[:],
                                                      ALU.mult, ALU.subtract),
                               "V", [b_hrow, b_cc4, b_hn], [b_hr])
                        k.pre(v, "V", reads=[b_hr, b_hn], writes=[b_ht])
                        k.post(v.scalar_tensor_tensor(ht[:], hr[:], 0.5, hn[:],
                                                      ALU.is_ge, ALU.add), "V",
                               [b_hr, b_hn], [b_ht])
                        k.pre(v, "V", reads=[b_hr, b_ht], writes=[b_hqq])
                        k.post(v.scalar_tensor_tensor(hqq[:, sl], hr[:], -0.5, ht[:],
                                                      ALU.is_le, ALU.subtract),
                               "V", [b_hr, b_ht], [b_hqq])

                for gb in range(6):
                    n = 8 if gb < 5 else 4

                    @blk.tensor
                    def _(t, gb=gb, n=n):
                        k.pre(t, "T", reads=[b_hqq, b_id], writes=b_ptr[:n])
                        for i in range(n):
                            hc = gb * 8 + i
                            k.post(t.transpose(ptr[i][:], hqq[:, hc * 128:(hc + 1) * 128],
                                               e["ident_s"][:]), "T", [b_hqq, b_id], [b_ptr[i]])

                    @blk.vector
                    def _(v, gb=gb, n=n):
                        k.pre(v, "V", reads=b_ptr[:n], writes=[b_hst])
                        ins = None
                        for i in range(n):
                            hc = gb * 8 + i
                            ins = v.tensor_copy(hstage[:, hc * 128:(hc + 1) * 128], ptr[i][:])
                        k.post(ins, "V", b_ptr[:n], [b_hst])

                @blk.scalar
                def _(s, s_=s_):
                    k.pre(s, "SS", reads=[b_hst], writes=[b_hqtd])
                    k.post(s.dma_start(
                        out=e["hqt_d"][:, :, s_ * 128:(s_ + 1) * 128].rearrange(
                            "c p t -> p c t"),
                        in_=hstage[:].rearrange("p (c t) -> p c t", c=44)),
                        "SS", [b_hst], [b_hqtd])

        es_c.close()
        k.barrier(blk)
        hqtb = [sb(f"p4_hqtb{i}", [128, NSLOT * 128], BF16) for i in range(2)]
        w3qc = [sb(f"p4_w3qc{i}", [128, D], BF16) for i in range(2)]
        vsl = sb("p4b_v", [128, D], F32)
        sq2 = sb("p4b_sq", [128, D], F32)
        outs = sb("p4b_out", [128, D], F32)
        x1b = sb("p4b_x1", [128, D], F32)
        b_hqtb = B("p4_hqtb0", "p4_hqtb1")
        b_w3qc = B("p4_w3qc0", "p4_w3qc1")
        b_v4, b_sq2, b_os = B("p4b_v", "p4b_sq", "p4b_out")
        b_x1b = B("p4b_x1")[0]

        with contextlib.ExitStack() as esp:
            psZ = [esp.enter_context(nc.psum_tensor(f"p4_psz{i}", [128, 2048], F32))
                   for i in range(2)]
            b_psZ = B("p4_psz0", "p4_psz1")
            for sp in range(4):
                slots = [2 * sp, 2 * sp + 1]
                for hc in range(44):
                    pa = hc % 2

                    @blk.sync
                    def _(sy, hc=hc, pa=pa):
                        k.pre(sy, "SH", reads=[b_hqtd, b_w3qn], writes=[b_hqtb[pa], b_w3qc[pa]])
                        k.post(sy.dma_start(out=hqtb[pa][:], in_=e["hqt_d"][hc, :, :]),
                               "SH", [b_hqtd], [b_hqtb[pa]])
                        for cb in range(4):
                            k.post(sy.dma_start_transpose(
                                w3qc[pa][:, cb * 512:(cb + 1) * 512],
                                e["w3qn"][cb * 512:(cb + 1) * 512, hc * 128:(hc + 1) * 128]),
                                "SH", [b_w3qn], [b_w3qc[pa]])

                    @blk.tensor
                    def _(t, hc=hc, pa=pa, slots=slots):
                        k.pre(t, "T", reads=[b_hqtb[pa], b_w3qc[pa]], writes=list(b_psZ))
                        ins = None
                        for si, s_ in enumerate(slots):
                            xs = hqtb[pa][:, s_ * 128:(s_ + 1) * 128]
                            t.ldweights(xs)
                            for cb in range(4):
                                ins = t.matmul(psZ[si][:, cb * 512:(cb + 1) * 512], xs,
                                               w3qc[pa][:, cb * 512:(cb + 1) * 512],
                                               start=(hc == 0), stop=(hc == 43),
                                               skip_group_check=True)
                        k.post(ins, "T", [b_hqtb[pa], b_w3qc[pa]], list(b_psZ))

                for si, s_ in enumerate(slots):
                    @blk.sync
                    def _(sy, s_=s_):
                        k.pre(sy, "SH", reads=[b_x1d], writes=[b_x1b])
                        k.post(sy.dma_start(out=x1b[:], in_=e["x1_d"][s_, :, :]),
                               "SH", [b_x1d], [b_x1b])

                    @blk.vector
                    def _(v, si=si, s_=s_):
                        k.pre(v, "V", reads=[b_psZ[si], b_ms3, b_x1b], writes=[b_v4, b_cc4])
                        k.post(v.scalar_tensor_tensor(
                            vsl[:], psZ[si][:], msc3[:, s_:s_ + 1],
                            x1b[:], ALU.mult, ALU.add,
                            accum_out=ccols[:, 0:1]),
                            "V", [b_psZ[si], b_ms3, b_x1b], [b_v4, b_cc4])

                    ln_epilogue(k, blk, e, vsl, b_v4, sq2, b_sq2, ccols, b_cc4,
                                outs[:], b_os, D)

                    @blk.scalar
                    def _(s, s_=s_):
                        k.pre(s, "SS", reads=[b_os], writes=[b_out])
                        k.post(s.dma_start(out=e["out"][s_, :, :], in_=outs[:]),
                               "SS", [b_os], [b_out])


def quant_scale_chain(k, blk, e, ccols, b_cc, mscA, b_msA, mscB, b_msB, s_, dim,
                      b_mA, b_mB, mA, mB):
    @blk.vector
    def _(v):
        k.pre(v, "V", reads=[b_cc], writes=[b_cc])
        k.post(v.tensor_scalar(ccols[:, 2:3], ccols[:, 1:2], 1.0 / dim, 1e-6,
                               ALU.mult, ALU.add), "V", [b_cc], [b_cc])
        k.pre(v, "V", reads=[b_cc], writes=[b_cc])
        k.post(v.reciprocal(ccols[:, 3:4], ccols[:, 2:3]), "V", [b_cc], [b_cc])

    @blk.scalar
    def _(s):
        k.pre(s, "A", reads=[b_cc], writes=[b_cc])
        k.post(s.activation(ccols[:, 4:5], ccols[:, 3:4], AF.Sqrt), "A", [b_cc], [b_cc])

    @blk.vector
    def _(v, s_=s_):
        k.pre(v, "V", reads=[b_cc], writes=[b_cc])
        k.post(v.scalar_tensor_tensor(ccols[:, 5:6], ccols[:, 0:1], 1.0, ccols[:, 4:5],
                                      ALU.mult, ALU.mult), "V", [b_cc], [b_cc])
        k.pre(v, "V", reads=[b_cc], writes=[b_cc])
        k.post(v.tensor_scalar(ccols[:, 5:6], ccols[:, 5:6], 1e-5, None, ALU.max),
               "V", [b_cc], [b_cc])
        k.pre(v, "V", reads=[b_cc], writes=[b_cc])
        k.post(v.reciprocal(ccols[:, 6:7], ccols[:, 5:6]), "V", [b_cc], [b_cc])
        k.pre(v, "V", reads=[b_cc], writes=[b_cc])
        k.post(v.scalar_tensor_tensor(ccols[:, 7:8], ccols[:, 6:7], 127.0, ccols[:, 4:5],
                                      ALU.mult, ALU.mult), "V", [b_cc], [b_cc])
        k.pre(v, "V", reads=[b_cc], writes=[b_cc])
        k.post(v.tensor_scalar(ccols[:, 6:7], ccols[:, 5:6], 1.0 / 127.0, None, ALU.mult),
               "V", [b_cc], [b_cc])
        k.pre(v, "V", reads=[b_cc, b_mA], writes=[b_msA])
        k.post(v.scalar_tensor_tensor(mscA[:, s_:s_ + 1], ccols[:, 6:7], 1.0, mA[:],
                                      ALU.mult, ALU.mult), "V", [b_cc, b_mA], [b_msA])
        if mscB is not None:
            k.pre(v, "V", reads=[b_cc, b_mB], writes=[b_msB])
            k.post(v.scalar_tensor_tensor(mscB[:, s_:s_ + 1], ccols[:, 6:7], 1.0, mB[:],
                                          ALU.mult, ALU.mult), "V", [b_cc, b_mB], [b_msB])


def make_inputs(core, x, ws):
    tiles = [8 * s + core for s in range(NSLOT)]
    xo = np.stack([x[t * 128:(t + 1) * 128, :] for t in tiles])
    qpos_c = np.stack([np.arange(t * 128, (t + 1) * 128, dtype=np.float32) for t in tiles],
                      axis=1)
    qrel_c = (np.arange(128, dtype=np.float32) + core * 128)[:, None].astype(np.float32)
    return dict(xf=x, xo=xo, qpos_c=qpos_c, qrel_c=qrel_c,
                ident=np.eye(128, dtype=bfdt), **ws)



_NC_CACHE = {}


def _get_nc():
    if "nc" not in _NC_CACHE:
        _NC_CACHE["nc"] = build(dev=False)
    return _NC_CACHE["nc"]


def kernel(L0_attn_x, L0_attn_wq, L0_attn_w_down, L0_attn_w_up, L0_attn_wo,
           L0_ffn_w1, L0_ffn_w2, L0_ffn_w3):
    x = np.ascontiguousarray(np.asarray(L0_attn_x, dtype=np.float32).reshape(T, D))
    ws = dict(
        wq=np.ascontiguousarray(np.asarray(L0_attn_wq, dtype=np.float32)),
        wdn=np.ascontiguousarray(np.asarray(L0_attn_w_down, dtype=np.float32)),
        wup=np.ascontiguousarray(np.asarray(L0_attn_w_up, dtype=np.float32)),
        wo_=np.ascontiguousarray(np.asarray(L0_attn_wo, dtype=np.float32)),
        w1=np.ascontiguousarray(np.asarray(L0_ffn_w1, dtype=np.float32)),
        w2=np.ascontiguousarray(np.asarray(L0_ffn_w2, dtype=np.float32)),
        w3=np.ascontiguousarray(np.asarray(L0_ffn_w3, dtype=np.float32)),
    )
    nc = _get_nc()
    in_maps = [make_inputs(c, x, ws) for c in range(8)]
    res = run_bass_kernel_spmd(nc, in_maps, core_ids=list(range(8)))
    out = np.empty((T, D), dtype=np.float32)
    for c in range(8):
        oc = np.asarray(res.results[c]["out"])
        for s in range(NSLOT):
            t = 8 * s + c
            out[t * 128:(t + 1) * 128, :] = oc[s]
    return out.reshape(1, T, D)


# version 08
# speedup vs baseline: 1.0217x; 1.0217x over previous
import sys
import contextlib
import math

import numpy as np
import ml_dtypes
import concourse.bass as bass
import concourse.mybir as mybir
from concourse.bass_utils import run_bass_kernel_spmd

F32 = mybir.dt.float32
BF16 = mybir.dt.bfloat16
I32 = mybir.dt.int32
AF = mybir.ActivationFunctionType
ALU = mybir.AluOpType
AX = mybir.AxisListType.X
bfdt = ml_dtypes.bfloat16

T = 8192
D = 2048
DLAT = 1024
NH = 16
DH = 128
HID = 5632
NSLOT = 8
SCALE = DH ** -0.5
TWO_PI = 2.0 * math.pi
C1 = 6.28125
C2 = TWO_PI - C1
INV_2PI = 1.0 / TWO_PI
LN1E4 = math.log(10000.0)

QUADS = [[7, 6, 1, 0], [5, 4, 3, 2]]
QCOL = {s: (qi, ci) for qi, qs in enumerate(QUADS) for ci, s in enumerate(qs)}

DMA_CH = ("SH", "SS", "SG")


class Buf:
    __slots__ = ("w", "rs", "name", "acc")

    def __init__(self, name=""):
        self.w = {}
        self.rs = []
        self.name = name
        self.acc = False


class FakeInstr:
    __slots__ = ("rec",)

    def __init__(self, rec):
        self.rec = rec

    def then_inc(self, sem, n):
        self.rec[3] = (sem, n)
        return self


class EngProxy:

    def __init__(self, q):
        self._q = q

    def __getattr__(self, name):
        def call(*args, **kwargs):
            rec = [name, args, kwargs, None]
            self._q.append(rec)
            return FakeInstr(rec)
        return call


class FakeBlock:
    def __init__(self):
        self.q = {"T": [], "V": [], "A": [], "G": [], "SY": []}
        self.prox = {k_: EngProxy(v) for k_, v in self.q.items()}

    def _dec(self, ch):
        def deco(fn):
            fn(self.prox[ch])
            return None
        return deco

    @property
    def tensor(self):
        return self._dec("T")

    @property
    def vector(self):
        return self._dec("V")

    @property
    def scalar(self):
        return self._dec("A")

    @property
    def gpsimd(self):
        return self._dec("G")

    @property
    def sync(self):
        return self._dec("SY")

    def flush(self, blk):
        for ch, eng_attr in [("T", "tensor"), ("V", "vector"), ("A", "scalar"),
                             ("G", "gpsimd"), ("SY", "sync")]:
            recs = self.q[ch]
            if not recs:
                continue

            def body(eng, recs=recs):
                for name, args, kwargs, inc in recs:
                    ins = getattr(eng, name)(*args, **kwargs)
                    if inc is not None:
                        ins.then_inc(*inc)
            getattr(blk, eng_attr)(body)


class B:
    def __init__(self, nc, es):
        self.nc = nc
        self.es = es
        self.sems = {}
        self.cnt = {}
        self.seen = {}

    def sem(self, name):
        if name not in self.sems:
            self.sems[name] = self.es.enter_context(self.nc.semaphore(name))
            self.cnt[name] = 0
        return self.sems[name]

    @staticmethod
    def ekey(ch):
        return {"SH": "SY", "SS": "A", "SG": "G"}.get(ch, ch)

    def pre(self, eng, ch, reads=(), writes=()):
        ek = self.ekey(ch)
        seen = self.seen.setdefault(ek, {})
        need = {}
        for b in reads:
            for c, v in b.w.items():
                need[c] = max(need.get(c, 0), v)
        for b in writes:
            if not b.acc:
                for c, v in b.w.items():
                    need[c] = max(need.get(c, 0), v)
            for c, v in b.rs:
                need[c] = max(need.get(c, 0), v)
        for c in sorted(need):
            v = need[c]
            if c == "T" and ch == "T":
                continue
            if seen.get(c, 0) >= v:
                continue
            eng.wait_ge(self.sem(c), v)
            seen[c] = v

    def post(self, instr, ch, reads=(), writes=()):
        if ch in DMA_CH:
            key = None
            for b in writes:
                if not b.acc:
                    key = b
                    break
            if key is None:
                for b in reads:
                    if not b.acc:
                        key = b
                        break
            sname = f"D_{ch}_{key.name if key is not None else 'misc'}"
            n = 16
        else:
            sname = ch
            n = 1
        s = self.sem(sname)
        self.cnt[sname] += n
        instr.then_inc(s, n)
        m = (sname, self.cnt[sname])
        for b in writes:
            if b.acc:
                b.w[m[0]] = m[1]
            else:
                b.w = {m[0]: m[1]}
                b.rs = []
        for b in reads:
            b.rs.append(m)
        return m

    def barrier(self, blk):
        chans = [c for c in self.cnt if self.cnt[c] > 0]

        def wait_all(eng, ek):
            seen = self.seen.setdefault(ek, {})
            for c in chans:
                v = self.cnt[c]
                if seen.get(c, 0) >= v:
                    continue
                eng.wait_ge(self.sem(c), v)
                seen[c] = v

        @blk.tensor
        def _(t):
            wait_all(t, "T")

        @blk.vector
        def _(v):
            wait_all(v, "V")

        @blk.scalar
        def _(s):
            wait_all(s, "A")

        @blk.gpsimd
        def _(g):
            wait_all(g, "G")

        @blk.sync
        def _(sy):
            wait_all(sy, "SY")

    def sb(self, name, shape, dtype):
        return self.es.enter_context(self.nc.sbuf_tensor(name, shape, dtype))

    def ps(self, name, shape, dtype):
        return self.es.enter_context(self.nc.psum_tensor(name, shape, dtype))


def build(dev=False, upto=99):
    nc = bass.Bass("TRN2", target_bir_lowering=False, debug=False)
    sk = "ExternalOutput" if dev else "Internal"

    io = {}
    io["xf"] = nc.dram_tensor("xf", [T, D], F32, kind="ExternalInput").ap()
    io["xo"] = nc.dram_tensor("xo", [NSLOT, 128, D], F32, kind="ExternalInput").ap()
    io["qpos_c"] = nc.dram_tensor("qpos_c", [128, NSLOT], F32, kind="ExternalInput").ap()
    io["qrel_c"] = nc.dram_tensor("qrel_c", [128, 1], F32, kind="ExternalInput").ap()
    io["wq"] = nc.dram_tensor("wq", [D, D], F32, kind="ExternalInput").ap()
    io["wdn"] = nc.dram_tensor("wdn", [D, DLAT], F32, kind="ExternalInput").ap()
    io["wup"] = nc.dram_tensor("wup", [DLAT, 2 * D], F32, kind="ExternalInput").ap()
    io["wo_"] = nc.dram_tensor("wo_", [D, D], F32, kind="ExternalInput").ap()
    io["w1"] = nc.dram_tensor("w1", [HID, D], F32, kind="ExternalInput").ap()
    io["w2"] = nc.dram_tensor("w2", [HID, D], F32, kind="ExternalInput").ap()
    io["w3"] = nc.dram_tensor("w3", [D, HID], F32, kind="ExternalInput").ap()
    io["ident"] = nc.dram_tensor("ident", [128, 128], BF16, kind="ExternalInput").ap()
    io["out"] = nc.dram_tensor("out", [NSLOT, 128, D], F32, kind="ExternalOutput").ap()
    io["ckv_d"] = nc.dram_tensor("ckv_d", [8, 128, T], BF16, kind=sk).ap()
    io["w1qn"] = nc.dram_tensor("w1qn", [HID, D], BF16, kind="Internal").ap()
    io["w2qn"] = nc.dram_tensor("w2qn", [HID, D], BF16, kind="Internal").ap()
    io["w3qn"] = nc.dram_tensor("w3qn", [D, HID], BF16, kind="Internal").ap()
    io["hq_d"] = nc.dram_tensor("hq_d", [NSLOT, 128, HID], F32, kind=sk).ap()
    io["hqt_d"] = nc.dram_tensor("hqt_d", [44, 128, NSLOT * 128], BF16, kind=sk).ap()
    io["x1_d"] = nc.dram_tensor("x1_d", [NSLOT, 128, D], F32, kind=sk).ap()
    for nm in ["dbg_q", "dbg_kf", "dbg_vt", "dbg_at"]:
        io[nm] = nc.dram_tensor(nm, [128, {"dbg_q": NSLOT * D, "dbg_kf": T, "dbg_vt": T,
                                           "dbg_at": NH * 1024}[nm]], BF16,
                                kind="ExternalOutput").ap() if dev else None
    io["dbg_x1"] = None

    with contextlib.ExitStack() as es:
        k = B(nc, es)
        e = dict(io)
        e["bufs"] = {}

        def bufs(*names):
            return [e["bufs"].setdefault(n, Buf(n)) for n in names]

        e["B"] = bufs
        for nm in ["ckv_d", "hq_d", "hqt_d", "x1_d", "w1qn", "w2qn", "w3qn", "out_d"]:
            bufs(nm)[0].acc = True

        e["ident_s"] = k.sb("ident_s", [128, 128], BF16)
        e["qposc"] = k.sb("qposc", [128, NSLOT], F32)
        e["qrelc"] = k.sb("qrelc", [128, 1], F32)
        e["IVF"] = k.sb("IVF", [128, 64], F32)
        e["onescol"] = k.sb("onescol", [128, 1], BF16)
        e["onescol_f"] = k.sb("onescol_f", [128, 1], F32)
        e["onesrow_f"] = k.sb("onesrow_f", [1, 128], F32)
        e["onesrow_b"] = k.sb("onesrow_b", [1, 128], BF16)
        e["identf"] = k.sb("identf", [128, 128], F32)
        e["thr"] = [k.sb(f"thrw{i}", [128, 1], F32) for i in range(3)]
        e["nthr"] = [k.sb(f"nthrw{i}", [128, 1], F32) for i in range(3)]
        e["mwb"] = [k.sb(f"mwb{i}", [128, 1], F32) for i in range(3)]

        with nc.Block() as real_blk:
            blk = FakeBlock()
            with contextlib.ExitStack() as es_t:
                sbt = lambda n, s, d: es_t.enter_context(nc.sbuf_tensor(n, s, d))
                e["CC"] = sbt("CC", [128, 64 * 128], BF16)
                e["SS"] = sbt("SS_t", [128, 64 * 128], BF16)
                e["bmq"] = sbt("bmq", [128, 1024], BF16)
                e["bmk"] = sbt("bmk", [128, 1024], BF16)
                e["qrot"] = sbt("qrot", [128, NSLOT * D], BF16)
                p0(k, blk, e)
                k.barrier(blk)
                if upto >= 1:
                    p0w(k, blk, e)
                    k.barrier(blk)
                if upto >= 2:
                    p1(k, blk, e)
                    k.barrier(blk)
                e["attn"] = sbt("attn", [128, NH * 1024], BF16)
                if upto >= 3:
                    p2(k, blk, e)
                    k.barrier(blk)
                if upto >= 4:
                    p3(k, blk, e)
                    k.barrier(blk)
            if upto >= 5:
                p4(k, blk, e)

            @blk.scalar
            def _(s):
                s.wait_ge(k.sem("SS"), k.cnt["SS"])

            @blk.sync
            def _(sy):
                sy.wait_ge(k.sem("SH"), k.cnt["SH"])

            @blk.gpsimd
            def _(g):
                g.wait_ge(k.sem("SG"), k.cnt["SG"])

            blk.flush(real_blk)
    return nc


def sincos(k, blk, e, ang, shape, nbuf, t1, t2, outbuf, shift, extra_reads=()):
    (b_ang, b_n, b_t1, b_t2, b_out) = e["B"](*[f"sc_{id(x)}" for x in (ang, nbuf, t1, t2, outbuf)])

    @blk.vector
    def _(v):
        k.pre(v, "V", reads=[b_ang, *extra_reads], writes=[b_t1])
        if shift != 0.0:
            k.post(v.tensor_scalar(t1[:], ang, shift, None, ALU.add), "V", [b_ang], [b_t1])
        else:
            k.post(v.tensor_copy(t1[:], ang), "V", [b_ang], [b_t1])

    @blk.scalar
    def _(s):
        k.pre(s, "A", reads=[b_t1], writes=[b_n])
        k.post(s.activation(nbuf[:], t1[:], AF.Identity, scale=INV_2PI), "A", [b_t1], [b_n])

    @blk.vector
    def _(v):
        k.pre(v, "V", reads=[b_n, b_t1], writes=[b_t2])
        k.post(v.scalar_tensor_tensor(t2[:], nbuf[:], -C1, t1[:], ALU.mult, ALU.add),
               "V", [b_n, b_t1], [b_t2])
        k.pre(v, "V", reads=[b_n, b_t2], writes=[b_t1])
        k.post(v.scalar_tensor_tensor(t1[:], nbuf[:], -C2, t2[:], ALU.mult, ALU.add),
               "V", [b_n, b_t2], [b_t1])
        k.pre(v, "V", reads=[b_t1], writes=[b_t2])
        k.post(v.tensor_scalar(t2[:], t1[:], math.pi, TWO_PI, ALU.is_gt, ALU.mult),
               "V", [b_t1], [b_t2])
        k.pre(v, "V", reads=[b_t1, b_t2], writes=[b_t1])
        k.post(v.tensor_tensor(t1[:], t1[:], t2[:], ALU.subtract), "V", [b_t1, b_t2], [b_t1])
        k.pre(v, "V", reads=[b_t1], writes=[b_t2])
        k.post(v.tensor_scalar(t2[:], t1[:], -math.pi, TWO_PI, ALU.is_lt, ALU.mult),
               "V", [b_t1], [b_t2])
        k.pre(v, "V", reads=[b_t1, b_t2], writes=[b_t1])
        k.post(v.tensor_tensor(t1[:], t1[:], t2[:], ALU.add), "V", [b_t1, b_t2], [b_t1])

    @blk.scalar
    def _(s):
        k.pre(s, "A", reads=[b_t1], writes=[b_out])
        k.post(s.activation(outbuf[:], t1[:], AF.Sin), "A", [b_t1], [b_out])
    return b_out


def p0(k, blk, e):
    nc = k.nc
    B = e["B"]
    es0 = contextlib.ExitStack()
    sb = lambda n, s, d: es0.enter_context(nc.sbuf_tensor(n, s, d))
    POS = sb("p0_pos", [128, 64], F32)
    J = sb("p0_j", [128, 64], F32)
    ANG = sb("p0_ang", [128, 4096], F32)
    NN = sb("p0_n", [128, 4096], I32)
    S1 = sb("p0_s1", [128, 4096], F32)
    S2 = sb("p0_s2", [128, 4096], F32)
    SINX = sb("p0_sin", [128, 4096], F32)
    COSX = sb("p0_cos", [128, 4096], F32)
    KJ = sb("p0_kj", [128, 1024], F32)
    e["_es0"] = es0
    (b_id, b_qp, b_qr, b_pos, b_j, b_kj, b_ivf, b_ang, b_cc, b_ss, b_bmq, b_ones) = B(
        "ident", "qposc", "qrelc", "pos", "j", "kj", "ivf", "ang", "CC", "SS", "bmq", "ones")

    @blk.sync
    def _(sy):
        k.post(sy.dma_start(out=e["ident_s"][:], in_=e["ident"][:, :]), "SH", [], [b_id])
        k.post(sy.dma_start(out=e["qposc"][:], in_=e["qpos_c"][:, :]), "SH", [], [b_qp])
        k.post(sy.dma_start(out=e["qrelc"][:], in_=e["qrel_c"][:, :]), "SH", [], [b_qr])

    @blk.gpsimd
    def _(g):
        k.post(g.iota(POS[:], pattern=[[128, 64]], base=0, channel_multiplier=1,
                      allow_small_or_imprecise_dtypes=True), "G", [], [b_pos])
        k.post(g.iota(J[:], pattern=[[1, 64]], base=0, channel_multiplier=0,
                      allow_small_or_imprecise_dtypes=True), "G", [], [b_j])
        k.post(g.iota(KJ[:], pattern=[[1, 1024]], base=0, channel_multiplier=0,
                      allow_small_or_imprecise_dtypes=True), "G", [], [b_kj])

    @blk.scalar
    def _(s):
        k.pre(s, "A", reads=[b_j], writes=[b_ivf])
        k.post(s.activation(e["IVF"][:], J[:], AF.Exp, scale=-LN1E4 / 64.0), "A", [b_j], [b_ivf])

    @blk.vector
    def _(v):
        k.post(v.memset(e["onescol"][:], 1.0), "V", [], [b_ones])
        k.post(v.memset(e["onescol_f"][:], 1.0), "V", [], [e["B"]("onescol_f")[0]])
        k.post(v.memset(e["onesrow_f"][:], 1.0), "V", [], [e["B"]("onesrow_f")[0]])
        k.post(v.memset(e["onesrow_b"][:], 1.0), "V", [], [e["B"]("onesrow_b")[0]])
        k.pre(v, "V", reads=[b_id], writes=[e["B"]("identf")[0]])
        k.post(v.tensor_copy(e["identf"][:], e["ident_s"][:]), "V", [b_id],
               [e["B"]("identf")[0]])
        k.pre(v, "V", reads=[b_pos, b_ivf], writes=[b_ang])
        k.post(v.tensor_tensor(
            ANG[:],
            POS[:].rearrange("p (kt one) -> p kt one", one=1).broadcast_to([128, 64, 64]),
            e["IVF"][:].rearrange("p (one j) -> p one j", one=1).broadcast_to([128, 64, 64]),
            ALU.mult), "V", [b_pos, b_ivf], [b_ang])

    b_sin = sincos(k, blk, e, ANG[:], None, NN, S1, S2, SINX, 0.0, extra_reads=[b_ang])
    b_cos = sincos(k, blk, e, ANG[:], None, NN, S1, S2, COSX, math.pi / 2.0, extra_reads=[b_ang])

    @blk.vector
    def _(v):
        cview = COSX[:].rearrange("p (kt j) -> p kt j", j=64)
        sview = SINX[:].rearrange("p (kt j) -> p kt j", j=64)
        cc = e["CC"][:].rearrange("p (kt j) -> p kt j", j=128)
        ss = e["SS"][:].rearrange("p (kt j) -> p kt j", j=128)
        k.pre(v, "V", reads=[b_sin, b_cos], writes=[b_cc, b_ss])
        k.post(v.tensor_copy(cc[:, :, 0:128:2], cview), "V", [b_cos], [b_cc])
        k.post(v.tensor_copy(cc[:, :, 1:128:2], cview), "V", [b_cos], [b_cc])
        k.post(v.tensor_copy(ss[:, :, 1:128:2], sview), "V", [b_sin], [b_ss])
        k.post(v.tensor_scalar(ss[:, :, 0:128:2], sview, -1.0, None, ALU.mult),
               "V", [b_sin], [b_ss])
        k.pre(v, "V", reads=[b_kj, b_qr], writes=[b_bmq])
        k.post(v.tensor_scalar(e["bmq"][:], KJ[:], e["qrelc"][:, 0:1], None, ALU.is_le),
               "V", [b_kj, b_qr], [b_bmq])
    ICOL = sb("p0_icol", [128, 1], F32)
    TC0 = sb("p0_tc0", [128, 1], F32)
    TCJ = sb("p0_tcj", [128, 8], F32)
    b_icol, b_tc0, b_tcj, b_bmk = B("p0_icol", "p0_tc0", "p0_tcj", "bmk")

    @blk.gpsimd
    def _(g):
        k.post(g.iota(ICOL[:], pattern=[[1, 1]], base=0, channel_multiplier=1,
                      allow_small_or_imprecise_dtypes=True), "G", [], [b_icol])

    @blk.vector
    def _(v):
        k.pre(v, "V", reads=[b_icol, b_qr], writes=[b_tc0])
        k.post(v.scalar_tensor_tensor(TC0[:], ICOL[:], 2.0, e["qrelc"][:], ALU.mult,
                                      ALU.subtract), "V", [b_icol, b_qr], [b_tc0])
        for jm in range(8):
            k.pre(v, "V", reads=[b_tc0], writes=[b_tcj])
            k.post(v.tensor_scalar(TCJ[:, jm:jm + 1], TC0[:], float(jm * 128), None,
                                   ALU.add), "V", [b_tc0], [b_tcj])
        for jm in range(8):
            k.pre(v, "V", reads=[b_tcj, b_kj], writes=[b_bmk])
            k.post(v.tensor_scalar(e["bmk"][:, jm * 128:(jm + 1) * 128], KJ[:, 0:128],
                                   TCJ[:, jm:jm + 1], None, ALU.is_ge),
                   "V", [b_tcj, b_kj], [b_bmk])
    e.pop("_es0").close()


def p0w(k, blk, e):
    nc = k.nc
    B = e["B"]
    CH = 4096
    nchunk = HID * D // (128 * CH)
    wl = [(e["w1"], None), (e["w2"], None), (e["w3"], None)]

    esw = contextlib.ExitStack()
    e["_esw"] = esw
    sbw = lambda n, s, d: esw.enter_context(nc.sbuf_tensor(n, s, d))
    wbuf = [sbw(f"p0w_buf{i}", [128, CH], F32) for i in range(2)]
    abscr = [sbw(f"p0w_abs{i}", [128, CH], F32) for i in range(2)]
    acc = sbw("p0w_acc", [128, 3 * nchunk + 3], F32)
    accr = sbw("p0w_accr", [128, nchunk], F32)
    csb = sbw("p0w_csb", [128, 4], F32)
    tot = sbw("p0w_tot", [1, 16], F32)
    ps_tot = esw.enter_context(nc.psum_tensor("p0w_pstot", [1, 128], F32))

    b_wb = B("p0w_wb0", "p0w_wb1")
    b_scr = B("p0w_scr0", "p0w_scr1")
    (b_acc, b_csb, b_tot, b_pst, b_ones) = B("p0w_acc", "p0w_csb", "p0w_tot", "p0w_pst", "ones")

    for wi, (wsrc, wdst) in enumerate(wl):
        wflat = wsrc.rearrange("a b -> (a b)").rearrange("(r p c) -> r p c", p=128, c=CH)
        for ci in range(nchunk):
            pa = ci % 2

            @blk.sync
            def _(sy, pa=pa, wflat=wflat, ci=ci):
                k.pre(sy, "SH", writes=[b_wb[pa]])
                k.post(sy.dma_start(out=wbuf[pa][:], in_=wflat[ci, :, :]), "SH", [], [b_wb[pa]])

            @blk.scalar
            def _(s, pa=pa, wi=wi, ci=ci):
                k.pre(s, "A", reads=[b_wb[pa]], writes=[b_scr[pa], b_acc])
                k.post(s.activation(abscr[pa][:], wbuf[pa][:], AF.Abs,
                                    accum_out=acc[:, wi * nchunk + ci: wi * nchunk + ci + 1]),
                       "A", [b_wb[pa]], [b_scr[pa], b_acc])


    CL2 = 64.0
    CL3 = CL2 * nchunk
    b_onesf = B("onescol_f")[0]

    @blk.vector
    def _(v):
        k.pre(v, "V", reads=[b_acc], writes=[b_acc, b_csb])
        for wi in range(3):
            k.post(v.tensor_scalar(accr[:],
                                   acc[:, wi * nchunk:(wi + 1) * nchunk], -CL2, None,
                                   ALU.add), "V", [b_acc], [b_acc])
            v.drain()
            k.post(v.reduce_sum(acc[:, 3 * nchunk + wi: 3 * nchunk + wi + 1],
                                accr[:], axis=AX),
                   "V", [b_acc], [b_acc])
            v.drain()
        k.pre(v, "V", reads=[b_acc], writes=[b_csb])
        k.post(v.tensor_copy(csb[:, 0:3], acc[:, 3 * nchunk:3 * nchunk + 3]),
               "V", [b_acc], [b_csb])

    @blk.tensor
    def _(t):
        k.pre(t, "T", reads=[b_csb, b_onesf], writes=[b_pst])
        for wi in range(3):
            k.post(t.matmul(ps_tot[0:1, wi * 8:wi * 8 + 1], csb[:, wi:wi + 1],
                            e["onescol_f"][:], start=True, stop=True, skip_group_check=True),
                   "T", [b_csb, b_onesf], [b_pst])

    @blk.vector
    def _(v):
        k.pre(v, "V", reads=[b_pst], writes=[b_tot])
        for wi in range(3):
            k.post(v.tensor_scalar(tot[0:1, wi:wi + 1], ps_tot[0:1, wi * 8:wi * 8 + 1],
                                   float(128.0 * CL3), None, ALU.add),
                   "V", [b_pst], [b_tot])

    b_mwb = B("mwb0", "mwb1", "mwb2")
    b_thr = B("thr0", "thr1", "thr2")
    b_nth = B("nthr0", "nthr1", "nthr2")
    b_orf = B("onesrow_f")[0]
    b_psbc = B("p0w_psbc")[0]
    ps_bc = esw.enter_context(nc.psum_tensor("p0w_psbc", [128, 4], F32))

    @blk.tensor
    def _(t):
        k.pre(t, "T", reads=[b_tot, b_orf], writes=[b_psbc])
        for wi in range(3):
            k.post(t.matmul(ps_bc[:, wi:wi + 1], e["onesrow_f"][0:1, :],
                            tot[0:1, wi:wi + 1], start=(wi == 0), stop=(wi == 2),
                            skip_group_check=True), "T", [b_tot, b_orf], [b_psbc])

    @blk.scalar
    def _(s):
        k.pre(s, "A", reads=[b_psbc], writes=list(b_mwb))
        for wi in range(3):
            k.post(s.copy(e["mwb"][wi][:], ps_bc[:, wi:wi + 1]), "A", [b_psbc], [b_mwb[wi]])

    CH2 = 512
    nch2 = HID * D // (128 * CH2)
    e["p0w2_sched"] = []
    for wi, (wsrc, qdst) in enumerate([(e["w1"], e["w1qn"]), (e["w2"], e["w2qn"]),
                                       (e["w3"], e["w3qn"])]):
        wf = wsrc.rearrange("a b -> (a b)").rearrange("(r p c) -> r p c", p=128, c=CH2)
        qf = qdst.rearrange("a b -> (a b)").rearrange("(r p c) -> r p c", p=128, c=CH2)
        for ci in range(nch2):
            e["p0w2_sched"].append((wf, qf, ci, wi))

    @blk.vector
    def _(v):
        numel = float(HID * D)
        for wi in range(3):
            k.pre(v, "V", reads=[b_mwb[wi]], writes=[b_mwb[wi]])
            k.post(v.tensor_scalar(e["mwb"][wi][:], e["mwb"][wi][:], 1.0 / numel, 1e-5,
                                   ALU.mult, ALU.max), "V", [b_mwb[wi]], [b_mwb[wi]])
            k.pre(v, "V", reads=[b_mwb[wi]], writes=[b_thr[wi], b_nth[wi]])
            k.post(v.tensor_scalar(e["thr"][wi][:], e["mwb"][wi][:], 0.5, None, ALU.mult),
                   "V", [b_mwb[wi]], [b_thr[wi]])
            k.post(v.tensor_scalar(e["nthr"][wi][:], e["mwb"][wi][:], -0.5, None, ALU.mult),
                   "V", [b_mwb[wi]], [b_nth[wi]])
    e.pop("_esw").close()


def p0w2_emit(k, blk, e, frac_lo, frac_hi, bufs2):
    B = e["B"]
    sched = e["p0w2_sched"]
    n = len(sched)
    lo = int(round(frac_lo * n))
    hi = int(round(frac_hi * n))
    wc2, pos2, q2 = bufs2
    b_wc2 = B("p2_wc2_0", "p2_wc2_1")
    b_pos2 = B("p2_pos2")[0]
    b_q2 = B("p2_q2_0", "p2_q2_1")
    b_thr = B("thr0", "thr1", "thr2")
    b_nth = B("nthr0", "nthr1", "nthr2")
    b_qn = B("w1qn", "w2qn", "w3qn")
    for idx in range(lo, hi):
        wf, qf, ci, wi = sched[idx]
        pa = idx % 2

        @blk.sync
        def _(sy, wf=wf, ci=ci, pa=pa):
            k.pre(sy, "SH", writes=[b_wc2[pa]])
            k.post(sy.dma_start(out=wc2[pa][:], in_=wf[ci, :, :]), "SH", [], [b_wc2[pa]])

        @blk.vector
        def _(v, pa=pa, wi=wi):
            k.pre(v, "V", reads=[b_wc2[pa], b_thr[wi]], writes=[b_pos2])
            k.post(v.tensor_scalar(pos2[:], wc2[pa][:], e["thr"][wi][:, 0:1], None,
                                   ALU.is_ge), "V", [b_wc2[pa], b_thr[wi]], [b_pos2])
            k.pre(v, "V", reads=[b_wc2[pa], b_pos2, b_nth[wi]], writes=[b_q2[pa]])
            k.post(v.scalar_tensor_tensor(q2[pa][:], wc2[pa][:], e["nthr"][wi][:, 0:1],
                                          pos2[:], ALU.is_le, ALU.subtract),
                   "V", [b_wc2[pa], b_pos2, b_nth[wi]], [b_q2[pa]])

        @blk.scalar
        def _(s, qf=qf, ci=ci, pa=pa, wi=wi):
            k.pre(s, "SS", reads=[b_q2[pa]], writes=[b_qn[wi]])
            k.post(s.dma_start(out=qf[ci, :, :], in_=q2[pa][:]), "SS", [b_q2[pa]], [b_qn[wi]])


def p1(k, blk, e):
    nc = k.nc
    B = e["B"]
    b_qrot = B("qrot")[0]
    b_id = B("ident")[0]
    b_ckvd = B("ckv_d")[0]

    with contextlib.ExitStack() as es1:
        sb = lambda n, s, d: es1.enter_context(nc.sbuf_tensor(n, s, d))
        psa = lambda n, s, d: es1.enter_context(nc.psum_tensor(n, s, d))
        es_a = contextlib.ExitStack()
        sba = lambda n, s, d: es_a.enter_context(nc.sbuf_tensor(n, s, d))
        wdn_sb = sba("p1_wdn", [128, 16 * 1024], BF16)
        xb = [sba(f"p1_xb{i}", [128, D], BF16) for i in range(2)]
        xT = sba("p1_xT", [128, 16 * 1024], BF16)
        ckv_ev = [sba(f"p1_ckve{i}", [128, 1024], BF16) for i in range(2)]
        es_ps1 = contextlib.ExitStack()
        ptr = [es_ps1.enter_context(nc.psum_tensor(f"p1_ptr{i}", [128, 128], BF16))
               for i in range(4)]
        ps_ckv = [es_ps1.enter_context(nc.psum_tensor(f"p1_psckv{i}", [128, 1024], F32))
                  for i in range(2)]

        b_wdn = B("p1_wdn")[0]
        b_xb = B("p1_xb0", "p1_xb1")
        b_xT = B("p1_xT")[0]
        b_ckve = B("p1_ckve0", "p1_ckve1")
        b_ptr = B(*[f"p1_ptr{i}" for i in range(4)])
        b_psckv = B("p1_psckv0", "p1_psckv1")

        @blk.gpsimd
        def _(g):
            k.pre(g, "SG", writes=[b_wdn])
            k.post(g.dma_start(out=wdn_sb[:].rearrange("p (kd l) -> p kd l", kd=16),
                               in_=e["wdn"].rearrange("(kd p) l -> p kd l", p=128)),
                   "SG", [], [b_wdn])

        for bp in range(8):
            for sub in range(8):
                tt_i = bp * 8 + sub
                pa = sub % 2

                @blk.gpsimd
                def _(g, pa=pa, tt_i=tt_i):
                    k.pre(g, "SG", writes=[b_xb[pa]])
                    k.post(g.dma_start(out=xb[pa][:],
                                       in_=e["xf"][tt_i * 128:(tt_i + 1) * 128, :]),
                           "SG", [], [b_xb[pa]])

                for qb in range(4):
                    @blk.tensor
                    def _(t, pa=pa, qb=qb):
                        k.pre(t, "T", reads=[b_xb[pa], b_id], writes=b_ptr)
                        for i in range(4):
                            kd = qb * 4 + i
                            k.post(t.transpose(ptr[i][:], xb[pa][:, kd * 128:(kd + 1) * 128],
                                               e["ident_s"][:]),
                                   "T", [b_xb[pa], b_id], [b_ptr[i]])

                    @blk.vector
                    def _(v, sub=sub, qb=qb):
                        k.pre(v, "V", reads=b_ptr, writes=[b_xT])
                        ins = None
                        for i in range(4):
                            kd = qb * 4 + i
                            ins = v.tensor_copy(
                                xT[:, kd * 1024 + sub * 128: kd * 1024 + (sub + 1) * 128],
                                ptr[i][:])
                        k.post(ins, "V", b_ptr, [b_xT])

            for l in range(8):
                pa = l % 2

                @blk.tensor
                def _(t, l=l, pa=pa):
                    k.pre(t, "T", reads=[b_xT, b_wdn], writes=[b_psckv[pa]])
                    for kd in range(16):
                        ws = wdn_sb[:, kd * 1024 + l * 128: kd * 1024 + (l + 1) * 128]
                        t.ldweights(ws)
                        ins = None
                        for b2 in range(2):
                            ins = t.matmul(
                                ps_ckv[pa][:, b2 * 512:(b2 + 1) * 512],
                                ws, xT[:, kd * 1024 + b2 * 512: kd * 1024 + (b2 + 1) * 512],
                                start=(kd == 0), stop=(kd == 15), skip_group_check=True)
                    k.post(ins, "T", [b_xT, b_wdn], [b_psckv[pa]])

                @blk.vector
                def _(v, l=l, pa=pa):
                    k.pre(v, "V", reads=[b_psckv[pa]], writes=[b_ckve[pa]])
                    k.post(v.tensor_copy(ckv_ev[pa][:], ps_ckv[pa][:]),
                           "V", [b_psckv[pa]], [b_ckve[pa]])

                @blk.scalar
                def _(s, l=l, pa=pa, bp=bp):
                    k.pre(s, "SS", reads=[b_ckve[pa]], writes=[b_ckvd])
                    k.post(s.dma_start(out=e["ckv_d"][l, :, bp * 1024:(bp + 1) * 1024],
                                       in_=ckv_ev[pa][:]),
                           "SS", [b_ckve[pa]], [b_ckvd])

        es_a.close()
        k.barrier(blk)
        xoT = sb("p1_xoT", [128, 16 * NSLOT * 128], BF16)
        xob = [sb(f"p1_xob{i}", [128, D], F32) for i in range(2)]
        xobb = sb("p1_xobb", [128, D], BF16)
        b_xoT = B("p1_xoT")[0]
        b_xob = B("p1_xob0", "p1_xob1")
        b_xobb = B("p1_xobb")[0]

        for s_ in range(NSLOT):
            pa = s_ % 2

            @blk.sync
            def _(sy, pa=pa, s_=s_):
                k.pre(sy, "SH", writes=[b_xob[pa]])
                k.post(sy.dma_start(out=xob[pa][:], in_=e["xo"][s_, :, :]),
                       "SH", [], [b_xob[pa]])

            @blk.vector
            def _(v, pa=pa):
                k.pre(v, "V", reads=[b_xob[pa]], writes=[b_xobb])
                k.post(v.tensor_copy(xobb[:], xob[pa][:]), "V", [b_xob[pa]], [b_xobb])

            for qb in range(4):
                @blk.tensor
                def _(t, qb=qb):
                    k.pre(t, "T", reads=[b_xobb, b_id], writes=b_ptr)
                    for i in range(4):
                        kd = qb * 4 + i
                        k.post(t.transpose(ptr[i][:], xobb[:, kd * 128:(kd + 1) * 128],
                                           e["ident_s"][:]), "T", [b_xobb, b_id], [b_ptr[i]])

                @blk.vector
                def _(v, s_=s_, qb=qb):
                    k.pre(v, "V", reads=b_ptr, writes=[b_xoT])
                    ins = None
                    for i in range(4):
                        kd = qb * 4 + i
                        ins = v.tensor_copy(
                            xoT[:, kd * 1024 + s_ * 128: kd * 1024 + s_ * 128 + 128], ptr[i][:])
                    k.post(ins, "V", b_ptr, [b_xoT])

        ccq = [sb(f"p1_ccq{i}", [128, 128], BF16) for i in range(NSLOT)]
        ssq = [sb(f"p1_ssq{i}", [128, 128], BF16) for i in range(NSLOT)]
        angq = sb("p1_angq", [128, 64], F32)
        nq = sb("p1_nq", [128, 64], I32)
        tq1 = sb("p1_tq1", [128, 64], F32)
        tq2 = sb("p1_tq2", [128, 64], F32)
        sinq = sb("p1_sinq", [128, 64], F32)
        cosq = sb("p1_cosq", [128, 64], F32)
        b_ccq = B(*[f"p1_ccq{i}" for i in range(NSLOT)])
        b_ssq = B(*[f"p1_ssq{i}" for i in range(NSLOT)])
        b_angq, b_ivf, b_qp = B("p1_angq", "ivf", "qposc")

        for s_ in range(NSLOT):
            @blk.vector
            def _(v, s_=s_):
                k.pre(v, "V", reads=[b_ivf, b_qp], writes=[b_angq])
                k.post(v.tensor_scalar(angq[:], e["IVF"][:], e["qposc"][:, s_:s_ + 1],
                                       None, ALU.mult), "V", [b_ivf, b_qp], [b_angq])
            b_sq = sincos(k, blk, e, angq[:], None, nq, tq1, tq2, sinq, 0.0, [b_angq])
            b_cq = sincos(k, blk, e, angq[:], None, nq, tq1, tq2, cosq, math.pi / 2.0, [b_angq])

            @blk.vector
            def _(v, s_=s_):
                k.pre(v, "V", reads=[b_sq, b_cq], writes=[b_ccq[s_], b_ssq[s_]])
                k.post(v.tensor_copy(ccq[s_][:, 0:128:2], cosq[:]), "V", [b_cq], [b_ccq[s_]])
                k.post(v.tensor_copy(ccq[s_][:, 1:128:2], cosq[:]), "V", [b_cq], [b_ccq[s_]])
                k.post(v.tensor_copy(ssq[s_][:, 1:128:2], sinq[:]), "V", [b_sq], [b_ssq[s_]])
                k.post(v.tensor_scalar(ssq[s_][:, 0:128:2], sinq[:], -1.0, None, ALU.mult),
                       "V", [b_sq], [b_ssq[s_]])

        es_ps1.close()
        k.barrier(blk)
        wqb = [sb(f"p1_wqb{i}", [128, 1024], BF16) for i in range(2)]
        ps_q = [psa(f"p1_psq{i}", [128, 1024], F32) for i in range(4)]
        rt1 = sb("p1_rt1", [128, 1024], BF16)
        rt2 = sb("p1_rt2", [128, 1024], BF16)
        b_wqb = B("p1_wqb0", "p1_wqb1")
        b_psq = B(*[f"p1_psq{i}" for i in range(4)])
        b_rt1, b_rt2 = B("p1_rt1", "p1_rt2")
        wq_r = e["wq"].rearrange("(kd p) n -> kd p n", p=128)

        for sq in range(2):
            slots = [4 * sq + i for i in range(4)]
            for hf in range(2):
                for kd in range(16):
                    pa = kd % 2

                    @blk.gpsimd
                    def _(g, kd=kd, hf=hf, pa=pa):
                        k.pre(g, "SG", writes=[b_wqb[pa]])
                        k.post(g.dma_start(out=wqb[pa][:],
                                           in_=wq_r[kd, :, hf * 1024:(hf + 1) * 1024]),
                               "SG", [], [b_wqb[pa]])

                    @blk.tensor
                    def _(t, kd=kd, pa=pa, slots=slots):
                        k.pre(t, "T", reads=[b_wqb[pa], b_xoT], writes=b_psq)
                        ins = None
                        for si, s_ in enumerate(slots):
                            xs = xoT[:, kd * 1024 + s_ * 128: kd * 1024 + s_ * 128 + 128]
                            t.ldweights(xs)
                            for b2 in range(2):
                                ins = t.matmul(ps_q[si][:, b2 * 512:(b2 + 1) * 512], xs,
                                               wqb[pa][:, b2 * 512:(b2 + 1) * 512],
                                               start=(kd == 0), stop=(kd == 15),
                                               skip_group_check=True)
                        k.post(ins, "T", [b_wqb[pa], b_xoT], list(b_psq))

                for si, s_ in enumerate(slots):
                    @blk.vector
                    def _(v, si=si, s_=s_, hf=hf):
                        src = ps_q[si][:]
                        swp = src.rearrange("p (h pr two) -> p h pr two", pr=64, two=2)[
                            :, :, :, ::-1]
                        ccb = ccq[s_][:].rearrange("p (one j) -> p one j", one=1).broadcast_to(
                            [128, 8, 128])
                        ssb = ssq[s_][:].rearrange("p (one j) -> p one j", one=1).broadcast_to(
                            [128, 8, 128])
                        k.pre(v, "V", reads=[b_psq[si], b_ccq[s_], b_ssq[s_]],
                              writes=[b_rt1, b_rt2])
                        k.post(v.tensor_tensor(rt1[:], src, ccb, ALU.mult),
                               "V", [b_psq[si], b_ccq[s_]], [b_rt1])
                        k.post(v.tensor_tensor(rt2[:], swp, ssb, ALU.mult),
                               "V", [b_psq[si], b_ssq[s_]], [b_rt2])
                        k.pre(v, "V", reads=[b_rt1, b_rt2], writes=[b_qrot])
                        k.post(v.tensor_tensor(
                            e["qrot"][:, s_ * D + hf * 1024: s_ * D + (hf + 1) * 1024],
                            rt1[:], rt2[:], ALU.add), "V", [b_rt1, b_rt2], [b_qrot])

    if e["dbg_q"] is not None:
        @blk.scalar
        def _(s):
            k.pre(s, "SS", reads=[b_qrot])
            k.post(s.dma_start(out=e["dbg_q"][:, :], in_=e["qrot"][:]), "SS", [b_qrot], [])


def p2(k, blk, e):
    nc = k.nc
    B = e["B"]
    b_attn = B("attn")[0]
    b_id = B("ident")[0]
    b_qrot = B("qrot")[0]
    b_ckvd = B("ckv_d")[0]
    b_cc, b_ss, b_bmq = B("CC", "SS", "bmq")

    with contextlib.ExitStack() as es2:
        sb = lambda n, s, d: es2.enter_context(nc.sbuf_tensor(n, s, d))
        psa = lambda n, s, d: es2.enter_context(nc.psum_tensor(n, s, d))
        Kf = sb("p2_Kf", [128, T], BF16)
        Vt = sb("p2_Vt", [128, T], BF16)
        ckvq = [sb(f"p2_ckvq{i}", [128, 8 * 1024], BF16) for i in range(2)]
        wkv = sb("p2_wkv", [128, 8 * 256], BF16)
        kraw = [sb(f"p2_kraw{i}", [128, 1024], BF16) for i in range(2)]
        ktok = [sb(f"p2_ktok{i}", [128, 512], BF16) for i in range(2)]
        rr1 = [sb(f"p2_rr1_{i}", [128, 512], F32) for i in range(2)]
        rr2 = [sb(f"p2_rr2_{i}", [128, 512], F32) for i in range(2)]
        qfq = sb("p2_qfq", [128, 512], BF16)
        PT = [sb(f"p2_PT{i}", [128, 512], BF16) for i in range(2)]
        ptmpb = sb("p2_ptmpb", [128, 128], BF16)
        zq = sb("p2_zq", [1, 512], F32)
        zr = sb("p2_zr", [1, 512], F32)
        zbc = sb("p2_zbc", [128, 512], F32)
        wc2 = [sb(f"p2_wc2_{i}", [128, 512], F32) for i in range(2)]
        pos2 = sb("p2_pos2", [128, 512], BF16)
        q2 = [sb(f"p2_q2_{i}", [128, 512], BF16) for i in range(2)]

        psKV = psa("p2_psKV", [128, 1024], F32)
        psc = [psa(f"p2_psc{i}", [128, 512], F32) for i in range(2)]
        psAV = psa("p2_psAV", [128, 512], F32)
        psZr = psa("p2_psZr", [1, 512], F32)
        ptr = [psa(f"p2_ptr{i}", [128, 128], BF16) for i in range(2)]

        b_Kf, b_Vt, b_wkv = B("p2_Kf", "p2_Vt", "p2_wkv")
        b_ckvq = B("p2_ckvq0", "p2_ckvq1")
        b_kraw = B("p2_kraw0", "p2_kraw1")
        b_ktok = B("p2_ktok0", "p2_ktok1")
        b_rr1 = B("p2_rr1_0", "p2_rr1_1")
        b_rr2 = B("p2_rr2_0", "p2_rr2_1")
        b_qfq, b_PTb0, b_PTb1, b_ptmpb = B("p2_qfq", "p2_PT0", "p2_PT1", "p2_ptmpb")
        b_PT = [b_PTb0, b_PTb1]
        b_zq, b_zr, b_zbc = B("p2_zq", "p2_zr", "p2_zbc")
        b_idf = B("identf")[0]
        b_kvps = B("p2_psKV_kv")[0]
        b_psc = B("p2_psc0", "p2_psc1")
        b_psAV, b_psZr = B("p2_psAV", "p2_psZr")
        b_ptr = B(*[f"p2_ptr{i}" for i in range(2)])
        b_bmk = B("bmk")[0]
        b_ones = B("ones")[0]

        wup_r = e["wup"].rearrange("(l p) n -> l p n", p=128)

        for h in range(NH):
            p0w2_emit(k, blk, e, h / NH, (h + 1) / NH, (wc2, pos2, q2))

            @blk.gpsimd
            def _(g, h=h):
                k.pre(g, "SG", writes=[b_wkv])
                k.post(g.dma_start(out=wkv[:].rearrange("p (l c) -> p l c", l=8),
                                   in_=wup_r[:, :, h * 256:(h + 1) * 256]), "SG", [], [b_wkv])

            for er in range(8):
                cpa = er % 2

                @blk.sync
                def _(sy, er=er, cpa=cpa):
                    k.pre(sy, "SH", reads=[b_ckvd], writes=[b_ckvq[cpa]])
                    k.post(sy.dma_start(out=ckvq[cpa][:].rearrange("p (l t) -> p l t", l=8),
                                        in_=e["ckv_d"][:, :, er * 1024:(er + 1) * 1024]),
                           "SH", [b_ckvd], [b_ckvq[cpa]])

                for part in range(2):
                    base = part * 128
                    rpa = part

                    @blk.tensor
                    def _(t, base=base, cpa=cpa, rpa=rpa):
                        k.pre(t, "T", reads=[b_ckvq[cpa], b_wkv], writes=[b_kvps])
                        ins = None
                        for l in range(8):
                            w = wkv[:, l * 256 + base: l * 256 + base + 128]
                            t.ldweights(w)
                            for b in range(2):
                                ins = t.matmul(
                                    psKV[:, b * 512:(b + 1) * 512], w,
                                    ckvq[cpa][:, l * 1024 + b * 512: l * 1024 + (b + 1) * 512],
                                    start=(l == 0), stop=(l == 7), skip_group_check=True)
                        k.post(ins, "T", [b_ckvq[cpa], b_wkv], [b_kvps])

                    @blk.scalar
                    def _(s, rpa=rpa):
                        k.pre(s, "A", reads=[b_kvps], writes=[b_kraw[rpa]])
                        k.post(s.activation(kraw[rpa][:], psKV[:], AF.Identity),
                               "A", [b_kvps], [b_kraw[rpa]])

                    if part == 1:
                        for g2 in range(4):
                            @blk.tensor
                            def _(t, g2=g2, rpa=rpa):
                                k.pre(t, "T", reads=[b_kraw[rpa], b_id], writes=list(b_ptr))
                                for i in range(2):
                                    kt = g2 * 2 + i
                                    k.post(t.transpose(ptr[i][:],
                                                       kraw[rpa][:, kt * 128:(kt + 1) * 128],
                                                       e["ident_s"][:]),
                                           "T", [b_kraw[rpa], b_id], [b_ptr[i]])

                            @blk.scalar
                            def _(s, er=er, g2=g2):
                                k.pre(s, "A", reads=list(b_ptr), writes=[b_Vt])
                                ins = None
                                for i in range(2):
                                    kt_g = er * 8 + g2 * 2 + i
                                    ins = s.activation(Vt[:, kt_g * 128:(kt_g + 1) * 128],
                                                       ptr[i][:], AF.Identity)
                                k.post(ins, "A", list(b_ptr), [b_Vt])
                    else:
                        for b4 in range(2):
                            gpa = b4

                            for h2 in range(2):
                                @blk.tensor
                                def _(t, b4=b4, h2=h2, rpa=rpa):
                                    k.pre(t, "T", reads=[b_kraw[rpa], b_id], writes=list(b_ptr))
                                    for i in range(2):
                                        kt = b4 * 4 + h2 * 2 + i
                                        k.post(t.transpose(ptr[i][:],
                                                           kraw[rpa][:, kt * 128:(kt + 1) * 128],
                                                           e["ident_s"][:]),
                                               "T", [b_kraw[rpa], b_id], [b_ptr[i]])

                                @blk.vector
                                def _(v, er=er, b4=b4, h2=h2, gpa=gpa):
                                    bkt = er * 8 + b4 * 4 + h2 * 2
                                    k.pre(v, "V", reads=list(b_ptr) + [b_cc, b_ss],
                                          writes=[b_rr1[gpa], b_rr2[gpa]])
                                    i1 = i2 = None
                                    for i in range(2):
                                        col = (h2 * 2 + i) * 128
                                        srcp = ptr[i][:]
                                        swp = srcp.rearrange("p (pr two) -> p pr two", two=2)[
                                            :, :, ::-1]
                                        i1 = v.tensor_tensor(
                                            rr1[gpa][:, col:col + 128], srcp,
                                            e["CC"][:, (bkt + i) * 128:(bkt + i + 1) * 128],
                                            ALU.mult)
                                        i2 = v.tensor_tensor(
                                            rr2[gpa][:, col:col + 128], swp,
                                            e["SS"][:, (bkt + i) * 128:(bkt + i + 1) * 128],
                                            ALU.mult)
                                    k.post(i1, "V", list(b_ptr) + [b_cc], [b_rr1[gpa]])
                                    k.post(i2, "V", list(b_ptr) + [b_ss], [b_rr2[gpa]])

                            @blk.gpsimd
                            def _(g, gpa=gpa):
                                k.pre(g, "G", reads=[b_rr1[gpa], b_rr2[gpa]],
                                      writes=[b_ktok[gpa]])
                                k.post(g.tensor_tensor(ktok[gpa][:], rr1[gpa][:], rr2[gpa][:],
                                                       ALU.add),
                                       "G", [b_rr1[gpa], b_rr2[gpa]], [b_ktok[gpa]])

                            for h2 in range(2):
                                @blk.tensor
                                def _(t, h2=h2, gpa=gpa):
                                    k.pre(t, "T", reads=[b_ktok[gpa], b_id], writes=list(b_ptr))
                                    for i in range(2):
                                        kt = h2 * 2 + i
                                        k.post(t.transpose(ptr[i][:],
                                                           ktok[gpa][:, kt * 128:(kt + 1) * 128],
                                                           e["ident_s"][:]),
                                               "T", [b_ktok[gpa], b_id], [b_ptr[i]])

                                @blk.scalar
                                def _(s, er=er, b4=b4, h2=h2):
                                    k.pre(s, "A", reads=list(b_ptr), writes=[b_Kf])
                                    ins = None
                                    for i in range(2):
                                        kt_g = er * 8 + b4 * 4 + h2 * 2 + i
                                        ins = s.activation(Kf[:, kt_g * 128:(kt_g + 1) * 128],
                                                           ptr[i][:], AF.Identity)
                                    k.post(ins, "A", list(b_ptr), [b_Kf])
            if e["dbg_kf"] is not None and h == 0:
                @blk.scalar
                def _(s):
                    k.pre(s, "SS", reads=[b_Kf, b_Vt])
                    k.post(s.dma_start(out=e["dbg_kf"][:, :], in_=Kf[:]), "SS", [b_Kf], [])
                    k.post(s.dma_start(out=e["dbg_vt"][:, :], in_=Vt[:]), "SS", [b_Vt], [])

            for qi, slots in enumerate(QUADS):
                for ci, s_ in enumerate(slots):
                    @blk.tensor
                    def _(t, s_=s_, ci=ci, h=h):
                        k.pre(t, "T", reads=[b_qrot, b_id], writes=[b_ptr[ci % 2]])
                        k.post(t.transpose(ptr[ci % 2][:],
                                           e["qrot"][:, s_ * D + h * 128: s_ * D + (h + 1) * 128],
                                           e["ident_s"][:]), "T", [b_qrot, b_id],
                               [b_ptr[ci % 2]])

                    @blk.scalar
                    def _(s, ci=ci):
                        k.pre(s, "A", reads=[b_ptr[ci % 2]], writes=[b_qfq])
                        k.post(s.activation(qfq[:, ci * 128:(ci + 1) * 128], ptr[ci % 2][:],
                                            AF.Identity), "A", [b_ptr[ci % 2]], [b_qfq])

                maxc = (slots[0] + 1) * 8

                def emit_sct(j):
                    nval = sum(1 for s_ in slots if (s_ + 1) * 8 > j)
                    ncol = nval * 128
                    pa = j % 2

                    @blk.tensor
                    def _(t, j=j, pa=pa, ncol=ncol):
                        k.pre(t, "T", reads=[b_Kf, b_qfq], writes=[b_psc[pa]])
                        t.ldweights(Kf[:, j * 128:(j + 1) * 128])
                        k.post(t.matmul(psc[pa][:, 0:ncol], Kf[:, j * 128:(j + 1) * 128],
                                        qfq[:, 0:ncol], start=True, stop=True,
                                        skip_group_check=True),
                               "T", [b_Kf, b_qfq], [b_psc[pa]])

                def emit_tail(j):
                    nval = sum(1 for s_ in slots if (s_ + 1) * 8 > j)
                    ncol = nval * 128
                    pa = j % 2
                    sb_ = j // 8
                    boundary = sb_ in slots and slots.index(sb_) == nval - 1

                    if not boundary:
                        @blk.scalar
                        def _(s, j=j, pa=pa, ncol=ncol):
                            k.pre(s, "A", reads=[b_psc[pa]], writes=[b_PT[pa]])
                            k.post(s.activation(PT[pa][:, 0:ncol], psc[pa][:, 0:ncol],
                                                AF.Exp, scale=SCALE),
                                   "A", [b_psc[pa]], [b_PT[pa]])
                    else:
                        @blk.scalar
                        def _(s, j=j, pa=pa, ncol=ncol):
                            k.pre(s, "A", reads=[b_psc[pa]], writes=[b_PT[pa], b_ptmpb])
                            ins = None
                            if ncol > 128:
                                ins = s.activation(PT[pa][:, 0:ncol - 128],
                                                   psc[pa][:, 0:ncol - 128], AF.Exp,
                                                   scale=SCALE)
                            ins2 = s.activation(ptmpb[:], psc[pa][:, ncol - 128:ncol],
                                                AF.Exp, scale=SCALE)
                            if ins is not None:
                                k.post(ins, "A", [b_psc[pa]], [b_PT[pa]])
                            k.post(ins2, "A", [b_psc[pa]], [b_ptmpb])

                        @blk.vector
                        def _(v, j=j, pa=pa, ncol=ncol):
                            k.pre(v, "V", reads=[b_ptmpb, b_bmk], writes=[b_PT[pa]])
                            jm = j % 8
                            k.post(v.tensor_tensor(PT[pa][:, ncol - 128:ncol], ptmpb[:],
                                                   e["bmk"][:, jm * 128:(jm + 1) * 128],
                                                   ALU.mult), "V", [b_ptmpb, b_bmk],
                                   [b_PT[pa]])

                    @blk.tensor
                    def _(t, j=j, pa=pa, ncol=ncol, maxc=maxc):
                        k.pre(t, "T", reads=[b_PT[pa], b_Vt, b_ones], writes=[b_psAV, b_psZr])
                        t.ldweights(Vt[:, j * 128:(j + 1) * 128])
                        k.post(t.matmul(psAV[:, 0:ncol], Vt[:, j * 128:(j + 1) * 128],
                                        PT[pa][:, 0:ncol], start=(j == 0),
                                        stop=(j == maxc - 1), skip_group_check=True),
                               "T", [b_PT[pa], b_Vt], [b_psAV])
                        t.ldweights(e["onescol"][:])
                        k.post(t.matmul(psZr[0:1, 0:ncol], e["onescol"][:],
                                        PT[pa][:, 0:ncol], start=(j == 0),
                                        stop=(j == maxc - 1), skip_group_check=True),
                               "T", [b_PT[pa], b_psZr and b_ones], [b_psZr])

                emit_sct(0)
                for j in range(maxc):
                    if j + 1 < maxc:
                        emit_sct(j + 1)
                    emit_tail(j)

                @blk.scalar
                def _(s):
                    k.pre(s, "A", reads=[b_psZr], writes=[b_zq])
                    k.post(s.activation(zq[0:1, :], psZr[0:1, :], AF.Identity),
                           "A", [b_psZr], [b_zq])

                @blk.vector
                def _(v):
                    k.pre(v, "V", reads=[b_zq], writes=[b_zr])
                    k.post(v.reciprocal(zr[0:1, :], zq[0:1, :]), "V", [b_zq], [b_zr])

                @blk.tensor
                def _(t):
                    k.pre(t, "T", reads=[b_zr, B("onesrow_f")[0]], writes=[b_psc[0]])
                    k.post(t.matmul(psc[0][:, 0:512], e["onesrow_f"][0:1, :], zr[0:1, :],
                                    start=True, stop=True, skip_group_check=True),
                           "T", [b_zr, B("onesrow_f")[0]], [b_psc[0]])

                @blk.scalar
                def _(s):
                    k.pre(s, "A", reads=[b_psc[0]], writes=[b_zbc])
                    k.post(s.activation(zbc[:], psc[0][:, 0:512], AF.Identity),
                           "A", [b_psc[0]], [b_zbc])

                @blk.vector
                def _(v, h=h, qi=qi):
                    k.pre(v, "V", reads=[b_psAV, b_zbc], writes=[b_attn])
                    k.post(v.tensor_tensor(
                        e["attn"][:, h * 1024 + qi * 512: h * 1024 + (qi + 1) * 512],
                        psAV[:], zbc[:], ALU.mult), "V", [b_psAV, b_zbc], [b_attn])

    if e["dbg_at"] is not None:
        @blk.scalar
        def _(s):
            k.pre(s, "SS", reads=[b_attn])
            k.post(s.dma_start(out=e["dbg_at"][:, :], in_=e["attn"][:]), "SS", [b_attn], [])


def p3(k, blk, e):
    nc = k.nc
    B = e["B"]
    b_x1d = B("x1_d")[0]
    b_attn = B("attn")[0]

    with contextlib.ExitStack() as es3:
        sb = lambda n, s, d: es3.enter_context(nc.sbuf_tensor(n, s, d))
        psa = lambda n, s, d: es3.enter_context(nc.psum_tensor(n, s, d))
        wob = [sb(f"p3_wob{i}", [128, D], BF16) for i in range(2)]
        xot = [sb(f"p3_xot{i}", [128, D], F32) for i in range(2)]
        vsl = sb("p3_v", [128, D], F32)
        sqs = sb("p3_sq", [128, D], F32)
        x1t = sb("p3_x1t", [128, D], F32)
        cols = sb("p3_cols", [128, 8], F32)
        ps_wo = [psa(f"p3_pswo{i}", [128, 2048], F32) for i in range(2)]
        b_wob = B("p3_wob0", "p3_wob1")
        b_xot = B("p3_xot0", "p3_xot1")
        b_v, b_sq, b_cols = B("p3_v", "p3_sq", "p3_cols")
        b_pswo = B("p3_pswo0", "p3_pswo1")
        wo_r = e["wo_"].rearrange("(f p) n -> f p n", p=128)

        for sp in range(4):
            slots = [2 * sp, 2 * sp + 1]
            for f in range(16):
                pa = f % 2

                @blk.gpsimd
                def _(g, f=f, pa=pa):
                    k.pre(g, "SG", writes=[b_wob[pa]])
                    k.post(g.dma_start(out=wob[pa][:], in_=wo_r[f, :, :]), "SG", [], [b_wob[pa]])

                @blk.tensor
                def _(t, f=f, pa=pa, slots=slots):
                    k.pre(t, "T", reads=[b_wob[pa], b_attn], writes=list(b_pswo))
                    ins = None
                    for si, s_ in enumerate(slots):
                        qi, ci = QCOL[s_]
                        col = f * 1024 + qi * 512 + ci * 128
                        t.ldweights(e["attn"][:, col:col + 128])
                        for cb in range(4):
                            ins = t.matmul(ps_wo[si][:, cb * 512:(cb + 1) * 512],
                                           e["attn"][:, col:col + 128],
                                           wob[pa][:, cb * 512:(cb + 1) * 512],
                                           start=(f == 0), stop=(f == 15),
                                           skip_group_check=True)
                    k.post(ins, "T", [b_wob[pa], b_attn], list(b_pswo))

            for si, s_ in enumerate(slots):
                @blk.sync
                def _(sy, si=si, s_=s_):
                    k.pre(sy, "SH", writes=[b_xot[si]])
                    k.post(sy.dma_start(out=xot[si][:], in_=e["xo"][s_, :, :]),
                           "SH", [], [b_xot[si]])

                @blk.vector
                def _(v, si=si):
                    k.pre(v, "V", reads=[b_pswo[si], b_xot[si]], writes=[b_v, b_cols])
                    k.post(v.scalar_tensor_tensor(vsl[:], ps_wo[si][:], 0.0, xot[si][:],
                                                  ALU.add, ALU.add, accum_out=cols[:, 0:1]),
                           "V", [b_pswo[si], b_xot[si]], [b_v, b_cols])

                b_x1t = B("p3_x1t")[0]
                ln_epilogue(k, blk, e, vsl, b_v, sqs, b_sq, cols, b_cols,
                            x1t[:], b_x1t, D)

                @blk.scalar
                def _(s, s_=s_):
                    k.pre(s, "SS", reads=[b_x1t], writes=[b_x1d])
                    k.post(s.dma_start(out=e["x1_d"][s_, :, :], in_=x1t[:]),
                           "SS", [b_x1t], [b_x1d])


def ln_epilogue(k, blk, e, vsl, b_v, sqs, b_sq, cols, b_cols, out_ap, b_out, dim):
    @blk.scalar
    def _(s):
        k.pre(s, "A", reads=[b_v], writes=[b_sq, b_cols])
        k.post(s.activation(sqs[:], vsl[:], AF.Square, accum_out=cols[:, 1:2]),
               "A", [b_v], [b_sq, b_cols])

    @blk.vector
    def _(v):
        k.pre(v, "V", reads=[b_cols], writes=[b_cols])
        k.post(v.tensor_scalar(cols[:, 2:3], cols[:, 0:1], 1.0 / dim, None, ALU.mult),
               "V", [b_cols], [b_cols])
        k.pre(v, "V", reads=[b_cols], writes=[b_cols])
        k.post(v.tensor_tensor(cols[:, 3:4], cols[:, 2:3], cols[:, 2:3], ALU.mult),
               "V", [b_cols], [b_cols])
        k.pre(v, "V", reads=[b_cols], writes=[b_cols])
        k.post(v.scalar_tensor_tensor(cols[:, 4:5], cols[:, 1:2], 1.0 / dim, cols[:, 3:4],
                                      ALU.mult, ALU.subtract), "V", [b_cols], [b_cols])
        k.pre(v, "V", reads=[b_cols], writes=[b_cols])
        k.post(v.tensor_scalar(cols[:, 5:6], cols[:, 4:5], 1e-5, None, ALU.add),
               "V", [b_cols], [b_cols])
        k.pre(v, "V", reads=[b_cols], writes=[b_cols])
        k.post(v.reciprocal(cols[:, 6:7], cols[:, 5:6]), "V", [b_cols], [b_cols])

    @blk.scalar
    def _(s):
        k.pre(s, "A", reads=[b_cols], writes=[b_cols])
        k.post(s.activation(cols[:, 7:8], cols[:, 6:7], AF.Sqrt), "A", [b_cols], [b_cols])

    @blk.vector
    def _(v):
        k.pre(v, "V", reads=[b_cols], writes=[b_cols])
        k.post(v.scalar_tensor_tensor(cols[:, 3:4], cols[:, 2:3], -1.0, cols[:, 7:8],
                                      ALU.mult, ALU.mult), "V", [b_cols], [b_cols])

    @blk.scalar
    def _(s):
        k.pre(s, "A", reads=[b_v, b_cols], writes=[b_out])
        k.post(s.activation(out_ap, vsl[:], AF.Identity, bias=cols[:, 3:4],
                            scale=cols[:, 7:8]), "A", [b_v, b_cols], [b_out])


def p4(k, blk, e):
    nc = k.nc
    B = e["B"]
    b_x1d = B("x1_d")[0]
    b_id = B("ident")[0]
    b_mwb = B("mwb0", "mwb1", "mwb2")
    b_thr = B("thr0", "thr1", "thr2")
    b_nth = B("nthr0", "nthr1", "nthr2")
    b_w1qn, b_w2qn, b_w3qn = B("w1qn", "w2qn", "w3qn")
    b_hqd, b_hqtd, b_out = B("hq_d", "hqt_d", "out_d")

    with contextlib.ExitStack() as es4:
        sb = lambda n, s, d: es4.enter_context(nc.sbuf_tensor(n, s, d))
        xqT = sb("p4_xqT", [128, 16 * NSLOT * 128], BF16)
        msc1 = sb("p4_msc1", [128, NSLOT], F32)
        msc2 = sb("p4_msc2", [128, NSLOT], F32)
        msc3 = sb("p4_msc3", [128, NSLOT], F32)
        ccols = sb("p4_cols", [128, 8], F32)
        posb = sb("p4_pos", [128, 512], BF16)
        b_xqT, b_ms1, b_ms2, b_ms3, b_cc4 = B(
            "p4_xqT", "p4_msc1", "p4_msc2", "p4_msc3", "p4_cols")
        b_xqt, b_n4, b_r4, b_t4, b_sq4 = B("p4_xqtok", "p4_n", "p4_r", "p4_t", "p4_sq")
        b_pos = B("p4_pos")[0]

        with contextlib.ExitStack() as esp:
            sba = lambda n, s, d: esp.enter_context(nc.sbuf_tensor(n, s, d))
            xq_tok = sba("p4_xqtok", [128, D], BF16)
            ni32 = sba("p4_n", [128, D], I32)
            rbuf = sba("p4_r", [128, D], F32)
            tbuf = sba("p4_t", [128, D], F32)
            sqs = sba("p4_sq", [128, D], F32)
            x1row = sba("p4_x1row", [128, D], F32)
            b_x1r = B("p4_x1row")[0]
            ptr = [esp.enter_context(nc.psum_tensor(f"p4_ptr{i}", [128, 128], BF16))
                   for i in range(8)]
            b_ptr = B(*[f"p4_ptr{i}" for i in range(8)])
            for s_ in range(NSLOT):
                @blk.sync
                def _(sy, s_=s_):
                    k.pre(sy, "SH", reads=[b_x1d], writes=[b_x1r])
                    k.post(sy.dma_start(out=x1row[:], in_=e["x1_d"][s_, :, :]),
                           "SH", [b_x1d], [b_x1r])
                x1s = x1row[:]

                @blk.vector
                def _(v, x1s=x1s):
                    k.pre(v, "V", reads=[b_x1r], writes=[b_cc4])
                    k.post(v.reduce_max(ccols[:, 0:1], x1s, axis=AX,
                                        apply_absolute_value=True), "V", [b_x1r], [b_cc4])

                @blk.scalar
                def _(s, x1s=x1s):
                    k.pre(s, "A", reads=[b_x1r], writes=[b_sq4, b_cc4])
                    k.post(s.activation(sqs[:], x1s, AF.Square, accum_out=ccols[:, 1:2]),
                           "A", [b_x1r], [b_sq4, b_cc4])

                quant_scale_chain(k, blk, e, ccols, b_cc4, msc1, b_ms1, msc2, b_ms2, s_,
                                  D, b_mwb[0], b_mwb[1], e["mwb"][0], e["mwb"][1])

                @blk.scalar
                def _(s, x1s=x1s):
                    k.pre(s, "A", reads=[b_x1r, b_cc4], writes=[b_n4])
                    k.post(s.activation(ni32[:], x1s, AF.Identity, scale=ccols[:, 7:8]),
                           "A", [b_x1r, b_cc4], [b_n4])

                @blk.vector
                def _(v, x1s=x1s):
                    k.pre(v, "V", reads=[b_x1r, b_cc4, b_n4], writes=[b_r4])
                    k.post(v.scalar_tensor_tensor(rbuf[:], x1s, ccols[:, 7:8], ni32[:],
                                                  ALU.mult, ALU.subtract),
                           "V", [b_x1r, b_cc4, b_n4], [b_r4])
                    k.pre(v, "V", reads=[b_r4, b_n4], writes=[b_t4])
                    k.post(v.scalar_tensor_tensor(tbuf[:], rbuf[:], 0.5, ni32[:],
                                                  ALU.is_ge, ALU.add), "V", [b_r4, b_n4], [b_t4])
                    k.pre(v, "V", reads=[b_r4, b_t4], writes=[b_xqt])
                    k.post(v.scalar_tensor_tensor(xq_tok[:], rbuf[:], -0.5, tbuf[:],
                                                  ALU.is_le, ALU.subtract),
                           "V", [b_r4, b_t4], [b_xqt])

                for half in range(2):
                    @blk.tensor
                    def _(t, half=half):
                        k.pre(t, "T", reads=[b_xqt, b_id], writes=b_ptr)
                        for i in range(8):
                            kd = half * 8 + i
                            k.post(t.transpose(ptr[i][:], xq_tok[:, kd * 128:(kd + 1) * 128],
                                               e["ident_s"][:]), "T", [b_xqt, b_id], [b_ptr[i]])

                    @blk.vector
                    def _(v, s_=s_, half=half):
                        k.pre(v, "V", reads=b_ptr, writes=[b_xqT])
                        ins = None
                        for i in range(8):
                            kd = half * 8 + i
                            ins = v.tensor_copy(
                                xqT[:, kd * 1024 + s_ * 128: kd * 1024 + s_ * 128 + 128],
                                ptr[i][:])
                        k.post(ins, "V", b_ptr, [b_xqT])

        k.barrier(blk)
        es_b = contextlib.ExitStack()
        sbb = lambda n, s, d: es_b.enter_context(nc.sbuf_tensor(n, s, d))
        w1qb = sbb("p4_w1qb", [128, 16 * 512], BF16)
        w2qb = sbb("p4_w2qb", [128, 16 * 512], BF16)
        gtmp = sbb("p4_gtmp", [128, 512], F32)
        utmp = sbb("p4_utmp", [128, 512], F32)
        hqc = sbb("p4_hqc", [128, 512], F32)
        b_w1q, b_w2q, b_g, b_u, b_hqc = B("p4_w1qb", "p4_w2qb", "p4_g", "p4_u", "p4_hqc")

        with contextlib.ExitStack() as esp:
            psGU = [esp.enter_context(nc.psum_tensor(f"p4_gu{i}", [128, 512], F32))
                    for i in range(8)]
            b_gu = B(*[f"p4_gu{i}" for i in range(8)])
            for hb in range(11):
                for wsel, (wqn_, qdst, b_qdst) in enumerate(
                        [(e["w1qn"], w1qb, b_w1q), (e["w2qn"], w2qb, b_w2q)]):
                    b_wnx = b_w1qn if wsel == 0 else b_w2qn
                    for kd in range(16):
                        @blk.sync
                        def _(sy, wqn_=wqn_, hb=hb, kd=kd, qdst=qdst, b_qdst=b_qdst,
                              b_wnx=b_wnx):
                            k.pre(sy, "SH", reads=[b_wnx], writes=[b_qdst])
                            k.post(sy.dma_start_transpose(
                                qdst[:, kd * 512:(kd + 1) * 512],
                                wqn_[hb * 512:(hb + 1) * 512, kd * 128:(kd + 1) * 128]),
                                "SH", [b_wnx], [b_qdst])

                for half in range(2):
                    slots = [half * 4 + i for i in range(4)]

                    @blk.tensor
                    def _(t, slots=slots):
                        k.pre(t, "T", reads=[b_w1q, b_w2q, b_xqT], writes=list(b_gu))
                        ins = None
                        for kd in range(16):
                            for si, s_ in enumerate(slots):
                                xs = xqT[:, kd * 1024 + s_ * 128: kd * 1024 + s_ * 128 + 128]
                                t.ldweights(xs)
                                t.matmul(psGU[si][:], xs, w1qb[:, kd * 512:(kd + 1) * 512],
                                         start=(kd == 0), stop=(kd == 15), skip_group_check=True)
                                ins = t.matmul(psGU[4 + si][:], xs,
                                               w2qb[:, kd * 512:(kd + 1) * 512],
                                               start=(kd == 0), stop=(kd == 15),
                                               skip_group_check=True)
                        k.post(ins, "T", [b_w1q, b_w2q, b_xqT], list(b_gu))

                    for si, s_ in enumerate(slots):
                        @blk.scalar
                        def _(s, si=si, s_=s_):
                            k.pre(s, "A", reads=[b_gu[si], b_ms1], writes=[b_g])
                            k.post(s.activation(gtmp[:], psGU[si][:], AF.Silu,
                                                scale=msc1[:, s_:s_ + 1]),
                                   "A", [b_gu[si], b_ms1], [b_g])
                            k.pre(s, "A", reads=[b_gu[4 + si], b_ms2], writes=[b_u])
                            k.post(s.activation(utmp[:], psGU[4 + si][:], AF.Identity,
                                                scale=msc2[:, s_:s_ + 1]),
                                   "A", [b_gu[4 + si], b_ms2], [b_u])

                        @blk.vector
                        def _(v):
                            k.pre(v, "V", reads=[b_g, b_u], writes=[b_hqc])
                            k.post(v.tensor_tensor(hqc[:], gtmp[:], utmp[:], ALU.mult),
                                   "V", [b_g, b_u], [b_hqc])

                        @blk.scalar
                        def _(s, s_=s_, hb=hb):
                            k.pre(s, "SS", reads=[b_hqc], writes=[b_hqd])
                            k.post(s.dma_start(out=e["hq_d"][s_, :, hb * 512:(hb + 1) * 512],
                                               in_=hqc[:]), "SS", [b_hqc], [b_hqd])

        es_b.close()
        k.barrier(blk)
        es_c = contextlib.ExitStack()
        sbc = lambda n, s, d: es_c.enter_context(nc.sbuf_tensor(n, s, d))
        HC4 = HID // 4
        hrow = sbc("p4_hrow", [128, HID], F32)
        hsq = sbc("p4_hsq", [128, HC4], F32)
        hn = sbc("p4_hn", [128, HC4], I32)
        hr = sbc("p4_hr", [128, HC4], F32)
        ht = sbc("p4_ht", [128, HC4], F32)
        hqq = sbc("p4_hqq", [128, HID], BF16)
        hstage = sbc("p4_hstage", [128, 44 * 128], BF16)
        hsums = sbc("p4_hsums", [128, 4], F32)
        b_hrow, b_hsq, b_hn, b_hr, b_ht, b_hqq, b_hst, b_hsu = B(
            "p4_hrow", "p4_hsq", "p4_hn", "p4_hr", "p4_ht", "p4_hqq", "p4_hstage",
            "p4_hsums")

        with contextlib.ExitStack() as esp:
            ptr = [esp.enter_context(nc.psum_tensor(f"p4c_ptr{i}", [128, 128], BF16))
                   for i in range(8)]
            b_ptr = B(*[f"p4c_ptr{i}" for i in range(8)])
            for s_ in range(NSLOT):
                @blk.sync
                def _(sy, s_=s_):
                    k.pre(sy, "SH", reads=[b_hqd], writes=[b_hrow])
                    k.post(sy.dma_start(out=hrow[:], in_=e["hq_d"][s_, :, :]),
                           "SH", [b_hqd], [b_hrow])

                @blk.vector
                def _(v):
                    k.pre(v, "V", reads=[b_hrow], writes=[b_cc4])
                    k.post(v.reduce_max(ccols[:, 0:1], hrow[:], axis=AX,
                                        apply_absolute_value=True), "V", [b_hrow], [b_cc4])

                for c4 in range(4):
                    @blk.scalar
                    def _(s, c4=c4):
                        k.pre(s, "A", reads=[b_hrow], writes=[b_hsq, b_hsu])
                        k.post(s.activation(hsq[:], hrow[:, c4 * HC4:(c4 + 1) * HC4],
                                            AF.Square, accum_out=hsums[:, c4:c4 + 1]),
                               "A", [b_hrow], [b_hsq, b_hsu])

                @blk.vector
                def _(v):
                    k.pre(v, "V", reads=[b_hsu], writes=[b_cc4])
                    k.post(v.reduce_sum(ccols[:, 1:2], hsums[:], axis=AX),
                           "V", [b_hsu], [b_cc4])

                quant_scale_chain(k, blk, e, ccols, b_cc4, msc3, b_ms3, None, None, s_,
                                  HID, b_mwb[2], None, e["mwb"][2], None)

                for c4 in range(4):
                    sl = slice(c4 * HC4, (c4 + 1) * HC4)

                    @blk.scalar
                    def _(s, sl=sl):
                        k.pre(s, "A", reads=[b_hrow, b_cc4], writes=[b_hn])
                        k.post(s.activation(hn[:], hrow[:, sl], AF.Identity,
                                            scale=ccols[:, 7:8]), "A", [b_hrow, b_cc4], [b_hn])

                    @blk.vector
                    def _(v, sl=sl):
                        k.pre(v, "V", reads=[b_hrow, b_cc4, b_hn], writes=[b_hr])
                        k.post(v.scalar_tensor_tensor(hr[:], hrow[:, sl], ccols[:, 7:8], hn[:],
                                                      ALU.mult, ALU.subtract),
                               "V", [b_hrow, b_cc4, b_hn], [b_hr])
                        k.pre(v, "V", reads=[b_hr, b_hn], writes=[b_ht])
                        k.post(v.scalar_tensor_tensor(ht[:], hr[:], 0.5, hn[:],
                                                      ALU.is_ge, ALU.add), "V",
                               [b_hr, b_hn], [b_ht])
                        k.pre(v, "V", reads=[b_hr, b_ht], writes=[b_hqq])
                        k.post(v.scalar_tensor_tensor(hqq[:, sl], hr[:], -0.5, ht[:],
                                                      ALU.is_le, ALU.subtract),
                               "V", [b_hr, b_ht], [b_hqq])

                for gb in range(6):
                    n = 8 if gb < 5 else 4

                    @blk.tensor
                    def _(t, gb=gb, n=n):
                        k.pre(t, "T", reads=[b_hqq, b_id], writes=b_ptr[:n])
                        for i in range(n):
                            hc = gb * 8 + i
                            k.post(t.transpose(ptr[i][:], hqq[:, hc * 128:(hc + 1) * 128],
                                               e["ident_s"][:]), "T", [b_hqq, b_id], [b_ptr[i]])

                    @blk.vector
                    def _(v, gb=gb, n=n):
                        k.pre(v, "V", reads=b_ptr[:n], writes=[b_hst])
                        ins = None
                        for i in range(n):
                            hc = gb * 8 + i
                            ins = v.tensor_copy(hstage[:, hc * 128:(hc + 1) * 128], ptr[i][:])
                        k.post(ins, "V", b_ptr[:n], [b_hst])

                @blk.scalar
                def _(s, s_=s_):
                    k.pre(s, "SS", reads=[b_hst], writes=[b_hqtd])
                    k.post(s.dma_start(
                        out=e["hqt_d"][:, :, s_ * 128:(s_ + 1) * 128].rearrange(
                            "c p t -> p c t"),
                        in_=hstage[:].rearrange("p (c t) -> p c t", c=44)),
                        "SS", [b_hst], [b_hqtd])

        es_c.close()
        k.barrier(blk)
        hqtb = [sb(f"p4_hqtb{i}", [128, NSLOT * 128], BF16) for i in range(2)]
        w3qc = [sb(f"p4_w3qc{i}", [128, D], BF16) for i in range(2)]
        vsl = sb("p4b_v", [128, D], F32)
        sq2 = sb("p4b_sq", [128, D], F32)
        outs = sb("p4b_out", [128, D], F32)
        x1b = sb("p4b_x1", [128, D], F32)
        b_hqtb = B("p4_hqtb0", "p4_hqtb1")
        b_w3qc = B("p4_w3qc0", "p4_w3qc1")
        b_v4, b_sq2, b_os = B("p4b_v", "p4b_sq", "p4b_out")
        b_x1b = B("p4b_x1")[0]

        with contextlib.ExitStack() as esp:
            psZ = [esp.enter_context(nc.psum_tensor(f"p4_psz{i}", [128, 2048], F32))
                   for i in range(2)]
            b_psZ = B("p4_psz0", "p4_psz1")
            for sp in range(4):
                slots = [2 * sp, 2 * sp + 1]
                for hc in range(44):
                    pa = hc % 2

                    @blk.sync
                    def _(sy, hc=hc, pa=pa):
                        k.pre(sy, "SH", reads=[b_hqtd, b_w3qn], writes=[b_hqtb[pa], b_w3qc[pa]])
                        k.post(sy.dma_start(out=hqtb[pa][:], in_=e["hqt_d"][hc, :, :]),
                               "SH", [b_hqtd], [b_hqtb[pa]])
                        for cb in range(4):
                            k.post(sy.dma_start_transpose(
                                w3qc[pa][:, cb * 512:(cb + 1) * 512],
                                e["w3qn"][cb * 512:(cb + 1) * 512, hc * 128:(hc + 1) * 128]),
                                "SH", [b_w3qn], [b_w3qc[pa]])

                    @blk.tensor
                    def _(t, hc=hc, pa=pa, slots=slots):
                        k.pre(t, "T", reads=[b_hqtb[pa], b_w3qc[pa]], writes=list(b_psZ))
                        ins = None
                        for si, s_ in enumerate(slots):
                            xs = hqtb[pa][:, s_ * 128:(s_ + 1) * 128]
                            t.ldweights(xs)
                            for cb in range(4):
                                ins = t.matmul(psZ[si][:, cb * 512:(cb + 1) * 512], xs,
                                               w3qc[pa][:, cb * 512:(cb + 1) * 512],
                                               start=(hc == 0), stop=(hc == 43),
                                               skip_group_check=True)
                        k.post(ins, "T", [b_hqtb[pa], b_w3qc[pa]], list(b_psZ))

                for si, s_ in enumerate(slots):
                    @blk.sync
                    def _(sy, s_=s_):
                        k.pre(sy, "SH", reads=[b_x1d], writes=[b_x1b])
                        k.post(sy.dma_start(out=x1b[:], in_=e["x1_d"][s_, :, :]),
                               "SH", [b_x1d], [b_x1b])

                    @blk.vector
                    def _(v, si=si, s_=s_):
                        k.pre(v, "V", reads=[b_psZ[si], b_ms3, b_x1b], writes=[b_v4, b_cc4])
                        k.post(v.scalar_tensor_tensor(
                            vsl[:], psZ[si][:], msc3[:, s_:s_ + 1],
                            x1b[:], ALU.mult, ALU.add,
                            accum_out=ccols[:, 0:1]),
                            "V", [b_psZ[si], b_ms3, b_x1b], [b_v4, b_cc4])

                    ln_epilogue(k, blk, e, vsl, b_v4, sq2, b_sq2, ccols, b_cc4,
                                outs[:], b_os, D)

                    @blk.scalar
                    def _(s, s_=s_):
                        k.pre(s, "SS", reads=[b_os], writes=[b_out])
                        k.post(s.dma_start(out=e["out"][s_, :, :], in_=outs[:]),
                               "SS", [b_os], [b_out])


def quant_scale_chain(k, blk, e, ccols, b_cc, mscA, b_msA, mscB, b_msB, s_, dim,
                      b_mA, b_mB, mA, mB):
    @blk.vector
    def _(v):
        k.pre(v, "V", reads=[b_cc], writes=[b_cc])
        k.post(v.tensor_scalar(ccols[:, 2:3], ccols[:, 1:2], 1.0 / dim, 1e-6,
                               ALU.mult, ALU.add), "V", [b_cc], [b_cc])
        k.pre(v, "V", reads=[b_cc], writes=[b_cc])
        k.post(v.reciprocal(ccols[:, 3:4], ccols[:, 2:3]), "V", [b_cc], [b_cc])

    @blk.scalar
    def _(s):
        k.pre(s, "A", reads=[b_cc], writes=[b_cc])
        k.post(s.activation(ccols[:, 4:5], ccols[:, 3:4], AF.Sqrt), "A", [b_cc], [b_cc])

    @blk.vector
    def _(v, s_=s_):
        k.pre(v, "V", reads=[b_cc], writes=[b_cc])
        k.post(v.scalar_tensor_tensor(ccols[:, 5:6], ccols[:, 0:1], 1.0, ccols[:, 4:5],
                                      ALU.mult, ALU.mult), "V", [b_cc], [b_cc])
        k.pre(v, "V", reads=[b_cc], writes=[b_cc])
        k.post(v.tensor_scalar(ccols[:, 5:6], ccols[:, 5:6], 1e-5, None, ALU.max),
               "V", [b_cc], [b_cc])
        k.pre(v, "V", reads=[b_cc], writes=[b_cc])
        k.post(v.reciprocal(ccols[:, 6:7], ccols[:, 5:6]), "V", [b_cc], [b_cc])
        k.pre(v, "V", reads=[b_cc], writes=[b_cc])
        k.post(v.scalar_tensor_tensor(ccols[:, 7:8], ccols[:, 6:7], 127.0, ccols[:, 4:5],
                                      ALU.mult, ALU.mult), "V", [b_cc], [b_cc])
        k.pre(v, "V", reads=[b_cc], writes=[b_cc])
        k.post(v.tensor_scalar(ccols[:, 6:7], ccols[:, 5:6], 1.0 / 127.0, None, ALU.mult),
               "V", [b_cc], [b_cc])
        k.pre(v, "V", reads=[b_cc, b_mA], writes=[b_msA])
        k.post(v.scalar_tensor_tensor(mscA[:, s_:s_ + 1], ccols[:, 6:7], 1.0, mA[:],
                                      ALU.mult, ALU.mult), "V", [b_cc, b_mA], [b_msA])
        if mscB is not None:
            k.pre(v, "V", reads=[b_cc, b_mB], writes=[b_msB])
            k.post(v.scalar_tensor_tensor(mscB[:, s_:s_ + 1], ccols[:, 6:7], 1.0, mB[:],
                                          ALU.mult, ALU.mult), "V", [b_cc, b_mB], [b_msB])


def make_inputs(core, x, ws):
    tiles = [8 * s + core for s in range(NSLOT)]
    xo = np.stack([x[t * 128:(t + 1) * 128, :] for t in tiles])
    qpos_c = np.stack([np.arange(t * 128, (t + 1) * 128, dtype=np.float32) for t in tiles],
                      axis=1)
    qrel_c = (np.arange(128, dtype=np.float32) + core * 128)[:, None].astype(np.float32)
    return dict(xf=x, xo=xo, qpos_c=qpos_c, qrel_c=qrel_c,
                ident=np.eye(128, dtype=bfdt), **ws)



_NC_CACHE = {}


def _get_nc():
    if "nc" not in _NC_CACHE:
        _NC_CACHE["nc"] = build(dev=False)
    return _NC_CACHE["nc"]


def kernel(L0_attn_x, L0_attn_wq, L0_attn_w_down, L0_attn_w_up, L0_attn_wo,
           L0_ffn_w1, L0_ffn_w2, L0_ffn_w3):
    x = np.ascontiguousarray(np.asarray(L0_attn_x, dtype=np.float32).reshape(T, D))
    ws = dict(
        wq=np.ascontiguousarray(np.asarray(L0_attn_wq, dtype=np.float32)),
        wdn=np.ascontiguousarray(np.asarray(L0_attn_w_down, dtype=np.float32)),
        wup=np.ascontiguousarray(np.asarray(L0_attn_w_up, dtype=np.float32)),
        wo_=np.ascontiguousarray(np.asarray(L0_attn_wo, dtype=np.float32)),
        w1=np.ascontiguousarray(np.asarray(L0_ffn_w1, dtype=np.float32)),
        w2=np.ascontiguousarray(np.asarray(L0_ffn_w2, dtype=np.float32)),
        w3=np.ascontiguousarray(np.asarray(L0_ffn_w3, dtype=np.float32)),
    )
    nc = _get_nc()
    in_maps = [make_inputs(c, x, ws) for c in range(8)]
    res = run_bass_kernel_spmd(nc, in_maps, core_ids=list(range(8)))
    out = np.empty((T, D), dtype=np.float32)
    for c in range(8):
        oc = np.asarray(res.results[c]["out"])
        for s in range(NSLOT):
            t = 8 * s + c
            out[t * 128:(t + 1) * 128, :] = oc[s]
    return out.reshape(1, T, D)


# version 09
# speedup vs baseline: 1.1979x; 1.1724x over previous
import sys
import contextlib
import math

import numpy as np
import ml_dtypes
import concourse.bass as bass
import concourse.mybir as mybir
from concourse.bass_utils import run_bass_kernel_spmd

F32 = mybir.dt.float32
BF16 = mybir.dt.bfloat16
I32 = mybir.dt.int32
AF = mybir.ActivationFunctionType
ALU = mybir.AluOpType
AX = mybir.AxisListType.X
bfdt = ml_dtypes.bfloat16

T = 8192
D = 2048
DLAT = 1024
NH = 16
DH = 128
HID = 5632
NSLOT = 8
SCALE = DH ** -0.5
TWO_PI = 2.0 * math.pi
C1 = 6.28125
C2 = TWO_PI - C1
INV_2PI = 1.0 / TWO_PI
LN1E4 = math.log(10000.0)

QUADS = [[7, 6, 1, 0], [5, 4, 3, 2]]
QCOL = {s: (qi, ci) for qi, qs in enumerate(QUADS) for ci, s in enumerate(qs)}

DMA_CH = ("SH", "SS", "SG")


class Buf:
    __slots__ = ("w", "rs", "name", "acc")

    def __init__(self, name=""):
        self.w = {}
        self.rs = []
        self.name = name
        self.acc = False


class FakeInstr:
    __slots__ = ("rec",)

    def __init__(self, rec):
        self.rec = rec

    def then_inc(self, sem, n):
        self.rec[3] = (sem, n)
        return self


class EngProxy:

    def __init__(self, q):
        self._q = q

    def __getattr__(self, name):
        def call(*args, **kwargs):
            rec = [name, args, kwargs, None]
            self._q.append(rec)
            return FakeInstr(rec)
        return call


class FakeBlock:
    def __init__(self):
        self.q = {"T": [], "V": [], "A": [], "G": [], "SY": []}
        self.prox = {k_: EngProxy(v) for k_, v in self.q.items()}

    def _dec(self, ch):
        def deco(fn):
            fn(self.prox[ch])
            return None
        return deco

    @property
    def tensor(self):
        return self._dec("T")

    @property
    def vector(self):
        return self._dec("V")

    @property
    def scalar(self):
        return self._dec("A")

    @property
    def gpsimd(self):
        return self._dec("G")

    @property
    def sync(self):
        return self._dec("SY")

    def flush(self, blk):
        for ch, eng_attr in [("T", "tensor"), ("V", "vector"), ("A", "scalar"),
                             ("G", "gpsimd"), ("SY", "sync")]:
            recs = self.q[ch]
            if not recs:
                continue

            def body(eng, recs=recs):
                for name, args, kwargs, inc in recs:
                    ins = getattr(eng, name)(*args, **kwargs)
                    if inc is not None:
                        ins.then_inc(*inc)
            getattr(blk, eng_attr)(body)


class B:
    def __init__(self, nc, es):
        self.nc = nc
        self.es = es
        self.sems = {}
        self.cnt = {}
        self.seen = {}

    def sem(self, name):
        if name not in self.sems:
            self.sems[name] = self.es.enter_context(self.nc.semaphore(name))
            self.cnt[name] = 0
        return self.sems[name]

    @staticmethod
    def ekey(ch):
        return {"SH": "SY", "SS": "A", "SG": "G"}.get(ch, ch)

    def pre(self, eng, ch, reads=(), writes=()):
        ek = self.ekey(ch)
        seen = self.seen.setdefault(ek, {})
        need = {}
        for b in reads:
            for c, v in b.w.items():
                need[c] = max(need.get(c, 0), v)
        for b in writes:
            if not b.acc:
                for c, v in b.w.items():
                    need[c] = max(need.get(c, 0), v)
            for c, v in b.rs:
                need[c] = max(need.get(c, 0), v)
        for c in sorted(need):
            v = need[c]
            if c == "T" and ch == "T":
                continue
            if seen.get(c, 0) >= v:
                continue
            eng.wait_ge(self.sem(c), v)
            seen[c] = v

    def post(self, instr, ch, reads=(), writes=()):
        if ch in DMA_CH:
            key = None
            for b in writes:
                if not b.acc:
                    key = b
                    break
            if key is None:
                for b in reads:
                    if not b.acc:
                        key = b
                        break
            sname = f"D_{ch}_{key.name if key is not None else 'misc'}"
            n = 16
        else:
            sname = ch
            n = 1
        s = self.sem(sname)
        self.cnt[sname] += n
        instr.then_inc(s, n)
        m = (sname, self.cnt[sname])
        for b in writes:
            if b.acc:
                b.w[m[0]] = m[1]
            else:
                b.w = {m[0]: m[1]}
                b.rs = []
        for b in reads:
            b.rs.append(m)
        return m

    def barrier(self, blk):
        chans = [c for c in self.cnt if self.cnt[c] > 0]

        def wait_all(eng, ek):
            seen = self.seen.setdefault(ek, {})
            for c in chans:
                v = self.cnt[c]
                if seen.get(c, 0) >= v:
                    continue
                eng.wait_ge(self.sem(c), v)
                seen[c] = v

        @blk.tensor
        def _(t):
            wait_all(t, "T")

        @blk.vector
        def _(v):
            wait_all(v, "V")

        @blk.scalar
        def _(s):
            wait_all(s, "A")

        @blk.gpsimd
        def _(g):
            wait_all(g, "G")

        @blk.sync
        def _(sy):
            wait_all(sy, "SY")

    def sb(self, name, shape, dtype):
        return self.es.enter_context(self.nc.sbuf_tensor(name, shape, dtype))

    def ps(self, name, shape, dtype):
        return self.es.enter_context(self.nc.psum_tensor(name, shape, dtype))


def build(dev=False, upto=99):
    nc = bass.Bass("TRN2", target_bir_lowering=False, debug=False)
    sk = "ExternalOutput" if dev else "Internal"

    io = {}
    io["xf"] = nc.dram_tensor("xf", [T, D], F32, kind="ExternalInput").ap()
    io["xo"] = nc.dram_tensor("xo", [NSLOT, 128, D], F32, kind="ExternalInput").ap()
    io["qpos_c"] = nc.dram_tensor("qpos_c", [128, NSLOT], F32, kind="ExternalInput").ap()
    io["qrel_c"] = nc.dram_tensor("qrel_c", [128, 1], F32, kind="ExternalInput").ap()
    io["wq"] = nc.dram_tensor("wq", [D, D], F32, kind="ExternalInput").ap()
    io["wdn"] = nc.dram_tensor("wdn", [D, DLAT], F32, kind="ExternalInput").ap()
    io["wup"] = nc.dram_tensor("wup", [DLAT, 2 * D], F32, kind="ExternalInput").ap()
    io["wo_"] = nc.dram_tensor("wo_", [D, D], F32, kind="ExternalInput").ap()
    io["w1"] = nc.dram_tensor("w1", [HID, D], F32, kind="ExternalInput").ap()
    io["w2"] = nc.dram_tensor("w2", [HID, D], F32, kind="ExternalInput").ap()
    io["w3"] = nc.dram_tensor("w3", [D, HID], F32, kind="ExternalInput").ap()
    io["ident"] = nc.dram_tensor("ident", [128, 128], BF16, kind="ExternalInput").ap()
    io["out"] = nc.dram_tensor("out", [NSLOT, 128, D], F32, kind="ExternalOutput").ap()
    io["ckv_d"] = nc.dram_tensor("ckv_d", [8, 128, T], BF16, kind=sk).ap()
    io["w1qn"] = nc.dram_tensor("w1qn", [HID, D], BF16, kind="Internal").ap()
    io["w2qn"] = nc.dram_tensor("w2qn", [HID, D], BF16, kind="Internal").ap()
    io["w3qn"] = nc.dram_tensor("w3qn", [D, HID], BF16, kind="Internal").ap()
    io["w3qt"] = nc.dram_tensor("w3qt", [44, 128, D], BF16, kind="Internal").ap()
    io["hq_d"] = nc.dram_tensor("hq_d", [NSLOT, 128, HID], F32, kind=sk).ap()
    io["hqt_d"] = nc.dram_tensor("hqt_d", [44, 128, NSLOT * 128], BF16, kind=sk).ap()
    io["x1_d"] = nc.dram_tensor("x1_d", [NSLOT, 128, D], F32, kind=sk).ap()
    for nm in ["dbg_q", "dbg_kf", "dbg_vt", "dbg_at"]:
        io[nm] = nc.dram_tensor(nm, [128, {"dbg_q": NSLOT * D, "dbg_kf": T, "dbg_vt": T,
                                           "dbg_at": NH * 1024}[nm]], BF16,
                                kind="ExternalOutput").ap() if dev else None
    io["dbg_x1"] = None

    with contextlib.ExitStack() as es:
        k = B(nc, es)
        e = dict(io)
        e["bufs"] = {}

        def bufs(*names):
            return [e["bufs"].setdefault(n, Buf(n)) for n in names]

        e["B"] = bufs
        for nm in ["ckv_d", "hq_d", "hqt_d", "x1_d", "w1qn", "w2qn", "w3qn", "w3qt",
                   "out_d"]:
            bufs(nm)[0].acc = True

        e["ident_s"] = k.sb("ident_s", [128, 128], BF16)
        e["qposc"] = k.sb("qposc", [128, NSLOT], F32)
        e["qrelc"] = k.sb("qrelc", [128, 1], F32)
        e["IVF"] = k.sb("IVF", [128, 64], F32)
        e["onescol"] = k.sb("onescol", [128, 1], BF16)
        e["ones128"] = k.sb("ones128", [128, 128], BF16)
        e["onescol_f"] = k.sb("onescol_f", [128, 1], F32)
        e["onesrow_f"] = k.sb("onesrow_f", [1, 128], F32)
        e["onesrow_b"] = k.sb("onesrow_b", [1, 128], BF16)
        e["identf"] = k.sb("identf", [128, 128], F32)
        e["thr"] = [k.sb(f"thrw{i}", [128, 1], F32) for i in range(3)]
        e["nthr"] = [k.sb(f"nthrw{i}", [128, 1], F32) for i in range(3)]
        e["mwb"] = [k.sb(f"mwb{i}", [128, 1], F32) for i in range(3)]

        with nc.Block() as real_blk:
            blk = FakeBlock()
            with contextlib.ExitStack() as es_t:
                sbt = lambda n, s, d: es_t.enter_context(nc.sbuf_tensor(n, s, d))
                e["CC"] = sbt("CC", [128, 64 * 128], BF16)
                e["SS"] = sbt("SS_t", [128, 64 * 128], BF16)
                e["bmq"] = sbt("bmq", [128, 1024], BF16)
                e["bmk"] = sbt("bmk", [128, 1024], BF16)
                e["qrot"] = sbt("qrot", [128, NSLOT * D], BF16)
                p0(k, blk, e)
                k.barrier(blk)
                if upto >= 1:
                    p0w(k, blk, e)
                    k.barrier(blk)
                if upto >= 2:
                    p1(k, blk, e)
                    k.barrier(blk)
                e["attn"] = sbt("attn", [128, NH * 1024], BF16)
                if upto >= 3:
                    p2(k, blk, e)
                    k.barrier(blk)
                if upto >= 4:
                    p3(k, blk, e)
                    k.barrier(blk)
            if upto >= 5:
                p4(k, blk, e)

            @blk.scalar
            def _(s):
                s.wait_ge(k.sem("SS"), k.cnt["SS"])

            @blk.sync
            def _(sy):
                sy.wait_ge(k.sem("SH"), k.cnt["SH"])

            @blk.gpsimd
            def _(g):
                g.wait_ge(k.sem("SG"), k.cnt["SG"])

            blk.flush(real_blk)
    return nc


def sincos(k, blk, e, ang, shape, nbuf, t1, t2, outbuf, shift, extra_reads=()):
    (b_ang, b_n, b_t1, b_t2, b_out) = e["B"](*[f"sc_{id(x)}" for x in (ang, nbuf, t1, t2, outbuf)])

    @blk.vector
    def _(v):
        k.pre(v, "V", reads=[b_ang, *extra_reads], writes=[b_t1])
        if shift != 0.0:
            k.post(v.tensor_scalar(t1[:], ang, shift, None, ALU.add), "V", [b_ang], [b_t1])
        else:
            k.post(v.tensor_copy(t1[:], ang), "V", [b_ang], [b_t1])

    @blk.scalar
    def _(s):
        k.pre(s, "A", reads=[b_t1], writes=[b_n])
        k.post(s.activation(nbuf[:], t1[:], AF.Identity, scale=INV_2PI), "A", [b_t1], [b_n])

    @blk.vector
    def _(v):
        k.pre(v, "V", reads=[b_n, b_t1], writes=[b_t2])
        k.post(v.scalar_tensor_tensor(t2[:], nbuf[:], -C1, t1[:], ALU.mult, ALU.add),
               "V", [b_n, b_t1], [b_t2])
        k.pre(v, "V", reads=[b_n, b_t2], writes=[b_t1])
        k.post(v.scalar_tensor_tensor(t1[:], nbuf[:], -C2, t2[:], ALU.mult, ALU.add),
               "V", [b_n, b_t2], [b_t1])
        k.pre(v, "V", reads=[b_t1], writes=[b_t2])
        k.post(v.tensor_scalar(t2[:], t1[:], math.pi, TWO_PI, ALU.is_gt, ALU.mult),
               "V", [b_t1], [b_t2])
        k.pre(v, "V", reads=[b_t1, b_t2], writes=[b_t1])
        k.post(v.tensor_tensor(t1[:], t1[:], t2[:], ALU.subtract), "V", [b_t1, b_t2], [b_t1])
        k.pre(v, "V", reads=[b_t1], writes=[b_t2])
        k.post(v.tensor_scalar(t2[:], t1[:], -math.pi, TWO_PI, ALU.is_lt, ALU.mult),
               "V", [b_t1], [b_t2])
        k.pre(v, "V", reads=[b_t1, b_t2], writes=[b_t1])
        k.post(v.tensor_tensor(t1[:], t1[:], t2[:], ALU.add), "V", [b_t1, b_t2], [b_t1])

    @blk.scalar
    def _(s):
        k.pre(s, "A", reads=[b_t1], writes=[b_out])
        k.post(s.activation(outbuf[:], t1[:], AF.Sin), "A", [b_t1], [b_out])
    return b_out


def p0(k, blk, e):
    nc = k.nc
    B = e["B"]
    es0 = contextlib.ExitStack()
    sb = lambda n, s, d: es0.enter_context(nc.sbuf_tensor(n, s, d))
    POS = sb("p0_pos", [128, 64], F32)
    J = sb("p0_j", [128, 64], F32)
    ANG = sb("p0_ang", [128, 4096], F32)
    NN = sb("p0_n", [128, 4096], I32)
    S1 = sb("p0_s1", [128, 4096], F32)
    S2 = sb("p0_s2", [128, 4096], F32)
    SINX = sb("p0_sin", [128, 4096], F32)
    COSX = sb("p0_cos", [128, 4096], F32)
    KJ = sb("p0_kj", [128, 1024], F32)
    e["_es0"] = es0
    (b_id, b_qp, b_qr, b_pos, b_j, b_kj, b_ivf, b_ang, b_cc, b_ss, b_bmq, b_ones) = B(
        "ident", "qposc", "qrelc", "pos", "j", "kj", "ivf", "ang", "CC", "SS", "bmq", "ones")

    @blk.sync
    def _(sy):
        k.post(sy.dma_start(out=e["ident_s"][:], in_=e["ident"][:, :]), "SH", [], [b_id])
        k.post(sy.dma_start(out=e["qposc"][:], in_=e["qpos_c"][:, :]), "SH", [], [b_qp])
        k.post(sy.dma_start(out=e["qrelc"][:], in_=e["qrel_c"][:, :]), "SH", [], [b_qr])

    @blk.gpsimd
    def _(g):
        k.post(g.iota(POS[:], pattern=[[128, 64]], base=0, channel_multiplier=1,
                      allow_small_or_imprecise_dtypes=True), "G", [], [b_pos])
        k.post(g.iota(J[:], pattern=[[1, 64]], base=0, channel_multiplier=0,
                      allow_small_or_imprecise_dtypes=True), "G", [], [b_j])
        k.post(g.iota(KJ[:], pattern=[[1, 1024]], base=0, channel_multiplier=0,
                      allow_small_or_imprecise_dtypes=True), "G", [], [b_kj])

    @blk.scalar
    def _(s):
        k.pre(s, "A", reads=[b_j], writes=[b_ivf])
        k.post(s.activation(e["IVF"][:], J[:], AF.Exp, scale=-LN1E4 / 64.0), "A", [b_j], [b_ivf])

    @blk.vector
    def _(v):
        k.post(v.memset(e["onescol"][:], 1.0), "V", [], [b_ones])
        k.post(v.memset(e["ones128"][:], 1.0), "V", [], [e["B"]("ones128")[0]])
        k.post(v.memset(e["onescol_f"][:], 1.0), "V", [], [e["B"]("onescol_f")[0]])
        k.post(v.memset(e["onesrow_f"][:], 1.0), "V", [], [e["B"]("onesrow_f")[0]])
        k.post(v.memset(e["onesrow_b"][:], 1.0), "V", [], [e["B"]("onesrow_b")[0]])
        k.pre(v, "V", reads=[b_id], writes=[e["B"]("identf")[0]])
        k.post(v.tensor_copy(e["identf"][:], e["ident_s"][:]), "V", [b_id],
               [e["B"]("identf")[0]])
        k.pre(v, "V", reads=[b_pos, b_ivf], writes=[b_ang])
        k.post(v.tensor_tensor(
            ANG[:],
            POS[:].rearrange("p (kt one) -> p kt one", one=1).broadcast_to([128, 64, 64]),
            e["IVF"][:].rearrange("p (one j) -> p one j", one=1).broadcast_to([128, 64, 64]),
            ALU.mult), "V", [b_pos, b_ivf], [b_ang])

    b_sin = sincos(k, blk, e, ANG[:], None, NN, S1, S2, SINX, 0.0, extra_reads=[b_ang])
    b_cos = sincos(k, blk, e, ANG[:], None, NN, S1, S2, COSX, math.pi / 2.0, extra_reads=[b_ang])

    @blk.vector
    def _(v):
        cview = COSX[:].rearrange("p (kt j) -> p kt j", j=64)
        sview = SINX[:].rearrange("p (kt j) -> p kt j", j=64)
        cc = e["CC"][:].rearrange("p (kt j) -> p kt j", j=128)
        ss = e["SS"][:].rearrange("p (kt j) -> p kt j", j=128)
        k.pre(v, "V", reads=[b_sin, b_cos], writes=[b_cc, b_ss])
        k.post(v.tensor_copy(cc[:, :, 0:128:2], cview), "V", [b_cos], [b_cc])
        k.post(v.tensor_copy(cc[:, :, 1:128:2], cview), "V", [b_cos], [b_cc])
        k.post(v.tensor_copy(ss[:, :, 1:128:2], sview), "V", [b_sin], [b_ss])
        k.post(v.tensor_scalar(ss[:, :, 0:128:2], sview, -1.0, None, ALU.mult),
               "V", [b_sin], [b_ss])
        k.pre(v, "V", reads=[b_kj, b_qr], writes=[b_bmq])
        k.post(v.tensor_scalar(e["bmq"][:], KJ[:], e["qrelc"][:, 0:1], None, ALU.is_le),
               "V", [b_kj, b_qr], [b_bmq])
    ICOL = sb("p0_icol", [128, 1], F32)
    TC0 = sb("p0_tc0", [128, 1], F32)
    TCJ = sb("p0_tcj", [128, 8], F32)
    b_icol, b_tc0, b_tcj, b_bmk = B("p0_icol", "p0_tc0", "p0_tcj", "bmk")

    @blk.gpsimd
    def _(g):
        k.post(g.iota(ICOL[:], pattern=[[1, 1]], base=0, channel_multiplier=1,
                      allow_small_or_imprecise_dtypes=True), "G", [], [b_icol])

    @blk.vector
    def _(v):
        k.pre(v, "V", reads=[b_icol, b_qr], writes=[b_tc0])
        k.post(v.scalar_tensor_tensor(TC0[:], ICOL[:], 2.0, e["qrelc"][:], ALU.mult,
                                      ALU.subtract), "V", [b_icol, b_qr], [b_tc0])
        for jm in range(8):
            k.pre(v, "V", reads=[b_tc0], writes=[b_tcj])
            k.post(v.tensor_scalar(TCJ[:, jm:jm + 1], TC0[:], float(jm * 128), None,
                                   ALU.add), "V", [b_tc0], [b_tcj])
        for jm in range(8):
            k.pre(v, "V", reads=[b_tcj, b_kj], writes=[b_bmk])
            k.post(v.tensor_scalar(e["bmk"][:, jm * 128:(jm + 1) * 128], KJ[:, 0:128],
                                   TCJ[:, jm:jm + 1], None, ALU.is_ge),
                   "V", [b_tcj, b_kj], [b_bmk])
    e.pop("_es0").close()


def p0w(k, blk, e):
    nc = k.nc
    B = e["B"]
    CH = 4096
    nchunk = HID * D // (128 * CH)
    wl = [(e["w1"], None), (e["w2"], None), (e["w3"], None)]

    esw = contextlib.ExitStack()
    e["_esw"] = esw
    sbw = lambda n, s, d: esw.enter_context(nc.sbuf_tensor(n, s, d))
    wbuf = [sbw(f"p0w_buf{i}", [128, CH], F32) for i in range(2)]
    abscr = [sbw(f"p0w_abs{i}", [128, CH], F32) for i in range(2)]
    acc = sbw("p0w_acc", [128, 3 * nchunk + 3], F32)
    accr = sbw("p0w_accr", [128, nchunk], F32)
    csb = sbw("p0w_csb", [128, 4], F32)
    tot = sbw("p0w_tot", [1, 16], F32)
    ps_tot = esw.enter_context(nc.psum_tensor("p0w_pstot", [1, 128], F32))

    b_wb = B("p0w_wb0", "p0w_wb1")
    b_scr = B("p0w_scr0", "p0w_scr1")
    (b_acc, b_csb, b_tot, b_pst, b_ones) = B("p0w_acc", "p0w_csb", "p0w_tot", "p0w_pst", "ones")

    for wi, (wsrc, wdst) in enumerate(wl):
        wflat = wsrc.rearrange("a b -> (a b)").rearrange("(r p c) -> r p c", p=128, c=CH)
        for ci in range(nchunk):
            pa = ci % 2

            @blk.sync
            def _(sy, pa=pa, wflat=wflat, ci=ci):
                k.pre(sy, "SH", writes=[b_wb[pa]])
                k.post(sy.dma_start(out=wbuf[pa][:], in_=wflat[ci, :, :]), "SH", [], [b_wb[pa]])

            @blk.scalar
            def _(s, pa=pa, wi=wi, ci=ci):
                k.pre(s, "A", reads=[b_wb[pa]], writes=[b_scr[pa], b_acc])
                k.post(s.activation(abscr[pa][:], wbuf[pa][:], AF.Abs,
                                    accum_out=acc[:, wi * nchunk + ci: wi * nchunk + ci + 1]),
                       "A", [b_wb[pa]], [b_scr[pa], b_acc])


    CL2 = 64.0
    CL3 = CL2 * nchunk
    b_onesf = B("onescol_f")[0]

    @blk.vector
    def _(v):
        k.pre(v, "V", reads=[b_acc], writes=[b_acc, b_csb])
        for wi in range(3):
            k.post(v.tensor_scalar(accr[:],
                                   acc[:, wi * nchunk:(wi + 1) * nchunk], -CL2, None,
                                   ALU.add), "V", [b_acc], [b_acc])
            v.drain()
            k.post(v.reduce_sum(acc[:, 3 * nchunk + wi: 3 * nchunk + wi + 1],
                                accr[:], axis=AX),
                   "V", [b_acc], [b_acc])
            v.drain()
        k.pre(v, "V", reads=[b_acc], writes=[b_csb])
        k.post(v.tensor_copy(csb[:, 0:3], acc[:, 3 * nchunk:3 * nchunk + 3]),
               "V", [b_acc], [b_csb])

    @blk.tensor
    def _(t):
        k.pre(t, "T", reads=[b_csb, b_onesf], writes=[b_pst])
        for wi in range(3):
            k.post(t.matmul(ps_tot[0:1, wi * 8:wi * 8 + 1], csb[:, wi:wi + 1],
                            e["onescol_f"][:], start=True, stop=True, skip_group_check=True),
                   "T", [b_csb, b_onesf], [b_pst])

    @blk.vector
    def _(v):
        k.pre(v, "V", reads=[b_pst], writes=[b_tot])
        for wi in range(3):
            k.post(v.tensor_scalar(tot[0:1, wi:wi + 1], ps_tot[0:1, wi * 8:wi * 8 + 1],
                                   float(128.0 * CL3), None, ALU.add),
                   "V", [b_pst], [b_tot])

    b_mwb = B("mwb0", "mwb1", "mwb2")
    b_thr = B("thr0", "thr1", "thr2")
    b_nth = B("nthr0", "nthr1", "nthr2")
    b_orf = B("onesrow_f")[0]
    b_psbc = B("p0w_psbc")[0]
    ps_bc = esw.enter_context(nc.psum_tensor("p0w_psbc", [128, 4], F32))

    @blk.tensor
    def _(t):
        k.pre(t, "T", reads=[b_tot, b_orf], writes=[b_psbc])
        for wi in range(3):
            k.post(t.matmul(ps_bc[:, wi:wi + 1], e["onesrow_f"][0:1, :],
                            tot[0:1, wi:wi + 1], start=(wi == 0), stop=(wi == 2),
                            skip_group_check=True), "T", [b_tot, b_orf], [b_psbc])

    @blk.scalar
    def _(s):
        k.pre(s, "A", reads=[b_psbc], writes=list(b_mwb))
        for wi in range(3):
            k.post(s.copy(e["mwb"][wi][:], ps_bc[:, wi:wi + 1]), "A", [b_psbc], [b_mwb[wi]])

    CH2 = 512
    nch2 = HID * D // (128 * CH2)
    e["p0w2_sched"] = []
    for wi, (wsrc, qdst) in enumerate([(e["w1"], e["w1qn"]), (e["w2"], e["w2qn"]),
                                       (e["w3"], e["w3qn"])]):
        wf = wsrc.rearrange("a b -> (a b)").rearrange("(r p c) -> r p c", p=128, c=CH2)
        qf = qdst.rearrange("a b -> (a b)").rearrange("(r p c) -> r p c", p=128, c=CH2)
        for ci in range(nch2):
            e["p0w2_sched"].append((wf, qf, ci, wi))

    @blk.vector
    def _(v):
        numel = float(HID * D)
        for wi in range(3):
            k.pre(v, "V", reads=[b_mwb[wi]], writes=[b_mwb[wi]])
            k.post(v.tensor_scalar(e["mwb"][wi][:], e["mwb"][wi][:], 1.0 / numel, 1e-5,
                                   ALU.mult, ALU.max), "V", [b_mwb[wi]], [b_mwb[wi]])
            k.pre(v, "V", reads=[b_mwb[wi]], writes=[b_thr[wi], b_nth[wi]])
            k.post(v.tensor_scalar(e["thr"][wi][:], e["mwb"][wi][:], 0.5, None, ALU.mult),
                   "V", [b_mwb[wi]], [b_thr[wi]])
            k.post(v.tensor_scalar(e["nthr"][wi][:], e["mwb"][wi][:], -0.5, None, ALU.mult),
                   "V", [b_mwb[wi]], [b_nth[wi]])
    e.pop("_esw").close()


def p0w2_emit(k, blk, e, frac_lo, frac_hi, bufs2):
    B = e["B"]
    sched = e["p0w2_sched"]
    n = len(sched)
    lo = int(round(frac_lo * n))
    hi = int(round(frac_hi * n))
    wc2, pos2, q2 = bufs2
    b_wc2 = B("p2_wc2_0", "p2_wc2_1")
    b_pos2 = B("p2_pos2")[0]
    b_q2 = B("p2_q2_0", "p2_q2_1")
    b_thr = B("thr0", "thr1", "thr2")
    b_nth = B("nthr0", "nthr1", "nthr2")
    b_qn = B("w1qn", "w2qn", "w3qn")
    for idx in range(lo, hi):
        wf, qf, ci, wi = sched[idx]
        pa = idx % 2

        @blk.sync
        def _(sy, wf=wf, ci=ci, pa=pa):
            k.pre(sy, "SH", writes=[b_wc2[pa]])
            k.post(sy.dma_start(out=wc2[pa][:], in_=wf[ci, :, :]), "SH", [], [b_wc2[pa]])

        @blk.vector
        def _(v, pa=pa, wi=wi):
            k.pre(v, "V", reads=[b_wc2[pa], b_thr[wi]], writes=[b_pos2])
            k.post(v.tensor_scalar(pos2[:], wc2[pa][:], e["thr"][wi][:, 0:1], None,
                                   ALU.is_ge), "V", [b_wc2[pa], b_thr[wi]], [b_pos2])
            k.pre(v, "V", reads=[b_wc2[pa], b_pos2, b_nth[wi]], writes=[b_q2[pa]])
            k.post(v.scalar_tensor_tensor(q2[pa][:], wc2[pa][:], e["nthr"][wi][:, 0:1],
                                          pos2[:], ALU.is_le, ALU.subtract),
                   "V", [b_wc2[pa], b_pos2, b_nth[wi]], [b_q2[pa]])

        @blk.scalar
        def _(s, qf=qf, ci=ci, pa=pa, wi=wi):
            k.pre(s, "SS", reads=[b_q2[pa]], writes=[b_qn[wi]])
            k.post(s.dma_start(out=qf[ci, :, :], in_=q2[pa][:]), "SS", [b_q2[pa]], [b_qn[wi]])


def p1(k, blk, e):
    nc = k.nc
    B = e["B"]
    b_qrot = B("qrot")[0]
    b_id = B("ident")[0]
    b_ckvd = B("ckv_d")[0]

    with contextlib.ExitStack() as es1:
        sb = lambda n, s, d: es1.enter_context(nc.sbuf_tensor(n, s, d))
        psa = lambda n, s, d: es1.enter_context(nc.psum_tensor(n, s, d))
        es_a = contextlib.ExitStack()
        sba = lambda n, s, d: es_a.enter_context(nc.sbuf_tensor(n, s, d))
        wdn_sb = sba("p1_wdn", [128, 16 * 1024], BF16)
        xb = [sba(f"p1_xb{i}", [128, D], BF16) for i in range(2)]
        xT = sba("p1_xT", [128, 16 * 1024], BF16)
        ckv_ev = [sba(f"p1_ckve{i}", [128, 1024], BF16) for i in range(2)]
        es_ps1 = contextlib.ExitStack()
        ptr = [es_ps1.enter_context(nc.psum_tensor(f"p1_ptr{i}", [128, 128], BF16))
               for i in range(4)]
        ps_ckv = [es_ps1.enter_context(nc.psum_tensor(f"p1_psckv{i}", [128, 1024], F32))
                  for i in range(2)]

        b_wdn = B("p1_wdn")[0]
        b_xb = B("p1_xb0", "p1_xb1")
        b_xT = B("p1_xT")[0]
        b_ckve = B("p1_ckve0", "p1_ckve1")
        b_ptr = B(*[f"p1_ptr{i}" for i in range(4)])
        b_psckv = B("p1_psckv0", "p1_psckv1")

        @blk.gpsimd
        def _(g):
            k.pre(g, "SG", writes=[b_wdn])
            k.post(g.dma_start(out=wdn_sb[:].rearrange("p (kd l) -> p kd l", kd=16),
                               in_=e["wdn"].rearrange("(kd p) l -> p kd l", p=128)),
                   "SG", [], [b_wdn])

        for bp in range(8):
            for sub in range(8):
                tt_i = bp * 8 + sub
                pa = sub % 2

                @blk.gpsimd
                def _(g, pa=pa, tt_i=tt_i):
                    k.pre(g, "SG", writes=[b_xb[pa]])
                    k.post(g.dma_start(out=xb[pa][:],
                                       in_=e["xf"][tt_i * 128:(tt_i + 1) * 128, :]),
                           "SG", [], [b_xb[pa]])

                for qb in range(4):
                    @blk.tensor
                    def _(t, pa=pa, qb=qb):
                        k.pre(t, "T", reads=[b_xb[pa], b_id], writes=b_ptr)
                        for i in range(4):
                            kd = qb * 4 + i
                            k.post(t.transpose(ptr[i][:], xb[pa][:, kd * 128:(kd + 1) * 128],
                                               e["ident_s"][:]),
                                   "T", [b_xb[pa], b_id], [b_ptr[i]])

                    @blk.vector
                    def _(v, sub=sub, qb=qb):
                        k.pre(v, "V", reads=b_ptr, writes=[b_xT])
                        ins = None
                        for i in range(4):
                            kd = qb * 4 + i
                            ins = v.tensor_copy(
                                xT[:, kd * 1024 + sub * 128: kd * 1024 + (sub + 1) * 128],
                                ptr[i][:])
                        k.post(ins, "V", b_ptr, [b_xT])

            for l in range(8):
                pa = l % 2

                @blk.tensor
                def _(t, l=l, pa=pa):
                    k.pre(t, "T", reads=[b_xT, b_wdn], writes=[b_psckv[pa]])
                    for kd in range(16):
                        ws = wdn_sb[:, kd * 1024 + l * 128: kd * 1024 + (l + 1) * 128]
                        t.ldweights(ws)
                        ins = None
                        for b2 in range(2):
                            ins = t.matmul(
                                ps_ckv[pa][:, b2 * 512:(b2 + 1) * 512],
                                ws, xT[:, kd * 1024 + b2 * 512: kd * 1024 + (b2 + 1) * 512],
                                start=(kd == 0), stop=(kd == 15), skip_group_check=True)
                    k.post(ins, "T", [b_xT, b_wdn], [b_psckv[pa]])

                @blk.vector
                def _(v, l=l, pa=pa):
                    k.pre(v, "V", reads=[b_psckv[pa]], writes=[b_ckve[pa]])
                    k.post(v.tensor_copy(ckv_ev[pa][:], ps_ckv[pa][:]),
                           "V", [b_psckv[pa]], [b_ckve[pa]])

                @blk.scalar
                def _(s, l=l, pa=pa, bp=bp):
                    k.pre(s, "SS", reads=[b_ckve[pa]], writes=[b_ckvd])
                    k.post(s.dma_start(out=e["ckv_d"][l, :, bp * 1024:(bp + 1) * 1024],
                                       in_=ckv_ev[pa][:]),
                           "SS", [b_ckve[pa]], [b_ckvd])

        es_a.close()
        k.barrier(blk)
        xoT = sb("p1_xoT", [128, 16 * NSLOT * 128], BF16)
        xob = [sb(f"p1_xob{i}", [128, D], F32) for i in range(2)]
        xobb = sb("p1_xobb", [128, D], BF16)
        b_xoT = B("p1_xoT")[0]
        b_xob = B("p1_xob0", "p1_xob1")
        b_xobb = B("p1_xobb")[0]

        for s_ in range(NSLOT):
            pa = s_ % 2

            @blk.sync
            def _(sy, pa=pa, s_=s_):
                k.pre(sy, "SH", writes=[b_xob[pa]])
                k.post(sy.dma_start(out=xob[pa][:], in_=e["xo"][s_, :, :]),
                       "SH", [], [b_xob[pa]])

            @blk.vector
            def _(v, pa=pa):
                k.pre(v, "V", reads=[b_xob[pa]], writes=[b_xobb])
                k.post(v.tensor_copy(xobb[:], xob[pa][:]), "V", [b_xob[pa]], [b_xobb])

            for qb in range(4):
                @blk.tensor
                def _(t, qb=qb):
                    k.pre(t, "T", reads=[b_xobb, b_id], writes=b_ptr)
                    for i in range(4):
                        kd = qb * 4 + i
                        k.post(t.transpose(ptr[i][:], xobb[:, kd * 128:(kd + 1) * 128],
                                           e["ident_s"][:]), "T", [b_xobb, b_id], [b_ptr[i]])

                @blk.vector
                def _(v, s_=s_, qb=qb):
                    k.pre(v, "V", reads=b_ptr, writes=[b_xoT])
                    ins = None
                    for i in range(4):
                        kd = qb * 4 + i
                        ins = v.tensor_copy(
                            xoT[:, kd * 1024 + s_ * 128: kd * 1024 + s_ * 128 + 128], ptr[i][:])
                    k.post(ins, "V", b_ptr, [b_xoT])

        ccq = [sb(f"p1_ccq{i}", [128, 128], BF16) for i in range(NSLOT)]
        ssq = [sb(f"p1_ssq{i}", [128, 128], BF16) for i in range(NSLOT)]
        angq = sb("p1_angq", [128, 64], F32)
        nq = sb("p1_nq", [128, 64], I32)
        tq1 = sb("p1_tq1", [128, 64], F32)
        tq2 = sb("p1_tq2", [128, 64], F32)
        sinq = sb("p1_sinq", [128, 64], F32)
        cosq = sb("p1_cosq", [128, 64], F32)
        b_ccq = B(*[f"p1_ccq{i}" for i in range(NSLOT)])
        b_ssq = B(*[f"p1_ssq{i}" for i in range(NSLOT)])
        b_angq, b_ivf, b_qp = B("p1_angq", "ivf", "qposc")

        for s_ in range(NSLOT):
            @blk.vector
            def _(v, s_=s_):
                k.pre(v, "V", reads=[b_ivf, b_qp], writes=[b_angq])
                k.post(v.tensor_scalar(angq[:], e["IVF"][:], e["qposc"][:, s_:s_ + 1],
                                       None, ALU.mult), "V", [b_ivf, b_qp], [b_angq])
            b_sq = sincos(k, blk, e, angq[:], None, nq, tq1, tq2, sinq, 0.0, [b_angq])
            b_cq = sincos(k, blk, e, angq[:], None, nq, tq1, tq2, cosq, math.pi / 2.0, [b_angq])

            @blk.vector
            def _(v, s_=s_):
                k.pre(v, "V", reads=[b_sq, b_cq], writes=[b_ccq[s_], b_ssq[s_]])
                k.post(v.tensor_copy(ccq[s_][:, 0:128:2], cosq[:]), "V", [b_cq], [b_ccq[s_]])
                k.post(v.tensor_copy(ccq[s_][:, 1:128:2], cosq[:]), "V", [b_cq], [b_ccq[s_]])
                k.post(v.tensor_copy(ssq[s_][:, 1:128:2], sinq[:]), "V", [b_sq], [b_ssq[s_]])
                k.post(v.tensor_scalar(ssq[s_][:, 0:128:2], sinq[:], -1.0, None, ALU.mult),
                       "V", [b_sq], [b_ssq[s_]])

        es_ps1.close()
        k.barrier(blk)
        wqb = [sb(f"p1_wqb{i}", [128, 1024], BF16) for i in range(2)]
        ps_q = [psa(f"p1_psq{i}", [128, 1024], F32) for i in range(4)]
        rt1 = sb("p1_rt1", [128, 1024], BF16)
        rt2 = sb("p1_rt2", [128, 1024], BF16)
        b_wqb = B("p1_wqb0", "p1_wqb1")
        b_psq = B(*[f"p1_psq{i}" for i in range(4)])
        b_rt1, b_rt2 = B("p1_rt1", "p1_rt2")
        wq_r = e["wq"].rearrange("(kd p) n -> kd p n", p=128)

        for sq in range(2):
            slots = [4 * sq + i for i in range(4)]
            for hf in range(2):
                for kd in range(16):
                    pa = kd % 2

                    @blk.gpsimd
                    def _(g, kd=kd, hf=hf, pa=pa):
                        k.pre(g, "SG", writes=[b_wqb[pa]])
                        k.post(g.dma_start(out=wqb[pa][:],
                                           in_=wq_r[kd, :, hf * 1024:(hf + 1) * 1024]),
                               "SG", [], [b_wqb[pa]])

                    @blk.tensor
                    def _(t, kd=kd, pa=pa, slots=slots):
                        k.pre(t, "T", reads=[b_wqb[pa], b_xoT], writes=b_psq)
                        ins = None
                        for si, s_ in enumerate(slots):
                            xs = xoT[:, kd * 1024 + s_ * 128: kd * 1024 + s_ * 128 + 128]
                            t.ldweights(xs)
                            for b2 in range(2):
                                ins = t.matmul(ps_q[si][:, b2 * 512:(b2 + 1) * 512], xs,
                                               wqb[pa][:, b2 * 512:(b2 + 1) * 512],
                                               start=(kd == 0), stop=(kd == 15),
                                               skip_group_check=True)
                        k.post(ins, "T", [b_wqb[pa], b_xoT], list(b_psq))

                for si, s_ in enumerate(slots):
                    @blk.vector
                    def _(v, si=si, s_=s_, hf=hf):
                        src = ps_q[si][:]
                        swp = src.rearrange("p (h pr two) -> p h pr two", pr=64, two=2)[
                            :, :, :, ::-1]
                        ccb = ccq[s_][:].rearrange("p (one j) -> p one j", one=1).broadcast_to(
                            [128, 8, 128])
                        ssb = ssq[s_][:].rearrange("p (one j) -> p one j", one=1).broadcast_to(
                            [128, 8, 128])
                        k.pre(v, "V", reads=[b_psq[si], b_ccq[s_], b_ssq[s_]],
                              writes=[b_rt1, b_rt2])
                        k.post(v.tensor_tensor(rt1[:], src, ccb, ALU.mult),
                               "V", [b_psq[si], b_ccq[s_]], [b_rt1])
                        k.post(v.tensor_tensor(rt2[:], swp, ssb, ALU.mult),
                               "V", [b_psq[si], b_ssq[s_]], [b_rt2])
                        k.pre(v, "V", reads=[b_rt1, b_rt2], writes=[b_qrot])
                        k.post(v.tensor_tensor(
                            e["qrot"][:, s_ * D + hf * 1024: s_ * D + (hf + 1) * 1024],
                            rt1[:], rt2[:], ALU.add), "V", [b_rt1, b_rt2], [b_qrot])

    if e["dbg_q"] is not None:
        @blk.scalar
        def _(s):
            k.pre(s, "SS", reads=[b_qrot])
            k.post(s.dma_start(out=e["dbg_q"][:, :], in_=e["qrot"][:]), "SS", [b_qrot], [])


def p2(k, blk, e):
    nc = k.nc
    B = e["B"]
    b_attn = B("attn")[0]
    b_id = B("ident")[0]
    b_qrot = B("qrot")[0]
    b_ckvd = B("ckv_d")[0]
    b_cc, b_ss, b_bmq = B("CC", "SS", "bmq")

    with contextlib.ExitStack() as es2:
        sb = lambda n, s, d: es2.enter_context(nc.sbuf_tensor(n, s, d))
        psa = lambda n, s, d: es2.enter_context(nc.psum_tensor(n, s, d))
        Kf = sb("p2_Kf", [128, T], BF16)
        Vt = sb("p2_Vt", [128, T], BF16)
        ckvq = [sb(f"p2_ckvq{i}", [128, 8 * 1024], BF16) for i in range(2)]
        wkv = sb("p2_wkv", [128, 8 * 256], BF16)
        kraw = [sb(f"p2_kraw{i}", [128, 1024], BF16) for i in range(2)]
        ktok = [sb(f"p2_ktok{i}", [128, 512], BF16) for i in range(2)]
        rr1 = [sb(f"p2_rr1_{i}", [128, 512], F32) for i in range(2)]
        rr2 = [sb(f"p2_rr2_{i}", [128, 512], F32) for i in range(2)]
        qfq = sb("p2_qfq", [128, 512], BF16)
        PT = [sb(f"p2_PT{i}", [128, 512], BF16) for i in range(2)]
        ptmpb = sb("p2_ptmpb", [128, 128], BF16)
        zq = sb("p2_zq", [1, 512], F32)
        zr = sb("p2_zr", [1, 512], F32)
        zbc = sb("p2_zbc", [128, 512], F32)
        wc2 = [sb(f"p2_wc2_{i}", [128, 512], F32) for i in range(2)]
        pos2 = sb("p2_pos2", [128, 512], BF16)
        q2 = [sb(f"p2_q2_{i}", [128, 512], BF16) for i in range(2)]

        psKV = psa("p2_psKV", [128, 1024], F32)
        psc = [psa(f"p2_psc{i}", [128, 512], F32) for i in range(2)]
        psAV = psa("p2_psAV", [128, 512], F32)
        psZr = psa("p2_psZr", [128, 512], F32)
        ptr = [psa(f"p2_ptr{i}", [128, 128], BF16) for i in range(2)]

        b_Kf, b_Vt, b_wkv = B("p2_Kf", "p2_Vt", "p2_wkv")
        b_ckvq = B("p2_ckvq0", "p2_ckvq1")
        b_kraw = B("p2_kraw0", "p2_kraw1")
        b_ktok = B("p2_ktok0", "p2_ktok1")
        b_rr1 = B("p2_rr1_0", "p2_rr1_1")
        b_rr2 = B("p2_rr2_0", "p2_rr2_1")
        b_qfq, b_PTb0, b_PTb1, b_ptmpb = B("p2_qfq", "p2_PT0", "p2_PT1", "p2_ptmpb")
        b_PT = [b_PTb0, b_PTb1]
        b_zq, b_zr, b_zbc = B("p2_zq", "p2_zr", "p2_zbc")
        b_idf = B("identf")[0]
        b_kvps = B("p2_psKV_kv")[0]
        b_psc = B("p2_psc0", "p2_psc1")
        b_psAV, b_psZr = B("p2_psAV", "p2_psZr")
        b_ptr = B(*[f"p2_ptr{i}" for i in range(2)])
        b_bmk = B("bmk")[0]
        b_ones = B("ones")[0]

        wup_r = e["wup"].rearrange("(l p) n -> l p n", p=128)

        for h in range(NH):
            p0w2_emit(k, blk, e, h / NH, (h + 1) / NH, (wc2, pos2, q2))

            @blk.gpsimd
            def _(g, h=h):
                k.pre(g, "SG", writes=[b_wkv])
                k.post(g.dma_start(out=wkv[:].rearrange("p (l c) -> p l c", l=8),
                                   in_=wup_r[:, :, h * 256:(h + 1) * 256]), "SG", [], [b_wkv])

            for er in range(8):
                cpa = er % 2

                @blk.sync
                def _(sy, er=er, cpa=cpa):
                    k.pre(sy, "SH", reads=[b_ckvd], writes=[b_ckvq[cpa]])
                    k.post(sy.dma_start(out=ckvq[cpa][:].rearrange("p (l t) -> p l t", l=8),
                                        in_=e["ckv_d"][:, :, er * 1024:(er + 1) * 1024]),
                           "SH", [b_ckvd], [b_ckvq[cpa]])

                for part in range(2):
                    base = part * 128
                    rpa = part

                    @blk.tensor
                    def _(t, base=base, cpa=cpa, rpa=rpa):
                        k.pre(t, "T", reads=[b_ckvq[cpa], b_wkv], writes=[b_kvps])
                        ins = None
                        for l in range(8):
                            w = wkv[:, l * 256 + base: l * 256 + base + 128]
                            t.ldweights(w)
                            for b in range(2):
                                ins = t.matmul(
                                    psKV[:, b * 512:(b + 1) * 512], w,
                                    ckvq[cpa][:, l * 1024 + b * 512: l * 1024 + (b + 1) * 512],
                                    start=(l == 0), stop=(l == 7), skip_group_check=True)
                        k.post(ins, "T", [b_ckvq[cpa], b_wkv], [b_kvps])

                    @blk.scalar
                    def _(s, rpa=rpa):
                        k.pre(s, "A", reads=[b_kvps], writes=[b_kraw[rpa]])
                        k.post(s.activation(kraw[rpa][:], psKV[:], AF.Identity),
                               "A", [b_kvps], [b_kraw[rpa]])

                    if part == 1:
                        for g2 in range(4):
                            @blk.tensor
                            def _(t, g2=g2, rpa=rpa):
                                k.pre(t, "T", reads=[b_kraw[rpa], b_id], writes=list(b_ptr))
                                for i in range(2):
                                    kt = g2 * 2 + i
                                    k.post(t.transpose(ptr[i][:],
                                                       kraw[rpa][:, kt * 128:(kt + 1) * 128],
                                                       e["ident_s"][:]),
                                           "T", [b_kraw[rpa], b_id], [b_ptr[i]])

                            @blk.scalar
                            def _(s, er=er, g2=g2):
                                k.pre(s, "A", reads=list(b_ptr), writes=[b_Vt])
                                ins = None
                                for i in range(2):
                                    kt_g = er * 8 + g2 * 2 + i
                                    ins = s.activation(Vt[:, kt_g * 128:(kt_g + 1) * 128],
                                                       ptr[i][:], AF.Identity)
                                k.post(ins, "A", list(b_ptr), [b_Vt])
                    else:
                        for b4 in range(2):
                            gpa = b4

                            for h2 in range(2):
                                @blk.tensor
                                def _(t, b4=b4, h2=h2, rpa=rpa):
                                    k.pre(t, "T", reads=[b_kraw[rpa], b_id], writes=list(b_ptr))
                                    for i in range(2):
                                        kt = b4 * 4 + h2 * 2 + i
                                        k.post(t.transpose(ptr[i][:],
                                                           kraw[rpa][:, kt * 128:(kt + 1) * 128],
                                                           e["ident_s"][:]),
                                               "T", [b_kraw[rpa], b_id], [b_ptr[i]])

                                @blk.vector
                                def _(v, er=er, b4=b4, h2=h2, gpa=gpa):
                                    bkt = er * 8 + b4 * 4 + h2 * 2
                                    k.pre(v, "V", reads=list(b_ptr) + [b_cc, b_ss],
                                          writes=[b_rr1[gpa], b_rr2[gpa]])
                                    i1 = i2 = None
                                    for i in range(2):
                                        col = (h2 * 2 + i) * 128
                                        srcp = ptr[i][:]
                                        swp = srcp.rearrange("p (pr two) -> p pr two", two=2)[
                                            :, :, ::-1]
                                        i1 = v.tensor_tensor(
                                            rr1[gpa][:, col:col + 128], srcp,
                                            e["CC"][:, (bkt + i) * 128:(bkt + i + 1) * 128],
                                            ALU.mult)
                                        i2 = v.tensor_tensor(
                                            rr2[gpa][:, col:col + 128], swp,
                                            e["SS"][:, (bkt + i) * 128:(bkt + i + 1) * 128],
                                            ALU.mult)
                                    k.post(i1, "V", list(b_ptr) + [b_cc], [b_rr1[gpa]])
                                    k.post(i2, "V", list(b_ptr) + [b_ss], [b_rr2[gpa]])

                            @blk.gpsimd
                            def _(g, gpa=gpa):
                                k.pre(g, "G", reads=[b_rr1[gpa], b_rr2[gpa]],
                                      writes=[b_ktok[gpa]])
                                k.post(g.tensor_tensor(ktok[gpa][:], rr1[gpa][:], rr2[gpa][:],
                                                       ALU.add),
                                       "G", [b_rr1[gpa], b_rr2[gpa]], [b_ktok[gpa]])

                            for h2 in range(2):
                                @blk.tensor
                                def _(t, h2=h2, gpa=gpa):
                                    k.pre(t, "T", reads=[b_ktok[gpa], b_id], writes=list(b_ptr))
                                    for i in range(2):
                                        kt = h2 * 2 + i
                                        k.post(t.transpose(ptr[i][:],
                                                           ktok[gpa][:, kt * 128:(kt + 1) * 128],
                                                           e["ident_s"][:]),
                                               "T", [b_ktok[gpa], b_id], [b_ptr[i]])

                                @blk.scalar
                                def _(s, er=er, b4=b4, h2=h2):
                                    k.pre(s, "A", reads=list(b_ptr), writes=[b_Kf])
                                    ins = None
                                    for i in range(2):
                                        kt_g = er * 8 + b4 * 4 + h2 * 2 + i
                                        ins = s.activation(Kf[:, kt_g * 128:(kt_g + 1) * 128],
                                                           ptr[i][:], AF.Identity)
                                    k.post(ins, "A", list(b_ptr), [b_Kf])
            if e["dbg_kf"] is not None and h == 0:
                @blk.scalar
                def _(s):
                    k.pre(s, "SS", reads=[b_Kf, b_Vt])
                    k.post(s.dma_start(out=e["dbg_kf"][:, :], in_=Kf[:]), "SS", [b_Kf], [])
                    k.post(s.dma_start(out=e["dbg_vt"][:, :], in_=Vt[:]), "SS", [b_Vt], [])

            for qi, slots in enumerate(QUADS):
                for ci, s_ in enumerate(slots):
                    @blk.tensor
                    def _(t, s_=s_, ci=ci, h=h):
                        k.pre(t, "T", reads=[b_qrot, b_id], writes=[b_ptr[ci % 2]])
                        k.post(t.transpose(ptr[ci % 2][:],
                                           e["qrot"][:, s_ * D + h * 128: s_ * D + (h + 1) * 128],
                                           e["ident_s"][:]), "T", [b_qrot, b_id],
                               [b_ptr[ci % 2]])

                    @blk.scalar
                    def _(s, ci=ci):
                        k.pre(s, "A", reads=[b_ptr[ci % 2]], writes=[b_qfq])
                        k.post(s.activation(qfq[:, ci * 128:(ci + 1) * 128], ptr[ci % 2][:],
                                            AF.Identity), "A", [b_ptr[ci % 2]], [b_qfq])

                maxc = (slots[0] + 1) * 8

                def emit_sct(j):
                    nval = sum(1 for s_ in slots if (s_ + 1) * 8 > j)
                    ncol = nval * 128
                    pa = j % 2

                    @blk.tensor
                    def _(t, j=j, pa=pa, ncol=ncol):
                        k.pre(t, "T", reads=[b_Kf, b_qfq], writes=[b_psc[pa]])
                        t.ldweights(Kf[:, j * 128:(j + 1) * 128])
                        k.post(t.matmul(psc[pa][:, 0:ncol], Kf[:, j * 128:(j + 1) * 128],
                                        qfq[:, 0:ncol], start=True, stop=True,
                                        skip_group_check=True),
                               "T", [b_Kf, b_qfq], [b_psc[pa]])

                def emit_tail(j):
                    nval = sum(1 for s_ in slots if (s_ + 1) * 8 > j)
                    ncol = nval * 128
                    pa = j % 2
                    sb_ = j // 8
                    boundary = sb_ in slots and slots.index(sb_) == nval - 1

                    if not boundary:
                        @blk.scalar
                        def _(s, j=j, pa=pa, ncol=ncol):
                            k.pre(s, "A", reads=[b_psc[pa]], writes=[b_PT[pa]])
                            k.post(s.activation(PT[pa][:, 0:ncol], psc[pa][:, 0:ncol],
                                                AF.Exp, scale=SCALE),
                                   "A", [b_psc[pa]], [b_PT[pa]])
                    else:
                        @blk.scalar
                        def _(s, j=j, pa=pa, ncol=ncol):
                            k.pre(s, "A", reads=[b_psc[pa]], writes=[b_PT[pa], b_ptmpb])
                            ins = None
                            if ncol > 128:
                                ins = s.activation(PT[pa][:, 0:ncol - 128],
                                                   psc[pa][:, 0:ncol - 128], AF.Exp,
                                                   scale=SCALE)
                            ins2 = s.activation(ptmpb[:], psc[pa][:, ncol - 128:ncol],
                                                AF.Exp, scale=SCALE)
                            if ins is not None:
                                k.post(ins, "A", [b_psc[pa]], [b_PT[pa]])
                            k.post(ins2, "A", [b_psc[pa]], [b_ptmpb])

                        @blk.vector
                        def _(v, j=j, pa=pa, ncol=ncol):
                            k.pre(v, "V", reads=[b_ptmpb, b_bmk], writes=[b_PT[pa]])
                            jm = j % 8
                            k.post(v.tensor_tensor(PT[pa][:, ncol - 128:ncol], ptmpb[:],
                                                   e["bmk"][:, jm * 128:(jm + 1) * 128],
                                                   ALU.mult), "V", [b_ptmpb, b_bmk],
                                   [b_PT[pa]])

                    @blk.tensor
                    def _(t, j=j, pa=pa, ncol=ncol, maxc=maxc):
                        k.pre(t, "T", reads=[b_PT[pa], b_Vt, b_ones, e["B"]("ones128")[0]], writes=[b_psAV, b_psZr])
                        t.ldweights(Vt[:, j * 128:(j + 1) * 128])
                        k.post(t.matmul(psAV[:, 0:ncol], Vt[:, j * 128:(j + 1) * 128],
                                        PT[pa][:, 0:ncol], start=(j == 0),
                                        stop=(j == maxc - 1), skip_group_check=True),
                               "T", [b_PT[pa], b_Vt], [b_psAV])
                        t.ldweights(e["ones128"][:])
                        k.post(t.matmul(psZr[:, 0:ncol], e["ones128"][:],
                                        PT[pa][:, 0:ncol], start=(j == 0),
                                        stop=(j == maxc - 1), skip_group_check=True),
                               "T", [b_PT[pa], e["B"]("ones128")[0]], [b_psZr])

                emit_sct(0)
                for j in range(maxc):
                    if j + 1 < maxc:
                        emit_sct(j + 1)
                    emit_tail(j)

                @blk.scalar
                def _(s):
                    k.pre(s, "A", reads=[b_psZr], writes=[b_zq])
                    k.post(s.activation(zq[0:1, :], psZr[0:1, :], AF.Identity),
                           "A", [b_psZr], [b_zq])

                @blk.vector
                def _(v):
                    k.pre(v, "V", reads=[b_zq], writes=[b_zr])
                    k.post(v.reciprocal(zr[0:1, :], zq[0:1, :]), "V", [b_zq], [b_zr])

                @blk.tensor
                def _(t):
                    k.pre(t, "T", reads=[b_zr, B("onesrow_f")[0]], writes=[b_psc[0]])
                    k.post(t.matmul(psc[0][:, 0:512], e["onesrow_f"][0:1, :], zr[0:1, :],
                                    start=True, stop=True, skip_group_check=True),
                           "T", [b_zr, B("onesrow_f")[0]], [b_psc[0]])

                @blk.scalar
                def _(s):
                    k.pre(s, "A", reads=[b_psc[0]], writes=[b_zbc])
                    k.post(s.activation(zbc[:], psc[0][:, 0:512], AF.Identity),
                           "A", [b_psc[0]], [b_zbc])

                @blk.vector
                def _(v, h=h, qi=qi):
                    k.pre(v, "V", reads=[b_psAV, b_zbc], writes=[b_attn])
                    k.post(v.tensor_tensor(
                        e["attn"][:, h * 1024 + qi * 512: h * 1024 + (qi + 1) * 512],
                        psAV[:], zbc[:], ALU.mult), "V", [b_psAV, b_zbc], [b_attn])

    if e["dbg_at"] is not None:
        @blk.scalar
        def _(s):
            k.pre(s, "SS", reads=[b_attn])
            k.post(s.dma_start(out=e["dbg_at"][:, :], in_=e["attn"][:]), "SS", [b_attn], [])


def p3(k, blk, e):
    nc = k.nc
    B = e["B"]
    b_x1d = B("x1_d")[0]
    b_attn = B("attn")[0]

    with contextlib.ExitStack() as es3:
        sb = lambda n, s, d: es3.enter_context(nc.sbuf_tensor(n, s, d))
        psa = lambda n, s, d: es3.enter_context(nc.psum_tensor(n, s, d))
        wob = [sb(f"p3_wob{i}", [128, D], BF16) for i in range(2)]
        xot = [sb(f"p3_xot{i}", [128, D], F32) for i in range(2)]
        vsl = sb("p3_v", [128, D], F32)
        sqs = sb("p3_sq", [128, D], F32)
        x1t = sb("p3_x1t", [128, D], F32)
        cols = sb("p3_cols", [128, 8], F32)
        ps_wo = [psa(f"p3_pswo{i}", [128, 2048], F32) for i in range(2)]
        b_wob = B("p3_wob0", "p3_wob1")
        b_xot = B("p3_xot0", "p3_xot1")
        b_v, b_sq, b_cols = B("p3_v", "p3_sq", "p3_cols")
        b_pswo = B("p3_pswo0", "p3_pswo1")
        wo_r = e["wo_"].rearrange("(f p) n -> f p n", p=128)

        for sp in range(4):
            slots = [2 * sp, 2 * sp + 1]
            for f in range(16):
                pa = f % 2

                @blk.gpsimd
                def _(g, f=f, pa=pa):
                    k.pre(g, "SG", writes=[b_wob[pa]])
                    k.post(g.dma_start(out=wob[pa][:], in_=wo_r[f, :, :]), "SG", [], [b_wob[pa]])

                @blk.tensor
                def _(t, f=f, pa=pa, slots=slots):
                    k.pre(t, "T", reads=[b_wob[pa], b_attn], writes=list(b_pswo))
                    ins = None
                    for si, s_ in enumerate(slots):
                        qi, ci = QCOL[s_]
                        col = f * 1024 + qi * 512 + ci * 128
                        t.ldweights(e["attn"][:, col:col + 128])
                        for cb in range(4):
                            ins = t.matmul(ps_wo[si][:, cb * 512:(cb + 1) * 512],
                                           e["attn"][:, col:col + 128],
                                           wob[pa][:, cb * 512:(cb + 1) * 512],
                                           start=(f == 0), stop=(f == 15),
                                           skip_group_check=True)
                    k.post(ins, "T", [b_wob[pa], b_attn], list(b_pswo))

            for si, s_ in enumerate(slots):
                @blk.sync
                def _(sy, si=si, s_=s_):
                    k.pre(sy, "SH", writes=[b_xot[si]])
                    k.post(sy.dma_start(out=xot[si][:], in_=e["xo"][s_, :, :]),
                           "SH", [], [b_xot[si]])

                @blk.vector
                def _(v, si=si):
                    k.pre(v, "V", reads=[b_pswo[si], b_xot[si]], writes=[b_v, b_cols])
                    k.post(v.scalar_tensor_tensor(vsl[:], ps_wo[si][:], 0.0, xot[si][:],
                                                  ALU.add, ALU.add, accum_out=cols[:, 0:1]),
                           "V", [b_pswo[si], b_xot[si]], [b_v, b_cols])

                b_x1t = B("p3_x1t")[0]
                ln_epilogue(k, blk, e, vsl, b_v, sqs, b_sq, cols, b_cols,
                            x1t[:], b_x1t, D)

                @blk.scalar
                def _(s, s_=s_):
                    k.pre(s, "SS", reads=[b_x1t], writes=[b_x1d])
                    k.post(s.dma_start(out=e["x1_d"][s_, :, :], in_=x1t[:]),
                           "SS", [b_x1t], [b_x1d])


def ln_epilogue(k, blk, e, vsl, b_v, sqs, b_sq, cols, b_cols, out_ap, b_out, dim):
    @blk.scalar
    def _(s):
        k.pre(s, "A", reads=[b_v], writes=[b_sq, b_cols])
        k.post(s.activation(sqs[:], vsl[:], AF.Square, accum_out=cols[:, 1:2]),
               "A", [b_v], [b_sq, b_cols])

    @blk.vector
    def _(v):
        k.pre(v, "V", reads=[b_cols], writes=[b_cols])
        k.post(v.tensor_scalar(cols[:, 2:3], cols[:, 0:1], 1.0 / dim, None, ALU.mult),
               "V", [b_cols], [b_cols])
        k.pre(v, "V", reads=[b_cols], writes=[b_cols])
        k.post(v.tensor_tensor(cols[:, 3:4], cols[:, 2:3], cols[:, 2:3], ALU.mult),
               "V", [b_cols], [b_cols])
        k.pre(v, "V", reads=[b_cols], writes=[b_cols])
        k.post(v.scalar_tensor_tensor(cols[:, 4:5], cols[:, 1:2], 1.0 / dim, cols[:, 3:4],
                                      ALU.mult, ALU.subtract), "V", [b_cols], [b_cols])
        k.pre(v, "V", reads=[b_cols], writes=[b_cols])
        k.post(v.tensor_scalar(cols[:, 5:6], cols[:, 4:5], 1e-5, None, ALU.add),
               "V", [b_cols], [b_cols])
        k.pre(v, "V", reads=[b_cols], writes=[b_cols])
        k.post(v.reciprocal(cols[:, 6:7], cols[:, 5:6]), "V", [b_cols], [b_cols])

    @blk.scalar
    def _(s):
        k.pre(s, "A", reads=[b_cols], writes=[b_cols])
        k.post(s.activation(cols[:, 7:8], cols[:, 6:7], AF.Sqrt), "A", [b_cols], [b_cols])

    @blk.vector
    def _(v):
        k.pre(v, "V", reads=[b_cols], writes=[b_cols])
        k.post(v.scalar_tensor_tensor(cols[:, 3:4], cols[:, 2:3], -1.0, cols[:, 7:8],
                                      ALU.mult, ALU.mult), "V", [b_cols], [b_cols])

    @blk.scalar
    def _(s):
        k.pre(s, "A", reads=[b_v, b_cols], writes=[b_out])
        k.post(s.activation(out_ap, vsl[:], AF.Identity, bias=cols[:, 3:4],
                            scale=cols[:, 7:8]), "A", [b_v, b_cols], [b_out])


def p4(k, blk, e):
    nc = k.nc
    B = e["B"]
    b_x1d = B("x1_d")[0]
    b_id = B("ident")[0]
    b_mwb = B("mwb0", "mwb1", "mwb2")
    b_thr = B("thr0", "thr1", "thr2")
    b_nth = B("nthr0", "nthr1", "nthr2")
    b_w1qn, b_w2qn, b_w3qn = B("w1qn", "w2qn", "w3qn")
    b_hqd, b_hqtd, b_out = B("hq_d", "hqt_d", "out_d")

    with contextlib.ExitStack() as es4:
        sb = lambda n, s, d: es4.enter_context(nc.sbuf_tensor(n, s, d))
        xqT = sb("p4_xqT", [128, 16 * NSLOT * 128], BF16)
        msc1 = sb("p4_msc1", [128, NSLOT], F32)
        msc2 = sb("p4_msc2", [128, NSLOT], F32)
        msc3 = sb("p4_msc3", [128, NSLOT], F32)
        ccols = sb("p4_cols", [128, 8], F32)
        posb = sb("p4_pos", [128, 512], BF16)
        b_xqT, b_ms1, b_ms2, b_ms3, b_cc4 = B(
            "p4_xqT", "p4_msc1", "p4_msc2", "p4_msc3", "p4_cols")
        b_xqt, b_n4, b_r4, b_t4, b_sq4 = B("p4_xqtok", "p4_n", "p4_r", "p4_t", "p4_sq")
        b_pos = B("p4_pos")[0]

        with contextlib.ExitStack() as esp:
            sba = lambda n, s, d: esp.enter_context(nc.sbuf_tensor(n, s, d))
            xq_tok = sba("p4_xqtok", [128, D], BF16)
            ni32 = sba("p4_n", [128, D], I32)
            rbuf = sba("p4_r", [128, D], F32)
            tbuf = sba("p4_t", [128, D], F32)
            sqs = sba("p4_sq", [128, D], F32)
            x1row = sba("p4_x1row", [128, D], F32)
            b_x1r = B("p4_x1row")[0]
            ptr = [esp.enter_context(nc.psum_tensor(f"p4_ptr{i}", [128, 128], BF16))
                   for i in range(8)]
            b_ptr = B(*[f"p4_ptr{i}" for i in range(8)])
            for s_ in range(NSLOT):
                @blk.sync
                def _(sy, s_=s_):
                    k.pre(sy, "SH", reads=[b_x1d], writes=[b_x1r])
                    k.post(sy.dma_start(out=x1row[:], in_=e["x1_d"][s_, :, :]),
                           "SH", [b_x1d], [b_x1r])
                x1s = x1row[:]

                @blk.vector
                def _(v, x1s=x1s):
                    k.pre(v, "V", reads=[b_x1r], writes=[b_cc4])
                    k.post(v.reduce_max(ccols[:, 0:1], x1s, axis=AX,
                                        apply_absolute_value=True), "V", [b_x1r], [b_cc4])

                @blk.scalar
                def _(s, x1s=x1s):
                    k.pre(s, "A", reads=[b_x1r], writes=[b_sq4, b_cc4])
                    k.post(s.activation(sqs[:], x1s, AF.Square, accum_out=ccols[:, 1:2]),
                           "A", [b_x1r], [b_sq4, b_cc4])

                quant_scale_chain(k, blk, e, ccols, b_cc4, msc1, b_ms1, msc2, b_ms2, s_,
                                  D, b_mwb[0], b_mwb[1], e["mwb"][0], e["mwb"][1])

                @blk.scalar
                def _(s, x1s=x1s):
                    k.pre(s, "A", reads=[b_x1r, b_cc4], writes=[b_n4])
                    k.post(s.activation(ni32[:], x1s, AF.Identity, scale=ccols[:, 7:8]),
                           "A", [b_x1r, b_cc4], [b_n4])

                @blk.vector
                def _(v, x1s=x1s):
                    k.pre(v, "V", reads=[b_x1r, b_cc4, b_n4], writes=[b_r4])
                    k.post(v.scalar_tensor_tensor(rbuf[:], x1s, ccols[:, 7:8], ni32[:],
                                                  ALU.mult, ALU.subtract),
                           "V", [b_x1r, b_cc4, b_n4], [b_r4])
                    k.pre(v, "V", reads=[b_r4, b_n4], writes=[b_t4])
                    k.post(v.scalar_tensor_tensor(tbuf[:], rbuf[:], 0.5, ni32[:],
                                                  ALU.is_ge, ALU.add), "V", [b_r4, b_n4], [b_t4])
                    k.pre(v, "V", reads=[b_r4, b_t4], writes=[b_xqt])
                    k.post(v.scalar_tensor_tensor(xq_tok[:], rbuf[:], -0.5, tbuf[:],
                                                  ALU.is_le, ALU.subtract),
                           "V", [b_r4, b_t4], [b_xqt])

                for half in range(2):
                    @blk.tensor
                    def _(t, half=half):
                        k.pre(t, "T", reads=[b_xqt, b_id], writes=b_ptr)
                        for i in range(8):
                            kd = half * 8 + i
                            k.post(t.transpose(ptr[i][:], xq_tok[:, kd * 128:(kd + 1) * 128],
                                               e["ident_s"][:]), "T", [b_xqt, b_id], [b_ptr[i]])

                    @blk.vector
                    def _(v, s_=s_, half=half):
                        k.pre(v, "V", reads=b_ptr, writes=[b_xqT])
                        ins = None
                        for i in range(8):
                            kd = half * 8 + i
                            ins = v.tensor_copy(
                                xqT[:, kd * 1024 + s_ * 128: kd * 1024 + s_ * 128 + 128],
                                ptr[i][:])
                        k.post(ins, "V", b_ptr, [b_xqT])

        k.barrier(blk)
        es_b = contextlib.ExitStack()
        sbb = lambda n, s, d: es_b.enter_context(nc.sbuf_tensor(n, s, d))
        w1qb = sbb("p4_w1qb", [128, 16 * 512], BF16)
        w2qb = sbb("p4_w2qb", [128, 16 * 512], BF16)
        gtmp = sbb("p4_gtmp", [128, 512], F32)
        utmp = sbb("p4_utmp", [128, 512], F32)
        hqc = sbb("p4_hqc", [128, 512], F32)
        b_w1q, b_w2q, b_g, b_u, b_hqc = B("p4_w1qb", "p4_w2qb", "p4_g", "p4_u", "p4_hqc")

        with contextlib.ExitStack() as esp:
            psGU = [esp.enter_context(nc.psum_tensor(f"p4_gu{i}", [128, 512], F32))
                    for i in range(8)]
            b_gu = B(*[f"p4_gu{i}" for i in range(8)])
            for hb in range(11):
                for wsel, (wqn_, qdst, b_qdst) in enumerate(
                        [(e["w1qn"], w1qb, b_w1q), (e["w2qn"], w2qb, b_w2q)]):
                    b_wnx = b_w1qn if wsel == 0 else b_w2qn
                    for kd in range(16):
                        @blk.sync
                        def _(sy, wqn_=wqn_, hb=hb, kd=kd, qdst=qdst, b_qdst=b_qdst,
                              b_wnx=b_wnx):
                            k.pre(sy, "SH", reads=[b_wnx], writes=[b_qdst])
                            k.post(sy.dma_start_transpose(
                                qdst[:, kd * 512:(kd + 1) * 512],
                                wqn_[hb * 512:(hb + 1) * 512, kd * 128:(kd + 1) * 128]),
                                "SH", [b_wnx], [b_qdst])

                for half in range(2):
                    slots = [half * 4 + i for i in range(4)]

                    @blk.tensor
                    def _(t, slots=slots):
                        k.pre(t, "T", reads=[b_w1q, b_w2q, b_xqT], writes=list(b_gu))
                        ins = None
                        for kd in range(16):
                            for si, s_ in enumerate(slots):
                                xs = xqT[:, kd * 1024 + s_ * 128: kd * 1024 + s_ * 128 + 128]
                                t.ldweights(xs)
                                t.matmul(psGU[si][:], xs, w1qb[:, kd * 512:(kd + 1) * 512],
                                         start=(kd == 0), stop=(kd == 15), skip_group_check=True)
                                ins = t.matmul(psGU[4 + si][:], xs,
                                               w2qb[:, kd * 512:(kd + 1) * 512],
                                               start=(kd == 0), stop=(kd == 15),
                                               skip_group_check=True)
                        k.post(ins, "T", [b_w1q, b_w2q, b_xqT], list(b_gu))

                    for si, s_ in enumerate(slots):
                        @blk.scalar
                        def _(s, si=si, s_=s_):
                            k.pre(s, "A", reads=[b_gu[si], b_ms1], writes=[b_g])
                            k.post(s.activation(gtmp[:], psGU[si][:], AF.Silu,
                                                scale=msc1[:, s_:s_ + 1]),
                                   "A", [b_gu[si], b_ms1], [b_g])
                            k.pre(s, "A", reads=[b_gu[4 + si], b_ms2], writes=[b_u])
                            k.post(s.activation(utmp[:], psGU[4 + si][:], AF.Identity,
                                                scale=msc2[:, s_:s_ + 1]),
                                   "A", [b_gu[4 + si], b_ms2], [b_u])

                        @blk.vector
                        def _(v):
                            k.pre(v, "V", reads=[b_g, b_u], writes=[b_hqc])
                            k.post(v.tensor_tensor(hqc[:], gtmp[:], utmp[:], ALU.mult),
                                   "V", [b_g, b_u], [b_hqc])

                        @blk.scalar
                        def _(s, s_=s_, hb=hb):
                            k.pre(s, "SS", reads=[b_hqc], writes=[b_hqd])
                            k.post(s.dma_start(out=e["hq_d"][s_, :, hb * 512:(hb + 1) * 512],
                                               in_=hqc[:]), "SS", [b_hqc], [b_hqd])

        es_b.close()
        k.barrier(blk)
        es_c = contextlib.ExitStack()
        sbc = lambda n, s, d: es_c.enter_context(nc.sbuf_tensor(n, s, d))
        HC4 = HID // 4
        hrow = sbc("p4_hrow", [128, HID], F32)
        hsq = sbc("p4_hsq", [128, HC4], F32)
        hn = sbc("p4_hn", [128, HC4], I32)
        hr = sbc("p4_hr", [128, HC4], F32)
        ht = sbc("p4_ht", [128, HC4], F32)
        hqq = sbc("p4_hqq", [128, HID], BF16)
        hstage = sbc("p4_hstage", [128, 44 * 128], BF16)
        hsums = sbc("p4_hsums", [128, 4], F32)
        b_hrow, b_hsq, b_hn, b_hr, b_ht, b_hqq, b_hst, b_hsu = B(
            "p4_hrow", "p4_hsq", "p4_hn", "p4_hr", "p4_ht", "p4_hqq", "p4_hstage",
            "p4_hsums")

        with contextlib.ExitStack() as esp:
            ptr = [esp.enter_context(nc.psum_tensor(f"p4c_ptr{i}", [128, 128], BF16))
                   for i in range(8)]
            b_ptr = B(*[f"p4c_ptr{i}" for i in range(8)])
            for s_ in range(NSLOT):
                @blk.sync
                def _(sy, s_=s_):
                    k.pre(sy, "SH", reads=[b_hqd], writes=[b_hrow])
                    k.post(sy.dma_start(out=hrow[:], in_=e["hq_d"][s_, :, :]),
                           "SH", [b_hqd], [b_hrow])

                @blk.vector
                def _(v):
                    k.pre(v, "V", reads=[b_hrow], writes=[b_cc4])
                    k.post(v.reduce_max(ccols[:, 0:1], hrow[:], axis=AX,
                                        apply_absolute_value=True), "V", [b_hrow], [b_cc4])

                for c4 in range(4):
                    @blk.scalar
                    def _(s, c4=c4):
                        k.pre(s, "A", reads=[b_hrow], writes=[b_hsq, b_hsu])
                        k.post(s.activation(hsq[:], hrow[:, c4 * HC4:(c4 + 1) * HC4],
                                            AF.Square, accum_out=hsums[:, c4:c4 + 1]),
                               "A", [b_hrow], [b_hsq, b_hsu])

                @blk.vector
                def _(v):
                    k.pre(v, "V", reads=[b_hsu], writes=[b_cc4])
                    k.post(v.reduce_sum(ccols[:, 1:2], hsums[:], axis=AX),
                           "V", [b_hsu], [b_cc4])

                quant_scale_chain(k, blk, e, ccols, b_cc4, msc3, b_ms3, None, None, s_,
                                  HID, b_mwb[2], None, e["mwb"][2], None)

                for c4 in range(4):
                    sl = slice(c4 * HC4, (c4 + 1) * HC4)

                    @blk.scalar
                    def _(s, sl=sl):
                        k.pre(s, "A", reads=[b_hrow, b_cc4], writes=[b_hn])
                        k.post(s.activation(hn[:], hrow[:, sl], AF.Identity,
                                            scale=ccols[:, 7:8]), "A", [b_hrow, b_cc4], [b_hn])

                    @blk.vector
                    def _(v, sl=sl):
                        k.pre(v, "V", reads=[b_hrow, b_cc4, b_hn], writes=[b_hr])
                        k.post(v.scalar_tensor_tensor(hr[:], hrow[:, sl], ccols[:, 7:8], hn[:],
                                                      ALU.mult, ALU.subtract),
                               "V", [b_hrow, b_cc4, b_hn], [b_hr])
                        k.pre(v, "V", reads=[b_hr, b_hn], writes=[b_ht])
                        k.post(v.scalar_tensor_tensor(ht[:], hr[:], 0.5, hn[:],
                                                      ALU.is_ge, ALU.add), "V",
                               [b_hr, b_hn], [b_ht])
                        k.pre(v, "V", reads=[b_hr, b_ht], writes=[b_hqq])
                        k.post(v.scalar_tensor_tensor(hqq[:, sl], hr[:], -0.5, ht[:],
                                                      ALU.is_le, ALU.subtract),
                               "V", [b_hr, b_ht], [b_hqq])

                for gb in range(6):
                    n = 8 if gb < 5 else 4

                    @blk.tensor
                    def _(t, gb=gb, n=n):
                        k.pre(t, "T", reads=[b_hqq, b_id], writes=b_ptr[:n])
                        for i in range(n):
                            hc = gb * 8 + i
                            k.post(t.transpose(ptr[i][:], hqq[:, hc * 128:(hc + 1) * 128],
                                               e["ident_s"][:]), "T", [b_hqq, b_id], [b_ptr[i]])

                    @blk.vector
                    def _(v, gb=gb, n=n):
                        k.pre(v, "V", reads=b_ptr[:n], writes=[b_hst])
                        ins = None
                        for i in range(n):
                            hc = gb * 8 + i
                            ins = v.tensor_copy(hstage[:, hc * 128:(hc + 1) * 128], ptr[i][:])
                        k.post(ins, "V", b_ptr[:n], [b_hst])

                @blk.scalar
                def _(s, s_=s_):
                    k.pre(s, "SS", reads=[b_hst], writes=[b_hqtd])
                    k.post(s.dma_start(
                        out=e["hqt_d"][:, :, s_ * 128:(s_ + 1) * 128].rearrange(
                            "c p t -> p c t"),
                        in_=hstage[:].rearrange("p (c t) -> p c t", c=44)),
                        "SS", [b_hst], [b_hqtd])

        es_c.close()
        k.barrier(blk)
        hqtb = [sb(f"p4_hqtb{i}", [128, NSLOT * 128], BF16) for i in range(2)]
        w3qc = [sb(f"p4_w3qc{i}", [128, D], BF16) for i in range(2)]
        vsl = sb("p4b_v", [128, D], F32)
        sq2 = sb("p4b_sq", [128, D], F32)
        outs = sb("p4b_out", [128, D], F32)
        x1b = sb("p4b_x1", [128, D], F32)
        b_hqtb = B("p4_hqtb0", "p4_hqtb1")
        b_w3qc = B("p4_w3qc0", "p4_w3qc1")
        b_v4, b_sq2, b_os = B("p4b_v", "p4b_sq", "p4b_out")
        b_x1b = B("p4b_x1")[0]

        with contextlib.ExitStack() as esp:
            psZ = [esp.enter_context(nc.psum_tensor(f"p4_psz{i}", [128, 2048], F32))
                   for i in range(2)]
            b_psZ = B("p4_psz0", "p4_psz1")
            for sp in range(4):
                slots = [2 * sp, 2 * sp + 1]
                for hc in range(44):
                    pa = hc % 2

                    b_w3qt = B("w3qt")[0]

                    @blk.sync
                    def _(sy, hc=hc, pa=pa, sp=sp):
                        k.pre(sy, "SH", reads=[b_hqtd, b_w3qn, b_w3qt],
                              writes=[b_hqtb[pa], b_w3qc[pa]])
                        k.post(sy.dma_start(out=hqtb[pa][:], in_=e["hqt_d"][hc, :, :]),
                               "SH", [b_hqtd], [b_hqtb[pa]])
                        if sp == 0:
                            for cb in range(4):
                                k.post(sy.dma_start_transpose(
                                    w3qc[pa][:, cb * 512:(cb + 1) * 512],
                                    e["w3qn"][cb * 512:(cb + 1) * 512,
                                              hc * 128:(hc + 1) * 128]),
                                    "SH", [b_w3qn], [b_w3qc[pa]])
                        else:
                            k.post(sy.dma_start(out=w3qc[pa][:], in_=e["w3qt"][hc, :, :]),
                                   "SH", [b_w3qt], [b_w3qc[pa]])

                    if sp == 0:
                        @blk.gpsimd
                        def _(g, hc=hc, pa=pa):
                            k.pre(g, "SG", reads=[b_w3qc[pa]], writes=[b_w3qt])
                            k.post(g.dma_start(out=e["w3qt"][hc, :, :], in_=w3qc[pa][:]),
                                   "SG", [b_w3qc[pa]], [b_w3qt])

                    @blk.tensor
                    def _(t, hc=hc, pa=pa, slots=slots):
                        k.pre(t, "T", reads=[b_hqtb[pa], b_w3qc[pa]], writes=list(b_psZ))
                        ins = None
                        for si, s_ in enumerate(slots):
                            xs = hqtb[pa][:, s_ * 128:(s_ + 1) * 128]
                            t.ldweights(xs)
                            for cb in range(4):
                                ins = t.matmul(psZ[si][:, cb * 512:(cb + 1) * 512], xs,
                                               w3qc[pa][:, cb * 512:(cb + 1) * 512],
                                               start=(hc == 0), stop=(hc == 43),
                                               skip_group_check=True)
                        k.post(ins, "T", [b_hqtb[pa], b_w3qc[pa]], list(b_psZ))

                for si, s_ in enumerate(slots):
                    @blk.sync
                    def _(sy, s_=s_):
                        k.pre(sy, "SH", reads=[b_x1d], writes=[b_x1b])
                        k.post(sy.dma_start(out=x1b[:], in_=e["x1_d"][s_, :, :]),
                               "SH", [b_x1d], [b_x1b])

                    @blk.vector
                    def _(v, si=si, s_=s_):
                        k.pre(v, "V", reads=[b_psZ[si], b_ms3, b_x1b], writes=[b_v4, b_cc4])
                        k.post(v.scalar_tensor_tensor(
                            vsl[:], psZ[si][:], msc3[:, s_:s_ + 1],
                            x1b[:], ALU.mult, ALU.add,
                            accum_out=ccols[:, 0:1]),
                            "V", [b_psZ[si], b_ms3, b_x1b], [b_v4, b_cc4])

                    ln_epilogue(k, blk, e, vsl, b_v4, sq2, b_sq2, ccols, b_cc4,
                                outs[:], b_os, D)

                    @blk.scalar
                    def _(s, s_=s_):
                        k.pre(s, "SS", reads=[b_os], writes=[b_out])
                        k.post(s.dma_start(out=e["out"][s_, :, :], in_=outs[:]),
                               "SS", [b_os], [b_out])


def quant_scale_chain(k, blk, e, ccols, b_cc, mscA, b_msA, mscB, b_msB, s_, dim,
                      b_mA, b_mB, mA, mB):
    @blk.vector
    def _(v):
        k.pre(v, "V", reads=[b_cc], writes=[b_cc])
        k.post(v.tensor_scalar(ccols[:, 2:3], ccols[:, 1:2], 1.0 / dim, 1e-6,
                               ALU.mult, ALU.add), "V", [b_cc], [b_cc])
        k.pre(v, "V", reads=[b_cc], writes=[b_cc])
        k.post(v.reciprocal(ccols[:, 3:4], ccols[:, 2:3]), "V", [b_cc], [b_cc])

    @blk.scalar
    def _(s):
        k.pre(s, "A", reads=[b_cc], writes=[b_cc])
        k.post(s.activation(ccols[:, 4:5], ccols[:, 3:4], AF.Sqrt), "A", [b_cc], [b_cc])

    @blk.vector
    def _(v, s_=s_):
        k.pre(v, "V", reads=[b_cc], writes=[b_cc])
        k.post(v.scalar_tensor_tensor(ccols[:, 5:6], ccols[:, 0:1], 1.0, ccols[:, 4:5],
                                      ALU.mult, ALU.mult), "V", [b_cc], [b_cc])
        k.pre(v, "V", reads=[b_cc], writes=[b_cc])
        k.post(v.tensor_scalar(ccols[:, 5:6], ccols[:, 5:6], 1e-5, None, ALU.max),
               "V", [b_cc], [b_cc])
        k.pre(v, "V", reads=[b_cc], writes=[b_cc])
        k.post(v.reciprocal(ccols[:, 6:7], ccols[:, 5:6]), "V", [b_cc], [b_cc])
        k.pre(v, "V", reads=[b_cc], writes=[b_cc])
        k.post(v.scalar_tensor_tensor(ccols[:, 7:8], ccols[:, 6:7], 127.0, ccols[:, 4:5],
                                      ALU.mult, ALU.mult), "V", [b_cc], [b_cc])
        k.pre(v, "V", reads=[b_cc], writes=[b_cc])
        k.post(v.tensor_scalar(ccols[:, 6:7], ccols[:, 5:6], 1.0 / 127.0, None, ALU.mult),
               "V", [b_cc], [b_cc])
        k.pre(v, "V", reads=[b_cc, b_mA], writes=[b_msA])
        k.post(v.scalar_tensor_tensor(mscA[:, s_:s_ + 1], ccols[:, 6:7], 1.0, mA[:],
                                      ALU.mult, ALU.mult), "V", [b_cc, b_mA], [b_msA])
        if mscB is not None:
            k.pre(v, "V", reads=[b_cc, b_mB], writes=[b_msB])
            k.post(v.scalar_tensor_tensor(mscB[:, s_:s_ + 1], ccols[:, 6:7], 1.0, mB[:],
                                          ALU.mult, ALU.mult), "V", [b_cc, b_mB], [b_msB])


def make_inputs(core, x, ws):
    tiles = [8 * s + core for s in range(NSLOT)]
    xo = np.stack([x[t * 128:(t + 1) * 128, :] for t in tiles])
    qpos_c = np.stack([np.arange(t * 128, (t + 1) * 128, dtype=np.float32) for t in tiles],
                      axis=1)
    qrel_c = (np.arange(128, dtype=np.float32) + core * 128)[:, None].astype(np.float32)
    return dict(xf=x, xo=xo, qpos_c=qpos_c, qrel_c=qrel_c,
                ident=np.eye(128, dtype=bfdt), **ws)



_NC_CACHE = {}


def _get_nc():
    if "nc" not in _NC_CACHE:
        _NC_CACHE["nc"] = build(dev=False)
    return _NC_CACHE["nc"]


def kernel(L0_attn_x, L0_attn_wq, L0_attn_w_down, L0_attn_w_up, L0_attn_wo,
           L0_ffn_w1, L0_ffn_w2, L0_ffn_w3):
    x = np.ascontiguousarray(np.asarray(L0_attn_x, dtype=np.float32).reshape(T, D))
    ws = dict(
        wq=np.ascontiguousarray(np.asarray(L0_attn_wq, dtype=np.float32)),
        wdn=np.ascontiguousarray(np.asarray(L0_attn_w_down, dtype=np.float32)),
        wup=np.ascontiguousarray(np.asarray(L0_attn_w_up, dtype=np.float32)),
        wo_=np.ascontiguousarray(np.asarray(L0_attn_wo, dtype=np.float32)),
        w1=np.ascontiguousarray(np.asarray(L0_ffn_w1, dtype=np.float32)),
        w2=np.ascontiguousarray(np.asarray(L0_ffn_w2, dtype=np.float32)),
        w3=np.ascontiguousarray(np.asarray(L0_ffn_w3, dtype=np.float32)),
    )
    nc = _get_nc()
    in_maps = [make_inputs(c, x, ws) for c in range(8)]
    res = run_bass_kernel_spmd(nc, in_maps, core_ids=list(range(8)))
    out = np.empty((T, D), dtype=np.float32)
    for c in range(8):
        oc = np.asarray(res.results[c]["out"])
        for s in range(NSLOT):
            t = 8 * s + c
            out[t * 128:(t + 1) * 128, :] = oc[s]
    return out.reshape(1, T, D)


# version 10
# speedup vs baseline: 1.2164x; 1.0155x over previous
import sys
import contextlib
import math

import numpy as np
import ml_dtypes
import concourse.bass as bass
import concourse.mybir as mybir
from concourse.bass_utils import run_bass_kernel_spmd

F32 = mybir.dt.float32
BF16 = mybir.dt.bfloat16
I32 = mybir.dt.int32
AF = mybir.ActivationFunctionType
ALU = mybir.AluOpType
AX = mybir.AxisListType.X
bfdt = ml_dtypes.bfloat16

T = 8192
D = 2048
DLAT = 1024
NH = 16
DH = 128
HID = 5632
NSLOT = 8
SCALE = DH ** -0.5
TWO_PI = 2.0 * math.pi
C1 = 6.28125
C2 = TWO_PI - C1
INV_2PI = 1.0 / TWO_PI
LN1E4 = math.log(10000.0)

QUADS = [[7, 6, 1, 0], [5, 4, 3, 2]]
QCOL = {s: (qi, ci) for qi, qs in enumerate(QUADS) for ci, s in enumerate(qs)}

DMA_CH = ("SH", "SS", "SG")


class Buf:
    __slots__ = ("w", "rs", "name", "acc")

    def __init__(self, name=""):
        self.w = {}
        self.rs = []
        self.name = name
        self.acc = False


class FakeInstr:
    __slots__ = ("rec",)

    def __init__(self, rec):
        self.rec = rec

    def then_inc(self, sem, n):
        self.rec[3] = (sem, n)
        return self


class EngProxy:

    def __init__(self, q):
        self._q = q

    def __getattr__(self, name):
        def call(*args, **kwargs):
            rec = [name, args, kwargs, None]
            self._q.append(rec)
            return FakeInstr(rec)
        return call


class FakeBlock:
    def __init__(self):
        self.q = {"T": [], "V": [], "A": [], "G": [], "SY": []}
        self.prox = {k_: EngProxy(v) for k_, v in self.q.items()}

    def _dec(self, ch):
        def deco(fn):
            fn(self.prox[ch])
            return None
        return deco

    @property
    def tensor(self):
        return self._dec("T")

    @property
    def vector(self):
        return self._dec("V")

    @property
    def scalar(self):
        return self._dec("A")

    @property
    def gpsimd(self):
        return self._dec("G")

    @property
    def sync(self):
        return self._dec("SY")

    def flush(self, blk):
        for ch, eng_attr in [("T", "tensor"), ("V", "vector"), ("A", "scalar"),
                             ("G", "gpsimd"), ("SY", "sync")]:
            recs = self.q[ch]
            if not recs:
                continue

            def body(eng, recs=recs):
                for name, args, kwargs, inc in recs:
                    ins = getattr(eng, name)(*args, **kwargs)
                    if inc is not None:
                        ins.then_inc(*inc)
            getattr(blk, eng_attr)(body)


class B:
    def __init__(self, nc, es):
        self.nc = nc
        self.es = es
        self.sems = {}
        self.cnt = {}
        self.seen = {}

    def sem(self, name):
        if name not in self.sems:
            self.sems[name] = self.es.enter_context(self.nc.semaphore(name))
            self.cnt[name] = 0
        return self.sems[name]

    @staticmethod
    def ekey(ch):
        return {"SH": "SY", "SS": "A", "SG": "G"}.get(ch, ch)

    def pre(self, eng, ch, reads=(), writes=()):
        ek = self.ekey(ch)
        seen = self.seen.setdefault(ek, {})
        need = {}
        for b in reads:
            for c, v in b.w.items():
                need[c] = max(need.get(c, 0), v)
        for b in writes:
            if not b.acc:
                for c, v in b.w.items():
                    need[c] = max(need.get(c, 0), v)
            for c, v in b.rs:
                need[c] = max(need.get(c, 0), v)
        for c in sorted(need):
            v = need[c]
            if c == "T" and ch == "T":
                continue
            if seen.get(c, 0) >= v:
                continue
            eng.wait_ge(self.sem(c), v)
            seen[c] = v

    def post(self, instr, ch, reads=(), writes=()):
        if ch in DMA_CH:
            key = None
            for b in writes:
                if not b.acc:
                    key = b
                    break
            if key is None:
                for b in reads:
                    if not b.acc:
                        key = b
                        break
            sname = f"D_{ch}_{key.name if key is not None else 'misc'}"
            n = 16
        else:
            sname = ch
            n = 1
        s = self.sem(sname)
        self.cnt[sname] += n
        instr.then_inc(s, n)
        m = (sname, self.cnt[sname])
        for b in writes:
            if b.acc:
                b.w[m[0]] = m[1]
            else:
                b.w = {m[0]: m[1]}
                b.rs = []
        for b in reads:
            b.rs.append(m)
        return m

    def barrier(self, blk):
        chans = [c for c in self.cnt if self.cnt[c] > 0]

        def wait_all(eng, ek):
            seen = self.seen.setdefault(ek, {})
            for c in chans:
                v = self.cnt[c]
                if seen.get(c, 0) >= v:
                    continue
                eng.wait_ge(self.sem(c), v)
                seen[c] = v

        @blk.tensor
        def _(t):
            wait_all(t, "T")

        @blk.vector
        def _(v):
            wait_all(v, "V")

        @blk.scalar
        def _(s):
            wait_all(s, "A")

        @blk.gpsimd
        def _(g):
            wait_all(g, "G")

        @blk.sync
        def _(sy):
            wait_all(sy, "SY")

    def sb(self, name, shape, dtype):
        return self.es.enter_context(self.nc.sbuf_tensor(name, shape, dtype))

    def ps(self, name, shape, dtype):
        return self.es.enter_context(self.nc.psum_tensor(name, shape, dtype))


def build(dev=False, upto=99):
    nc = bass.Bass("TRN2", target_bir_lowering=False, debug=False)
    sk = "ExternalOutput" if dev else "Internal"

    io = {}
    io["xf"] = nc.dram_tensor("xf", [T, D], F32, kind="ExternalInput").ap()
    io["xo"] = nc.dram_tensor("xo", [NSLOT, 128, D], F32, kind="ExternalInput").ap()
    io["qpos_c"] = nc.dram_tensor("qpos_c", [128, NSLOT], F32, kind="ExternalInput").ap()
    io["qrel_c"] = nc.dram_tensor("qrel_c", [128, 1], F32, kind="ExternalInput").ap()
    io["wq"] = nc.dram_tensor("wq", [D, D], F32, kind="ExternalInput").ap()
    io["wdn"] = nc.dram_tensor("wdn", [D, DLAT], F32, kind="ExternalInput").ap()
    io["wup"] = nc.dram_tensor("wup", [DLAT, 2 * D], F32, kind="ExternalInput").ap()
    io["wo_"] = nc.dram_tensor("wo_", [D, D], F32, kind="ExternalInput").ap()
    io["w1"] = nc.dram_tensor("w1", [HID, D], F32, kind="ExternalInput").ap()
    io["w2"] = nc.dram_tensor("w2", [HID, D], F32, kind="ExternalInput").ap()
    io["w3"] = nc.dram_tensor("w3", [D, HID], F32, kind="ExternalInput").ap()
    io["ident"] = nc.dram_tensor("ident", [128, 128], BF16, kind="ExternalInput").ap()
    io["out"] = nc.dram_tensor("out", [NSLOT, 128, D], F32, kind="ExternalOutput").ap()
    io["ckv_d"] = nc.dram_tensor("ckv_d", [8, 128, T], BF16, kind=sk).ap()
    io["w1qn"] = nc.dram_tensor("w1qn", [HID, D], BF16, kind="Internal").ap()
    io["w2qn"] = nc.dram_tensor("w2qn", [HID, D], BF16, kind="Internal").ap()
    io["w3qn"] = nc.dram_tensor("w3qn", [D, HID], BF16, kind="Internal").ap()
    io["w3qt"] = nc.dram_tensor("w3qt", [44, 128, D], BF16, kind="Internal").ap()
    io["hq_d"] = nc.dram_tensor("hq_d", [NSLOT, 128, HID], F32, kind=sk).ap()
    io["hqt_d"] = nc.dram_tensor("hqt_d", [44, 128, NSLOT * 128], BF16, kind=sk).ap()
    io["x1_d"] = nc.dram_tensor("x1_d", [NSLOT, 128, D], F32, kind=sk).ap()
    for nm in ["dbg_q", "dbg_kf", "dbg_vt", "dbg_at"]:
        io[nm] = nc.dram_tensor(nm, [128, {"dbg_q": NSLOT * D, "dbg_kf": T, "dbg_vt": T,
                                           "dbg_at": NH * 1024}[nm]], BF16,
                                kind="ExternalOutput").ap() if dev else None
    io["dbg_x1"] = None

    with contextlib.ExitStack() as es:
        k = B(nc, es)
        e = dict(io)
        e["bufs"] = {}

        def bufs(*names):
            return [e["bufs"].setdefault(n, Buf(n)) for n in names]

        e["B"] = bufs
        for nm in ["ckv_d", "hq_d", "hqt_d", "x1_d", "w1qn", "w2qn", "w3qn", "w3qt",
                   "out_d"]:
            bufs(nm)[0].acc = True

        e["ident_s"] = k.sb("ident_s", [128, 128], BF16)
        e["qposc"] = k.sb("qposc", [128, NSLOT], F32)
        e["qrelc"] = k.sb("qrelc", [128, 1], F32)
        e["IVF"] = k.sb("IVF", [128, 64], F32)
        e["onescol"] = k.sb("onescol", [128, 1], BF16)
        e["ones128"] = k.sb("ones128", [128, 128], BF16)
        e["onescol_f"] = k.sb("onescol_f", [128, 1], F32)
        e["onesrow_f"] = k.sb("onesrow_f", [1, 128], F32)
        e["onesrow_b"] = k.sb("onesrow_b", [1, 128], BF16)
        e["identf"] = k.sb("identf", [128, 128], F32)
        e["thr"] = [k.sb(f"thrw{i}", [128, 1], F32) for i in range(3)]
        e["nthr"] = [k.sb(f"nthrw{i}", [128, 1], F32) for i in range(3)]
        e["mwb"] = [k.sb(f"mwb{i}", [128, 1], F32) for i in range(3)]

        with nc.Block() as real_blk:
            blk = FakeBlock()
            with contextlib.ExitStack() as es_t:
                sbt = lambda n, s, d: es_t.enter_context(nc.sbuf_tensor(n, s, d))
                e["CC"] = sbt("CC", [128, 64 * 128], BF16)
                e["SS"] = sbt("SS_t", [128, 64 * 128], BF16)
                e["bmq"] = sbt("bmq", [128, 1024], BF16)
                e["bmk"] = sbt("bmk", [128, 1024], BF16)
                e["qrot"] = sbt("qrot", [128, NSLOT * D], BF16)
                p0(k, blk, e)
                k.barrier(blk)
                if upto >= 1:
                    p0w(k, blk, e)
                    k.barrier(blk)
                if upto >= 2:
                    p1(k, blk, e)
                    k.barrier(blk)
                e["attn"] = sbt("attn", [128, NH * 1024], BF16)
                if upto >= 3:
                    p2(k, blk, e)
                    k.barrier(blk)
                if upto >= 4:
                    p3(k, blk, e)
                    k.barrier(blk)
            if upto >= 5:
                p4(k, blk, e)

            @blk.scalar
            def _(s):
                s.wait_ge(k.sem("SS"), k.cnt["SS"])

            @blk.sync
            def _(sy):
                sy.wait_ge(k.sem("SH"), k.cnt["SH"])

            @blk.gpsimd
            def _(g):
                g.wait_ge(k.sem("SG"), k.cnt["SG"])

            blk.flush(real_blk)
    return nc


def sincos(k, blk, e, ang, shape, nbuf, t1, t2, outbuf, shift, extra_reads=()):
    (b_ang, b_n, b_t1, b_t2, b_out) = e["B"](*[f"sc_{id(x)}" for x in (ang, nbuf, t1, t2, outbuf)])

    @blk.vector
    def _(v):
        k.pre(v, "V", reads=[b_ang, *extra_reads], writes=[b_t1])
        if shift != 0.0:
            k.post(v.tensor_scalar(t1[:], ang, shift, None, ALU.add), "V", [b_ang], [b_t1])
        else:
            k.post(v.tensor_copy(t1[:], ang), "V", [b_ang], [b_t1])

    @blk.scalar
    def _(s):
        k.pre(s, "A", reads=[b_t1], writes=[b_n])
        k.post(s.activation(nbuf[:], t1[:], AF.Identity, scale=INV_2PI), "A", [b_t1], [b_n])

    @blk.vector
    def _(v):
        k.pre(v, "V", reads=[b_n, b_t1], writes=[b_t2])
        k.post(v.scalar_tensor_tensor(t2[:], nbuf[:], -C1, t1[:], ALU.mult, ALU.add),
               "V", [b_n, b_t1], [b_t2])
        k.pre(v, "V", reads=[b_n, b_t2], writes=[b_t1])
        k.post(v.scalar_tensor_tensor(t1[:], nbuf[:], -C2, t2[:], ALU.mult, ALU.add),
               "V", [b_n, b_t2], [b_t1])
        k.pre(v, "V", reads=[b_t1], writes=[b_t2])
        k.post(v.tensor_scalar(t2[:], t1[:], math.pi, TWO_PI, ALU.is_gt, ALU.mult),
               "V", [b_t1], [b_t2])
        k.pre(v, "V", reads=[b_t1, b_t2], writes=[b_t1])
        k.post(v.tensor_tensor(t1[:], t1[:], t2[:], ALU.subtract), "V", [b_t1, b_t2], [b_t1])
        k.pre(v, "V", reads=[b_t1], writes=[b_t2])
        k.post(v.tensor_scalar(t2[:], t1[:], -math.pi, TWO_PI, ALU.is_lt, ALU.mult),
               "V", [b_t1], [b_t2])
        k.pre(v, "V", reads=[b_t1, b_t2], writes=[b_t1])
        k.post(v.tensor_tensor(t1[:], t1[:], t2[:], ALU.add), "V", [b_t1, b_t2], [b_t1])

    @blk.scalar
    def _(s):
        k.pre(s, "A", reads=[b_t1], writes=[b_out])
        k.post(s.activation(outbuf[:], t1[:], AF.Sin), "A", [b_t1], [b_out])
    return b_out


def p0(k, blk, e):
    nc = k.nc
    B = e["B"]
    es0 = contextlib.ExitStack()
    sb = lambda n, s, d: es0.enter_context(nc.sbuf_tensor(n, s, d))
    POS = sb("p0_pos", [128, 64], F32)
    J = sb("p0_j", [128, 64], F32)
    ANG = sb("p0_ang", [128, 4096], F32)
    NN = sb("p0_n", [128, 4096], I32)
    S1 = sb("p0_s1", [128, 4096], F32)
    S2 = sb("p0_s2", [128, 4096], F32)
    SINX = sb("p0_sin", [128, 4096], F32)
    COSX = sb("p0_cos", [128, 4096], F32)
    KJ = sb("p0_kj", [128, 1024], F32)
    e["_es0"] = es0
    (b_id, b_qp, b_qr, b_pos, b_j, b_kj, b_ivf, b_ang, b_cc, b_ss, b_bmq, b_ones) = B(
        "ident", "qposc", "qrelc", "pos", "j", "kj", "ivf", "ang", "CC", "SS", "bmq", "ones")

    @blk.sync
    def _(sy):
        k.post(sy.dma_start(out=e["ident_s"][:], in_=e["ident"][:, :]), "SH", [], [b_id])
        k.post(sy.dma_start(out=e["qposc"][:], in_=e["qpos_c"][:, :]), "SH", [], [b_qp])
        k.post(sy.dma_start(out=e["qrelc"][:], in_=e["qrel_c"][:, :]), "SH", [], [b_qr])

    @blk.gpsimd
    def _(g):
        k.post(g.iota(POS[:], pattern=[[128, 64]], base=0, channel_multiplier=1,
                      allow_small_or_imprecise_dtypes=True), "G", [], [b_pos])
        k.post(g.iota(J[:], pattern=[[1, 64]], base=0, channel_multiplier=0,
                      allow_small_or_imprecise_dtypes=True), "G", [], [b_j])
        k.post(g.iota(KJ[:], pattern=[[1, 1024]], base=0, channel_multiplier=0,
                      allow_small_or_imprecise_dtypes=True), "G", [], [b_kj])

    @blk.scalar
    def _(s):
        k.pre(s, "A", reads=[b_j], writes=[b_ivf])
        k.post(s.activation(e["IVF"][:], J[:], AF.Exp, scale=-LN1E4 / 64.0), "A", [b_j], [b_ivf])

    @blk.vector
    def _(v):
        k.post(v.memset(e["onescol"][:], 1.0), "V", [], [b_ones])
        k.post(v.memset(e["ones128"][:], 1.0), "V", [], [e["B"]("ones128")[0]])
        k.post(v.memset(e["onescol_f"][:], 1.0), "V", [], [e["B"]("onescol_f")[0]])
        k.post(v.memset(e["onesrow_f"][:], 1.0), "V", [], [e["B"]("onesrow_f")[0]])
        k.post(v.memset(e["onesrow_b"][:], 1.0), "V", [], [e["B"]("onesrow_b")[0]])
        k.pre(v, "V", reads=[b_id], writes=[e["B"]("identf")[0]])
        k.post(v.tensor_copy(e["identf"][:], e["ident_s"][:]), "V", [b_id],
               [e["B"]("identf")[0]])
        k.pre(v, "V", reads=[b_pos, b_ivf], writes=[b_ang])
        k.post(v.tensor_tensor(
            ANG[:],
            POS[:].rearrange("p (kt one) -> p kt one", one=1).broadcast_to([128, 64, 64]),
            e["IVF"][:].rearrange("p (one j) -> p one j", one=1).broadcast_to([128, 64, 64]),
            ALU.mult), "V", [b_pos, b_ivf], [b_ang])

    b_sin = sincos(k, blk, e, ANG[:], None, NN, S1, S2, SINX, 0.0, extra_reads=[b_ang])
    b_cos = sincos(k, blk, e, ANG[:], None, NN, S1, S2, COSX, math.pi / 2.0, extra_reads=[b_ang])

    @blk.vector
    def _(v):
        cview = COSX[:].rearrange("p (kt j) -> p kt j", j=64)
        sview = SINX[:].rearrange("p (kt j) -> p kt j", j=64)
        cc = e["CC"][:].rearrange("p (kt j) -> p kt j", j=128)
        ss = e["SS"][:].rearrange("p (kt j) -> p kt j", j=128)
        k.pre(v, "V", reads=[b_sin, b_cos], writes=[b_cc, b_ss])
        k.post(v.tensor_copy(cc[:, :, 0:128:2], cview), "V", [b_cos], [b_cc])
        k.post(v.tensor_copy(cc[:, :, 1:128:2], cview), "V", [b_cos], [b_cc])
        k.post(v.tensor_copy(ss[:, :, 1:128:2], sview), "V", [b_sin], [b_ss])
        k.post(v.tensor_scalar(ss[:, :, 0:128:2], sview, -1.0, None, ALU.mult),
               "V", [b_sin], [b_ss])
        k.pre(v, "V", reads=[b_kj, b_qr], writes=[b_bmq])
        k.post(v.tensor_scalar(e["bmq"][:], KJ[:], e["qrelc"][:, 0:1], None, ALU.is_le),
               "V", [b_kj, b_qr], [b_bmq])
    ICOL = sb("p0_icol", [128, 1], F32)
    TC0 = sb("p0_tc0", [128, 1], F32)
    TCJ = sb("p0_tcj", [128, 8], F32)
    b_icol, b_tc0, b_tcj, b_bmk = B("p0_icol", "p0_tc0", "p0_tcj", "bmk")

    @blk.gpsimd
    def _(g):
        k.post(g.iota(ICOL[:], pattern=[[1, 1]], base=0, channel_multiplier=1,
                      allow_small_or_imprecise_dtypes=True), "G", [], [b_icol])

    @blk.vector
    def _(v):
        k.pre(v, "V", reads=[b_icol, b_qr], writes=[b_tc0])
        k.post(v.scalar_tensor_tensor(TC0[:], ICOL[:], 2.0, e["qrelc"][:], ALU.mult,
                                      ALU.subtract), "V", [b_icol, b_qr], [b_tc0])
        for jm in range(8):
            k.pre(v, "V", reads=[b_tc0], writes=[b_tcj])
            k.post(v.tensor_scalar(TCJ[:, jm:jm + 1], TC0[:], float(jm * 128), None,
                                   ALU.add), "V", [b_tc0], [b_tcj])
        for jm in range(8):
            k.pre(v, "V", reads=[b_tcj, b_kj], writes=[b_bmk])
            k.post(v.tensor_scalar(e["bmk"][:, jm * 128:(jm + 1) * 128], KJ[:, 0:128],
                                   TCJ[:, jm:jm + 1], None, ALU.is_ge),
                   "V", [b_tcj, b_kj], [b_bmk])
    e.pop("_es0").close()


def p0w(k, blk, e):
    nc = k.nc
    B = e["B"]
    CH = 4096
    nchunk = HID * D // (128 * CH)
    wl = [(e["w1"], None), (e["w2"], None), (e["w3"], None)]

    esw = contextlib.ExitStack()
    e["_esw"] = esw
    sbw = lambda n, s, d: esw.enter_context(nc.sbuf_tensor(n, s, d))
    wbuf = [sbw(f"p0w_buf{i}", [128, CH], F32) for i in range(2)]
    abscr = [sbw(f"p0w_abs{i}", [128, CH], F32) for i in range(2)]
    acc = sbw("p0w_acc", [128, 3 * nchunk + 3], F32)
    accr = sbw("p0w_accr", [128, nchunk], F32)
    csb = sbw("p0w_csb", [128, 4], F32)
    tot = sbw("p0w_tot", [1, 16], F32)
    ps_tot = esw.enter_context(nc.psum_tensor("p0w_pstot", [1, 128], F32))

    b_wb = B("p0w_wb0", "p0w_wb1")
    b_scr = B("p0w_scr0", "p0w_scr1")
    (b_acc, b_csb, b_tot, b_pst, b_ones) = B("p0w_acc", "p0w_csb", "p0w_tot", "p0w_pst", "ones")

    for wi, (wsrc, wdst) in enumerate(wl):
        wflat = wsrc.rearrange("a b -> (a b)").rearrange("(r p c) -> r p c", p=128, c=CH)
        for ci in range(nchunk):
            pa = ci % 2

            @blk.sync
            def _(sy, pa=pa, wflat=wflat, ci=ci):
                k.pre(sy, "SH", writes=[b_wb[pa]])
                k.post(sy.dma_start(out=wbuf[pa][:], in_=wflat[ci, :, :]), "SH", [], [b_wb[pa]])

            @blk.scalar
            def _(s, pa=pa, wi=wi, ci=ci):
                k.pre(s, "A", reads=[b_wb[pa]], writes=[b_scr[pa], b_acc])
                k.post(s.activation(abscr[pa][:], wbuf[pa][:], AF.Abs,
                                    accum_out=acc[:, wi * nchunk + ci: wi * nchunk + ci + 1]),
                       "A", [b_wb[pa]], [b_scr[pa], b_acc])


    CL2 = 64.0
    CL3 = CL2 * nchunk
    b_onesf = B("onescol_f")[0]

    @blk.vector
    def _(v):
        k.pre(v, "V", reads=[b_acc], writes=[b_acc, b_csb])
        for wi in range(3):
            k.post(v.tensor_scalar(accr[:],
                                   acc[:, wi * nchunk:(wi + 1) * nchunk], -CL2, None,
                                   ALU.add), "V", [b_acc], [b_acc])
            v.drain()
            k.post(v.reduce_sum(acc[:, 3 * nchunk + wi: 3 * nchunk + wi + 1],
                                accr[:], axis=AX),
                   "V", [b_acc], [b_acc])
            v.drain()
        k.pre(v, "V", reads=[b_acc], writes=[b_csb])
        k.post(v.tensor_copy(csb[:, 0:3], acc[:, 3 * nchunk:3 * nchunk + 3]),
               "V", [b_acc], [b_csb])

    @blk.tensor
    def _(t):
        k.pre(t, "T", reads=[b_csb, b_onesf], writes=[b_pst])
        for wi in range(3):
            k.post(t.matmul(ps_tot[0:1, wi * 8:wi * 8 + 1], csb[:, wi:wi + 1],
                            e["onescol_f"][:], start=True, stop=True, skip_group_check=True),
                   "T", [b_csb, b_onesf], [b_pst])

    @blk.vector
    def _(v):
        k.pre(v, "V", reads=[b_pst], writes=[b_tot])
        for wi in range(3):
            k.post(v.tensor_scalar(tot[0:1, wi:wi + 1], ps_tot[0:1, wi * 8:wi * 8 + 1],
                                   float(128.0 * CL3), None, ALU.add),
                   "V", [b_pst], [b_tot])

    b_mwb = B("mwb0", "mwb1", "mwb2")
    b_thr = B("thr0", "thr1", "thr2")
    b_nth = B("nthr0", "nthr1", "nthr2")
    b_orf = B("onesrow_f")[0]
    b_psbc = B("p0w_psbc")[0]
    ps_bc = esw.enter_context(nc.psum_tensor("p0w_psbc", [128, 4], F32))

    @blk.tensor
    def _(t):
        k.pre(t, "T", reads=[b_tot, b_orf], writes=[b_psbc])
        for wi in range(3):
            k.post(t.matmul(ps_bc[:, wi:wi + 1], e["onesrow_f"][0:1, :],
                            tot[0:1, wi:wi + 1], start=(wi == 0), stop=(wi == 2),
                            skip_group_check=True), "T", [b_tot, b_orf], [b_psbc])

    @blk.scalar
    def _(s):
        k.pre(s, "A", reads=[b_psbc], writes=list(b_mwb))
        for wi in range(3):
            k.post(s.copy(e["mwb"][wi][:], ps_bc[:, wi:wi + 1]), "A", [b_psbc], [b_mwb[wi]])

    CH2 = 512
    nch2 = HID * D // (128 * CH2)
    e["p0w2_sched"] = []
    for wi, (wsrc, qdst) in enumerate([(e["w1"], e["w1qn"]), (e["w2"], e["w2qn"]),
                                       (e["w3"], e["w3qn"])]):
        wf = wsrc.rearrange("a b -> (a b)").rearrange("(r p c) -> r p c", p=128, c=CH2)
        qf = qdst.rearrange("a b -> (a b)").rearrange("(r p c) -> r p c", p=128, c=CH2)
        for ci in range(nch2):
            e["p0w2_sched"].append((wf, qf, ci, wi))

    @blk.vector
    def _(v):
        numel = float(HID * D)
        for wi in range(3):
            k.pre(v, "V", reads=[b_mwb[wi]], writes=[b_mwb[wi]])
            k.post(v.tensor_scalar(e["mwb"][wi][:], e["mwb"][wi][:], 1.0 / numel, 1e-5,
                                   ALU.mult, ALU.max), "V", [b_mwb[wi]], [b_mwb[wi]])
            k.pre(v, "V", reads=[b_mwb[wi]], writes=[b_thr[wi], b_nth[wi]])
            k.post(v.tensor_scalar(e["thr"][wi][:], e["mwb"][wi][:], 0.5, None, ALU.mult),
                   "V", [b_mwb[wi]], [b_thr[wi]])
            k.post(v.tensor_scalar(e["nthr"][wi][:], e["mwb"][wi][:], -0.5, None, ALU.mult),
                   "V", [b_mwb[wi]], [b_nth[wi]])
    e.pop("_esw").close()


def p0w2_emit(k, blk, e, frac_lo, frac_hi, bufs2):
    B = e["B"]
    sched = e["p0w2_sched"]
    n = len(sched)
    lo = int(round(frac_lo * n))
    hi = int(round(frac_hi * n))
    wc2, pos2, q2 = bufs2
    b_wc2 = B("p2_wc2_0", "p2_wc2_1")
    b_pos2 = B("p2_pos2")[0]
    b_q2 = B("p2_q2_0", "p2_q2_1")
    b_thr = B("thr0", "thr1", "thr2")
    b_nth = B("nthr0", "nthr1", "nthr2")
    b_qn = B("w1qn", "w2qn", "w3qn")
    for idx in range(lo, hi):
        wf, qf, ci, wi = sched[idx]
        pa = idx % 2

        @blk.sync
        def _(sy, wf=wf, ci=ci, pa=pa):
            k.pre(sy, "SH", writes=[b_wc2[pa]])
            k.post(sy.dma_start(out=wc2[pa][:], in_=wf[ci, :, :]), "SH", [], [b_wc2[pa]])

        @blk.vector
        def _(v, pa=pa, wi=wi):
            k.pre(v, "V", reads=[b_wc2[pa], b_thr[wi]], writes=[b_pos2])
            k.post(v.tensor_scalar(pos2[:], wc2[pa][:], e["thr"][wi][:, 0:1], None,
                                   ALU.is_ge), "V", [b_wc2[pa], b_thr[wi]], [b_pos2])
            k.pre(v, "V", reads=[b_wc2[pa], b_pos2, b_nth[wi]], writes=[b_q2[pa]])
            k.post(v.scalar_tensor_tensor(q2[pa][:], wc2[pa][:], e["nthr"][wi][:, 0:1],
                                          pos2[:], ALU.is_le, ALU.subtract),
                   "V", [b_wc2[pa], b_pos2, b_nth[wi]], [b_q2[pa]])

        @blk.scalar
        def _(s, qf=qf, ci=ci, pa=pa, wi=wi):
            k.pre(s, "SS", reads=[b_q2[pa]], writes=[b_qn[wi]])
            k.post(s.dma_start(out=qf[ci, :, :], in_=q2[pa][:]), "SS", [b_q2[pa]], [b_qn[wi]])


def p1(k, blk, e):
    nc = k.nc
    B = e["B"]
    b_qrot = B("qrot")[0]
    b_id = B("ident")[0]
    b_ckvd = B("ckv_d")[0]

    with contextlib.ExitStack() as es1:
        sb = lambda n, s, d: es1.enter_context(nc.sbuf_tensor(n, s, d))
        psa = lambda n, s, d: es1.enter_context(nc.psum_tensor(n, s, d))
        es_a = contextlib.ExitStack()
        sba = lambda n, s, d: es_a.enter_context(nc.sbuf_tensor(n, s, d))
        wdn_sb = sba("p1_wdn", [128, 16 * 1024], BF16)
        xb = [sba(f"p1_xb{i}", [128, D], BF16) for i in range(2)]
        xT = sba("p1_xT", [128, 16 * 1024], BF16)
        ckv_ev = [sba(f"p1_ckve{i}", [128, 1024], BF16) for i in range(2)]
        es_ps1 = contextlib.ExitStack()
        ptr = [es_ps1.enter_context(nc.psum_tensor(f"p1_ptr{i}", [128, 128], BF16))
               for i in range(4)]
        ps_ckv = [es_ps1.enter_context(nc.psum_tensor(f"p1_psckv{i}", [128, 1024], F32))
                  for i in range(2)]

        b_wdn = B("p1_wdn")[0]
        b_xb = B("p1_xb0", "p1_xb1")
        b_xT = B("p1_xT")[0]
        b_ckve = B("p1_ckve0", "p1_ckve1")
        b_ptr = B(*[f"p1_ptr{i}" for i in range(4)])
        b_psckv = B("p1_psckv0", "p1_psckv1")

        @blk.gpsimd
        def _(g):
            k.pre(g, "SG", writes=[b_wdn])
            k.post(g.dma_start(out=wdn_sb[:].rearrange("p (kd l) -> p kd l", kd=16),
                               in_=e["wdn"].rearrange("(kd p) l -> p kd l", p=128)),
                   "SG", [], [b_wdn])

        for bp in range(8):
            for sub in range(8):
                tt_i = bp * 8 + sub
                pa = sub % 2

                @blk.gpsimd
                def _(g, pa=pa, tt_i=tt_i):
                    k.pre(g, "SG", writes=[b_xb[pa]])
                    k.post(g.dma_start(out=xb[pa][:],
                                       in_=e["xf"][tt_i * 128:(tt_i + 1) * 128, :]),
                           "SG", [], [b_xb[pa]])

                for qb in range(4):
                    @blk.tensor
                    def _(t, pa=pa, qb=qb):
                        k.pre(t, "T", reads=[b_xb[pa], b_id], writes=b_ptr)
                        for i in range(4):
                            kd = qb * 4 + i
                            k.post(t.transpose(ptr[i][:], xb[pa][:, kd * 128:(kd + 1) * 128],
                                               e["ident_s"][:]),
                                   "T", [b_xb[pa], b_id], [b_ptr[i]])

                    @blk.vector
                    def _(v, sub=sub, qb=qb):
                        k.pre(v, "V", reads=b_ptr, writes=[b_xT])
                        ins = None
                        for i in range(4):
                            kd = qb * 4 + i
                            ins = v.tensor_copy(
                                xT[:, kd * 1024 + sub * 128: kd * 1024 + (sub + 1) * 128],
                                ptr[i][:])
                        k.post(ins, "V", b_ptr, [b_xT])

            for l in range(8):
                pa = l % 2

                @blk.tensor
                def _(t, l=l, pa=pa):
                    k.pre(t, "T", reads=[b_xT, b_wdn], writes=[b_psckv[pa]])
                    for kd in range(16):
                        ws = wdn_sb[:, kd * 1024 + l * 128: kd * 1024 + (l + 1) * 128]
                        t.ldweights(ws)
                        ins = None
                        for b2 in range(2):
                            ins = t.matmul(
                                ps_ckv[pa][:, b2 * 512:(b2 + 1) * 512],
                                ws, xT[:, kd * 1024 + b2 * 512: kd * 1024 + (b2 + 1) * 512],
                                start=(kd == 0), stop=(kd == 15), skip_group_check=True)
                    k.post(ins, "T", [b_xT, b_wdn], [b_psckv[pa]])

                @blk.vector
                def _(v, l=l, pa=pa):
                    k.pre(v, "V", reads=[b_psckv[pa]], writes=[b_ckve[pa]])
                    k.post(v.tensor_copy(ckv_ev[pa][:], ps_ckv[pa][:]),
                           "V", [b_psckv[pa]], [b_ckve[pa]])

                @blk.scalar
                def _(s, l=l, pa=pa, bp=bp):
                    k.pre(s, "SS", reads=[b_ckve[pa]], writes=[b_ckvd])
                    k.post(s.dma_start(out=e["ckv_d"][l, :, bp * 1024:(bp + 1) * 1024],
                                       in_=ckv_ev[pa][:]),
                           "SS", [b_ckve[pa]], [b_ckvd])

        es_a.close()
        k.barrier(blk)
        xoT = sb("p1_xoT", [128, 16 * NSLOT * 128], BF16)
        xob = [sb(f"p1_xob{i}", [128, D], F32) for i in range(2)]
        xobb = sb("p1_xobb", [128, D], BF16)
        b_xoT = B("p1_xoT")[0]
        b_xob = B("p1_xob0", "p1_xob1")
        b_xobb = B("p1_xobb")[0]

        for s_ in range(NSLOT):
            pa = s_ % 2

            @blk.sync
            def _(sy, pa=pa, s_=s_):
                k.pre(sy, "SH", writes=[b_xob[pa]])
                k.post(sy.dma_start(out=xob[pa][:], in_=e["xo"][s_, :, :]),
                       "SH", [], [b_xob[pa]])

            @blk.vector
            def _(v, pa=pa):
                k.pre(v, "V", reads=[b_xob[pa]], writes=[b_xobb])
                k.post(v.tensor_copy(xobb[:], xob[pa][:]), "V", [b_xob[pa]], [b_xobb])

            for qb in range(4):
                @blk.tensor
                def _(t, qb=qb):
                    k.pre(t, "T", reads=[b_xobb, b_id], writes=b_ptr)
                    for i in range(4):
                        kd = qb * 4 + i
                        k.post(t.transpose(ptr[i][:], xobb[:, kd * 128:(kd + 1) * 128],
                                           e["ident_s"][:]), "T", [b_xobb, b_id], [b_ptr[i]])

                @blk.vector
                def _(v, s_=s_, qb=qb):
                    k.pre(v, "V", reads=b_ptr, writes=[b_xoT])
                    ins = None
                    for i in range(4):
                        kd = qb * 4 + i
                        ins = v.tensor_copy(
                            xoT[:, kd * 1024 + s_ * 128: kd * 1024 + s_ * 128 + 128], ptr[i][:])
                    k.post(ins, "V", b_ptr, [b_xoT])

        ccq = [sb(f"p1_ccq{i}", [128, 128], BF16) for i in range(NSLOT)]
        ssq = [sb(f"p1_ssq{i}", [128, 128], BF16) for i in range(NSLOT)]
        angq = sb("p1_angq", [128, 64], F32)
        nq = sb("p1_nq", [128, 64], I32)
        tq1 = sb("p1_tq1", [128, 64], F32)
        tq2 = sb("p1_tq2", [128, 64], F32)
        sinq = sb("p1_sinq", [128, 64], F32)
        cosq = sb("p1_cosq", [128, 64], F32)
        b_ccq = B(*[f"p1_ccq{i}" for i in range(NSLOT)])
        b_ssq = B(*[f"p1_ssq{i}" for i in range(NSLOT)])
        b_angq, b_ivf, b_qp = B("p1_angq", "ivf", "qposc")

        for s_ in range(NSLOT):
            @blk.vector
            def _(v, s_=s_):
                k.pre(v, "V", reads=[b_ivf, b_qp], writes=[b_angq])
                k.post(v.tensor_scalar(angq[:], e["IVF"][:], e["qposc"][:, s_:s_ + 1],
                                       None, ALU.mult), "V", [b_ivf, b_qp], [b_angq])
            b_sq = sincos(k, blk, e, angq[:], None, nq, tq1, tq2, sinq, 0.0, [b_angq])
            b_cq = sincos(k, blk, e, angq[:], None, nq, tq1, tq2, cosq, math.pi / 2.0, [b_angq])

            @blk.vector
            def _(v, s_=s_):
                k.pre(v, "V", reads=[b_sq, b_cq], writes=[b_ccq[s_], b_ssq[s_]])
                k.post(v.tensor_copy(ccq[s_][:, 0:128:2], cosq[:]), "V", [b_cq], [b_ccq[s_]])
                k.post(v.tensor_copy(ccq[s_][:, 1:128:2], cosq[:]), "V", [b_cq], [b_ccq[s_]])
                k.post(v.tensor_copy(ssq[s_][:, 1:128:2], sinq[:]), "V", [b_sq], [b_ssq[s_]])
                k.post(v.tensor_scalar(ssq[s_][:, 0:128:2], sinq[:], -1.0, None, ALU.mult),
                       "V", [b_sq], [b_ssq[s_]])

        es_ps1.close()
        k.barrier(blk)
        wqb = [sb(f"p1_wqb{i}", [128, 1024], BF16) for i in range(2)]
        ps_q = [psa(f"p1_psq{i}", [128, 1024], F32) for i in range(4)]
        rt1 = sb("p1_rt1", [128, 1024], BF16)
        rt2 = sb("p1_rt2", [128, 1024], BF16)
        b_wqb = B("p1_wqb0", "p1_wqb1")
        b_psq = B(*[f"p1_psq{i}" for i in range(4)])
        b_rt1, b_rt2 = B("p1_rt1", "p1_rt2")
        wq_r = e["wq"].rearrange("(kd p) n -> kd p n", p=128)

        for sq in range(2):
            slots = [4 * sq + i for i in range(4)]
            for hf in range(2):
                for kd in range(16):
                    pa = kd % 2

                    @blk.gpsimd
                    def _(g, kd=kd, hf=hf, pa=pa):
                        k.pre(g, "SG", writes=[b_wqb[pa]])
                        k.post(g.dma_start(out=wqb[pa][:],
                                           in_=wq_r[kd, :, hf * 1024:(hf + 1) * 1024]),
                               "SG", [], [b_wqb[pa]])

                    @blk.tensor
                    def _(t, kd=kd, pa=pa, slots=slots):
                        k.pre(t, "T", reads=[b_wqb[pa], b_xoT], writes=b_psq)
                        ins = None
                        for si, s_ in enumerate(slots):
                            xs = xoT[:, kd * 1024 + s_ * 128: kd * 1024 + s_ * 128 + 128]
                            t.ldweights(xs)
                            for b2 in range(2):
                                ins = t.matmul(ps_q[si][:, b2 * 512:(b2 + 1) * 512], xs,
                                               wqb[pa][:, b2 * 512:(b2 + 1) * 512],
                                               start=(kd == 0), stop=(kd == 15),
                                               skip_group_check=True)
                        k.post(ins, "T", [b_wqb[pa], b_xoT], list(b_psq))

                for si, s_ in enumerate(slots):
                    @blk.vector
                    def _(v, si=si, s_=s_, hf=hf):
                        src = ps_q[si][:]
                        swp = src.rearrange("p (h pr two) -> p h pr two", pr=64, two=2)[
                            :, :, :, ::-1]
                        ccb = ccq[s_][:].rearrange("p (one j) -> p one j", one=1).broadcast_to(
                            [128, 8, 128])
                        ssb = ssq[s_][:].rearrange("p (one j) -> p one j", one=1).broadcast_to(
                            [128, 8, 128])
                        k.pre(v, "V", reads=[b_psq[si], b_ccq[s_], b_ssq[s_]],
                              writes=[b_rt1, b_rt2])
                        k.post(v.tensor_tensor(rt1[:], src, ccb, ALU.mult),
                               "V", [b_psq[si], b_ccq[s_]], [b_rt1])
                        k.post(v.tensor_tensor(rt2[:], swp, ssb, ALU.mult),
                               "V", [b_psq[si], b_ssq[s_]], [b_rt2])
                        k.pre(v, "V", reads=[b_rt1, b_rt2], writes=[b_qrot])
                        k.post(v.tensor_tensor(
                            e["qrot"][:, s_ * D + hf * 1024: s_ * D + (hf + 1) * 1024],
                            rt1[:], rt2[:], ALU.add), "V", [b_rt1, b_rt2], [b_qrot])

    if e["dbg_q"] is not None:
        @blk.scalar
        def _(s):
            k.pre(s, "SS", reads=[b_qrot])
            k.post(s.dma_start(out=e["dbg_q"][:, :], in_=e["qrot"][:]), "SS", [b_qrot], [])


def p2(k, blk, e):
    nc = k.nc
    B = e["B"]
    b_attn = B("attn")[0]
    b_id = B("ident")[0]
    b_qrot = B("qrot")[0]
    b_ckvd = B("ckv_d")[0]
    b_cc, b_ss, b_bmq = B("CC", "SS", "bmq")

    with contextlib.ExitStack() as es2:
        sb = lambda n, s, d: es2.enter_context(nc.sbuf_tensor(n, s, d))
        psa = lambda n, s, d: es2.enter_context(nc.psum_tensor(n, s, d))
        Kf = sb("p2_Kf", [128, T], BF16)
        Vt = sb("p2_Vt", [128, T], BF16)
        ckvq = [sb(f"p2_ckvq{i}", [128, 8 * 1024], BF16) for i in range(2)]
        wkv = sb("p2_wkv", [128, 8 * 256], BF16)
        kraw = [sb(f"p2_kraw{i}", [128, 1024], BF16) for i in range(2)]
        ktok = [sb(f"p2_ktok{i}", [128, 512], BF16) for i in range(2)]
        rr1 = [sb(f"p2_rr1_{i}", [128, 512], F32) for i in range(2)]
        rr2 = [sb(f"p2_rr2_{i}", [128, 512], F32) for i in range(2)]
        qfq = sb("p2_qfq", [128, 512], BF16)
        PT = [sb(f"p2_PT{i}", [128, 512], BF16) for i in range(2)]
        ptmpb = [sb(f"p2_ptmpb{i}", [128, 128], BF16) for i in range(2)]
        zq = sb("p2_zq", [1, 512], F32)
        zr = sb("p2_zr", [1, 512], F32)
        zbc = sb("p2_zbc", [128, 512], F32)
        wc2 = [sb(f"p2_wc2_{i}", [128, 512], F32) for i in range(2)]
        pos2 = sb("p2_pos2", [128, 512], BF16)
        q2 = [sb(f"p2_q2_{i}", [128, 512], BF16) for i in range(2)]

        psKV = psa("p2_psKV", [128, 1024], F32)
        psc = [psa(f"p2_psc{i}", [128, 512], F32) for i in range(2)]
        psAV = psa("p2_psAV", [128, 512], F32)
        psZr = psa("p2_psZr", [128, 512], F32)
        ptr = [psa(f"p2_ptr{i}", [128, 128], BF16) for i in range(2)]

        b_Kf, b_Vt, b_wkv = B("p2_Kf", "p2_Vt", "p2_wkv")
        b_ckvq = B("p2_ckvq0", "p2_ckvq1")
        b_kraw = B("p2_kraw0", "p2_kraw1")
        b_ktok = B("p2_ktok0", "p2_ktok1")
        b_rr1 = B("p2_rr1_0", "p2_rr1_1")
        b_rr2 = B("p2_rr2_0", "p2_rr2_1")
        b_qfq, b_PTb0, b_PTb1 = B("p2_qfq", "p2_PT0", "p2_PT1")
        b_ptmpb = B("p2_ptmpb0", "p2_ptmpb1")
        b_PT = [b_PTb0, b_PTb1]
        b_zq, b_zr, b_zbc = B("p2_zq", "p2_zr", "p2_zbc")
        b_idf = B("identf")[0]
        b_kvps = B("p2_psKV_kv")[0]
        b_psc = B("p2_psc0", "p2_psc1")
        b_psAV, b_psZr = B("p2_psAV", "p2_psZr")
        b_ptr = B(*[f"p2_ptr{i}" for i in range(2)])
        b_bmk = B("bmk")[0]
        b_ones = B("ones")[0]

        wup_r = e["wup"].rearrange("(l p) n -> l p n", p=128)

        for h in range(NH):
            p0w2_emit(k, blk, e, h / NH, (h + 1) / NH, (wc2, pos2, q2))

            @blk.gpsimd
            def _(g, h=h):
                k.pre(g, "SG", writes=[b_wkv])
                k.post(g.dma_start(out=wkv[:].rearrange("p (l c) -> p l c", l=8),
                                   in_=wup_r[:, :, h * 256:(h + 1) * 256]), "SG", [], [b_wkv])

            for er in range(8):
                cpa = er % 2

                @blk.sync
                def _(sy, er=er, cpa=cpa):
                    k.pre(sy, "SH", reads=[b_ckvd], writes=[b_ckvq[cpa]])
                    k.post(sy.dma_start(out=ckvq[cpa][:].rearrange("p (l t) -> p l t", l=8),
                                        in_=e["ckv_d"][:, :, er * 1024:(er + 1) * 1024]),
                           "SH", [b_ckvd], [b_ckvq[cpa]])

                for part in range(2):
                    base = part * 128
                    rpa = part

                    @blk.tensor
                    def _(t, base=base, cpa=cpa, rpa=rpa):
                        k.pre(t, "T", reads=[b_ckvq[cpa], b_wkv], writes=[b_kvps])
                        ins = None
                        for l in range(8):
                            w = wkv[:, l * 256 + base: l * 256 + base + 128]
                            t.ldweights(w)
                            for b in range(2):
                                ins = t.matmul(
                                    psKV[:, b * 512:(b + 1) * 512], w,
                                    ckvq[cpa][:, l * 1024 + b * 512: l * 1024 + (b + 1) * 512],
                                    start=(l == 0), stop=(l == 7), skip_group_check=True)
                        k.post(ins, "T", [b_ckvq[cpa], b_wkv], [b_kvps])

                    @blk.scalar
                    def _(s, rpa=rpa):
                        k.pre(s, "A", reads=[b_kvps], writes=[b_kraw[rpa]])
                        k.post(s.activation(kraw[rpa][:], psKV[:], AF.Identity),
                               "A", [b_kvps], [b_kraw[rpa]])

                    if part == 1:
                        for g2 in range(4):
                            @blk.tensor
                            def _(t, g2=g2, rpa=rpa):
                                k.pre(t, "T", reads=[b_kraw[rpa], b_id], writes=list(b_ptr))
                                for i in range(2):
                                    kt = g2 * 2 + i
                                    k.post(t.transpose(ptr[i][:],
                                                       kraw[rpa][:, kt * 128:(kt + 1) * 128],
                                                       e["ident_s"][:]),
                                           "T", [b_kraw[rpa], b_id], [b_ptr[i]])

                            @blk.scalar
                            def _(s, er=er, g2=g2):
                                k.pre(s, "A", reads=list(b_ptr), writes=[b_Vt])
                                ins = None
                                for i in range(2):
                                    kt_g = er * 8 + g2 * 2 + i
                                    ins = s.activation(Vt[:, kt_g * 128:(kt_g + 1) * 128],
                                                       ptr[i][:], AF.Identity)
                                k.post(ins, "A", list(b_ptr), [b_Vt])
                    else:
                        for b4 in range(2):
                            gpa = b4

                            for h2 in range(2):
                                @blk.tensor
                                def _(t, b4=b4, h2=h2, rpa=rpa):
                                    k.pre(t, "T", reads=[b_kraw[rpa], b_id], writes=list(b_ptr))
                                    for i in range(2):
                                        kt = b4 * 4 + h2 * 2 + i
                                        k.post(t.transpose(ptr[i][:],
                                                           kraw[rpa][:, kt * 128:(kt + 1) * 128],
                                                           e["ident_s"][:]),
                                               "T", [b_kraw[rpa], b_id], [b_ptr[i]])

                                @blk.vector
                                def _(v, er=er, b4=b4, h2=h2, gpa=gpa):
                                    bkt = er * 8 + b4 * 4 + h2 * 2
                                    k.pre(v, "V", reads=list(b_ptr) + [b_cc, b_ss],
                                          writes=[b_rr1[gpa], b_rr2[gpa]])
                                    i1 = i2 = None
                                    for i in range(2):
                                        col = (h2 * 2 + i) * 128
                                        srcp = ptr[i][:]
                                        swp = srcp.rearrange("p (pr two) -> p pr two", two=2)[
                                            :, :, ::-1]
                                        i1 = v.tensor_tensor(
                                            rr1[gpa][:, col:col + 128], srcp,
                                            e["CC"][:, (bkt + i) * 128:(bkt + i + 1) * 128],
                                            ALU.mult)
                                        i2 = v.tensor_tensor(
                                            rr2[gpa][:, col:col + 128], swp,
                                            e["SS"][:, (bkt + i) * 128:(bkt + i + 1) * 128],
                                            ALU.mult)
                                    k.post(i1, "V", list(b_ptr) + [b_cc], [b_rr1[gpa]])
                                    k.post(i2, "V", list(b_ptr) + [b_ss], [b_rr2[gpa]])

                            @blk.gpsimd
                            def _(g, gpa=gpa):
                                k.pre(g, "G", reads=[b_rr1[gpa], b_rr2[gpa]],
                                      writes=[b_ktok[gpa]])
                                k.post(g.tensor_tensor(ktok[gpa][:], rr1[gpa][:], rr2[gpa][:],
                                                       ALU.add),
                                       "G", [b_rr1[gpa], b_rr2[gpa]], [b_ktok[gpa]])

                            for h2 in range(2):
                                @blk.tensor
                                def _(t, h2=h2, gpa=gpa):
                                    k.pre(t, "T", reads=[b_ktok[gpa], b_id], writes=list(b_ptr))
                                    for i in range(2):
                                        kt = h2 * 2 + i
                                        k.post(t.transpose(ptr[i][:],
                                                           ktok[gpa][:, kt * 128:(kt + 1) * 128],
                                                           e["ident_s"][:]),
                                               "T", [b_ktok[gpa], b_id], [b_ptr[i]])

                                @blk.scalar
                                def _(s, er=er, b4=b4, h2=h2):
                                    k.pre(s, "A", reads=list(b_ptr), writes=[b_Kf])
                                    ins = None
                                    for i in range(2):
                                        kt_g = er * 8 + b4 * 4 + h2 * 2 + i
                                        ins = s.activation(Kf[:, kt_g * 128:(kt_g + 1) * 128],
                                                           ptr[i][:], AF.Identity)
                                    k.post(ins, "A", list(b_ptr), [b_Kf])
            if e["dbg_kf"] is not None and h == 0:
                @blk.scalar
                def _(s):
                    k.pre(s, "SS", reads=[b_Kf, b_Vt])
                    k.post(s.dma_start(out=e["dbg_kf"][:, :], in_=Kf[:]), "SS", [b_Kf], [])
                    k.post(s.dma_start(out=e["dbg_vt"][:, :], in_=Vt[:]), "SS", [b_Vt], [])

            for qi, slots in enumerate(QUADS):
                for ci, s_ in enumerate(slots):
                    @blk.tensor
                    def _(t, s_=s_, ci=ci, h=h):
                        k.pre(t, "T", reads=[b_qrot, b_id], writes=[b_ptr[ci % 2]])
                        k.post(t.transpose(ptr[ci % 2][:],
                                           e["qrot"][:, s_ * D + h * 128: s_ * D + (h + 1) * 128],
                                           e["ident_s"][:]), "T", [b_qrot, b_id],
                               [b_ptr[ci % 2]])

                    @blk.scalar
                    def _(s, ci=ci):
                        k.pre(s, "A", reads=[b_ptr[ci % 2]], writes=[b_qfq])
                        k.post(s.activation(qfq[:, ci * 128:(ci + 1) * 128], ptr[ci % 2][:],
                                            AF.Identity), "A", [b_ptr[ci % 2]], [b_qfq])

                maxc = (slots[0] + 1) * 8

                def emit_sct(j):
                    nval = sum(1 for s_ in slots if (s_ + 1) * 8 > j)
                    ncol = nval * 128
                    pa = j % 2

                    @blk.tensor
                    def _(t, j=j, pa=pa, ncol=ncol):
                        k.pre(t, "T", reads=[b_Kf, b_qfq], writes=[b_psc[pa]])
                        t.ldweights(Kf[:, j * 128:(j + 1) * 128])
                        k.post(t.matmul(psc[pa][:, 0:ncol], Kf[:, j * 128:(j + 1) * 128],
                                        qfq[:, 0:ncol], start=True, stop=True,
                                        skip_group_check=True),
                               "T", [b_Kf, b_qfq], [b_psc[pa]])

                def emit_tail(j):
                    nval = sum(1 for s_ in slots if (s_ + 1) * 8 > j)
                    ncol = nval * 128
                    pa = j % 2
                    sb_ = j // 8
                    boundary = sb_ in slots and slots.index(sb_) == nval - 1

                    if not boundary:
                        @blk.scalar
                        def _(s, j=j, pa=pa, ncol=ncol):
                            k.pre(s, "A", reads=[b_psc[pa]], writes=[b_PT[pa]])
                            k.post(s.activation(PT[pa][:, 0:ncol], psc[pa][:, 0:ncol],
                                                AF.Exp, scale=SCALE),
                                   "A", [b_psc[pa]], [b_PT[pa]])
                    else:
                        @blk.scalar
                        def _(s, j=j, pa=pa, ncol=ncol):
                            k.pre(s, "A", reads=[b_psc[pa]], writes=[b_PT[pa], b_ptmpb[pa]])
                            ins = None
                            if ncol > 128:
                                ins = s.activation(PT[pa][:, 0:ncol - 128],
                                                   psc[pa][:, 0:ncol - 128], AF.Exp,
                                                   scale=SCALE)
                            ins2 = s.activation(ptmpb[pa][:], psc[pa][:, ncol - 128:ncol],
                                                AF.Exp, scale=SCALE)
                            if ins is not None:
                                k.post(ins, "A", [b_psc[pa]], [b_PT[pa]])
                            k.post(ins2, "A", [b_psc[pa]], [b_ptmpb[pa]])

                        @blk.vector
                        def _(v, j=j, pa=pa, ncol=ncol):
                            k.pre(v, "V", reads=[b_ptmpb[pa], b_bmk], writes=[b_PT[pa]])
                            jm = j % 8
                            k.post(v.tensor_tensor(PT[pa][:, ncol - 128:ncol], ptmpb[pa][:],
                                                   e["bmk"][:, jm * 128:(jm + 1) * 128],
                                                   ALU.mult), "V", [b_ptmpb[pa], b_bmk],
                                   [b_PT[pa]])

                    @blk.tensor
                    def _(t, j=j, pa=pa, ncol=ncol, maxc=maxc):
                        k.pre(t, "T", reads=[b_PT[pa], b_Vt, b_ones, e["B"]("ones128")[0]], writes=[b_psAV, b_psZr])
                        t.ldweights(Vt[:, j * 128:(j + 1) * 128])
                        k.post(t.matmul(psAV[:, 0:ncol], Vt[:, j * 128:(j + 1) * 128],
                                        PT[pa][:, 0:ncol], start=(j == 0),
                                        stop=(j == maxc - 1), skip_group_check=True),
                               "T", [b_PT[pa], b_Vt], [b_psAV])
                        t.ldweights(e["ones128"][:])
                        k.post(t.matmul(psZr[:, 0:ncol], e["ones128"][:],
                                        PT[pa][:, 0:ncol], start=(j == 0),
                                        stop=(j == maxc - 1), skip_group_check=True),
                               "T", [b_PT[pa], e["B"]("ones128")[0]], [b_psZr])

                emit_sct(0)
                for j in range(maxc):
                    if j + 1 < maxc:
                        emit_sct(j + 1)
                    emit_tail(j)

                @blk.vector
                def _(v):
                    k.pre(v, "V", reads=[b_psZr], writes=[b_zbc])
                    k.post(v.reciprocal(zbc[:], psZr[:, 0:512]), "V", [b_psZr], [b_zbc])

                @blk.vector
                def _(v, h=h, qi=qi):
                    k.pre(v, "V", reads=[b_psAV, b_zbc], writes=[b_attn])
                    k.post(v.tensor_tensor(
                        e["attn"][:, h * 1024 + qi * 512: h * 1024 + (qi + 1) * 512],
                        psAV[:], zbc[:], ALU.mult), "V", [b_psAV, b_zbc], [b_attn])

    if e["dbg_at"] is not None:
        @blk.scalar
        def _(s):
            k.pre(s, "SS", reads=[b_attn])
            k.post(s.dma_start(out=e["dbg_at"][:, :], in_=e["attn"][:]), "SS", [b_attn], [])


def p3(k, blk, e):
    nc = k.nc
    B = e["B"]
    b_x1d = B("x1_d")[0]
    b_attn = B("attn")[0]

    with contextlib.ExitStack() as es3:
        sb = lambda n, s, d: es3.enter_context(nc.sbuf_tensor(n, s, d))
        psa = lambda n, s, d: es3.enter_context(nc.psum_tensor(n, s, d))
        wob = [sb(f"p3_wob{i}", [128, D], BF16) for i in range(2)]
        xot = [sb(f"p3_xot{i}", [128, D], F32) for i in range(2)]
        vsl = sb("p3_v", [128, D], F32)
        sqs = sb("p3_sq", [128, D], F32)
        x1t = sb("p3_x1t", [128, D], F32)
        cols = sb("p3_cols", [128, 8], F32)
        ps_wo = [psa(f"p3_pswo{i}", [128, 2048], F32) for i in range(2)]
        b_wob = B("p3_wob0", "p3_wob1")
        b_xot = B("p3_xot0", "p3_xot1")
        b_v, b_sq, b_cols = B("p3_v", "p3_sq", "p3_cols")
        b_pswo = B("p3_pswo0", "p3_pswo1")
        wo_r = e["wo_"].rearrange("(f p) n -> f p n", p=128)

        for sp in range(4):
            slots = [2 * sp, 2 * sp + 1]
            for f in range(16):
                pa = f % 2

                @blk.gpsimd
                def _(g, f=f, pa=pa):
                    k.pre(g, "SG", writes=[b_wob[pa]])
                    k.post(g.dma_start(out=wob[pa][:], in_=wo_r[f, :, :]), "SG", [], [b_wob[pa]])

                @blk.tensor
                def _(t, f=f, pa=pa, slots=slots):
                    k.pre(t, "T", reads=[b_wob[pa], b_attn], writes=list(b_pswo))
                    ins = None
                    for si, s_ in enumerate(slots):
                        qi, ci = QCOL[s_]
                        col = f * 1024 + qi * 512 + ci * 128
                        t.ldweights(e["attn"][:, col:col + 128])
                        for cb in range(4):
                            ins = t.matmul(ps_wo[si][:, cb * 512:(cb + 1) * 512],
                                           e["attn"][:, col:col + 128],
                                           wob[pa][:, cb * 512:(cb + 1) * 512],
                                           start=(f == 0), stop=(f == 15),
                                           skip_group_check=True)
                    k.post(ins, "T", [b_wob[pa], b_attn], list(b_pswo))

            for si, s_ in enumerate(slots):
                @blk.sync
                def _(sy, si=si, s_=s_):
                    k.pre(sy, "SH", writes=[b_xot[si]])
                    k.post(sy.dma_start(out=xot[si][:], in_=e["xo"][s_, :, :]),
                           "SH", [], [b_xot[si]])

                @blk.vector
                def _(v, si=si):
                    k.pre(v, "V", reads=[b_pswo[si], b_xot[si]], writes=[b_v, b_cols])
                    k.post(v.scalar_tensor_tensor(vsl[:], ps_wo[si][:], 0.0, xot[si][:],
                                                  ALU.add, ALU.add, accum_out=cols[:, 0:1]),
                           "V", [b_pswo[si], b_xot[si]], [b_v, b_cols])

                b_x1t = B("p3_x1t")[0]
                ln_epilogue(k, blk, e, vsl, b_v, sqs, b_sq, cols, b_cols,
                            x1t[:], b_x1t, D)

                @blk.scalar
                def _(s, s_=s_):
                    k.pre(s, "SS", reads=[b_x1t], writes=[b_x1d])
                    k.post(s.dma_start(out=e["x1_d"][s_, :, :], in_=x1t[:]),
                           "SS", [b_x1t], [b_x1d])


def ln_epilogue(k, blk, e, vsl, b_v, sqs, b_sq, cols, b_cols, out_ap, b_out, dim):
    @blk.scalar
    def _(s):
        k.pre(s, "A", reads=[b_v], writes=[b_sq, b_cols])
        k.post(s.activation(sqs[:], vsl[:], AF.Square, accum_out=cols[:, 1:2]),
               "A", [b_v], [b_sq, b_cols])

    @blk.vector
    def _(v):
        k.pre(v, "V", reads=[b_cols], writes=[b_cols])
        k.post(v.tensor_scalar(cols[:, 2:3], cols[:, 0:1], 1.0 / dim, None, ALU.mult),
               "V", [b_cols], [b_cols])
        k.pre(v, "V", reads=[b_cols], writes=[b_cols])
        k.post(v.tensor_tensor(cols[:, 3:4], cols[:, 2:3], cols[:, 2:3], ALU.mult),
               "V", [b_cols], [b_cols])
        k.pre(v, "V", reads=[b_cols], writes=[b_cols])
        k.post(v.scalar_tensor_tensor(cols[:, 4:5], cols[:, 1:2], 1.0 / dim, cols[:, 3:4],
                                      ALU.mult, ALU.subtract), "V", [b_cols], [b_cols])
        k.pre(v, "V", reads=[b_cols], writes=[b_cols])
        k.post(v.tensor_scalar(cols[:, 5:6], cols[:, 4:5], 1e-5, None, ALU.add),
               "V", [b_cols], [b_cols])
        k.pre(v, "V", reads=[b_cols], writes=[b_cols])
        k.post(v.reciprocal(cols[:, 6:7], cols[:, 5:6]), "V", [b_cols], [b_cols])

    @blk.scalar
    def _(s):
        k.pre(s, "A", reads=[b_cols], writes=[b_cols])
        k.post(s.activation(cols[:, 7:8], cols[:, 6:7], AF.Sqrt), "A", [b_cols], [b_cols])

    @blk.vector
    def _(v):
        k.pre(v, "V", reads=[b_cols], writes=[b_cols])
        k.post(v.scalar_tensor_tensor(cols[:, 3:4], cols[:, 2:3], -1.0, cols[:, 7:8],
                                      ALU.mult, ALU.mult), "V", [b_cols], [b_cols])

    @blk.scalar
    def _(s):
        k.pre(s, "A", reads=[b_v, b_cols], writes=[b_out])
        k.post(s.activation(out_ap, vsl[:], AF.Identity, bias=cols[:, 3:4],
                            scale=cols[:, 7:8]), "A", [b_v, b_cols], [b_out])


def p4(k, blk, e):
    nc = k.nc
    B = e["B"]
    b_x1d = B("x1_d")[0]
    b_id = B("ident")[0]
    b_mwb = B("mwb0", "mwb1", "mwb2")
    b_thr = B("thr0", "thr1", "thr2")
    b_nth = B("nthr0", "nthr1", "nthr2")
    b_w1qn, b_w2qn, b_w3qn = B("w1qn", "w2qn", "w3qn")
    b_hqd, b_hqtd, b_out = B("hq_d", "hqt_d", "out_d")

    with contextlib.ExitStack() as es4:
        sb = lambda n, s, d: es4.enter_context(nc.sbuf_tensor(n, s, d))
        xqT = sb("p4_xqT", [128, 16 * NSLOT * 128], BF16)
        msc1 = sb("p4_msc1", [128, NSLOT], F32)
        msc2 = sb("p4_msc2", [128, NSLOT], F32)
        msc3 = sb("p4_msc3", [128, NSLOT], F32)
        ccols = sb("p4_cols", [128, 8], F32)
        posb = sb("p4_pos", [128, 512], BF16)
        b_xqT, b_ms1, b_ms2, b_ms3, b_cc4 = B(
            "p4_xqT", "p4_msc1", "p4_msc2", "p4_msc3", "p4_cols")
        b_xqt, b_n4, b_r4, b_t4, b_sq4 = B("p4_xqtok", "p4_n", "p4_r", "p4_t", "p4_sq")
        b_pos = B("p4_pos")[0]

        with contextlib.ExitStack() as esp:
            sba = lambda n, s, d: esp.enter_context(nc.sbuf_tensor(n, s, d))
            xq_tok = sba("p4_xqtok", [128, D], BF16)
            ni32 = sba("p4_n", [128, D], I32)
            rbuf = sba("p4_r", [128, D], F32)
            tbuf = sba("p4_t", [128, D], F32)
            sqs = sba("p4_sq", [128, D], F32)
            x1row = sba("p4_x1row", [128, D], F32)
            b_x1r = B("p4_x1row")[0]
            ptr = [esp.enter_context(nc.psum_tensor(f"p4_ptr{i}", [128, 128], BF16))
                   for i in range(8)]
            b_ptr = B(*[f"p4_ptr{i}" for i in range(8)])
            for s_ in range(NSLOT):
                @blk.sync
                def _(sy, s_=s_):
                    k.pre(sy, "SH", reads=[b_x1d], writes=[b_x1r])
                    k.post(sy.dma_start(out=x1row[:], in_=e["x1_d"][s_, :, :]),
                           "SH", [b_x1d], [b_x1r])
                x1s = x1row[:]

                @blk.vector
                def _(v, x1s=x1s):
                    k.pre(v, "V", reads=[b_x1r], writes=[b_cc4])
                    k.post(v.reduce_max(ccols[:, 0:1], x1s, axis=AX,
                                        apply_absolute_value=True), "V", [b_x1r], [b_cc4])

                @blk.scalar
                def _(s, x1s=x1s):
                    k.pre(s, "A", reads=[b_x1r], writes=[b_sq4, b_cc4])
                    k.post(s.activation(sqs[:], x1s, AF.Square, accum_out=ccols[:, 1:2]),
                           "A", [b_x1r], [b_sq4, b_cc4])

                quant_scale_chain(k, blk, e, ccols, b_cc4, msc1, b_ms1, msc2, b_ms2, s_,
                                  D, b_mwb[0], b_mwb[1], e["mwb"][0], e["mwb"][1])

                @blk.scalar
                def _(s, x1s=x1s):
                    k.pre(s, "A", reads=[b_x1r, b_cc4], writes=[b_n4])
                    k.post(s.activation(ni32[:], x1s, AF.Identity, scale=ccols[:, 7:8]),
                           "A", [b_x1r, b_cc4], [b_n4])

                @blk.vector
                def _(v, x1s=x1s):
                    k.pre(v, "V", reads=[b_x1r, b_cc4, b_n4], writes=[b_r4])
                    k.post(v.scalar_tensor_tensor(rbuf[:], x1s, ccols[:, 7:8], ni32[:],
                                                  ALU.mult, ALU.subtract),
                           "V", [b_x1r, b_cc4, b_n4], [b_r4])
                    k.pre(v, "V", reads=[b_r4, b_n4], writes=[b_t4])
                    k.post(v.scalar_tensor_tensor(tbuf[:], rbuf[:], 0.5, ni32[:],
                                                  ALU.is_ge, ALU.add), "V", [b_r4, b_n4], [b_t4])
                    k.pre(v, "V", reads=[b_r4, b_t4], writes=[b_xqt])
                    k.post(v.scalar_tensor_tensor(xq_tok[:], rbuf[:], -0.5, tbuf[:],
                                                  ALU.is_le, ALU.subtract),
                           "V", [b_r4, b_t4], [b_xqt])

                for half in range(2):
                    @blk.tensor
                    def _(t, half=half):
                        k.pre(t, "T", reads=[b_xqt, b_id], writes=b_ptr)
                        for i in range(8):
                            kd = half * 8 + i
                            k.post(t.transpose(ptr[i][:], xq_tok[:, kd * 128:(kd + 1) * 128],
                                               e["ident_s"][:]), "T", [b_xqt, b_id], [b_ptr[i]])

                    @blk.vector
                    def _(v, s_=s_, half=half):
                        k.pre(v, "V", reads=b_ptr, writes=[b_xqT])
                        ins = None
                        for i in range(8):
                            kd = half * 8 + i
                            ins = v.tensor_copy(
                                xqT[:, kd * 1024 + s_ * 128: kd * 1024 + s_ * 128 + 128],
                                ptr[i][:])
                        k.post(ins, "V", b_ptr, [b_xqT])

        k.barrier(blk)
        es_b = contextlib.ExitStack()
        sbb = lambda n, s, d: es_b.enter_context(nc.sbuf_tensor(n, s, d))
        w1qb = sbb("p4_w1qb", [128, 16 * 512], BF16)
        w2qb = sbb("p4_w2qb", [128, 16 * 512], BF16)
        gtmp = sbb("p4_gtmp", [128, 512], F32)
        utmp = sbb("p4_utmp", [128, 512], F32)
        hqc = sbb("p4_hqc", [128, 512], F32)
        b_w1q, b_w2q, b_g, b_u, b_hqc = B("p4_w1qb", "p4_w2qb", "p4_g", "p4_u", "p4_hqc")

        with contextlib.ExitStack() as esp:
            psGU = [esp.enter_context(nc.psum_tensor(f"p4_gu{i}", [128, 512], F32))
                    for i in range(8)]
            b_gu = B(*[f"p4_gu{i}" for i in range(8)])
            for hb in range(11):
                for wsel, (wqn_, qdst, b_qdst) in enumerate(
                        [(e["w1qn"], w1qb, b_w1q), (e["w2qn"], w2qb, b_w2q)]):
                    b_wnx = b_w1qn if wsel == 0 else b_w2qn
                    for kd in range(16):
                        @blk.sync
                        def _(sy, wqn_=wqn_, hb=hb, kd=kd, qdst=qdst, b_qdst=b_qdst,
                              b_wnx=b_wnx):
                            k.pre(sy, "SH", reads=[b_wnx], writes=[b_qdst])
                            k.post(sy.dma_start_transpose(
                                qdst[:, kd * 512:(kd + 1) * 512],
                                wqn_[hb * 512:(hb + 1) * 512, kd * 128:(kd + 1) * 128]),
                                "SH", [b_wnx], [b_qdst])

                for half in range(2):
                    slots = [half * 4 + i for i in range(4)]

                    @blk.tensor
                    def _(t, slots=slots):
                        k.pre(t, "T", reads=[b_w1q, b_w2q, b_xqT], writes=list(b_gu))
                        ins = None
                        for kd in range(16):
                            for si, s_ in enumerate(slots):
                                xs = xqT[:, kd * 1024 + s_ * 128: kd * 1024 + s_ * 128 + 128]
                                t.ldweights(xs)
                                t.matmul(psGU[si][:], xs, w1qb[:, kd * 512:(kd + 1) * 512],
                                         start=(kd == 0), stop=(kd == 15), skip_group_check=True)
                                ins = t.matmul(psGU[4 + si][:], xs,
                                               w2qb[:, kd * 512:(kd + 1) * 512],
                                               start=(kd == 0), stop=(kd == 15),
                                               skip_group_check=True)
                        k.post(ins, "T", [b_w1q, b_w2q, b_xqT], list(b_gu))

                    for si, s_ in enumerate(slots):
                        @blk.scalar
                        def _(s, si=si, s_=s_):
                            k.pre(s, "A", reads=[b_gu[si], b_ms1], writes=[b_g])
                            k.post(s.activation(gtmp[:], psGU[si][:], AF.Silu,
                                                scale=msc1[:, s_:s_ + 1]),
                                   "A", [b_gu[si], b_ms1], [b_g])
                            k.pre(s, "A", reads=[b_gu[4 + si], b_ms2], writes=[b_u])
                            k.post(s.activation(utmp[:], psGU[4 + si][:], AF.Identity,
                                                scale=msc2[:, s_:s_ + 1]),
                                   "A", [b_gu[4 + si], b_ms2], [b_u])

                        @blk.vector
                        def _(v):
                            k.pre(v, "V", reads=[b_g, b_u], writes=[b_hqc])
                            k.post(v.tensor_tensor(hqc[:], gtmp[:], utmp[:], ALU.mult),
                                   "V", [b_g, b_u], [b_hqc])

                        @blk.scalar
                        def _(s, s_=s_, hb=hb):
                            k.pre(s, "SS", reads=[b_hqc], writes=[b_hqd])
                            k.post(s.dma_start(out=e["hq_d"][s_, :, hb * 512:(hb + 1) * 512],
                                               in_=hqc[:]), "SS", [b_hqc], [b_hqd])

        es_b.close()
        k.barrier(blk)
        es_c = contextlib.ExitStack()
        sbc = lambda n, s, d: es_c.enter_context(nc.sbuf_tensor(n, s, d))
        HC4 = HID // 4
        hrow = sbc("p4_hrow", [128, HID], F32)
        hsq = sbc("p4_hsq", [128, HC4], F32)
        hn = sbc("p4_hn", [128, HC4], I32)
        hr = sbc("p4_hr", [128, HC4], F32)
        ht = sbc("p4_ht", [128, HC4], F32)
        hqq = sbc("p4_hqq", [128, HID], BF16)
        hstage = sbc("p4_hstage", [128, 44 * 128], BF16)
        hsums = sbc("p4_hsums", [128, 4], F32)
        b_hrow, b_hsq, b_hn, b_hr, b_ht, b_hqq, b_hst, b_hsu = B(
            "p4_hrow", "p4_hsq", "p4_hn", "p4_hr", "p4_ht", "p4_hqq", "p4_hstage",
            "p4_hsums")

        with contextlib.ExitStack() as esp:
            ptr = [esp.enter_context(nc.psum_tensor(f"p4c_ptr{i}", [128, 128], BF16))
                   for i in range(8)]
            b_ptr = B(*[f"p4c_ptr{i}" for i in range(8)])
            for s_ in range(NSLOT):
                @blk.sync
                def _(sy, s_=s_):
                    k.pre(sy, "SH", reads=[b_hqd], writes=[b_hrow])
                    k.post(sy.dma_start(out=hrow[:], in_=e["hq_d"][s_, :, :]),
                           "SH", [b_hqd], [b_hrow])

                @blk.vector
                def _(v):
                    k.pre(v, "V", reads=[b_hrow], writes=[b_cc4])
                    k.post(v.reduce_max(ccols[:, 0:1], hrow[:], axis=AX,
                                        apply_absolute_value=True), "V", [b_hrow], [b_cc4])

                for c4 in range(4):
                    @blk.scalar
                    def _(s, c4=c4):
                        k.pre(s, "A", reads=[b_hrow], writes=[b_hsq, b_hsu])
                        k.post(s.activation(hsq[:], hrow[:, c4 * HC4:(c4 + 1) * HC4],
                                            AF.Square, accum_out=hsums[:, c4:c4 + 1]),
                               "A", [b_hrow], [b_hsq, b_hsu])

                @blk.vector
                def _(v):
                    k.pre(v, "V", reads=[b_hsu], writes=[b_cc4])
                    k.post(v.reduce_sum(ccols[:, 1:2], hsums[:], axis=AX),
                           "V", [b_hsu], [b_cc4])

                quant_scale_chain(k, blk, e, ccols, b_cc4, msc3, b_ms3, None, None, s_,
                                  HID, b_mwb[2], None, e["mwb"][2], None)

                for c4 in range(4):
                    sl = slice(c4 * HC4, (c4 + 1) * HC4)

                    @blk.scalar
                    def _(s, sl=sl):
                        k.pre(s, "A", reads=[b_hrow, b_cc4], writes=[b_hn])
                        k.post(s.activation(hn[:], hrow[:, sl], AF.Identity,
                                            scale=ccols[:, 7:8]), "A", [b_hrow, b_cc4], [b_hn])

                    @blk.vector
                    def _(v, sl=sl):
                        k.pre(v, "V", reads=[b_hrow, b_cc4, b_hn], writes=[b_hr])
                        k.post(v.scalar_tensor_tensor(hr[:], hrow[:, sl], ccols[:, 7:8], hn[:],
                                                      ALU.mult, ALU.subtract),
                               "V", [b_hrow, b_cc4, b_hn], [b_hr])
                        k.pre(v, "V", reads=[b_hr, b_hn], writes=[b_ht])
                        k.post(v.scalar_tensor_tensor(ht[:], hr[:], 0.5, hn[:],
                                                      ALU.is_ge, ALU.add), "V",
                               [b_hr, b_hn], [b_ht])
                        k.pre(v, "V", reads=[b_hr, b_ht], writes=[b_hqq])
                        k.post(v.scalar_tensor_tensor(hqq[:, sl], hr[:], -0.5, ht[:],
                                                      ALU.is_le, ALU.subtract),
                               "V", [b_hr, b_ht], [b_hqq])

                for gb in range(6):
                    n = 8 if gb < 5 else 4

                    @blk.tensor
                    def _(t, gb=gb, n=n):
                        k.pre(t, "T", reads=[b_hqq, b_id], writes=b_ptr[:n])
                        for i in range(n):
                            hc = gb * 8 + i
                            k.post(t.transpose(ptr[i][:], hqq[:, hc * 128:(hc + 1) * 128],
                                               e["ident_s"][:]), "T", [b_hqq, b_id], [b_ptr[i]])

                    @blk.vector
                    def _(v, gb=gb, n=n):
                        k.pre(v, "V", reads=b_ptr[:n], writes=[b_hst])
                        ins = None
                        for i in range(n):
                            hc = gb * 8 + i
                            ins = v.tensor_copy(hstage[:, hc * 128:(hc + 1) * 128], ptr[i][:])
                        k.post(ins, "V", b_ptr[:n], [b_hst])

                @blk.scalar
                def _(s, s_=s_):
                    k.pre(s, "SS", reads=[b_hst], writes=[b_hqtd])
                    k.post(s.dma_start(
                        out=e["hqt_d"][:, :, s_ * 128:(s_ + 1) * 128].rearrange(
                            "c p t -> p c t"),
                        in_=hstage[:].rearrange("p (c t) -> p c t", c=44)),
                        "SS", [b_hst], [b_hqtd])

        es_c.close()
        k.barrier(blk)
        hqtb = [sb(f"p4_hqtb{i}", [128, NSLOT * 128], BF16) for i in range(2)]
        w3qc = [sb(f"p4_w3qc{i}", [128, D], BF16) for i in range(2)]
        vsl = sb("p4b_v", [128, D], F32)
        sq2 = sb("p4b_sq", [128, D], F32)
        outs = sb("p4b_out", [128, D], F32)
        x1b = sb("p4b_x1", [128, D], F32)
        b_hqtb = B("p4_hqtb0", "p4_hqtb1")
        b_w3qc = B("p4_w3qc0", "p4_w3qc1")
        b_v4, b_sq2, b_os = B("p4b_v", "p4b_sq", "p4b_out")
        b_x1b = B("p4b_x1")[0]

        with contextlib.ExitStack() as esp:
            psZ = [esp.enter_context(nc.psum_tensor(f"p4_psz{i}", [128, 2048], F32))
                   for i in range(2)]
            b_psZ = B("p4_psz0", "p4_psz1")
            for sp in range(4):
                slots = [2 * sp, 2 * sp + 1]
                for hc in range(44):
                    pa = hc % 2

                    b_w3qt = B("w3qt")[0]

                    @blk.sync
                    def _(sy, hc=hc, pa=pa, sp=sp):
                        k.pre(sy, "SH", reads=[b_hqtd, b_w3qn, b_w3qt],
                              writes=[b_hqtb[pa], b_w3qc[pa]])
                        k.post(sy.dma_start(out=hqtb[pa][:], in_=e["hqt_d"][hc, :, :]),
                               "SH", [b_hqtd], [b_hqtb[pa]])
                        if sp == 0:
                            for cb in range(4):
                                k.post(sy.dma_start_transpose(
                                    w3qc[pa][:, cb * 512:(cb + 1) * 512],
                                    e["w3qn"][cb * 512:(cb + 1) * 512,
                                              hc * 128:(hc + 1) * 128]),
                                    "SH", [b_w3qn], [b_w3qc[pa]])
                        else:
                            k.post(sy.dma_start(out=w3qc[pa][:], in_=e["w3qt"][hc, :, :]),
                                   "SH", [b_w3qt], [b_w3qc[pa]])

                    if sp == 0:
                        @blk.gpsimd
                        def _(g, hc=hc, pa=pa):
                            k.pre(g, "SG", reads=[b_w3qc[pa]], writes=[b_w3qt])
                            k.post(g.dma_start(out=e["w3qt"][hc, :, :], in_=w3qc[pa][:]),
                                   "SG", [b_w3qc[pa]], [b_w3qt])

                    @blk.tensor
                    def _(t, hc=hc, pa=pa, slots=slots):
                        k.pre(t, "T", reads=[b_hqtb[pa], b_w3qc[pa]], writes=list(b_psZ))
                        ins = None
                        for si, s_ in enumerate(slots):
                            xs = hqtb[pa][:, s_ * 128:(s_ + 1) * 128]
                            t.ldweights(xs)
                            for cb in range(4):
                                ins = t.matmul(psZ[si][:, cb * 512:(cb + 1) * 512], xs,
                                               w3qc[pa][:, cb * 512:(cb + 1) * 512],
                                               start=(hc == 0), stop=(hc == 43),
                                               skip_group_check=True)
                        k.post(ins, "T", [b_hqtb[pa], b_w3qc[pa]], list(b_psZ))

                for si, s_ in enumerate(slots):
                    @blk.sync
                    def _(sy, s_=s_):
                        k.pre(sy, "SH", reads=[b_x1d], writes=[b_x1b])
                        k.post(sy.dma_start(out=x1b[:], in_=e["x1_d"][s_, :, :]),
                               "SH", [b_x1d], [b_x1b])

                    @blk.vector
                    def _(v, si=si, s_=s_):
                        k.pre(v, "V", reads=[b_psZ[si], b_ms3, b_x1b], writes=[b_v4, b_cc4])
                        k.post(v.scalar_tensor_tensor(
                            vsl[:], psZ[si][:], msc3[:, s_:s_ + 1],
                            x1b[:], ALU.mult, ALU.add,
                            accum_out=ccols[:, 0:1]),
                            "V", [b_psZ[si], b_ms3, b_x1b], [b_v4, b_cc4])

                    ln_epilogue(k, blk, e, vsl, b_v4, sq2, b_sq2, ccols, b_cc4,
                                outs[:], b_os, D)

                    @blk.scalar
                    def _(s, s_=s_):
                        k.pre(s, "SS", reads=[b_os], writes=[b_out])
                        k.post(s.dma_start(out=e["out"][s_, :, :], in_=outs[:]),
                               "SS", [b_os], [b_out])


def quant_scale_chain(k, blk, e, ccols, b_cc, mscA, b_msA, mscB, b_msB, s_, dim,
                      b_mA, b_mB, mA, mB):
    @blk.vector
    def _(v):
        k.pre(v, "V", reads=[b_cc], writes=[b_cc])
        k.post(v.tensor_scalar(ccols[:, 2:3], ccols[:, 1:2], 1.0 / dim, 1e-6,
                               ALU.mult, ALU.add), "V", [b_cc], [b_cc])
        k.pre(v, "V", reads=[b_cc], writes=[b_cc])
        k.post(v.reciprocal(ccols[:, 3:4], ccols[:, 2:3]), "V", [b_cc], [b_cc])

    @blk.scalar
    def _(s):
        k.pre(s, "A", reads=[b_cc], writes=[b_cc])
        k.post(s.activation(ccols[:, 4:5], ccols[:, 3:4], AF.Sqrt), "A", [b_cc], [b_cc])

    @blk.vector
    def _(v, s_=s_):
        k.pre(v, "V", reads=[b_cc], writes=[b_cc])
        k.post(v.scalar_tensor_tensor(ccols[:, 5:6], ccols[:, 0:1], 1.0, ccols[:, 4:5],
                                      ALU.mult, ALU.mult), "V", [b_cc], [b_cc])
        k.pre(v, "V", reads=[b_cc], writes=[b_cc])
        k.post(v.tensor_scalar(ccols[:, 5:6], ccols[:, 5:6], 1e-5, None, ALU.max),
               "V", [b_cc], [b_cc])
        k.pre(v, "V", reads=[b_cc], writes=[b_cc])
        k.post(v.reciprocal(ccols[:, 6:7], ccols[:, 5:6]), "V", [b_cc], [b_cc])
        k.pre(v, "V", reads=[b_cc], writes=[b_cc])
        k.post(v.scalar_tensor_tensor(ccols[:, 7:8], ccols[:, 6:7], 127.0, ccols[:, 4:5],
                                      ALU.mult, ALU.mult), "V", [b_cc], [b_cc])
        k.pre(v, "V", reads=[b_cc], writes=[b_cc])
        k.post(v.tensor_scalar(ccols[:, 6:7], ccols[:, 5:6], 1.0 / 127.0, None, ALU.mult),
               "V", [b_cc], [b_cc])
        k.pre(v, "V", reads=[b_cc, b_mA], writes=[b_msA])
        k.post(v.scalar_tensor_tensor(mscA[:, s_:s_ + 1], ccols[:, 6:7], 1.0, mA[:],
                                      ALU.mult, ALU.mult), "V", [b_cc, b_mA], [b_msA])
        if mscB is not None:
            k.pre(v, "V", reads=[b_cc, b_mB], writes=[b_msB])
            k.post(v.scalar_tensor_tensor(mscB[:, s_:s_ + 1], ccols[:, 6:7], 1.0, mB[:],
                                          ALU.mult, ALU.mult), "V", [b_cc, b_mB], [b_msB])


def make_inputs(core, x, ws):
    tiles = [8 * s + core for s in range(NSLOT)]
    xo = np.stack([x[t * 128:(t + 1) * 128, :] for t in tiles])
    qpos_c = np.stack([np.arange(t * 128, (t + 1) * 128, dtype=np.float32) for t in tiles],
                      axis=1)
    qrel_c = (np.arange(128, dtype=np.float32) + core * 128)[:, None].astype(np.float32)
    return dict(xf=x, xo=xo, qpos_c=qpos_c, qrel_c=qrel_c,
                ident=np.eye(128, dtype=bfdt), **ws)



_NC_CACHE = {}


def _get_nc():
    if "nc" not in _NC_CACHE:
        _NC_CACHE["nc"] = build(dev=False)
    return _NC_CACHE["nc"]


def kernel(L0_attn_x, L0_attn_wq, L0_attn_w_down, L0_attn_w_up, L0_attn_wo,
           L0_ffn_w1, L0_ffn_w2, L0_ffn_w3):
    x = np.ascontiguousarray(np.asarray(L0_attn_x, dtype=np.float32).reshape(T, D))
    ws = dict(
        wq=np.ascontiguousarray(np.asarray(L0_attn_wq, dtype=np.float32)),
        wdn=np.ascontiguousarray(np.asarray(L0_attn_w_down, dtype=np.float32)),
        wup=np.ascontiguousarray(np.asarray(L0_attn_w_up, dtype=np.float32)),
        wo_=np.ascontiguousarray(np.asarray(L0_attn_wo, dtype=np.float32)),
        w1=np.ascontiguousarray(np.asarray(L0_ffn_w1, dtype=np.float32)),
        w2=np.ascontiguousarray(np.asarray(L0_ffn_w2, dtype=np.float32)),
        w3=np.ascontiguousarray(np.asarray(L0_ffn_w3, dtype=np.float32)),
    )
    nc = _get_nc()
    in_maps = [make_inputs(c, x, ws) for c in range(8)]
    res = run_bass_kernel_spmd(nc, in_maps, core_ids=list(range(8)))
    out = np.empty((T, D), dtype=np.float32)
    for c in range(8):
        oc = np.asarray(res.results[c]["out"])
        for s in range(NSLOT):
            t = 8 * s + c
            out[t * 128:(t + 1) * 128, :] = oc[s]
    return out.reshape(1, T, D)


# version 11 (final)
# speedup vs baseline: 1.2943x; 1.0641x over previous
import sys
import contextlib
import math

import numpy as np
import ml_dtypes
import concourse.bass as bass
import concourse.mybir as mybir
from concourse.bass_utils import run_bass_kernel_spmd

F32 = mybir.dt.float32
BF16 = mybir.dt.bfloat16
I32 = mybir.dt.int32
AF = mybir.ActivationFunctionType
ALU = mybir.AluOpType
AX = mybir.AxisListType.X
bfdt = ml_dtypes.bfloat16

T = 8192
D = 2048
DLAT = 1024
NH = 16
DH = 128
HID = 5632
NSLOT = 8
SCALE = DH ** -0.5
TWO_PI = 2.0 * math.pi
C1 = 6.28125
C2 = TWO_PI - C1
INV_2PI = 1.0 / TWO_PI
LN1E4 = math.log(10000.0)

QUADS = [[7, 6, 1, 0], [5, 4, 3, 2]]
QCOL = {s: (qi, ci) for qi, qs in enumerate(QUADS) for ci, s in enumerate(qs)}

DMA_CH = ("SH", "SS", "SG")


class Buf:
    __slots__ = ("w", "rs", "name", "acc")

    def __init__(self, name=""):
        self.w = {}
        self.rs = []
        self.name = name
        self.acc = False


class FakeInstr:
    __slots__ = ("rec",)

    def __init__(self, rec):
        self.rec = rec

    def then_inc(self, sem, n):
        self.rec[3] = (sem, n)
        return self


class EngProxy:

    def __init__(self, q):
        self._q = q

    def __getattr__(self, name):
        def call(*args, **kwargs):
            rec = [name, args, kwargs, None]
            self._q.append(rec)
            return FakeInstr(rec)
        return call


class FakeBlock:
    def __init__(self):
        self.q = {"T": [], "V": [], "A": [], "G": [], "SY": []}
        self.prox = {k_: EngProxy(v) for k_, v in self.q.items()}

    def _dec(self, ch):
        def deco(fn):
            fn(self.prox[ch])
            return None
        return deco

    @property
    def tensor(self):
        return self._dec("T")

    @property
    def vector(self):
        return self._dec("V")

    @property
    def scalar(self):
        return self._dec("A")

    @property
    def gpsimd(self):
        return self._dec("G")

    @property
    def sync(self):
        return self._dec("SY")

    def flush(self, blk):
        for ch, eng_attr in [("T", "tensor"), ("V", "vector"), ("A", "scalar"),
                             ("G", "gpsimd"), ("SY", "sync")]:
            recs = self.q[ch]
            if not recs:
                continue

            def body(eng, recs=recs):
                for name, args, kwargs, inc in recs:
                    ins = getattr(eng, name)(*args, **kwargs)
                    if inc is not None:
                        ins.then_inc(*inc)
            getattr(blk, eng_attr)(body)


class B:
    def __init__(self, nc, es):
        self.nc = nc
        self.es = es
        self.sems = {}
        self.cnt = {}
        self.seen = {}

    def sem(self, name):
        if name not in self.sems:
            self.sems[name] = self.es.enter_context(self.nc.semaphore(name))
            self.cnt[name] = 0
        return self.sems[name]

    @staticmethod
    def ekey(ch):
        return {"SH": "SY", "SS": "A", "SG": "G"}.get(ch, ch)

    def pre(self, eng, ch, reads=(), writes=()):
        ek = self.ekey(ch)
        seen = self.seen.setdefault(ek, {})
        need = {}
        for b in reads:
            for c, v in b.w.items():
                need[c] = max(need.get(c, 0), v)
        for b in writes:
            if not b.acc:
                for c, v in b.w.items():
                    need[c] = max(need.get(c, 0), v)
            for c, v in b.rs:
                need[c] = max(need.get(c, 0), v)
        for c in sorted(need):
            v = need[c]
            if c == "T" and ch == "T":
                continue
            if seen.get(c, 0) >= v:
                continue
            eng.wait_ge(self.sem(c), v)
            seen[c] = v

    def post(self, instr, ch, reads=(), writes=()):
        if ch in DMA_CH:
            key = None
            for b in writes:
                if not b.acc:
                    key = b
                    break
            if key is None:
                for b in reads:
                    if not b.acc:
                        key = b
                        break
            sname = f"D_{ch}_{key.name if key is not None else 'misc'}"
            n = 16
        else:
            sname = ch
            n = 1
        s = self.sem(sname)
        self.cnt[sname] += n
        instr.then_inc(s, n)
        m = (sname, self.cnt[sname])
        for b in writes:
            if b.acc:
                b.w[m[0]] = m[1]
            else:
                b.w = {m[0]: m[1]}
                b.rs = []
        for b in reads:
            b.rs.append(m)
        return m

    def barrier(self, blk):
        chans = [c for c in self.cnt if self.cnt[c] > 0]

        def wait_all(eng, ek):
            seen = self.seen.setdefault(ek, {})
            for c in chans:
                v = self.cnt[c]
                if seen.get(c, 0) >= v:
                    continue
                eng.wait_ge(self.sem(c), v)
                seen[c] = v

        @blk.tensor
        def _(t):
            wait_all(t, "T")

        @blk.vector
        def _(v):
            wait_all(v, "V")

        @blk.scalar
        def _(s):
            wait_all(s, "A")

        @blk.gpsimd
        def _(g):
            wait_all(g, "G")

        @blk.sync
        def _(sy):
            wait_all(sy, "SY")

    def sb(self, name, shape, dtype):
        return self.es.enter_context(self.nc.sbuf_tensor(name, shape, dtype))

    def ps(self, name, shape, dtype):
        return self.es.enter_context(self.nc.psum_tensor(name, shape, dtype))


def build(dev=False, upto=99):
    nc = bass.Bass("TRN2", target_bir_lowering=False, debug=False)
    sk = "ExternalOutput" if dev else "Internal"

    io = {}
    io["xf"] = nc.dram_tensor("xf", [T, D], F32, kind="ExternalInput").ap()
    io["xo"] = nc.dram_tensor("xo", [NSLOT, 128, D], F32, kind="ExternalInput").ap()
    io["qpos_c"] = nc.dram_tensor("qpos_c", [128, NSLOT], F32, kind="ExternalInput").ap()
    io["qrel_c"] = nc.dram_tensor("qrel_c", [128, 1], F32, kind="ExternalInput").ap()
    io["wq"] = nc.dram_tensor("wq", [D, D], F32, kind="ExternalInput").ap()
    io["wdn"] = nc.dram_tensor("wdn", [D, DLAT], F32, kind="ExternalInput").ap()
    io["wup"] = nc.dram_tensor("wup", [DLAT, 2 * D], F32, kind="ExternalInput").ap()
    io["wo_"] = nc.dram_tensor("wo_", [D, D], F32, kind="ExternalInput").ap()
    io["w1"] = nc.dram_tensor("w1", [HID, D], F32, kind="ExternalInput").ap()
    io["w2"] = nc.dram_tensor("w2", [HID, D], F32, kind="ExternalInput").ap()
    io["w3"] = nc.dram_tensor("w3", [D, HID], F32, kind="ExternalInput").ap()
    io["ident"] = nc.dram_tensor("ident", [128, 128], BF16, kind="ExternalInput").ap()
    io["out"] = nc.dram_tensor("out", [NSLOT, 128, D], F32, kind="ExternalOutput").ap()
    io["ckv_d"] = nc.dram_tensor("ckv_d", [8, 128, T], BF16, kind=sk).ap()
    io["w1qn"] = nc.dram_tensor("w1qn", [HID, D], BF16, kind="Internal").ap()
    io["w2qn"] = nc.dram_tensor("w2qn", [HID, D], BF16, kind="Internal").ap()
    io["w3qn"] = nc.dram_tensor("w3qn", [D, HID], BF16, kind="Internal").ap()
    io["w3qt"] = nc.dram_tensor("w3qt", [44, 128, D], BF16, kind="Internal").ap()
    io["hq_d"] = nc.dram_tensor("hq_d", [NSLOT, 128, HID], F32, kind=sk).ap()
    io["attn_d"] = nc.dram_tensor("attn_d", [NH, 128, 1024], BF16, kind=sk).ap()
    io["hqt_d"] = nc.dram_tensor("hqt_d", [44, 128, NSLOT * 128], BF16, kind=sk).ap()
    io["x1_d"] = nc.dram_tensor("x1_d", [NSLOT, 128, D], F32, kind=sk).ap()
    for nm in ["dbg_q", "dbg_kf", "dbg_vt", "dbg_at"]:
        io[nm] = nc.dram_tensor(nm, [128, {"dbg_q": NSLOT * D, "dbg_kf": T, "dbg_vt": T,
                                           "dbg_at": NH * 1024}[nm]], BF16,
                                kind="ExternalOutput").ap() if dev else None
    io["dbg_x1"] = None

    with contextlib.ExitStack() as es:
        k = B(nc, es)
        e = dict(io)
        e["bufs"] = {}

        def bufs(*names):
            return [e["bufs"].setdefault(n, Buf(n)) for n in names]

        e["B"] = bufs
        for nm in ["ckv_d", "hq_d", "hqt_d", "x1_d", "w1qn", "w2qn", "w3qn", "w3qt",
                   "attn_d", "out_d"]:
            bufs(nm)[0].acc = True

        e["ident_s"] = k.sb("ident_s", [128, 128], BF16)
        e["qposc"] = k.sb("qposc", [128, NSLOT], F32)
        e["qrelc"] = k.sb("qrelc", [128, 1], F32)
        e["IVF"] = k.sb("IVF", [128, 64], F32)
        e["onescol"] = k.sb("onescol", [128, 1], BF16)
        e["ones128"] = k.sb("ones128", [128, 128], BF16)
        e["onescol_f"] = k.sb("onescol_f", [128, 1], F32)
        e["onesrow_f"] = k.sb("onesrow_f", [1, 128], F32)
        e["onesrow_b"] = k.sb("onesrow_b", [1, 128], BF16)
        e["identf"] = k.sb("identf", [128, 128], F32)
        e["thr"] = [k.sb(f"thrw{i}", [128, 1], F32) for i in range(3)]
        e["nthr"] = [k.sb(f"nthrw{i}", [128, 1], F32) for i in range(3)]
        e["mwb"] = [k.sb(f"mwb{i}", [128, 1], F32) for i in range(3)]

        with nc.Block() as real_blk:
            blk = FakeBlock()
            with contextlib.ExitStack() as es_t:
                sbt = lambda n, s, d: es_t.enter_context(nc.sbuf_tensor(n, s, d))
                e["CC"] = sbt("CC", [128, 64 * 128], BF16)
                e["SS"] = sbt("SS_t", [128, 64 * 128], BF16)
                e["bmq"] = sbt("bmq", [128, 1024], BF16)
                e["bmk"] = sbt("bmk", [128, 1024], BF16)
                e["qrot"] = sbt("qrot", [128, NSLOT * D], BF16)
                p0(k, blk, e)
                k.barrier(blk)
                if upto >= 1:
                    p0w(k, blk, e)
                    k.barrier(blk)
                if upto >= 2:
                    p1(k, blk, e)
                    k.barrier(blk)
                if upto >= 3:
                    p2(k, blk, e)
                    k.barrier(blk)
                if upto >= 4:
                    p3(k, blk, e)
                    k.barrier(blk)
            if upto >= 5:
                p4(k, blk, e)

            @blk.scalar
            def _(s):
                s.wait_ge(k.sem("SS"), k.cnt["SS"])

            @blk.sync
            def _(sy):
                sy.wait_ge(k.sem("SH"), k.cnt["SH"])

            @blk.gpsimd
            def _(g):
                g.wait_ge(k.sem("SG"), k.cnt["SG"])

            blk.flush(real_blk)
    return nc


def sincos(k, blk, e, ang, shape, nbuf, t1, t2, outbuf, shift, extra_reads=()):
    (b_ang, b_n, b_t1, b_t2, b_out) = e["B"](*[f"sc_{id(x)}" for x in (ang, nbuf, t1, t2, outbuf)])

    @blk.vector
    def _(v):
        k.pre(v, "V", reads=[b_ang, *extra_reads], writes=[b_t1])
        if shift != 0.0:
            k.post(v.tensor_scalar(t1[:], ang, shift, None, ALU.add), "V", [b_ang], [b_t1])
        else:
            k.post(v.tensor_copy(t1[:], ang), "V", [b_ang], [b_t1])

    @blk.scalar
    def _(s):
        k.pre(s, "A", reads=[b_t1], writes=[b_n])
        k.post(s.activation(nbuf[:], t1[:], AF.Identity, scale=INV_2PI), "A", [b_t1], [b_n])

    @blk.vector
    def _(v):
        k.pre(v, "V", reads=[b_n, b_t1], writes=[b_t2])
        k.post(v.scalar_tensor_tensor(t2[:], nbuf[:], -C1, t1[:], ALU.mult, ALU.add),
               "V", [b_n, b_t1], [b_t2])
        k.pre(v, "V", reads=[b_n, b_t2], writes=[b_t1])
        k.post(v.scalar_tensor_tensor(t1[:], nbuf[:], -C2, t2[:], ALU.mult, ALU.add),
               "V", [b_n, b_t2], [b_t1])
        k.pre(v, "V", reads=[b_t1], writes=[b_t2])
        k.post(v.tensor_scalar(t2[:], t1[:], math.pi, TWO_PI, ALU.is_gt, ALU.mult),
               "V", [b_t1], [b_t2])
        k.pre(v, "V", reads=[b_t1, b_t2], writes=[b_t1])
        k.post(v.tensor_tensor(t1[:], t1[:], t2[:], ALU.subtract), "V", [b_t1, b_t2], [b_t1])
        k.pre(v, "V", reads=[b_t1], writes=[b_t2])
        k.post(v.tensor_scalar(t2[:], t1[:], -math.pi, TWO_PI, ALU.is_lt, ALU.mult),
               "V", [b_t1], [b_t2])
        k.pre(v, "V", reads=[b_t1, b_t2], writes=[b_t1])
        k.post(v.tensor_tensor(t1[:], t1[:], t2[:], ALU.add), "V", [b_t1, b_t2], [b_t1])

    @blk.scalar
    def _(s):
        k.pre(s, "A", reads=[b_t1], writes=[b_out])
        k.post(s.activation(outbuf[:], t1[:], AF.Sin), "A", [b_t1], [b_out])
    return b_out


def p0(k, blk, e):
    nc = k.nc
    B = e["B"]
    es0 = contextlib.ExitStack()
    sb = lambda n, s, d: es0.enter_context(nc.sbuf_tensor(n, s, d))
    POS = sb("p0_pos", [128, 64], F32)
    J = sb("p0_j", [128, 64], F32)
    ANG = sb("p0_ang", [128, 4096], F32)
    NN = sb("p0_n", [128, 4096], I32)
    S1 = sb("p0_s1", [128, 4096], F32)
    S2 = sb("p0_s2", [128, 4096], F32)
    SINX = sb("p0_sin", [128, 4096], F32)
    COSX = sb("p0_cos", [128, 4096], F32)
    KJ = sb("p0_kj", [128, 1024], F32)
    e["_es0"] = es0
    (b_id, b_qp, b_qr, b_pos, b_j, b_kj, b_ivf, b_ang, b_cc, b_ss, b_bmq, b_ones) = B(
        "ident", "qposc", "qrelc", "pos", "j", "kj", "ivf", "ang", "CC", "SS", "bmq", "ones")

    @blk.sync
    def _(sy):
        k.post(sy.dma_start(out=e["ident_s"][:], in_=e["ident"][:, :]), "SH", [], [b_id])
        k.post(sy.dma_start(out=e["qposc"][:], in_=e["qpos_c"][:, :]), "SH", [], [b_qp])
        k.post(sy.dma_start(out=e["qrelc"][:], in_=e["qrel_c"][:, :]), "SH", [], [b_qr])

    @blk.gpsimd
    def _(g):
        k.post(g.iota(POS[:], pattern=[[128, 64]], base=0, channel_multiplier=1,
                      allow_small_or_imprecise_dtypes=True), "G", [], [b_pos])
        k.post(g.iota(J[:], pattern=[[1, 64]], base=0, channel_multiplier=0,
                      allow_small_or_imprecise_dtypes=True), "G", [], [b_j])
        k.post(g.iota(KJ[:], pattern=[[1, 1024]], base=0, channel_multiplier=0,
                      allow_small_or_imprecise_dtypes=True), "G", [], [b_kj])

    @blk.scalar
    def _(s):
        k.pre(s, "A", reads=[b_j], writes=[b_ivf])
        k.post(s.activation(e["IVF"][:], J[:], AF.Exp, scale=-LN1E4 / 64.0), "A", [b_j], [b_ivf])

    @blk.vector
    def _(v):
        k.post(v.memset(e["onescol"][:], 1.0), "V", [], [b_ones])
        k.post(v.memset(e["ones128"][:], 1.0), "V", [], [e["B"]("ones128")[0]])
        k.post(v.memset(e["onescol_f"][:], 1.0), "V", [], [e["B"]("onescol_f")[0]])
        k.post(v.memset(e["onesrow_f"][:], 1.0), "V", [], [e["B"]("onesrow_f")[0]])
        k.post(v.memset(e["onesrow_b"][:], 1.0), "V", [], [e["B"]("onesrow_b")[0]])
        k.pre(v, "V", reads=[b_id], writes=[e["B"]("identf")[0]])
        k.post(v.tensor_copy(e["identf"][:], e["ident_s"][:]), "V", [b_id],
               [e["B"]("identf")[0]])
        k.pre(v, "V", reads=[b_pos, b_ivf], writes=[b_ang])
        k.post(v.tensor_tensor(
            ANG[:],
            POS[:].rearrange("p (kt one) -> p kt one", one=1).broadcast_to([128, 64, 64]),
            e["IVF"][:].rearrange("p (one j) -> p one j", one=1).broadcast_to([128, 64, 64]),
            ALU.mult), "V", [b_pos, b_ivf], [b_ang])

    b_sin = sincos(k, blk, e, ANG[:], None, NN, S1, S2, SINX, 0.0, extra_reads=[b_ang])
    b_cos = sincos(k, blk, e, ANG[:], None, NN, S1, S2, COSX, math.pi / 2.0, extra_reads=[b_ang])

    @blk.vector
    def _(v):
        cview = COSX[:].rearrange("p (kt j) -> p kt j", j=64)
        sview = SINX[:].rearrange("p (kt j) -> p kt j", j=64)
        cc = e["CC"][:].rearrange("p (kt j) -> p kt j", j=128)
        ss = e["SS"][:].rearrange("p (kt j) -> p kt j", j=128)
        k.pre(v, "V", reads=[b_sin, b_cos], writes=[b_cc, b_ss])
        k.post(v.tensor_copy(cc[:, :, 0:128:2], cview), "V", [b_cos], [b_cc])
        k.post(v.tensor_copy(cc[:, :, 1:128:2], cview), "V", [b_cos], [b_cc])
        k.post(v.tensor_copy(ss[:, :, 1:128:2], sview), "V", [b_sin], [b_ss])
        k.post(v.tensor_scalar(ss[:, :, 0:128:2], sview, -1.0, None, ALU.mult),
               "V", [b_sin], [b_ss])
        k.pre(v, "V", reads=[b_kj, b_qr], writes=[b_bmq])
        k.post(v.tensor_scalar(e["bmq"][:], KJ[:], e["qrelc"][:, 0:1], None, ALU.is_le),
               "V", [b_kj, b_qr], [b_bmq])
    ICOL = sb("p0_icol", [128, 1], F32)
    TC0 = sb("p0_tc0", [128, 1], F32)
    TCJ = sb("p0_tcj", [128, 8], F32)
    b_icol, b_tc0, b_tcj, b_bmk = B("p0_icol", "p0_tc0", "p0_tcj", "bmk")

    @blk.gpsimd
    def _(g):
        k.post(g.iota(ICOL[:], pattern=[[1, 1]], base=0, channel_multiplier=1,
                      allow_small_or_imprecise_dtypes=True), "G", [], [b_icol])

    @blk.vector
    def _(v):
        k.pre(v, "V", reads=[b_icol, b_qr], writes=[b_tc0])
        k.post(v.scalar_tensor_tensor(TC0[:], ICOL[:], 2.0, e["qrelc"][:], ALU.mult,
                                      ALU.subtract), "V", [b_icol, b_qr], [b_tc0])
        for jm in range(8):
            k.pre(v, "V", reads=[b_tc0], writes=[b_tcj])
            k.post(v.tensor_scalar(TCJ[:, jm:jm + 1], TC0[:], float(jm * 128), None,
                                   ALU.add), "V", [b_tc0], [b_tcj])
        for jm in range(8):
            k.pre(v, "V", reads=[b_tcj, b_kj], writes=[b_bmk])
            k.post(v.tensor_scalar(e["bmk"][:, jm * 128:(jm + 1) * 128], KJ[:, 0:128],
                                   TCJ[:, jm:jm + 1], None, ALU.is_ge),
                   "V", [b_tcj, b_kj], [b_bmk])
    e.pop("_es0").close()


def p0w(k, blk, e):
    nc = k.nc
    B = e["B"]
    CH = 4096
    nchunk = HID * D // (128 * CH)
    wl = [(e["w1"], None), (e["w2"], None), (e["w3"], None)]

    esw = contextlib.ExitStack()
    e["_esw"] = esw
    sbw = lambda n, s, d: esw.enter_context(nc.sbuf_tensor(n, s, d))
    wbuf = [sbw(f"p0w_buf{i}", [128, CH], F32) for i in range(2)]
    abscr = [sbw(f"p0w_abs{i}", [128, CH], F32) for i in range(2)]
    acc = sbw("p0w_acc", [128, 3 * nchunk + 3], F32)
    accr = sbw("p0w_accr", [128, nchunk], F32)
    csb = sbw("p0w_csb", [128, 4], F32)
    tot = sbw("p0w_tot", [1, 16], F32)
    ps_tot = esw.enter_context(nc.psum_tensor("p0w_pstot", [1, 128], F32))

    b_wb = B("p0w_wb0", "p0w_wb1")
    b_scr = B("p0w_scr0", "p0w_scr1")
    (b_acc, b_csb, b_tot, b_pst, b_ones) = B("p0w_acc", "p0w_csb", "p0w_tot", "p0w_pst", "ones")

    for wi, (wsrc, wdst) in enumerate(wl):
        wflat = wsrc.rearrange("a b -> (a b)").rearrange("(r p c) -> r p c", p=128, c=CH)
        for ci in range(nchunk):
            pa = ci % 2

            @blk.sync
            def _(sy, pa=pa, wflat=wflat, ci=ci):
                k.pre(sy, "SH", writes=[b_wb[pa]])
                k.post(sy.dma_start(out=wbuf[pa][:], in_=wflat[ci, :, :]), "SH", [], [b_wb[pa]])

            @blk.scalar
            def _(s, pa=pa, wi=wi, ci=ci):
                k.pre(s, "A", reads=[b_wb[pa]], writes=[b_scr[pa], b_acc])
                k.post(s.activation(abscr[pa][:], wbuf[pa][:], AF.Abs,
                                    accum_out=acc[:, wi * nchunk + ci: wi * nchunk + ci + 1]),
                       "A", [b_wb[pa]], [b_scr[pa], b_acc])


    CL2 = 64.0
    CL3 = CL2 * nchunk
    b_onesf = B("onescol_f")[0]

    @blk.vector
    def _(v):
        k.pre(v, "V", reads=[b_acc], writes=[b_acc, b_csb])
        for wi in range(3):
            k.post(v.tensor_scalar(accr[:],
                                   acc[:, wi * nchunk:(wi + 1) * nchunk], -CL2, None,
                                   ALU.add), "V", [b_acc], [b_acc])
            v.drain()
            k.post(v.reduce_sum(acc[:, 3 * nchunk + wi: 3 * nchunk + wi + 1],
                                accr[:], axis=AX),
                   "V", [b_acc], [b_acc])
            v.drain()
        k.pre(v, "V", reads=[b_acc], writes=[b_csb])
        k.post(v.tensor_copy(csb[:, 0:3], acc[:, 3 * nchunk:3 * nchunk + 3]),
               "V", [b_acc], [b_csb])

    @blk.tensor
    def _(t):
        k.pre(t, "T", reads=[b_csb, b_onesf], writes=[b_pst])
        for wi in range(3):
            k.post(t.matmul(ps_tot[0:1, wi * 8:wi * 8 + 1], csb[:, wi:wi + 1],
                            e["onescol_f"][:], start=True, stop=True, skip_group_check=True),
                   "T", [b_csb, b_onesf], [b_pst])

    @blk.vector
    def _(v):
        k.pre(v, "V", reads=[b_pst], writes=[b_tot])
        for wi in range(3):
            k.post(v.tensor_scalar(tot[0:1, wi:wi + 1], ps_tot[0:1, wi * 8:wi * 8 + 1],
                                   float(128.0 * CL3), None, ALU.add),
                   "V", [b_pst], [b_tot])

    b_mwb = B("mwb0", "mwb1", "mwb2")
    b_thr = B("thr0", "thr1", "thr2")
    b_nth = B("nthr0", "nthr1", "nthr2")
    b_orf = B("onesrow_f")[0]
    b_psbc = B("p0w_psbc")[0]
    ps_bc = esw.enter_context(nc.psum_tensor("p0w_psbc", [128, 4], F32))

    @blk.tensor
    def _(t):
        k.pre(t, "T", reads=[b_tot, b_orf], writes=[b_psbc])
        for wi in range(3):
            k.post(t.matmul(ps_bc[:, wi:wi + 1], e["onesrow_f"][0:1, :],
                            tot[0:1, wi:wi + 1], start=(wi == 0), stop=(wi == 2),
                            skip_group_check=True), "T", [b_tot, b_orf], [b_psbc])

    @blk.scalar
    def _(s):
        k.pre(s, "A", reads=[b_psbc], writes=list(b_mwb))
        for wi in range(3):
            k.post(s.copy(e["mwb"][wi][:], ps_bc[:, wi:wi + 1]), "A", [b_psbc], [b_mwb[wi]])

    CH2 = 512
    nch2 = HID * D // (128 * CH2)
    e["p0w2_sched"] = []
    for wi, (wsrc, qdst) in enumerate([(e["w1"], e["w1qn"]), (e["w2"], e["w2qn"]),
                                       (e["w3"], e["w3qn"])]):
        wf = wsrc.rearrange("a b -> (a b)").rearrange("(r p c) -> r p c", p=128, c=CH2)
        qf = qdst.rearrange("a b -> (a b)").rearrange("(r p c) -> r p c", p=128, c=CH2)
        for ci in range(nch2):
            e["p0w2_sched"].append((wf, qf, ci, wi))

    @blk.vector
    def _(v):
        numel = float(HID * D)
        for wi in range(3):
            k.pre(v, "V", reads=[b_mwb[wi]], writes=[b_mwb[wi]])
            k.post(v.tensor_scalar(e["mwb"][wi][:], e["mwb"][wi][:], 1.0 / numel, 1e-5,
                                   ALU.mult, ALU.max), "V", [b_mwb[wi]], [b_mwb[wi]])
            k.pre(v, "V", reads=[b_mwb[wi]], writes=[b_thr[wi], b_nth[wi]])
            k.post(v.tensor_scalar(e["thr"][wi][:], e["mwb"][wi][:], 0.5, None, ALU.mult),
                   "V", [b_mwb[wi]], [b_thr[wi]])
            k.post(v.tensor_scalar(e["nthr"][wi][:], e["mwb"][wi][:], -0.5, None, ALU.mult),
                   "V", [b_mwb[wi]], [b_nth[wi]])
    e.pop("_esw").close()


def p0w2_emit(k, blk, e, frac_lo, frac_hi, bufs2):
    B = e["B"]
    sched = e["p0w2_sched"]
    n = len(sched)
    lo = int(round(frac_lo * n))
    hi = int(round(frac_hi * n))
    wc2, pos2, q2 = bufs2
    b_wc2 = B("p2_wc2_0", "p2_wc2_1")
    b_pos2 = B("p2_pos2")[0]
    b_q2 = B("p2_q2_0", "p2_q2_1")
    b_thr = B("thr0", "thr1", "thr2")
    b_nth = B("nthr0", "nthr1", "nthr2")
    b_qn = B("w1qn", "w2qn", "w3qn")
    for idx in range(lo, hi):
        wf, qf, ci, wi = sched[idx]
        pa = idx % 2

        @blk.sync
        def _(sy, wf=wf, ci=ci, pa=pa):
            k.pre(sy, "SH", writes=[b_wc2[pa]])
            k.post(sy.dma_start(out=wc2[pa][:], in_=wf[ci, :, :]), "SH", [], [b_wc2[pa]])

        @blk.vector
        def _(v, pa=pa, wi=wi):
            k.pre(v, "V", reads=[b_wc2[pa], b_thr[wi]], writes=[b_pos2])
            k.post(v.tensor_scalar(pos2[:], wc2[pa][:], e["thr"][wi][:, 0:1], None,
                                   ALU.is_ge), "V", [b_wc2[pa], b_thr[wi]], [b_pos2])
            k.pre(v, "V", reads=[b_wc2[pa], b_pos2, b_nth[wi]], writes=[b_q2[pa]])
            k.post(v.scalar_tensor_tensor(q2[pa][:], wc2[pa][:], e["nthr"][wi][:, 0:1],
                                          pos2[:], ALU.is_le, ALU.subtract),
                   "V", [b_wc2[pa], b_pos2, b_nth[wi]], [b_q2[pa]])

        @blk.scalar
        def _(s, qf=qf, ci=ci, pa=pa, wi=wi):
            k.pre(s, "SS", reads=[b_q2[pa]], writes=[b_qn[wi]])
            k.post(s.dma_start(out=qf[ci, :, :], in_=q2[pa][:]), "SS", [b_q2[pa]], [b_qn[wi]])
        yield

def p1(k, blk, e):
    nc = k.nc
    B = e["B"]
    b_qrot = B("qrot")[0]
    b_id = B("ident")[0]
    b_ckvd = B("ckv_d")[0]

    with contextlib.ExitStack() as es1:
        sb = lambda n, s, d: es1.enter_context(nc.sbuf_tensor(n, s, d))
        psa = lambda n, s, d: es1.enter_context(nc.psum_tensor(n, s, d))
        es_a = contextlib.ExitStack()
        sba = lambda n, s, d: es_a.enter_context(nc.sbuf_tensor(n, s, d))
        wdn_sb = sba("p1_wdn", [128, 16 * 1024], BF16)
        xb = [sba(f"p1_xb{i}", [128, D], BF16) for i in range(2)]
        xT = sba("p1_xT", [128, 16 * 1024], BF16)
        ckv_ev = [sba(f"p1_ckve{i}", [128, 1024], BF16) for i in range(2)]
        es_ps1 = contextlib.ExitStack()
        ptr = [es_ps1.enter_context(nc.psum_tensor(f"p1_ptr{i}", [128, 128], BF16))
               for i in range(4)]
        ps_ckv = [es_ps1.enter_context(nc.psum_tensor(f"p1_psckv{i}", [128, 1024], F32))
                  for i in range(2)]

        b_wdn = B("p1_wdn")[0]
        b_xb = B("p1_xb0", "p1_xb1")
        b_xT = B("p1_xT")[0]
        b_ckve = B("p1_ckve0", "p1_ckve1")
        b_ptr = B(*[f"p1_ptr{i}" for i in range(4)])
        b_psckv = B("p1_psckv0", "p1_psckv1")

        @blk.gpsimd
        def _(g):
            k.pre(g, "SG", writes=[b_wdn])
            k.post(g.dma_start(out=wdn_sb[:].rearrange("p (kd l) -> p kd l", kd=16),
                               in_=e["wdn"].rearrange("(kd p) l -> p kd l", p=128)),
                   "SG", [], [b_wdn])

        for bp in range(8):
            for sub in range(8):
                tt_i = bp * 8 + sub
                pa = sub % 2

                @blk.gpsimd
                def _(g, pa=pa, tt_i=tt_i):
                    k.pre(g, "SG", writes=[b_xb[pa]])
                    k.post(g.dma_start(out=xb[pa][:],
                                       in_=e["xf"][tt_i * 128:(tt_i + 1) * 128, :]),
                           "SG", [], [b_xb[pa]])

                for qb in range(4):
                    @blk.tensor
                    def _(t, pa=pa, qb=qb):
                        k.pre(t, "T", reads=[b_xb[pa], b_id], writes=b_ptr)
                        for i in range(4):
                            kd = qb * 4 + i
                            k.post(t.transpose(ptr[i][:], xb[pa][:, kd * 128:(kd + 1) * 128],
                                               e["ident_s"][:]),
                                   "T", [b_xb[pa], b_id], [b_ptr[i]])

                    @blk.vector
                    def _(v, sub=sub, qb=qb):
                        k.pre(v, "V", reads=b_ptr, writes=[b_xT])
                        ins = None
                        for i in range(4):
                            kd = qb * 4 + i
                            ins = v.tensor_copy(
                                xT[:, kd * 1024 + sub * 128: kd * 1024 + (sub + 1) * 128],
                                ptr[i][:])
                        k.post(ins, "V", b_ptr, [b_xT])

            for l in range(8):
                pa = l % 2

                @blk.tensor
                def _(t, l=l, pa=pa):
                    k.pre(t, "T", reads=[b_xT, b_wdn], writes=[b_psckv[pa]])
                    for kd in range(16):
                        ws = wdn_sb[:, kd * 1024 + l * 128: kd * 1024 + (l + 1) * 128]
                        t.ldweights(ws)
                        ins = None
                        for b2 in range(2):
                            ins = t.matmul(
                                ps_ckv[pa][:, b2 * 512:(b2 + 1) * 512],
                                ws, xT[:, kd * 1024 + b2 * 512: kd * 1024 + (b2 + 1) * 512],
                                start=(kd == 0), stop=(kd == 15), skip_group_check=True)
                    k.post(ins, "T", [b_xT, b_wdn], [b_psckv[pa]])

                @blk.vector
                def _(v, l=l, pa=pa):
                    k.pre(v, "V", reads=[b_psckv[pa]], writes=[b_ckve[pa]])
                    k.post(v.tensor_copy(ckv_ev[pa][:], ps_ckv[pa][:]),
                           "V", [b_psckv[pa]], [b_ckve[pa]])

                @blk.scalar
                def _(s, l=l, pa=pa, bp=bp):
                    k.pre(s, "SS", reads=[b_ckve[pa]], writes=[b_ckvd])
                    k.post(s.dma_start(out=e["ckv_d"][l, :, bp * 1024:(bp + 1) * 1024],
                                       in_=ckv_ev[pa][:]),
                           "SS", [b_ckve[pa]], [b_ckvd])

        es_a.close()
        k.barrier(blk)
        xoT = sb("p1_xoT", [128, 16 * NSLOT * 128], BF16)
        xob = [sb(f"p1_xob{i}", [128, D], F32) for i in range(2)]
        xobb = sb("p1_xobb", [128, D], BF16)
        b_xoT = B("p1_xoT")[0]
        b_xob = B("p1_xob0", "p1_xob1")
        b_xobb = B("p1_xobb")[0]

        for s_ in range(NSLOT):
            pa = s_ % 2

            @blk.sync
            def _(sy, pa=pa, s_=s_):
                k.pre(sy, "SH", writes=[b_xob[pa]])
                k.post(sy.dma_start(out=xob[pa][:], in_=e["xo"][s_, :, :]),
                       "SH", [], [b_xob[pa]])

            @blk.vector
            def _(v, pa=pa):
                k.pre(v, "V", reads=[b_xob[pa]], writes=[b_xobb])
                k.post(v.tensor_copy(xobb[:], xob[pa][:]), "V", [b_xob[pa]], [b_xobb])

            for qb in range(4):
                @blk.tensor
                def _(t, qb=qb):
                    k.pre(t, "T", reads=[b_xobb, b_id], writes=b_ptr)
                    for i in range(4):
                        kd = qb * 4 + i
                        k.post(t.transpose(ptr[i][:], xobb[:, kd * 128:(kd + 1) * 128],
                                           e["ident_s"][:]), "T", [b_xobb, b_id], [b_ptr[i]])

                @blk.vector
                def _(v, s_=s_, qb=qb):
                    k.pre(v, "V", reads=b_ptr, writes=[b_xoT])
                    ins = None
                    for i in range(4):
                        kd = qb * 4 + i
                        ins = v.tensor_copy(
                            xoT[:, kd * 1024 + s_ * 128: kd * 1024 + s_ * 128 + 128], ptr[i][:])
                    k.post(ins, "V", b_ptr, [b_xoT])

        ccq = [sb(f"p1_ccq{i}", [128, 128], BF16) for i in range(NSLOT)]
        ssq = [sb(f"p1_ssq{i}", [128, 128], BF16) for i in range(NSLOT)]
        angq = sb("p1_angq", [128, 64], F32)
        nq = sb("p1_nq", [128, 64], I32)
        tq1 = sb("p1_tq1", [128, 64], F32)
        tq2 = sb("p1_tq2", [128, 64], F32)
        sinq = sb("p1_sinq", [128, 64], F32)
        cosq = sb("p1_cosq", [128, 64], F32)
        b_ccq = B(*[f"p1_ccq{i}" for i in range(NSLOT)])
        b_ssq = B(*[f"p1_ssq{i}" for i in range(NSLOT)])
        b_angq, b_ivf, b_qp = B("p1_angq", "ivf", "qposc")

        for s_ in range(NSLOT):
            @blk.vector
            def _(v, s_=s_):
                k.pre(v, "V", reads=[b_ivf, b_qp], writes=[b_angq])
                k.post(v.tensor_scalar(angq[:], e["IVF"][:], e["qposc"][:, s_:s_ + 1],
                                       None, ALU.mult), "V", [b_ivf, b_qp], [b_angq])
            b_sq = sincos(k, blk, e, angq[:], None, nq, tq1, tq2, sinq, 0.0, [b_angq])
            b_cq = sincos(k, blk, e, angq[:], None, nq, tq1, tq2, cosq, math.pi / 2.0, [b_angq])

            @blk.vector
            def _(v, s_=s_):
                k.pre(v, "V", reads=[b_sq, b_cq], writes=[b_ccq[s_], b_ssq[s_]])
                k.post(v.tensor_copy(ccq[s_][:, 0:128:2], cosq[:]), "V", [b_cq], [b_ccq[s_]])
                k.post(v.tensor_copy(ccq[s_][:, 1:128:2], cosq[:]), "V", [b_cq], [b_ccq[s_]])
                k.post(v.tensor_copy(ssq[s_][:, 1:128:2], sinq[:]), "V", [b_sq], [b_ssq[s_]])
                k.post(v.tensor_scalar(ssq[s_][:, 0:128:2], sinq[:], -1.0, None, ALU.mult),
                       "V", [b_sq], [b_ssq[s_]])

        es_ps1.close()
        k.barrier(blk)
        wqb = [sb(f"p1_wqb{i}", [128, 1024], BF16) for i in range(2)]
        ps_q = [psa(f"p1_psq{i}", [128, 1024], F32) for i in range(4)]
        rt1 = sb("p1_rt1", [128, 1024], BF16)
        rt2 = sb("p1_rt2", [128, 1024], BF16)
        b_wqb = B("p1_wqb0", "p1_wqb1")
        b_psq = B(*[f"p1_psq{i}" for i in range(4)])
        b_rt1, b_rt2 = B("p1_rt1", "p1_rt2")
        wq_r = e["wq"].rearrange("(kd p) n -> kd p n", p=128)

        for sq in range(2):
            slots = [4 * sq + i for i in range(4)]
            for hf in range(2):
                for kd in range(16):
                    pa = kd % 2

                    @blk.gpsimd
                    def _(g, kd=kd, hf=hf, pa=pa):
                        k.pre(g, "SG", writes=[b_wqb[pa]])
                        k.post(g.dma_start(out=wqb[pa][:],
                                           in_=wq_r[kd, :, hf * 1024:(hf + 1) * 1024]),
                               "SG", [], [b_wqb[pa]])

                    @blk.tensor
                    def _(t, kd=kd, pa=pa, slots=slots):
                        k.pre(t, "T", reads=[b_wqb[pa], b_xoT], writes=b_psq)
                        ins = None
                        for si, s_ in enumerate(slots):
                            xs = xoT[:, kd * 1024 + s_ * 128: kd * 1024 + s_ * 128 + 128]
                            t.ldweights(xs)
                            for b2 in range(2):
                                ins = t.matmul(ps_q[si][:, b2 * 512:(b2 + 1) * 512], xs,
                                               wqb[pa][:, b2 * 512:(b2 + 1) * 512],
                                               start=(kd == 0), stop=(kd == 15),
                                               skip_group_check=True)
                        k.post(ins, "T", [b_wqb[pa], b_xoT], list(b_psq))

                for si, s_ in enumerate(slots):
                    @blk.vector
                    def _(v, si=si, s_=s_, hf=hf):
                        src = ps_q[si][:]
                        swp = src.rearrange("p (h pr two) -> p h pr two", pr=64, two=2)[
                            :, :, :, ::-1]
                        ccb = ccq[s_][:].rearrange("p (one j) -> p one j", one=1).broadcast_to(
                            [128, 8, 128])
                        ssb = ssq[s_][:].rearrange("p (one j) -> p one j", one=1).broadcast_to(
                            [128, 8, 128])
                        k.pre(v, "V", reads=[b_psq[si], b_ccq[s_], b_ssq[s_]],
                              writes=[b_rt1, b_rt2])
                        k.post(v.tensor_tensor(rt1[:], src, ccb, ALU.mult),
                               "V", [b_psq[si], b_ccq[s_]], [b_rt1])
                        k.post(v.tensor_tensor(rt2[:], swp, ssb, ALU.mult),
                               "V", [b_psq[si], b_ssq[s_]], [b_rt2])
                        k.pre(v, "V", reads=[b_rt1, b_rt2], writes=[b_qrot])
                        k.post(v.tensor_tensor(
                            e["qrot"][:, s_ * D + hf * 1024: s_ * D + (hf + 1) * 1024],
                            rt1[:], rt2[:], ALU.add), "V", [b_rt1, b_rt2], [b_qrot])

    if e["dbg_q"] is not None:
        @blk.scalar
        def _(s):
            k.pre(s, "SS", reads=[b_qrot])
            k.post(s.dma_start(out=e["dbg_q"][:, :], in_=e["qrot"][:]), "SS", [b_qrot], [])


def p2(k, blk, e):
    nc = k.nc
    B = e["B"]
    b_id = B("ident")[0]
    b_qrot = B("qrot")[0]
    b_ckvd = B("ckv_d")[0]
    b_cc, b_ss, b_bmq = B("CC", "SS", "bmq")

    with contextlib.ExitStack() as es2:
        sb = lambda n, s, d: es2.enter_context(nc.sbuf_tensor(n, s, d))
        psa = lambda n, s, d: es2.enter_context(nc.psum_tensor(n, s, d))
        Kf = [sb(f"p2_Kf{i}", [128, T], BF16) for i in range(2)]
        Vt = [sb(f"p2_Vt{i}", [128, T], BF16) for i in range(2)]
        attn_sb = sb("p2_attnsb", [128, 1024], BF16)
        ckvq = [sb(f"p2_ckvq{i}", [128, 8 * 1024], BF16) for i in range(2)]
        wkv = sb("p2_wkv", [128, 8 * 256], BF16)
        kraw = [sb(f"p2_kraw{i}", [128, 1024], BF16) for i in range(2)]
        ktok = [sb(f"p2_ktok{i}", [128, 512], BF16) for i in range(2)]
        rr1 = [sb(f"p2_rr1_{i}", [128, 512], F32) for i in range(2)]
        rr2 = [sb(f"p2_rr2_{i}", [128, 512], F32) for i in range(2)]
        qfq = sb("p2_qfq", [128, 512], BF16)
        PT = [sb(f"p2_PT{i}", [128, 512], BF16) for i in range(2)]
        ptmpb = [sb(f"p2_ptmpb{i}", [128, 128], BF16) for i in range(2)]
        zq = sb("p2_zq", [1, 512], F32)
        zr = sb("p2_zr", [1, 512], F32)
        zbc = sb("p2_zbc", [128, 512], F32)
        wc2 = [sb(f"p2_wc2_{i}", [128, 512], F32) for i in range(2)]
        pos2 = sb("p2_pos2", [128, 512], BF16)
        q2 = [sb(f"p2_q2_{i}", [128, 512], BF16) for i in range(2)]

        psKV = psa("p2_psKV", [128, 1024], F32)
        psc = [psa(f"p2_psc{i}", [128, 512], F32) for i in range(2)]
        psAV = psa("p2_psAV", [128, 512], F32)
        psZr = psa("p2_psZr", [128, 512], F32)
        ptr = [psa(f"p2_ptr{i}", [128, 128], BF16) for i in range(2)]

        b_Kf = B("p2_Kf0", "p2_Kf1")
        b_Vt = B("p2_Vt0", "p2_Vt1")
        b_wkv = B("p2_wkv")[0]
        b_attnsb = B("p2_attnsb")[0]
        b_attnd = B("attn_d")[0]
        b_ckvq = B("p2_ckvq0", "p2_ckvq1")
        b_kraw = B("p2_kraw0", "p2_kraw1")
        b_ktok = B("p2_ktok0", "p2_ktok1")
        b_rr1 = B("p2_rr1_0", "p2_rr1_1")
        b_rr2 = B("p2_rr2_0", "p2_rr2_1")
        b_qfq, b_PTb0, b_PTb1 = B("p2_qfq", "p2_PT0", "p2_PT1")
        b_ptmpb = B("p2_ptmpb0", "p2_ptmpb1")
        b_PT = [b_PTb0, b_PTb1]
        b_zq, b_zr, b_zbc = B("p2_zq", "p2_zr", "p2_zbc")
        b_idf = B("identf")[0]
        b_kvps = B("p2_psKV_kv")[0]
        b_psc = B("p2_psc0", "p2_psc1")
        b_psAV, b_psZr = B("p2_psAV", "p2_psZr")
        b_ptr = B(*[f"p2_ptr{i}" for i in range(2)])
        b_bmk = B("bmk")[0]
        b_ones = B("ones")[0]

        wup_r = e["wup"].rearrange("(l p) n -> l p n", p=128)

        def a_emit(h):
            par = h % 2
            KfA, VtA = Kf[par], Vt[par]
            b_KfA, b_VtA = b_Kf[par], b_Vt[par]

            yield from p0w2_emit(k, blk, e, h / NH, (h + 1) / NH, (wc2, pos2, q2))

            @blk.gpsimd
            def _(g, h=h):
                k.pre(g, "SG", writes=[b_wkv])
                k.post(g.dma_start(out=wkv[:].rearrange("p (l c) -> p l c", l=8),
                                   in_=wup_r[:, :, h * 256:(h + 1) * 256]), "SG", [], [b_wkv])
            yield

            for er in range(8):
                cpa = er % 2

                @blk.sync
                def _(sy, er=er, cpa=cpa):
                    k.pre(sy, "SH", reads=[b_ckvd], writes=[b_ckvq[cpa]])
                    k.post(sy.dma_start(out=ckvq[cpa][:].rearrange("p (l t) -> p l t", l=8),
                                        in_=e["ckv_d"][:, :, er * 1024:(er + 1) * 1024]),
                           "SH", [b_ckvd], [b_ckvq[cpa]])
                yield

                for part in range(2):
                    base = part * 128
                    rpa = part

                    @blk.tensor
                    def _(t, base=base, cpa=cpa, rpa=rpa):
                        k.pre(t, "T", reads=[b_ckvq[cpa], b_wkv], writes=[b_kvps])
                        ins = None
                        for l in range(8):
                            w = wkv[:, l * 256 + base: l * 256 + base + 128]
                            t.ldweights(w)
                            for b in range(2):
                                ins = t.matmul(
                                    psKV[:, b * 512:(b + 1) * 512], w,
                                    ckvq[cpa][:, l * 1024 + b * 512: l * 1024 + (b + 1) * 512],
                                    start=(l == 0), stop=(l == 7), skip_group_check=True)
                        k.post(ins, "T", [b_ckvq[cpa], b_wkv], [b_kvps])

                    @blk.scalar
                    def _(s, rpa=rpa):
                        k.pre(s, "A", reads=[b_kvps], writes=[b_kraw[rpa]])
                        k.post(s.activation(kraw[rpa][:], psKV[:], AF.Identity),
                               "A", [b_kvps], [b_kraw[rpa]])
                    yield

                    if part == 1:
                        for g2 in range(4):
                            @blk.tensor
                            def _(t, g2=g2, rpa=rpa):
                                k.pre(t, "T", reads=[b_kraw[rpa], b_id], writes=list(b_ptr))
                                for i in range(2):
                                    kt = g2 * 2 + i
                                    k.post(t.transpose(ptr[i][:],
                                                       kraw[rpa][:, kt * 128:(kt + 1) * 128],
                                                       e["ident_s"][:]),
                                           "T", [b_kraw[rpa], b_id], [b_ptr[i]])

                            @blk.scalar
                            def _(s, er=er, g2=g2, VtA=VtA, b_VtA=b_VtA):
                                k.pre(s, "A", reads=list(b_ptr), writes=[b_VtA])
                                ins = None
                                for i in range(2):
                                    kt_g = er * 8 + g2 * 2 + i
                                    ins = s.activation(VtA[:, kt_g * 128:(kt_g + 1) * 128],
                                                       ptr[i][:], AF.Identity)
                                k.post(ins, "A", list(b_ptr), [b_VtA])
                            if g2 % 2 == 1:
                                yield
                    else:
                        for b4 in range(2):
                            gpa = b4

                            for h2 in range(2):
                                @blk.tensor
                                def _(t, b4=b4, h2=h2, rpa=rpa):
                                    k.pre(t, "T", reads=[b_kraw[rpa], b_id], writes=list(b_ptr))
                                    for i in range(2):
                                        kt = b4 * 4 + h2 * 2 + i
                                        k.post(t.transpose(ptr[i][:],
                                                           kraw[rpa][:, kt * 128:(kt + 1) * 128],
                                                           e["ident_s"][:]),
                                               "T", [b_kraw[rpa], b_id], [b_ptr[i]])

                                @blk.vector
                                def _(v, er=er, b4=b4, h2=h2, gpa=gpa):
                                    bkt = er * 8 + b4 * 4 + h2 * 2
                                    k.pre(v, "V", reads=list(b_ptr) + [b_cc, b_ss],
                                          writes=[b_rr1[gpa], b_rr2[gpa]])
                                    i1 = i2 = None
                                    for i in range(2):
                                        col = (h2 * 2 + i) * 128
                                        srcp = ptr[i][:]
                                        swp = srcp.rearrange("p (pr two) -> p pr two", two=2)[
                                            :, :, ::-1]
                                        i1 = v.tensor_tensor(
                                            rr1[gpa][:, col:col + 128], srcp,
                                            e["CC"][:, (bkt + i) * 128:(bkt + i + 1) * 128],
                                            ALU.mult)
                                        i2 = v.tensor_tensor(
                                            rr2[gpa][:, col:col + 128], swp,
                                            e["SS"][:, (bkt + i) * 128:(bkt + i + 1) * 128],
                                            ALU.mult)
                                    k.post(i1, "V", list(b_ptr) + [b_cc], [b_rr1[gpa]])
                                    k.post(i2, "V", list(b_ptr) + [b_ss], [b_rr2[gpa]])

                            @blk.gpsimd
                            def _(g, gpa=gpa):
                                k.pre(g, "G", reads=[b_rr1[gpa], b_rr2[gpa]],
                                      writes=[b_ktok[gpa]])
                                k.post(g.tensor_tensor(ktok[gpa][:], rr1[gpa][:], rr2[gpa][:],
                                                       ALU.add),
                                       "G", [b_rr1[gpa], b_rr2[gpa]], [b_ktok[gpa]])
                            yield

                            for h2 in range(2):
                                @blk.tensor
                                def _(t, h2=h2, gpa=gpa):
                                    k.pre(t, "T", reads=[b_ktok[gpa], b_id], writes=list(b_ptr))
                                    for i in range(2):
                                        kt = h2 * 2 + i
                                        k.post(t.transpose(ptr[i][:],
                                                           ktok[gpa][:, kt * 128:(kt + 1) * 128],
                                                           e["ident_s"][:]),
                                               "T", [b_ktok[gpa], b_id], [b_ptr[i]])

                                @blk.scalar
                                def _(s, er=er, b4=b4, h2=h2, KfA=KfA, b_KfA=b_KfA):
                                    k.pre(s, "A", reads=list(b_ptr), writes=[b_KfA])
                                    ins = None
                                    for i in range(2):
                                        kt_g = er * 8 + b4 * 4 + h2 * 2 + i
                                        ins = s.activation(KfA[:, kt_g * 128:(kt_g + 1) * 128],
                                                           ptr[i][:], AF.Identity)
                                    k.post(ins, "A", list(b_ptr), [b_KfA])
                            yield

        def b_emit(h):
            par = h % 2
            KfA, VtA = Kf[par], Vt[par]
            b_KfA, b_VtA = b_Kf[par], b_Vt[par]

            for qi, slots in enumerate(QUADS):
                for ci, s_ in enumerate(slots):
                    @blk.tensor
                    def _(t, s_=s_, ci=ci, h=h):
                        k.pre(t, "T", reads=[b_qrot, b_id], writes=[b_ptr[ci % 2]])
                        k.post(t.transpose(ptr[ci % 2][:],
                                           e["qrot"][:, s_ * D + h * 128: s_ * D + (h + 1) * 128],
                                           e["ident_s"][:]), "T", [b_qrot, b_id],
                               [b_ptr[ci % 2]])

                    @blk.scalar
                    def _(s, ci=ci):
                        k.pre(s, "A", reads=[b_ptr[ci % 2]], writes=[b_qfq])
                        k.post(s.activation(qfq[:, ci * 128:(ci + 1) * 128], ptr[ci % 2][:],
                                            AF.Identity), "A", [b_ptr[ci % 2]], [b_qfq])
                    yield

                maxc = (slots[0] + 1) * 8

                def emit_sct(j):
                    nval = sum(1 for s_ in slots if (s_ + 1) * 8 > j)
                    ncol = nval * 128
                    pa = j % 2

                    @blk.tensor
                    def _(t, j=j, pa=pa, ncol=ncol):
                        k.pre(t, "T", reads=[b_KfA, b_qfq], writes=[b_psc[pa]])
                        t.ldweights(KfA[:, j * 128:(j + 1) * 128])
                        k.post(t.matmul(psc[pa][:, 0:ncol], KfA[:, j * 128:(j + 1) * 128],
                                        qfq[:, 0:ncol], start=True, stop=True,
                                        skip_group_check=True),
                               "T", [b_KfA, b_qfq], [b_psc[pa]])

                def emit_tail(j):
                    nval = sum(1 for s_ in slots if (s_ + 1) * 8 > j)
                    ncol = nval * 128
                    pa = j % 2
                    sb_ = j // 8
                    boundary = sb_ in slots and slots.index(sb_) == nval - 1

                    if not boundary:
                        @blk.scalar
                        def _(s, j=j, pa=pa, ncol=ncol):
                            k.pre(s, "A", reads=[b_psc[pa]], writes=[b_PT[pa]])
                            k.post(s.activation(PT[pa][:, 0:ncol], psc[pa][:, 0:ncol],
                                                AF.Exp, scale=SCALE),
                                   "A", [b_psc[pa]], [b_PT[pa]])
                    else:
                        @blk.scalar
                        def _(s, j=j, pa=pa, ncol=ncol):
                            k.pre(s, "A", reads=[b_psc[pa]], writes=[b_PT[pa], b_ptmpb[pa]])
                            ins = None
                            if ncol > 128:
                                ins = s.activation(PT[pa][:, 0:ncol - 128],
                                                   psc[pa][:, 0:ncol - 128], AF.Exp,
                                                   scale=SCALE)
                            ins2 = s.activation(ptmpb[pa][:], psc[pa][:, ncol - 128:ncol],
                                                AF.Exp, scale=SCALE)
                            if ins is not None:
                                k.post(ins, "A", [b_psc[pa]], [b_PT[pa]])
                            k.post(ins2, "A", [b_psc[pa]], [b_ptmpb[pa]])

                        @blk.vector
                        def _(v, j=j, pa=pa, ncol=ncol):
                            k.pre(v, "V", reads=[b_ptmpb[pa], b_bmk], writes=[b_PT[pa]])
                            jm = j % 8
                            k.post(v.tensor_tensor(PT[pa][:, ncol - 128:ncol], ptmpb[pa][:],
                                                   e["bmk"][:, jm * 128:(jm + 1) * 128],
                                                   ALU.mult), "V", [b_ptmpb[pa], b_bmk],
                                   [b_PT[pa]])

                    @blk.tensor
                    def _(t, j=j, pa=pa, ncol=ncol, maxc=maxc):
                        k.pre(t, "T", reads=[b_PT[pa], b_VtA, b_ones,
                                             e["B"]("ones128")[0]], writes=[b_psAV, b_psZr])
                        t.ldweights(VtA[:, j * 128:(j + 1) * 128])
                        k.post(t.matmul(psAV[:, 0:ncol], VtA[:, j * 128:(j + 1) * 128],
                                        PT[pa][:, 0:ncol], start=(j == 0),
                                        stop=(j == maxc - 1), skip_group_check=True),
                               "T", [b_PT[pa], b_VtA], [b_psAV])
                        t.ldweights(e["ones128"][:])
                        k.post(t.matmul(psZr[:, 0:ncol], e["ones128"][:],
                                        PT[pa][:, 0:ncol], start=(j == 0),
                                        stop=(j == maxc - 1), skip_group_check=True),
                               "T", [b_PT[pa], e["B"]("ones128")[0]], [b_psZr])

                emit_sct(0)
                for j in range(maxc):
                    if j + 1 < maxc:
                        emit_sct(j + 1)
                    emit_tail(j)
                    yield

                @blk.vector
                def _(v):
                    k.pre(v, "V", reads=[b_psZr], writes=[b_zbc])
                    k.post(v.reciprocal(zbc[:], psZr[:, 0:512]), "V", [b_psZr], [b_zbc])

                @blk.vector
                def _(v, qi=qi):
                    k.pre(v, "V", reads=[b_psAV, b_zbc], writes=[b_attnsb])
                    k.post(v.tensor_tensor(attn_sb[:, qi * 512:(qi + 1) * 512],
                                           psAV[:], zbc[:], ALU.mult),
                           "V", [b_psAV, b_zbc], [b_attnsb])
                yield

            @blk.scalar
            def _(s, h=h):
                k.pre(s, "SS", reads=[b_attnsb], writes=[b_attnd])
                k.post(s.dma_start(out=e["attn_d"][h, :, :], in_=attn_sb[:]),
                       "SS", [b_attnsb], [b_attnd])
            yield

        for _ in a_emit(0):
            pass

        if e["dbg_kf"] is not None:
            @blk.scalar
            def _(s):
                k.pre(s, "SS", reads=[b_Kf[0], b_Vt[0]])
                k.post(s.dma_start(out=e["dbg_kf"][:, :], in_=Kf[0][:]), "SS", [b_Kf[0]], [])
                k.post(s.dma_start(out=e["dbg_vt"][:, :], in_=Vt[0][:]), "SS", [b_Vt[0]], [])

        for h in range(NH):
            ag = a_emit(h + 1) if h + 1 < NH else None
            bi = 0
            for _ in b_emit(h):
                bi += 1
                if ag is not None and bi % 2 == 0:
                    next(ag, None)
            if ag is not None:
                for _ in ag:
                    pass


def p3(k, blk, e):
    nc = k.nc
    B = e["B"]
    b_x1d = B("x1_d")[0]
    b_attnd = B("attn_d")[0]

    with contextlib.ExitStack() as es3:
        sb = lambda n, s, d: es3.enter_context(nc.sbuf_tensor(n, s, d))
        psa = lambda n, s, d: es3.enter_context(nc.psum_tensor(n, s, d))
        wob = [sb(f"p3_wob{i}", [128, D], BF16) for i in range(2)]
        attn_ld = [sb(f"p3_attnld{i}", [128, 32 * 128], BF16) for i in range(2)]
        xot = [sb(f"p3_xot{i}", [128, D], F32) for i in range(2)]
        vsl = sb("p3_v", [128, D], F32)
        sqs = sb("p3_sq", [128, D], F32)
        x1t = sb("p3_x1t", [128, D], F32)
        cols = sb("p3_cols", [128, 8], F32)
        ps_wo = [psa(f"p3_pswo{i}", [128, 2048], F32) for i in range(2)]
        b_wob = B("p3_wob0", "p3_wob1")
        b_attnld = B("p3_attnld0", "p3_attnld1")
        b_xot = B("p3_xot0", "p3_xot1")
        b_v, b_sq, b_cols = B("p3_v", "p3_sq", "p3_cols")
        b_pswo = B("p3_pswo0", "p3_pswo1")
        wo_r = e["wo_"].rearrange("(f p) n -> f p n", p=128)

        for sp in range(4):
            slots = [2 * sp, 2 * sp + 1]
            spa = sp % 2

            @blk.sync
            def _(sy, spa=spa, slots=slots):
                k.pre(sy, "SH", reads=[b_attnd], writes=[b_attnld[spa]])
                for si, s_ in enumerate(slots):
                    qi, ci = QCOL[s_]
                    col = qi * 512 + ci * 128
                    k.post(sy.dma_start(
                        out=attn_ld[spa][:].rearrange("p (f two c) -> p f two c",
                                                      two=2, c=128)[:, :, si, :],
                        in_=e["attn_d"][:, :, col:col + 128].rearrange("f p c -> p f c")),
                        "SH", [b_attnd], [b_attnld[spa]])

            for f in range(16):
                pa = f % 2

                @blk.gpsimd
                def _(g, f=f, pa=pa):
                    k.pre(g, "SG", writes=[b_wob[pa]])
                    k.post(g.dma_start(out=wob[pa][:], in_=wo_r[f, :, :]), "SG", [], [b_wob[pa]])

                @blk.tensor
                def _(t, f=f, pa=pa, spa=spa, slots=slots):
                    k.pre(t, "T", reads=[b_wob[pa], b_attnld[spa]], writes=list(b_pswo))
                    ins = None
                    for si, s_ in enumerate(slots):
                        xs = attn_ld[spa][:, (f * 2 + si) * 128:(f * 2 + si + 1) * 128]
                        t.ldweights(xs)
                        for cb in range(4):
                            ins = t.matmul(ps_wo[si][:, cb * 512:(cb + 1) * 512],
                                           xs,
                                           wob[pa][:, cb * 512:(cb + 1) * 512],
                                           start=(f == 0), stop=(f == 15),
                                           skip_group_check=True)
                    k.post(ins, "T", [b_wob[pa], b_attnld[spa]], list(b_pswo))

            for si, s_ in enumerate(slots):
                @blk.sync
                def _(sy, si=si, s_=s_):
                    k.pre(sy, "SH", writes=[b_xot[si]])
                    k.post(sy.dma_start(out=xot[si][:], in_=e["xo"][s_, :, :]),
                           "SH", [], [b_xot[si]])

                @blk.vector
                def _(v, si=si):
                    k.pre(v, "V", reads=[b_pswo[si], b_xot[si]], writes=[b_v, b_cols])
                    k.post(v.scalar_tensor_tensor(vsl[:], ps_wo[si][:], 0.0, xot[si][:],
                                                  ALU.add, ALU.add, accum_out=cols[:, 0:1]),
                           "V", [b_pswo[si], b_xot[si]], [b_v, b_cols])

                b_x1t = B("p3_x1t")[0]
                ln_epilogue(k, blk, e, vsl, b_v, sqs, b_sq, cols, b_cols,
                            x1t[:], b_x1t, D)

                @blk.scalar
                def _(s, s_=s_):
                    k.pre(s, "SS", reads=[b_x1t], writes=[b_x1d])
                    k.post(s.dma_start(out=e["x1_d"][s_, :, :], in_=x1t[:]),
                           "SS", [b_x1t], [b_x1d])


def ln_epilogue(k, blk, e, vsl, b_v, sqs, b_sq, cols, b_cols, out_ap, b_out, dim):
    @blk.scalar
    def _(s):
        k.pre(s, "A", reads=[b_v], writes=[b_sq, b_cols])
        k.post(s.activation(sqs[:], vsl[:], AF.Square, accum_out=cols[:, 1:2]),
               "A", [b_v], [b_sq, b_cols])

    @blk.vector
    def _(v):
        k.pre(v, "V", reads=[b_cols], writes=[b_cols])
        k.post(v.tensor_scalar(cols[:, 2:3], cols[:, 0:1], 1.0 / dim, None, ALU.mult),
               "V", [b_cols], [b_cols])
        k.pre(v, "V", reads=[b_cols], writes=[b_cols])
        k.post(v.tensor_tensor(cols[:, 3:4], cols[:, 2:3], cols[:, 2:3], ALU.mult),
               "V", [b_cols], [b_cols])
        k.pre(v, "V", reads=[b_cols], writes=[b_cols])
        k.post(v.scalar_tensor_tensor(cols[:, 4:5], cols[:, 1:2], 1.0 / dim, cols[:, 3:4],
                                      ALU.mult, ALU.subtract), "V", [b_cols], [b_cols])
        k.pre(v, "V", reads=[b_cols], writes=[b_cols])
        k.post(v.tensor_scalar(cols[:, 5:6], cols[:, 4:5], 1e-5, None, ALU.add),
               "V", [b_cols], [b_cols])
        k.pre(v, "V", reads=[b_cols], writes=[b_cols])
        k.post(v.reciprocal(cols[:, 6:7], cols[:, 5:6]), "V", [b_cols], [b_cols])

    @blk.scalar
    def _(s):
        k.pre(s, "A", reads=[b_cols], writes=[b_cols])
        k.post(s.activation(cols[:, 7:8], cols[:, 6:7], AF.Sqrt), "A", [b_cols], [b_cols])

    @blk.vector
    def _(v):
        k.pre(v, "V", reads=[b_cols], writes=[b_cols])
        k.post(v.scalar_tensor_tensor(cols[:, 3:4], cols[:, 2:3], -1.0, cols[:, 7:8],
                                      ALU.mult, ALU.mult), "V", [b_cols], [b_cols])

    @blk.scalar
    def _(s):
        k.pre(s, "A", reads=[b_v, b_cols], writes=[b_out])
        k.post(s.activation(out_ap, vsl[:], AF.Identity, bias=cols[:, 3:4],
                            scale=cols[:, 7:8]), "A", [b_v, b_cols], [b_out])


def p4(k, blk, e):
    nc = k.nc
    B = e["B"]
    b_x1d = B("x1_d")[0]
    b_id = B("ident")[0]
    b_mwb = B("mwb0", "mwb1", "mwb2")
    b_thr = B("thr0", "thr1", "thr2")
    b_nth = B("nthr0", "nthr1", "nthr2")
    b_w1qn, b_w2qn, b_w3qn = B("w1qn", "w2qn", "w3qn")
    b_hqd, b_hqtd, b_out = B("hq_d", "hqt_d", "out_d")

    with contextlib.ExitStack() as es4:
        sb = lambda n, s, d: es4.enter_context(nc.sbuf_tensor(n, s, d))
        xqT = sb("p4_xqT", [128, 16 * NSLOT * 128], BF16)
        msc1 = sb("p4_msc1", [128, NSLOT], F32)
        msc2 = sb("p4_msc2", [128, NSLOT], F32)
        msc3 = sb("p4_msc3", [128, NSLOT], F32)
        ccols = sb("p4_cols", [128, 8], F32)
        posb = sb("p4_pos", [128, 512], BF16)
        b_xqT, b_ms1, b_ms2, b_ms3, b_cc4 = B(
            "p4_xqT", "p4_msc1", "p4_msc2", "p4_msc3", "p4_cols")
        b_xqt, b_n4, b_r4, b_t4, b_sq4 = B("p4_xqtok", "p4_n", "p4_r", "p4_t", "p4_sq")
        b_pos = B("p4_pos")[0]

        with contextlib.ExitStack() as esp:
            sba = lambda n, s, d: esp.enter_context(nc.sbuf_tensor(n, s, d))
            xq_tok = sba("p4_xqtok", [128, D], BF16)
            ni32 = sba("p4_n", [128, D], I32)
            rbuf = sba("p4_r", [128, D], F32)
            tbuf = sba("p4_t", [128, D], F32)
            sqs = sba("p4_sq", [128, D], F32)
            x1row = sba("p4_x1row", [128, D], F32)
            b_x1r = B("p4_x1row")[0]
            ptr = [esp.enter_context(nc.psum_tensor(f"p4_ptr{i}", [128, 128], BF16))
                   for i in range(8)]
            b_ptr = B(*[f"p4_ptr{i}" for i in range(8)])
            for s_ in range(NSLOT):
                @blk.sync
                def _(sy, s_=s_):
                    k.pre(sy, "SH", reads=[b_x1d], writes=[b_x1r])
                    k.post(sy.dma_start(out=x1row[:], in_=e["x1_d"][s_, :, :]),
                           "SH", [b_x1d], [b_x1r])
                x1s = x1row[:]

                @blk.vector
                def _(v, x1s=x1s):
                    k.pre(v, "V", reads=[b_x1r], writes=[b_cc4])
                    k.post(v.reduce_max(ccols[:, 0:1], x1s, axis=AX,
                                        apply_absolute_value=True), "V", [b_x1r], [b_cc4])

                @blk.scalar
                def _(s, x1s=x1s):
                    k.pre(s, "A", reads=[b_x1r], writes=[b_sq4, b_cc4])
                    k.post(s.activation(sqs[:], x1s, AF.Square, accum_out=ccols[:, 1:2]),
                           "A", [b_x1r], [b_sq4, b_cc4])

                quant_scale_chain(k, blk, e, ccols, b_cc4, msc1, b_ms1, msc2, b_ms2, s_,
                                  D, b_mwb[0], b_mwb[1], e["mwb"][0], e["mwb"][1])

                @blk.scalar
                def _(s, x1s=x1s):
                    k.pre(s, "A", reads=[b_x1r, b_cc4], writes=[b_n4])
                    k.post(s.activation(ni32[:], x1s, AF.Identity, scale=ccols[:, 7:8]),
                           "A", [b_x1r, b_cc4], [b_n4])

                @blk.vector
                def _(v, x1s=x1s):
                    k.pre(v, "V", reads=[b_x1r, b_cc4, b_n4], writes=[b_r4])
                    k.post(v.scalar_tensor_tensor(rbuf[:], x1s, ccols[:, 7:8], ni32[:],
                                                  ALU.mult, ALU.subtract),
                           "V", [b_x1r, b_cc4, b_n4], [b_r4])
                    k.pre(v, "V", reads=[b_r4, b_n4], writes=[b_t4])
                    k.post(v.scalar_tensor_tensor(tbuf[:], rbuf[:], 0.5, ni32[:],
                                                  ALU.is_ge, ALU.add), "V", [b_r4, b_n4], [b_t4])
                    k.pre(v, "V", reads=[b_r4, b_t4], writes=[b_xqt])
                    k.post(v.scalar_tensor_tensor(xq_tok[:], rbuf[:], -0.5, tbuf[:],
                                                  ALU.is_le, ALU.subtract),
                           "V", [b_r4, b_t4], [b_xqt])

                for half in range(2):
                    @blk.tensor
                    def _(t, half=half):
                        k.pre(t, "T", reads=[b_xqt, b_id], writes=b_ptr)
                        for i in range(8):
                            kd = half * 8 + i
                            k.post(t.transpose(ptr[i][:], xq_tok[:, kd * 128:(kd + 1) * 128],
                                               e["ident_s"][:]), "T", [b_xqt, b_id], [b_ptr[i]])

                    @blk.vector
                    def _(v, s_=s_, half=half):
                        k.pre(v, "V", reads=b_ptr, writes=[b_xqT])
                        ins = None
                        for i in range(8):
                            kd = half * 8 + i
                            ins = v.tensor_copy(
                                xqT[:, kd * 1024 + s_ * 128: kd * 1024 + s_ * 128 + 128],
                                ptr[i][:])
                        k.post(ins, "V", b_ptr, [b_xqT])

        k.barrier(blk)
        es_b = contextlib.ExitStack()
        sbb = lambda n, s, d: es_b.enter_context(nc.sbuf_tensor(n, s, d))
        w1qb = sbb("p4_w1qb", [128, 16 * 512], BF16)
        w2qb = sbb("p4_w2qb", [128, 16 * 512], BF16)
        gtmp = sbb("p4_gtmp", [128, 512], F32)
        utmp = sbb("p4_utmp", [128, 512], F32)
        hqc = sbb("p4_hqc", [128, 512], F32)
        b_w1q, b_w2q, b_g, b_u, b_hqc = B("p4_w1qb", "p4_w2qb", "p4_g", "p4_u", "p4_hqc")

        with contextlib.ExitStack() as esp:
            psGU = [esp.enter_context(nc.psum_tensor(f"p4_gu{i}", [128, 512], F32))
                    for i in range(8)]
            b_gu = B(*[f"p4_gu{i}" for i in range(8)])
            for hb in range(11):
                for wsel, (wqn_, qdst, b_qdst) in enumerate(
                        [(e["w1qn"], w1qb, b_w1q), (e["w2qn"], w2qb, b_w2q)]):
                    b_wnx = b_w1qn if wsel == 0 else b_w2qn
                    for kd in range(16):
                        @blk.sync
                        def _(sy, wqn_=wqn_, hb=hb, kd=kd, qdst=qdst, b_qdst=b_qdst,
                              b_wnx=b_wnx):
                            k.pre(sy, "SH", reads=[b_wnx], writes=[b_qdst])
                            k.post(sy.dma_start_transpose(
                                qdst[:, kd * 512:(kd + 1) * 512],
                                wqn_[hb * 512:(hb + 1) * 512, kd * 128:(kd + 1) * 128]),
                                "SH", [b_wnx], [b_qdst])

                for half in range(2):
                    slots = [half * 4 + i for i in range(4)]

                    @blk.tensor
                    def _(t, slots=slots):
                        k.pre(t, "T", reads=[b_w1q, b_w2q, b_xqT], writes=list(b_gu))
                        ins = None
                        for kd in range(16):
                            for si, s_ in enumerate(slots):
                                xs = xqT[:, kd * 1024 + s_ * 128: kd * 1024 + s_ * 128 + 128]
                                t.ldweights(xs)
                                t.matmul(psGU[si][:], xs, w1qb[:, kd * 512:(kd + 1) * 512],
                                         start=(kd == 0), stop=(kd == 15), skip_group_check=True)
                                ins = t.matmul(psGU[4 + si][:], xs,
                                               w2qb[:, kd * 512:(kd + 1) * 512],
                                               start=(kd == 0), stop=(kd == 15),
                                               skip_group_check=True)
                        k.post(ins, "T", [b_w1q, b_w2q, b_xqT], list(b_gu))

                    for si, s_ in enumerate(slots):
                        @blk.scalar
                        def _(s, si=si, s_=s_):
                            k.pre(s, "A", reads=[b_gu[si], b_ms1], writes=[b_g])
                            k.post(s.activation(gtmp[:], psGU[si][:], AF.Silu,
                                                scale=msc1[:, s_:s_ + 1]),
                                   "A", [b_gu[si], b_ms1], [b_g])
                            k.pre(s, "A", reads=[b_gu[4 + si], b_ms2], writes=[b_u])
                            k.post(s.activation(utmp[:], psGU[4 + si][:], AF.Identity,
                                                scale=msc2[:, s_:s_ + 1]),
                                   "A", [b_gu[4 + si], b_ms2], [b_u])

                        @blk.vector
                        def _(v):
                            k.pre(v, "V", reads=[b_g, b_u], writes=[b_hqc])
                            k.post(v.tensor_tensor(hqc[:], gtmp[:], utmp[:], ALU.mult),
                                   "V", [b_g, b_u], [b_hqc])

                        @blk.scalar
                        def _(s, s_=s_, hb=hb):
                            k.pre(s, "SS", reads=[b_hqc], writes=[b_hqd])
                            k.post(s.dma_start(out=e["hq_d"][s_, :, hb * 512:(hb + 1) * 512],
                                               in_=hqc[:]), "SS", [b_hqc], [b_hqd])

        es_b.close()
        k.barrier(blk)
        es_c = contextlib.ExitStack()
        sbc = lambda n, s, d: es_c.enter_context(nc.sbuf_tensor(n, s, d))
        HC4 = HID // 4
        hrow = sbc("p4_hrow", [128, HID], F32)
        hsq = sbc("p4_hsq", [128, HC4], F32)
        hn = sbc("p4_hn", [128, HC4], I32)
        hr = sbc("p4_hr", [128, HC4], F32)
        ht = sbc("p4_ht", [128, HC4], F32)
        hqq = sbc("p4_hqq", [128, HID], BF16)
        hstage = sbc("p4_hstage", [128, 44 * 128], BF16)
        hsums = sbc("p4_hsums", [128, 4], F32)
        b_hrow, b_hsq, b_hn, b_hr, b_ht, b_hqq, b_hst, b_hsu = B(
            "p4_hrow", "p4_hsq", "p4_hn", "p4_hr", "p4_ht", "p4_hqq", "p4_hstage",
            "p4_hsums")

        with contextlib.ExitStack() as esp:
            ptr = [esp.enter_context(nc.psum_tensor(f"p4c_ptr{i}", [128, 128], BF16))
                   for i in range(8)]
            b_ptr = B(*[f"p4c_ptr{i}" for i in range(8)])
            for s_ in range(NSLOT):
                @blk.sync
                def _(sy, s_=s_):
                    k.pre(sy, "SH", reads=[b_hqd], writes=[b_hrow])
                    k.post(sy.dma_start(out=hrow[:], in_=e["hq_d"][s_, :, :]),
                           "SH", [b_hqd], [b_hrow])

                @blk.vector
                def _(v):
                    k.pre(v, "V", reads=[b_hrow], writes=[b_cc4])
                    k.post(v.reduce_max(ccols[:, 0:1], hrow[:], axis=AX,
                                        apply_absolute_value=True), "V", [b_hrow], [b_cc4])

                for c4 in range(4):
                    @blk.scalar
                    def _(s, c4=c4):
                        k.pre(s, "A", reads=[b_hrow], writes=[b_hsq, b_hsu])
                        k.post(s.activation(hsq[:], hrow[:, c4 * HC4:(c4 + 1) * HC4],
                                            AF.Square, accum_out=hsums[:, c4:c4 + 1]),
                               "A", [b_hrow], [b_hsq, b_hsu])

                @blk.vector
                def _(v):
                    k.pre(v, "V", reads=[b_hsu], writes=[b_cc4])
                    k.post(v.reduce_sum(ccols[:, 1:2], hsums[:], axis=AX),
                           "V", [b_hsu], [b_cc4])

                quant_scale_chain(k, blk, e, ccols, b_cc4, msc3, b_ms3, None, None, s_,
                                  HID, b_mwb[2], None, e["mwb"][2], None)

                for c4 in range(4):
                    sl = slice(c4 * HC4, (c4 + 1) * HC4)

                    @blk.scalar
                    def _(s, sl=sl):
                        k.pre(s, "A", reads=[b_hrow, b_cc4], writes=[b_hn])
                        k.post(s.activation(hn[:], hrow[:, sl], AF.Identity,
                                            scale=ccols[:, 7:8]), "A", [b_hrow, b_cc4], [b_hn])

                    @blk.vector
                    def _(v, sl=sl):
                        k.pre(v, "V", reads=[b_hrow, b_cc4, b_hn], writes=[b_hr])
                        k.post(v.scalar_tensor_tensor(hr[:], hrow[:, sl], ccols[:, 7:8], hn[:],
                                                      ALU.mult, ALU.subtract),
                               "V", [b_hrow, b_cc4, b_hn], [b_hr])
                        k.pre(v, "V", reads=[b_hr, b_hn], writes=[b_ht])
                        k.post(v.scalar_tensor_tensor(ht[:], hr[:], 0.5, hn[:],
                                                      ALU.is_ge, ALU.add), "V",
                               [b_hr, b_hn], [b_ht])
                        k.pre(v, "V", reads=[b_hr, b_ht], writes=[b_hqq])
                        k.post(v.scalar_tensor_tensor(hqq[:, sl], hr[:], -0.5, ht[:],
                                                      ALU.is_le, ALU.subtract),
                               "V", [b_hr, b_ht], [b_hqq])

                for gb in range(6):
                    n = 8 if gb < 5 else 4

                    @blk.tensor
                    def _(t, gb=gb, n=n):
                        k.pre(t, "T", reads=[b_hqq, b_id], writes=b_ptr[:n])
                        for i in range(n):
                            hc = gb * 8 + i
                            k.post(t.transpose(ptr[i][:], hqq[:, hc * 128:(hc + 1) * 128],
                                               e["ident_s"][:]), "T", [b_hqq, b_id], [b_ptr[i]])

                    @blk.vector
                    def _(v, gb=gb, n=n):
                        k.pre(v, "V", reads=b_ptr[:n], writes=[b_hst])
                        ins = None
                        for i in range(n):
                            hc = gb * 8 + i
                            ins = v.tensor_copy(hstage[:, hc * 128:(hc + 1) * 128], ptr[i][:])
                        k.post(ins, "V", b_ptr[:n], [b_hst])

                @blk.scalar
                def _(s, s_=s_):
                    k.pre(s, "SS", reads=[b_hst], writes=[b_hqtd])
                    k.post(s.dma_start(
                        out=e["hqt_d"][:, :, s_ * 128:(s_ + 1) * 128].rearrange(
                            "c p t -> p c t"),
                        in_=hstage[:].rearrange("p (c t) -> p c t", c=44)),
                        "SS", [b_hst], [b_hqtd])

        es_c.close()
        k.barrier(blk)
        hqtb = [sb(f"p4_hqtb{i}", [128, NSLOT * 128], BF16) for i in range(2)]
        w3qc = [sb(f"p4_w3qc{i}", [128, D], BF16) for i in range(2)]
        vsl = sb("p4b_v", [128, D], F32)
        sq2 = sb("p4b_sq", [128, D], F32)
        outs = sb("p4b_out", [128, D], F32)
        x1b = sb("p4b_x1", [128, D], F32)
        b_hqtb = B("p4_hqtb0", "p4_hqtb1")
        b_w3qc = B("p4_w3qc0", "p4_w3qc1")
        b_v4, b_sq2, b_os = B("p4b_v", "p4b_sq", "p4b_out")
        b_x1b = B("p4b_x1")[0]

        with contextlib.ExitStack() as esp:
            psZ = [esp.enter_context(nc.psum_tensor(f"p4_psz{i}", [128, 2048], F32))
                   for i in range(2)]
            b_psZ = B("p4_psz0", "p4_psz1")
            for sp in range(4):
                slots = [2 * sp, 2 * sp + 1]
                for hc in range(44):
                    pa = hc % 2

                    b_w3qt = B("w3qt")[0]

                    @blk.sync
                    def _(sy, hc=hc, pa=pa, sp=sp):
                        k.pre(sy, "SH", reads=[b_hqtd, b_w3qn, b_w3qt],
                              writes=[b_hqtb[pa], b_w3qc[pa]])
                        k.post(sy.dma_start(out=hqtb[pa][:], in_=e["hqt_d"][hc, :, :]),
                               "SH", [b_hqtd], [b_hqtb[pa]])
                        if sp == 0:
                            for cb in range(4):
                                k.post(sy.dma_start_transpose(
                                    w3qc[pa][:, cb * 512:(cb + 1) * 512],
                                    e["w3qn"][cb * 512:(cb + 1) * 512,
                                              hc * 128:(hc + 1) * 128]),
                                    "SH", [b_w3qn], [b_w3qc[pa]])
                        else:
                            k.post(sy.dma_start(out=w3qc[pa][:], in_=e["w3qt"][hc, :, :]),
                                   "SH", [b_w3qt], [b_w3qc[pa]])

                    if sp == 0:
                        @blk.gpsimd
                        def _(g, hc=hc, pa=pa):
                            k.pre(g, "SG", reads=[b_w3qc[pa]], writes=[b_w3qt])
                            k.post(g.dma_start(out=e["w3qt"][hc, :, :], in_=w3qc[pa][:]),
                                   "SG", [b_w3qc[pa]], [b_w3qt])

                    @blk.tensor
                    def _(t, hc=hc, pa=pa, slots=slots):
                        k.pre(t, "T", reads=[b_hqtb[pa], b_w3qc[pa]], writes=list(b_psZ))
                        ins = None
                        for si, s_ in enumerate(slots):
                            xs = hqtb[pa][:, s_ * 128:(s_ + 1) * 128]
                            t.ldweights(xs)
                            for cb in range(4):
                                ins = t.matmul(psZ[si][:, cb * 512:(cb + 1) * 512], xs,
                                               w3qc[pa][:, cb * 512:(cb + 1) * 512],
                                               start=(hc == 0), stop=(hc == 43),
                                               skip_group_check=True)
                        k.post(ins, "T", [b_hqtb[pa], b_w3qc[pa]], list(b_psZ))

                for si, s_ in enumerate(slots):
                    @blk.sync
                    def _(sy, s_=s_):
                        k.pre(sy, "SH", reads=[b_x1d], writes=[b_x1b])
                        k.post(sy.dma_start(out=x1b[:], in_=e["x1_d"][s_, :, :]),
                               "SH", [b_x1d], [b_x1b])

                    @blk.vector
                    def _(v, si=si, s_=s_):
                        k.pre(v, "V", reads=[b_psZ[si], b_ms3, b_x1b], writes=[b_v4, b_cc4])
                        k.post(v.scalar_tensor_tensor(
                            vsl[:], psZ[si][:], msc3[:, s_:s_ + 1],
                            x1b[:], ALU.mult, ALU.add,
                            accum_out=ccols[:, 0:1]),
                            "V", [b_psZ[si], b_ms3, b_x1b], [b_v4, b_cc4])

                    ln_epilogue(k, blk, e, vsl, b_v4, sq2, b_sq2, ccols, b_cc4,
                                outs[:], b_os, D)

                    @blk.scalar
                    def _(s, s_=s_):
                        k.pre(s, "SS", reads=[b_os], writes=[b_out])
                        k.post(s.dma_start(out=e["out"][s_, :, :], in_=outs[:]),
                               "SS", [b_os], [b_out])


def quant_scale_chain(k, blk, e, ccols, b_cc, mscA, b_msA, mscB, b_msB, s_, dim,
                      b_mA, b_mB, mA, mB):
    @blk.vector
    def _(v):
        k.pre(v, "V", reads=[b_cc], writes=[b_cc])
        k.post(v.tensor_scalar(ccols[:, 2:3], ccols[:, 1:2], 1.0 / dim, 1e-6,
                               ALU.mult, ALU.add), "V", [b_cc], [b_cc])
        k.pre(v, "V", reads=[b_cc], writes=[b_cc])
        k.post(v.reciprocal(ccols[:, 3:4], ccols[:, 2:3]), "V", [b_cc], [b_cc])

    @blk.scalar
    def _(s):
        k.pre(s, "A", reads=[b_cc], writes=[b_cc])
        k.post(s.activation(ccols[:, 4:5], ccols[:, 3:4], AF.Sqrt), "A", [b_cc], [b_cc])

    @blk.vector
    def _(v, s_=s_):
        k.pre(v, "V", reads=[b_cc], writes=[b_cc])
        k.post(v.scalar_tensor_tensor(ccols[:, 5:6], ccols[:, 0:1], 1.0, ccols[:, 4:5],
                                      ALU.mult, ALU.mult), "V", [b_cc], [b_cc])
        k.pre(v, "V", reads=[b_cc], writes=[b_cc])
        k.post(v.tensor_scalar(ccols[:, 5:6], ccols[:, 5:6], 1e-5, None, ALU.max),
               "V", [b_cc], [b_cc])
        k.pre(v, "V", reads=[b_cc], writes=[b_cc])
        k.post(v.reciprocal(ccols[:, 6:7], ccols[:, 5:6]), "V", [b_cc], [b_cc])
        k.pre(v, "V", reads=[b_cc], writes=[b_cc])
        k.post(v.scalar_tensor_tensor(ccols[:, 7:8], ccols[:, 6:7], 127.0, ccols[:, 4:5],
                                      ALU.mult, ALU.mult), "V", [b_cc], [b_cc])
        k.pre(v, "V", reads=[b_cc], writes=[b_cc])
        k.post(v.tensor_scalar(ccols[:, 6:7], ccols[:, 5:6], 1.0 / 127.0, None, ALU.mult),
               "V", [b_cc], [b_cc])
        k.pre(v, "V", reads=[b_cc, b_mA], writes=[b_msA])
        k.post(v.scalar_tensor_tensor(mscA[:, s_:s_ + 1], ccols[:, 6:7], 1.0, mA[:],
                                      ALU.mult, ALU.mult), "V", [b_cc, b_mA], [b_msA])
        if mscB is not None:
            k.pre(v, "V", reads=[b_cc, b_mB], writes=[b_msB])
            k.post(v.scalar_tensor_tensor(mscB[:, s_:s_ + 1], ccols[:, 6:7], 1.0, mB[:],
                                          ALU.mult, ALU.mult), "V", [b_cc, b_mB], [b_msB])


def make_inputs(core, x, ws):
    tiles = [8 * s + core for s in range(NSLOT)]
    xo = np.stack([x[t * 128:(t + 1) * 128, :] for t in tiles])
    qpos_c = np.stack([np.arange(t * 128, (t + 1) * 128, dtype=np.float32) for t in tiles],
                      axis=1)
    qrel_c = (np.arange(128, dtype=np.float32) + core * 128)[:, None].astype(np.float32)
    return dict(xf=x, xo=xo, qpos_c=qpos_c, qrel_c=qrel_c,
                ident=np.eye(128, dtype=bfdt), **ws)



_NC_CACHE = {}


def _get_nc():
    if "nc" not in _NC_CACHE:
        _NC_CACHE["nc"] = build(dev=False)
    return _NC_CACHE["nc"]


def kernel(L0_attn_x, L0_attn_wq, L0_attn_w_down, L0_attn_w_up, L0_attn_wo,
           L0_ffn_w1, L0_ffn_w2, L0_ffn_w3):
    x = np.ascontiguousarray(np.asarray(L0_attn_x, dtype=np.float32).reshape(T, D))
    ws = dict(
        wq=np.ascontiguousarray(np.asarray(L0_attn_wq, dtype=np.float32)),
        wdn=np.ascontiguousarray(np.asarray(L0_attn_w_down, dtype=np.float32)),
        wup=np.ascontiguousarray(np.asarray(L0_attn_w_up, dtype=np.float32)),
        wo_=np.ascontiguousarray(np.asarray(L0_attn_wo, dtype=np.float32)),
        w1=np.ascontiguousarray(np.asarray(L0_ffn_w1, dtype=np.float32)),
        w2=np.ascontiguousarray(np.asarray(L0_ffn_w2, dtype=np.float32)),
        w3=np.ascontiguousarray(np.asarray(L0_ffn_w3, dtype=np.float32)),
    )
    nc = _get_nc()
    in_maps = [make_inputs(c, x, ws) for c in range(8)]
    res = run_bass_kernel_spmd(nc, in_maps, core_ids=list(range(8)))
    out = np.empty((T, D), dtype=np.float32)
    for c in range(8):
        oc = np.asarray(res.results[c]["out"])
        for s in range(NSLOT):
            t = 8 * s + c
            out[t * 128:(t + 1) * 128, :] = oc[s]
    return out.reshape(1, T, D)
